# version 10
# baseline (speedup 1.0000x reference)
import contextlib
import numpy as np
import ml_dtypes
import concourse.bass as bass
import concourse.mybir as mybir
from concourse.bass_utils import run_bass_kernel_spmd

F32 = mybir.dt.float32
BF16 = mybir.dt.bfloat16
I32 = mybir.dt.int32
AF = mybir.ActivationFunctionType
ALU = mybir.AluOpType
AX = mybir.AxisListType

NCORE = 8
D = 1024
FUSED = True
RNE_C = 12582912.0


class Prog:

    def __init__(self):
        self.steps = []
        self.pre = {}
        self.ctx = {}

    def sync(self, fn, fuse=False, wait=None):
        self.steps.append(("sync", fn, 16, fuse, wait))

    def gps_dma(self, fn, fuse=False, wait=None):
        self.steps.append(("gpsimd", fn, 16, fuse, wait))

    def tensor(self, fn, fuse=False, wait=None):
        self.steps.append(("tensor", fn, 1, fuse, wait))

    def vector(self, fn, fuse=False, wait=None):
        self.steps.append(("vector", fn, 1, fuse, wait))

    def scalar(self, fn, fuse=False, wait=None):
        self.steps.append(("scalar", fn, 1, fuse, wait))

    def mark(self):
        return ("mark", len(self.steps))

    def raw(self, engine, fn, inc):
        self.steps.append((engine, ("raw", fn), inc, False, None))

    def par_dma(self, mk):
        def fn(e, t0, sem):
            if t0 > 0:
                e.wait_ge(sem, t0)
            with e.If_eq(self.ctx["b_par"], 0):
                mk(0)(e).then_inc(sem, 16)
            with e.Else():
                mk(1)(e).then_inc(sem, 16)
        self.raw("gpsimd", fn, 16)

    def resolve(self, tok):
        return self._cum[tok[1]]

    def emit(self, block, sem):
        cum = [0] * (len(self.steps) + 1)
        for i, st in enumerate(self.steps):
            cum[i + 1] = cum[i] + st[2]
        self._cum = cum
        slots = []
        prev_eng = None
        for i, (eng, fn, inc, fuse, wait) in enumerate(self.steps):
            t0 = cum[i]
            if wait is not None:
                wv = cum[wait[1]]
            else:
                skip_wait = (prev_eng == eng) and (fuse or eng == "tensor")
                wv = 0 if skip_wait else t0
            slots.append((eng, fn, inc, t0, wv))
            prev_eng = eng
        total = cum[-1]

        def make_body(engname):
            def body(eng):
                if engname in self.pre:
                    self.pre[engname](eng)
                for name, fn, inc, t0, wv in slots:
                    if name != engname:
                        continue
                    if isinstance(fn, tuple) and fn[0] == "raw":
                        fn[1](eng, t0, sem)
                        continue
                    if wv > 0:
                        eng.wait_ge(sem, wv)
                    fn(eng).then_inc(sem, inc)
                if engname == "sync":
                    eng.wait_ge(sem, total)
            return body

        for name in ["sync", "gpsimd", "tensor", "vector", "scalar"]:
            getattr(block, name)(make_body(name))



def emit_ln(p, x3, out, stat_a, stat_b, big_a, nt, feat=D, eps=1e-5):
    big3 = big_a.rearrange("q (n f) -> q n f", f=feat)
    p.vector(lambda e: e.reduce_sum(stat_a, x3, axis=AX.X))
    p.vector(lambda e: e.tensor_scalar(stat_a, stat_a, 1.0 / feat, None, ALU.mult))
    p.vector(lambda e: e.tensor_tensor(
        out.rearrange("q n f -> q n f") if False else out, x3,
        stat_a.unsqueeze(2).broadcast_to(x3.shape), ALU.subtract))
    p.scalar(lambda e: e.activation(big3, out, AF.Square))
    p.vector(lambda e: e.reduce_sum(stat_b, big3, axis=AX.X))
    p.vector(lambda e: e.tensor_scalar(stat_b, stat_b, 1.0 / feat, eps, ALU.mult, ALU.add))
    p.scalar(lambda e: e.activation(stat_b, stat_b, AF.Sqrt))
    p.vector(lambda e: e.reciprocal(stat_b, stat_b))
    p.vector(lambda e: e.tensor_tensor(
        out, out, stat_b.unsqueeze(2).broadcast_to(x3.shape), ALU.mult))


def emit_quant_fp6(p, u, out_bf, bA, bB, bC, bD, dscr):
    F = u.shape[1]
    nb = F // 32
    u3 = u.rearrange("q (n k) -> q n k", k=32)
    p.vector(lambda e: e.reduce_max(dscr, u3, axis=AX.X, apply_absolute_value=True))
    p.vector(lambda e: e.tensor_scalar(dscr, dscr, 1.0 / 60.0, 1.25e-31, ALU.mult, ALU.max))
    ivs = bB[:, 0:nb]
    p.vector(lambda e: e.reciprocal(ivs, dscr))
    bA3 = bA.rearrange("q (n k) -> q n k", k=32)
    p.vector(lambda e: e.tensor_tensor(
        bA3, u3, ivs.unsqueeze(2).broadcast_to(u3.shape), ALU.mult))
    p.scalar(lambda e: e.activation(bC, bA, AF.Abs))
    p.vector(lambda e: e.tensor_scalar(bD, bC, 16.0, 1.0, ALU.is_ge, ALU.add))
    p.vector(lambda e: e.tensor_scalar(bB, bC, 32.0, 1.0, ALU.is_ge, ALU.add))
    p.vector(lambda e: e.tensor_tensor(bC, bD, bB, ALU.mult))
    p.vector(lambda e: e.tensor_scalar(bD, bD, -0.5, 1.5, ALU.mult, ALU.add))
    p.vector(lambda e: e.tensor_scalar(bB, bB, -0.5, 1.5, ALU.mult, ALU.add))
    p.vector(lambda e: e.tensor_tensor(bD, bD, bB, ALU.mult))
    p.vector(lambda e: e.tensor_tensor(bD, bA, bD, ALU.mult))
    p.vector(lambda e: e.tensor_scalar(bD, bD, RNE_C, RNE_C, ALU.add, ALU.subtract))
    p.vector(lambda e: e.tensor_tensor(bD, bD, bC, ALU.mult))
    bD3 = bD.rearrange("q (n k) -> q n k", k=32)
    out3 = out_bf.rearrange("q (n k) -> q n k", k=32)
    p.vector(lambda e: e.tensor_tensor(
        out3, bD3, dscr.unsqueeze(2).broadcast_to(bD3.shape), ALU.mult))


def emit_quant_fp4(p, u, out_bf, bA, bB, bC, bD, dscr):
    F = u.shape[1]
    nb = F // 16
    u3 = u.rearrange("q (n k) -> q n k", k=16)
    p.vector(lambda e: e.reduce_max(dscr, u3, axis=AX.X, apply_absolute_value=True))
    p.vector(lambda e: e.tensor_scalar(dscr, dscr, 1.0 / 12.0, 5e-13, ALU.mult, ALU.max))
    ivs = bB[:, 0:nb]
    p.vector(lambda e: e.reciprocal(ivs, dscr))
    bA3 = bA.rearrange("q (n k) -> q n k", k=16)
    p.vector(lambda e: e.tensor_tensor(
        bA3, u3, ivs.unsqueeze(2).broadcast_to(u3.shape), ALU.mult))
    p.scalar(lambda e: e.activation(bC, bA, AF.Abs))
    p.vector(lambda e: e.tensor_scalar(bD, bC, 4.0, 1.0, ALU.is_ge, ALU.add))
    p.vector(lambda e: e.tensor_scalar(bB, bC, 8.0, 1.0, ALU.is_ge, ALU.add))
    p.vector(lambda e: e.tensor_tensor(bA, bD, bB, ALU.mult))
    p.vector(lambda e: e.tensor_scalar(bD, bD, -0.5, 1.5, ALU.mult, ALU.add))
    p.vector(lambda e: e.tensor_scalar(bB, bB, -0.5, 1.5, ALU.mult, ALU.add))
    p.vector(lambda e: e.tensor_tensor(bD, bD, bB, ALU.mult))
    p.vector(lambda e: e.tensor_tensor(bD, bC, bD, ALU.mult))
    p.vector(lambda e: e.tensor_scalar(bC, bD, RNE_C, RNE_C, ALU.add, ALU.subtract))
    p.vector(lambda e: e.tensor_tensor(bB, bC, bD, ALU.subtract))
    p.vector(lambda e: e.tensor_scalar(bB, bB, 0.5, None, ALU.is_equal))
    p.vector(lambda e: e.tensor_tensor(bC, bC, bB, ALU.subtract))
    p.vector(lambda e: e.tensor_tensor(bC, bC, bA, ALU.mult))
    p.scalar(lambda e: e.activation(bD, u, AF.Sign))
    p.vector(lambda e: e.tensor_tensor(bC, bC, bD, ALU.mult))
    bC3 = bC.rearrange("q (n k) -> q n k", k=16)
    out3 = out_bf.rearrange("q (n k) -> q n k", k=16)
    p.vector(lambda e: e.tensor_tensor(
        out3, bC3, dscr.unsqueeze(2).broadcast_to(bC3.shape), ALU.mult))


def emit_transpose_blocks(p, src_bf, dst_bf, nblk):
    for j in range(nblk):
        sl = slice(j * 128, (j + 1) * 128)
        p.sync(lambda e, sl=sl: e.dma_start_transpose(dst_bf[:, sl], src_bf[:, sl]),
               fuse=(j > 0))



def build_ffn0(ntok):
    nc = bass.Bass("TRN2", target_bir_lowering=False, debug=False)
    x_in = nc.dram_tensor("x_in", [ntok, D], F32, kind="ExternalInput").ap()
    w_in = nc.dram_tensor("w_in", [D, D], F32, kind="ExternalInput").ap()
    b_in = nc.dram_tensor("b_in", [D], F32, kind="ExternalInput").ap()
    x_out = nc.dram_tensor("x_out", [ntok, D], F32, kind="ExternalOutput").ap()

    CH = 512
    assert ntok % CH == 0
    nch = ntok // CH
    TB = CH // 128

    p = Prog()
    es = contextlib.ExitStack()
    with es:
        sb = lambda name, shape, dt=F32: es.enter_context(nc.sbuf_tensor(name, shape, dt))[:]
        sX = sb("sX", [128, TB, D])
        sU = sb("sU", [128, TB * D])
        sUd = sb("sUd", [128, TB * D], BF16)
        sUdT = sb("sUdT", [128, TB * 8, 128], BF16)
        bA = sb("bA", [128, TB * D])
        bB = sb("bB", [128, TB * D])
        bC = sb("bC", [128, TB * D])
        bD = sb("bD", [128, TB * D])
        dscr = sb("dscr", [128, TB * D // 32])
        stat_a = sb("stat_a", [128, TB])
        stat_b = sb("stat_b", [128, TB])
        sWrow = sb("sWrow", [128, D])
        sWd = sb("sWd", [128, D], BF16)
        sWT = sb("sWT", [128, 8, D], BF16)
        sBias = sb("sBias", [1, D])
        sBiasb = sb("sBiasb", [1, D], BF16)
        sOnes = sb("sOnes", [1, 128], BF16)
        wq_dscr = sb("wq_dscr", [128, D // 32])
        pT = es.enter_context(nc.psum_tensor("pT", [128, D], F32))[:]
        sem = es.enter_context(nc.semaphore("sem"))
        block = es.enter_context(nc.Block())

        w_r = w_in.rearrange("(oc q) k -> oc q k", q=128)
        for oc in range(8):
            p.sync(lambda e, oc=oc: e.dma_start(out=sWrow, in_=w_r[oc]))
            emit_quant_fp6(p, sWrow, sWd, bA[:, 0:D], bB[:, 0:D], bC[:, 0:D],
                           bD[:, 0:D], wq_dscr)
            for kc in range(8):
                p.sync(lambda e, oc=oc, kc=kc: e.dma_start_transpose(
                    sWT[:, kc, oc * 128:(oc + 1) * 128],
                    sWd[:, kc * 128:(kc + 1) * 128]))
        p.sync(lambda e: e.dma_start(out=sBias, in_=b_in.rearrange("(o f) -> o f", o=1)))
        p.vector(lambda e: e.tensor_copy(sBiasb, sBias))
        p.vector(lambda e: e.memset(sOnes, 1.0))

        x_r = x_in.rearrange("(c tb q) f -> c q tb f", q=128, tb=TB)
        xo_r = x_out.rearrange("(c tb q) f -> c q tb f", q=128, tb=TB)
        for c in range(nch):
            p.sync(lambda e, c=c: e.dma_start(out=sX, in_=x_r[c]))
            emit_ln(p, sX, sU.rearrange("q (n f) -> q n f", f=D), stat_a, stat_b, bA, TB)
            emit_quant_fp6(p, sU, sUd, bA, bB, bC, bD, dscr)
            emit_transpose_blocks(p, sUd, sUdT.rearrange("q a b -> q (a b)"), TB * 8)
            for tb in range(TB):
                for half in range(2):
                    osl = slice(half * 512, (half + 1) * 512)
                    for kc in range(8):
                        p.tensor(lambda e, tb=tb, kc=kc, osl=osl: e.matmul(
                            pT[:, osl], sUdT[:, tb * 8 + kc, :], sWT[:, kc, osl],
                            start=(kc == 0), stop=False))
                    p.tensor(lambda e, osl=osl: e.matmul(
                        pT[:, osl], sOnes, sBiasb[:, osl], start=False, stop=True))
                p.vector(lambda e, tb=tb: e.scalar_tensor_tensor(
                    sX[:, tb, :], pT, 0.5, sX[:, tb, :], ALU.mult, ALU.add))
            p.sync(lambda e, c=c: e.dma_start(out=xo_r[c], in_=sX))

        p.emit(block, sem)
    return nc


def run_ffn0(x, w, b):
    ntok = x.shape[1]
    nc = build_ffn0(ntok)
    in_maps = [{"x_in": np.ascontiguousarray(x[i]), "w_in": w, "b_in": b}
               for i in range(x.shape[0])]
    res = run_bass_kernel_spmd(nc, in_maps, core_ids=list(range(x.shape[0])))
    return np.stack([r["x_out"] for r in res.results])


IFF = 2816


def build_ffn1(ntok):
    nc = bass.Bass("TRN2", target_bir_lowering=False, debug=False)
    x_in = nc.dram_tensor("x_in", [ntok, D], F32, kind="ExternalInput").ap()
    wg_in = nc.dram_tensor("wg_in", [IFF, D], F32, kind="ExternalInput").ap()
    wu_in = nc.dram_tensor("wu_in", [IFF, D], F32, kind="ExternalInput").ap()
    wd_in = nc.dram_tensor("wd_in", [D, IFF], F32, kind="ExternalInput").ap()
    x_out = nc.dram_tensor("x_out", [ntok, D], F32, kind="ExternalOutput").ap()
    hscr = nc.dram_tensor("hscr", [ntok // 128, 128, IFF], BF16).ap()

    nch = ntok // 128
    OSEG = [(s, min(s + 512, IFF)) for s in range(0, IFF, 512)]

    p = Prog()
    with contextlib.ExitStack() as es:
        sb = lambda name, shape, dt=F32: es.enter_context(nc.sbuf_tensor(name, shape, dt))[:]
        sX = sb("sX", [128, D])
        sU = sb("sU", [128, D])
        sUd = sb("sUd", [128, D], BF16)
        sUdT = sb("sUdT", [128, 8, 128], BF16)
        bA = sb("bA", [128, IFF])
        bB = sb("bB", [128, IFF])
        bC = sb("bC", [128, IFF])
        bD = sb("bD", [128, IFF])
        dscr6 = sb("dscr6", [128, D // 16])
        dscrH = sb("dscrH", [128, IFF // 16])
        stat_a = sb("stat_a", [128, 1])
        stat_b = sb("stat_b", [128, 1])
        sWrow = sb("sWrow", [128, IFF])
        sWq = sb("sWq", [128, IFF], BF16)
        sGate = sb("sGate", [128, IFF])
        sY = sb("sY", [128, IFF])
        sHq = sb("sHq", [128, IFF], BF16)
        sHqT = sb("sHqT", [128, 22, 128], BF16)
        pP = es.enter_context(nc.psum_tensor("pP", [128, 1024], F32))[:]
        sem = es.enter_context(nc.semaphore("sem"))
        block = es.enter_context(nc.Block())

        x_r = x_in.rearrange("(c q) f -> c q f", q=128)
        xo_r = x_out.rearrange("(c q) f -> c q f", q=128)

        def quant_weight_and_transpose(w_ap, n_oc, kdim, wT):
            w_r = w_ap.rearrange("(oc q) k -> oc q k", q=128)
            for oc in range(n_oc):
                p.sync(lambda e, oc=oc: e.dma_start(out=sWrow[:, 0:kdim], in_=w_r[oc]))
                emit_quant_fp4(p, sWrow[:, 0:kdim], sWq[:, 0:kdim],
                               bA[:, 0:kdim], bB[:, 0:kdim], bC[:, 0:kdim],
                               bD[:, 0:kdim], dscrH[:, 0:kdim // 16])
                for kc in range(kdim // 128):
                    p.sync(lambda e, oc=oc, kc=kc: e.dma_start_transpose(
                        wT[:, kc, oc * 128:(oc + 1) * 128],
                        sWq[:, kc * 128:(kc + 1) * 128]))

        with contextlib.ExitStack() as es2:
            wgT = es2.enter_context(nc.sbuf_tensor("wgT", [128, 8, IFF], BF16))[:]
            wuT = es2.enter_context(nc.sbuf_tensor("wuT", [128, 8, IFF], BF16))[:]
            quant_weight_and_transpose(wg_in, 22, D, wgT)
            quant_weight_and_transpose(wu_in, 22, D, wuT)

            for c in range(nch):
                p.sync(lambda e, c=c: e.dma_start(out=sX, in_=x_r[c]))
                emit_ln(p, sX.rearrange("q (n f) -> q n f", n=1), sU.rearrange("q (n f) -> q n f", n=1),
                        stat_a, stat_b, bA[:, 0:D], 1)
                emit_quant_fp4(p, sU, sUd, bA[:, 0:D], bB[:, 0:D], bC[:, 0:D],
                               bD[:, 0:D], dscr6)
                emit_transpose_blocks(p, sUd, sUdT.rearrange("q a b -> q (a b)"), 8)
                for dst, wT in ((sGate, wgT), (sY, wuT)):
                    for (o0, o1) in OSEG:
                        for kc in range(8):
                            p.tensor(lambda e, kc=kc, o0=o0, o1=o1, wT=wT: e.matmul(
                                pP[:, 0:o1 - o0], sUdT[:, kc, :], wT[:, kc, o0:o1],
                                start=(kc == 0), stop=(kc == 7)))
                        p.vector(lambda e, dst=dst, o0=o0, o1=o1: e.tensor_copy(
                            dst[:, o0:o1], pP[:, 0:o1 - o0]))
                p.scalar(lambda e: e.activation(sGate, sGate, AF.Silu))
                p.vector(lambda e: e.tensor_mul(sY, sY, sGate))
                emit_quant_fp4(p, sY, sHq, bA, bB, bC, bD, dscrH)
                emit_transpose_blocks(p, sHq, sHqT.rearrange("q a b -> q (a b)"), 22)
                p.sync(lambda e, c=c: e.dma_start(
                    out=hscr.rearrange("c q f -> c q f")[c],
                    in_=sHqT.rearrange("q a b -> q (a b)")))

        wdT = es.enter_context(nc.sbuf_tensor("wdT", [128, 22, D], BF16))[:]
        quant_weight_and_transpose(wd_in, 8, IFF, wdT)
        for c in range(nch):
            p.sync(lambda e, c=c: e.dma_start(
                out=sHqT.rearrange("q a b -> q (a b)"), in_=hscr[c]))
            p.sync(lambda e, c=c: e.dma_start(out=sX, in_=x_r[c]))
            for half in range(2):
                osl = slice(half * 512, (half + 1) * 512)
                for kc in range(22):
                    p.tensor(lambda e, kc=kc, osl=osl: e.matmul(
                        pP[:, osl], sHqT[:, kc, :], wdT[:, kc, osl],
                        start=(kc == 0), stop=(kc == 21)))
            p.vector(lambda e: e.scalar_tensor_tensor(
                sX, pP, 0.5, sX, ALU.mult, ALU.add))
            p.sync(lambda e, c=c: e.dma_start(out=xo_r[c], in_=sX))

        p.emit(block, sem)
    return nc


def run_ffn1(x, wg, wu, wd):
    ntok = x.shape[1]
    nc = build_ffn1(ntok)
    in_maps = [{"x_in": np.ascontiguousarray(x[i]), "wg_in": wg, "wu_in": wu,
                "wd_in": wd} for i in range(x.shape[0])]
    res = run_bass_kernel_spmd(nc, in_maps, core_ids=list(range(x.shape[0])))
    return np.stack([r["x_out"] for r in res.results])


def build_l1qkv(ntok):
    nc = bass.Bass("TRN2", target_bir_lowering=False, debug=False)
    x_in = nc.dram_tensor("x_in", [ntok, D], F32, kind="ExternalInput").ap()
    wqT_in = nc.dram_tensor("wqT_in", [D, D], F32, kind="ExternalInput").ap()
    wkT_in = nc.dram_tensor("wkT_in", [D, D], F32, kind="ExternalInput").ap()
    wvT_in = nc.dram_tensor("wvT_in", [D, D], F32, kind="ExternalInput").ap()
    qT_out = nc.dram_tensor("qT_out", [D, ntok], BF16, kind="ExternalOutput").ap()
    kT_out = nc.dram_tensor("kT_out", [D, ntok], BF16, kind="ExternalOutput").ap()
    v_out = nc.dram_tensor("v_out", [ntok, D], BF16, kind="ExternalOutput").ap()

    ntile = ntok // 128
    p = Prog()
    with contextlib.ExitStack() as es:
        sb = lambda name, shape, dt=F32: es.enter_context(nc.sbuf_tensor(name, shape, dt))[:]
        sX = sb("sX", [128, D])
        sU = sb("sU", [128, D])
        sUb = sb("sUb", [128, D], BF16)
        sUbT = sb("sUbT", [128, 8, 128], BF16)
        bA = sb("bA", [128, D])
        stat_a = sb("stat_a", [128, 1])
        stat_b = sb("stat_b", [128, 1])
        sWqT = sb("sWqT", [128, 8, D], BF16)
        sWkT = sb("sWkT", [128, 8, D], BF16)
        sWvT = sb("sWvT", [128, 8, D], BF16)
        sQTt = sb("sQTt", [128, 8, 128], BF16)
        sKTt = sb("sKTt", [128, 8, 128], BF16)
        sV = sb("sV", [128, D], BF16)
        pF = es.enter_context(nc.psum_tensor("pF", [128, 128], F32))[:]
        pV = es.enter_context(nc.psum_tensor("pV", [128, 1024], F32))[:]
        sem = es.enter_context(nc.semaphore("sem"))
        block = es.enter_context(nc.Block())

        for w_in, sW in ((wqT_in, sWqT), (wkT_in, sWkT), (wvT_in, sWvT)):
            w_r = w_in.rearrange("(kc q) o -> kc q o", q=128)
            for kc in range(8):
                p.gps_dma(lambda e, kc=kc, w_r=w_r, sW=sW: e.dma_start(
                    out=sW[:, kc, :], in_=w_r[kc]))

        x_r = x_in.rearrange("(c q) f -> c q f", q=128)
        qT_r = qT_out.rearrange("(oc p) t -> p oc t", p=128)
        kT_r = kT_out.rearrange("(oc p) t -> p oc t", p=128)
        v_r = v_out.rearrange("(c q) f -> c q f", q=128)
        for c in range(ntile):
            tsl = slice(c * 128, (c + 1) * 128)
            p.sync(lambda e, c=c: e.dma_start(out=sX, in_=x_r[c]))
            emit_ln(p, sX.rearrange("q (n f) -> q n f", n=1),
                    sU.rearrange("q (n f) -> q n f", n=1), stat_a, stat_b, bA, 1)
            p.vector(lambda e: e.tensor_copy(sUb, sU))
            emit_transpose_blocks(p, sUb, sUbT.rearrange("q a b -> q (a b)"), 8)
            for sW, sOut in ((sWqT, sQTt), (sWkT, sKTt)):
                for oc in range(8):
                    for kc in range(8):
                        p.tensor(lambda e, oc=oc, kc=kc, sW=sW: e.matmul(
                            pF, sW[:, kc, oc * 128:(oc + 1) * 128], sUbT[:, kc, :],
                            start=(kc == 0), stop=(kc == 7)))
                    p.vector(lambda e, oc=oc, sOut=sOut: e.tensor_copy(sOut[:, oc, :], pF))
            for half in range(2):
                osl = slice(half * 512, (half + 1) * 512)
                for kc in range(8):
                    p.tensor(lambda e, kc=kc, osl=osl: e.matmul(
                        pV[:, osl], sUbT[:, kc, :], sWvT[:, kc, osl],
                        start=(kc == 0), stop=(kc == 7)))
            p.vector(lambda e: e.tensor_copy(sV, pV))
            p.sync(lambda e, tsl=tsl: e.dma_start(out=qT_r[:, :, tsl], in_=sQTt))
            p.sync(lambda e, tsl=tsl: e.dma_start(out=kT_r[:, :, tsl], in_=sKTt))
            p.sync(lambda e, c=c: e.dma_start(out=v_r[c], in_=sV))
        p.emit(block, sem)
    return nc


def run_l1qkv(x, wq, wk, wv):
    ntok = x.shape[1]
    nc = build_l1qkv(ntok)
    wqT = np.ascontiguousarray(wq.T)
    wkT = np.ascontiguousarray(wk.T)
    wvT = np.ascontiguousarray(wv.T)
    in_maps = [{"x_in": np.ascontiguousarray(x[i]), "wqT_in": wqT,
                "wkT_in": wkT, "wvT_in": wvT} for i in range(x.shape[0])]
    res = run_bass_kernel_spmd(nc, in_maps, core_ids=list(range(x.shape[0])))
    return (np.stack([r["qT_out"] for r in res.results]),
            np.stack([r["kT_out"] for r in res.results]),
            np.stack([r["v_out"] for r in res.results]))



def build_l1flash(nbh, seq):
    nc = bass.Bass("TRN2", target_bir_lowering=False, debug=False)
    qT_in = nc.dram_tensor("qT_in", [nbh, 64, seq], BF16, kind="ExternalInput").ap()
    kT_in = nc.dram_tensor("kT_in", [nbh, 64, seq], BF16, kind="ExternalInput").ap()
    v_in = nc.dram_tensor("v_in", [nbh, seq, 64], BF16, kind="ExternalInput").ap()
    ot_out = nc.dram_tensor("ot_out", [nbh, 65, seq], F32, kind="ExternalOutput").ap()

    nsc = seq // 128
    nts = seq // 512
    GRP = 6
    p = Prog()
    with contextlib.ExitStack() as es:
        sb = lambda name, shape, dt=F32: es.enter_context(nc.sbuf_tensor(name, shape, dt))[:]
        sKT = sb("sKT", [64, seq], BF16)
        sQT = sb("sQT", [64, seq], BF16)
        sVext = sb("sVext", [128, nsc, 65], BF16)
        sP = sb("sP", [128, GRP * 512], BF16)
        sOT = sb("sOT", [65, 512])
        sInv = sb("sInv", [1, 512])
        sOnes64 = sb("sOnes64", [1, 64])
        pS = es.enter_context(nc.psum_tensor("pS", [128, GRP * 512], F32))[:]
        pO = es.enter_context(nc.psum_tensor("pO", [65, 512], F32))[:]
        pB = es.enter_context(nc.psum_tensor("pB", [64, 512], F32))[:]
        sem = es.enter_context(nc.semaphore("sem"))
        block = es.enter_context(nc.Block())

        p.vector(lambda e: e.memset(sOnes64, 1.0))

        p.vector(lambda e: e.memset(sVext[:, :, 64:65], 1.0))
        v_r = v_in.rearrange("u (c q) d -> u q c d", q=128)
        for u in range(nbh):
            p.sync(lambda e, u=u: e.dma_start(out=sKT, in_=kT_in[u]))
            p.sync(lambda e, u=u: e.dma_start(out=sQT, in_=qT_in[u]))
            p.sync(lambda e, u=u: e.dma_start(out=sVext[:, :, 0:64], in_=v_r[u]))
            for ts in range(nts):
                tsl = slice(ts * 512, (ts + 1) * 512)
                sc0 = 0
                while sc0 < nsc:
                    n = min(GRP, nsc - sc0)
                    for i in range(n):
                        sc = sc0 + i
                        p.tensor(lambda e, sc=sc, i=i, tsl=tsl: e.matmul(
                            pS[:, i * 512:(i + 1) * 512],
                            sKT[:, sc * 128:(sc + 1) * 128], sQT[:, tsl],
                            start=True, stop=True))
                    p.scalar(lambda e, n=n: e.activation(
                        sP[:, 0:n * 512], pS[:, 0:n * 512], AF.Exp, scale=1.0 / 32.0))
                    for i in range(n):
                        sc = sc0 + i
                        p.tensor(lambda e, sc=sc, i=i: e.matmul(
                            pO, sVext[:, sc, :], sP[:, i * 512:(i + 1) * 512],
                            start=(sc == 0), stop=(sc == nsc - 1)))
                    sc0 += n
                p.vector(lambda e: e.tensor_copy(sOT, pO))
                p.vector(lambda e: e.reciprocal(sInv, sOT[64:65, :]))
                p.tensor(lambda e: e.matmul(pB, sOnes64, sInv, start=True, stop=True))
                p.vector(lambda e: e.tensor_tensor(sOT[0:64, :], sOT[0:64, :], pB, ALU.mult))
                p.sync(lambda e, u=u, tsl=tsl: e.dma_start(
                    out=ot_out[u][:, tsl], in_=sOT))
        p.emit(block, sem)
    return nc


def run_l1flash(qT, kT, v):
    ncore, nbh, _, seq = qT.shape
    nc = build_l1flash(nbh, seq)
    in_maps = [{"qT_in": np.ascontiguousarray(qT[i]),
                "kT_in": np.ascontiguousarray(kT[i]),
                "v_in": np.ascontiguousarray(v[i])} for i in range(ncore)]
    res = run_bass_kernel_spmd(nc, in_maps, core_ids=list(range(ncore)))
    return np.stack([r["ot_out"] for r in res.results])



def build_l1out(ntok):
    nc = bass.Bass("TRN2", target_bir_lowering=False, debug=False)
    x_in = nc.dram_tensor("x_in", [ntok, D], F32, kind="ExternalInput").ap()
    ot_in = nc.dram_tensor("ot_in", [D, ntok], F32, kind="ExternalInput").ap()
    wuT_in = nc.dram_tensor("wuT_in", [D, D], F32, kind="ExternalInput").ap()
    bu_in = nc.dram_tensor("bu_in", [D], F32, kind="ExternalInput").ap()
    x_out = nc.dram_tensor("x_out", [ntok, D], F32, kind="ExternalOutput").ap()

    ntile = ntok // 128
    p = Prog()
    with contextlib.ExitStack() as es:
        sb = lambda name, shape, dt=F32: es.enter_context(nc.sbuf_tensor(name, shape, dt))[:]
        sX = sb("sX", [128, D])
        sOTb = sb("sOTb", [128, 8, ntok], BF16)
        sWuT = sb("sWuT", [128, 8, D], BF16)
        sBu = sb("sBu", [1, D])
        sBub = sb("sBub", [1, D], BF16)
        sOnes = sb("sOnes", [1, 128], BF16)
        pP = es.enter_context(nc.psum_tensor("pP", [128, 1024], F32))[:]
        sem = es.enter_context(nc.semaphore("sem"))
        block = es.enter_context(nc.Block())

        ot_r = ot_in.rearrange("(kc q) t -> kc q t", q=128)
        for kc in range(8):
            p.gps_dma(lambda e, kc=kc: e.dma_start(out=sOTb[:, kc, :], in_=ot_r[kc]))
        w_r = wuT_in.rearrange("(kc q) o -> kc q o", q=128)
        for kc in range(8):
            p.gps_dma(lambda e, kc=kc: e.dma_start(out=sWuT[:, kc, :], in_=w_r[kc]))
        p.sync(lambda e: e.dma_start(out=sBu, in_=bu_in.rearrange("(o f) -> o f", o=1)))
        p.vector(lambda e: e.tensor_copy(sBub, sBu))
        p.vector(lambda e: e.memset(sOnes, 1.0))

        sBiasTile = sb("sBiasTile", [128, D])
        for half in range(2):
            osl = slice(half * 512, (half + 1) * 512)
            p.tensor(lambda e, osl=osl: e.matmul(
                pP[:, osl], sOnes, sBub[:, osl], start=True, stop=True))
        p.vector(lambda e: e.tensor_copy(sBiasTile, pP))

        x_r = x_in.rearrange("(c q) f -> c q f", q=128)
        xo_r = x_out.rearrange("(c q) f -> c q f", q=128)
        for c in range(ntile):
            p.sync(lambda e, c=c: e.dma_start(out=sX, in_=x_r[c]))
            p.vector(lambda e: e.tensor_add(sX, sX, sBiasTile))
            for half in range(2):
                osl = slice(half * 512, (half + 1) * 512)
                for kc in range(8):
                    p.tensor(lambda e, kc=kc, osl=osl, c=c: e.matmul(
                        pP[:, osl], sOTb[:, kc, c * 128:(c + 1) * 128], sWuT[:, kc, osl],
                        start=(kc == 0), stop=(kc == 7)))
            p.vector(lambda e: e.tensor_tensor(sX, pP, sX, ALU.add))
            p.sync(lambda e, c=c: e.dma_start(out=xo_r[c], in_=sX))
        p.emit(block, sem)
    return nc


def run_l1out(x, ot, wu, bu):
    ntok = x.shape[1]
    nc = build_l1out(ntok)
    wuT = np.ascontiguousarray(wu.T)
    in_maps = [{"x_in": np.ascontiguousarray(x[i]),
                "ot_in": np.ascontiguousarray(ot[i]),
                "wuT_in": wuT, "bu_in": bu} for i in range(x.shape[0])]
    res = run_bass_kernel_spmd(nc, in_maps, core_ids=list(range(x.shape[0])))
    return np.stack([r["x_out"] for r in res.results])


def build_l0mix(ntok):
    nc = bass.Bass("TRN2", target_bir_lowering=False, debug=False)
    xh_in = nc.dram_tensor("xh_in", [1024 + ntok, D], F32, kind="ExternalInput").ap()
    tt_in = nc.dram_tensor("tt_in", [128, 16 * 9 * 128], F32, kind="ExternalInput").ap()
    mphiT_in = nc.dram_tensor("mphiT_in", [16384, D], F32, kind="ExternalInput").ap()
    mb_in = nc.dram_tensor("mb_in", [128, 8], F32, kind="ExternalInput").ap()
    wqT_in = nc.dram_tensor("wqT_in", [D, D], F32, kind="ExternalInput").ap()
    wkT_in = nc.dram_tensor("wkT_in", [D, D], F32, kind="ExternalInput").ap()
    wvT_in = nc.dram_tensor("wvT_in", [D, D], F32, kind="ExternalInput").ap()
    qb_in = nc.dram_tensor("qb_in", [128, 8], F32, kind="ExternalInput").ap()
    kb_in = nc.dram_tensor("kb_in", [128, 8], F32, kind="ExternalInput").ap()
    vb_in = nc.dram_tensor("vb_in", [128, 8], F32, kind="ExternalInput").ap()
    qT_out = nc.dram_tensor("qT_out", [D, ntok], BF16, kind="ExternalOutput").ap()
    kT_out = nc.dram_tensor("kT_out", [D, ntok], BF16, kind="ExternalOutput").ap()
    vT_out = nc.dram_tensor("vT_out", [D, ntok], BF16, kind="ExternalOutput").ap()

    T = 256
    nch = ntok // T
    NRING = 10
    NSLAB = 4

    p = Prog()
    with contextlib.ExitStack() as es:
        sb = lambda name, shape, dt=F32: es.enter_context(nc.sbuf_tensor(name, shape, dt))[:]
        sXb = sb("sXb", [128, D])
        sU = sb("sU", [128, D])
        bA = sb("bA", [128, D])
        stat_a = sb("stat_a", [128, 1])
        stat_b = sb("stat_b", [128, 1])
        ring = sb("ring", [128, NRING, D], BF16)
        sTT = sb("sTT", [128, 144, 128], BF16)
        sConv = sb("sConv", [128, D], BF16)
        sCT = sb("sCT", [128, 128, T], BF16)
        sMph = sb("sMph", [128, NSLAB, D], BF16)
        sXtT = sb("sXtT", [128, 8, T], BF16)
        sMb = sb("sMb", [128, 8])
        sQb = sb("sQb", [128, 8])
        sKb = sb("sKb", [128, 8])
        sVb = sb("sVb", [128, 8])
        sWqT = sb("sWqT", [128, 8, D], BF16)
        sWkT = sb("sWkT", [128, 8, D], BF16)
        sWvT = sb("sWvT", [128, 8, D], BF16)
        sQo = sb("sQo", [128, 8, T], BF16)
        pC = es.enter_context(nc.psum_tensor("pC", [128, D], F32))[:]
        pXT = es.enter_context(nc.psum_tensor("pXT", [128, 8 * T], F32))[:]
        pQ = es.enter_context(nc.psum_tensor("pQ", [128, T], F32))[:]
        sem = es.enter_context(nc.semaphore("sem"))
        block = es.enter_context(nc.Block())

        p.gps_dma(lambda e: e.dma_start(
            out=sTT.rearrange("q a b -> q (a b)"), in_=tt_in))
        for w_in, sW in ((wqT_in, sWqT), (wkT_in, sWkT), (wvT_in, sWvT)):
            w_r = w_in.rearrange("(kc q) o -> kc q o", q=128)
            for kc in range(8):
                p.gps_dma(lambda e, kc=kc, w_r=w_r, sW=sW: e.dma_start(
                    out=sW[:, kc, :], in_=w_r[kc]))
        for b_in_, sB in ((mb_in, sMb), (qb_in, sQb), (kb_in, sKb), (vb_in, sVb)):
            p.sync(lambda e, b_in_=b_in_, sB=sB: e.dma_start(out=sB, in_=b_in_))

        xh_r = xh_in.rearrange("(ib q) f -> ib q f", q=128)

        def ln_to_ring(ib):
            p.sync(lambda e, ib=ib: e.dma_start(out=sXb, in_=xh_r[ib]))
            emit_ln(p, sXb.rearrange("q (n f) -> q n f", n=1),
                    sU.rearrange("q (n f) -> q n f", n=1), stat_a, stat_b, bA, 1)
            p.vector(lambda e, ib=ib: e.tensor_copy(ring[:, ib % NRING, :], sU))

        for ib in range(8):
            ln_to_ring(ib)

        qT_r = qT_out.rearrange("(oc p) t -> p oc t", p=128)
        kT_r = kT_out.rearrange("(oc p) t -> p oc t", p=128)
        vT_r = vT_out.rearrange("(oc p) t -> p oc t", p=128)
        mphi_r = mphiT_in.rearrange("(sl j q) o -> sl q j o", q=128, j=NSLAB)

        for c in range(nch):
            for bi in range(2):
                ln_to_ring(8 + 2 * c + bi)
            for bi in range(2):
                ob = 2 * c + bi
                for kf in range(16):
                    for d in range(9):
                        ib = ob + 8 - d
                        for half in range(2):
                            osl = slice(half * 512, (half + 1) * 512)
                            p.tensor(lambda e, kf=kf, d=d, ib=ib, osl=osl: e.matmul(
                                pC[:, osl], sTT[:, kf * 9 + d, :],
                                ring[:, ib % NRING, osl],
                                start=(d == 0), stop=(d == 8)))
                    p.vector(lambda e: e.tensor_copy(sConv, pC))
                    for cc in range(8):
                        p.sync(lambda e, kf=kf, cc=cc, bi=bi: e.dma_start_transpose(
                            sCT[:, kf * 8 + cc, bi * 128:(bi + 1) * 128],
                            sConv[:, cc * 128:(cc + 1) * 128]))
            for sl in range(128 // NSLAB):
                p.gps_dma(lambda e, sl=sl: e.dma_start(out=sMph, in_=mphi_r[sl]))
                for j in range(NSLAB):
                    kc = sl * NSLAB + j
                    slot = (kc // 8) * 8 + (kc % 8)
                    for oc in range(8):
                        p.tensor(lambda e, j=j, oc=oc, slot=slot, kc=kc: e.matmul(
                            pXT[:, oc * T:(oc + 1) * T],
                            sMph[:, j, oc * 128:(oc + 1) * 128],
                            sCT[:, slot, :],
                            start=(kc == 0 and oc % 2 == 0), stop=(kc == 127)))
            for oc in range(8):
                p.scalar(lambda e, oc=oc: e.activation(
                    sXtT[:, oc, :], pXT[:, oc * T:(oc + 1) * T], AF.Identity,
                    bias=sMb[:, oc:oc + 1]))
            for sW, sBv, out_r in ((sWqT, sQb, qT_r), (sWkT, sKb, kT_r),
                                   (sWvT, sVb, vT_r)):
                for oc in range(8):
                    for kc in range(8):
                        p.tensor(lambda e, oc=oc, kc=kc, sW=sW: e.matmul(
                            pQ, sW[:, kc, oc * 128:(oc + 1) * 128], sXtT[:, kc, :],
                            start=(kc == 0), stop=(kc == 7)))
                    p.scalar(lambda e, oc=oc, sBv=sBv: e.activation(
                        sQo[:, oc, :], pQ, AF.Identity, bias=sBv[:, oc:oc + 1]))
                p.sync(lambda e, c=c, out_r=out_r: e.dma_start(
                    out=out_r[:, :, c * T:(c + 1) * T], in_=sQo))
        p.emit(block, sem)
    return nc


def make_toeplitz(basis):
    TT = np.zeros((16, 9, 128, 128), dtype=np.float32)
    l_idx = np.arange(128)[None, :]
    m_idx = np.arange(128)[:, None]
    for kf in range(16):
        fpad = np.zeros(1280, np.float32)
        fpad[128:1152] = basis[:, kf]
        for d in range(9):
            TT[kf, d] = fpad[128 + 128 * d + l_idx - m_idx]
    return np.ascontiguousarray(TT.transpose(2, 0, 1, 3).reshape(128, 16 * 9 * 128))


def run_l0mix(xh, basis, mphi, mb, wq, qb, wk, kb, wv, vb):
    ntok = xh.shape[1] - 1024
    nc = build_l0mix(ntok)
    tt = make_toeplitz(basis)
    mphiT = np.ascontiguousarray(mphi.T)
    wqT = np.ascontiguousarray(wq.T)
    wkT = np.ascontiguousarray(wk.T)
    wvT = np.ascontiguousarray(wv.T)
    rb = lambda b: np.ascontiguousarray(b.reshape(8, 128).T)
    in_maps = [{"xh_in": np.ascontiguousarray(xh[i]), "tt_in": tt,
                "mphiT_in": mphiT, "mb_in": rb(mb), "wqT_in": wqT, "qb_in": rb(qb),
                "wkT_in": wkT, "kb_in": rb(kb), "wvT_in": wvT, "vb_in": rb(vb)}
               for i in range(xh.shape[0])]
    res = run_bass_kernel_spmd(nc, in_maps, core_ids=list(range(xh.shape[0])))
    return (np.stack([r["qT_out"] for r in res.results]),
            np.stack([r["kT_out"] for r in res.results]),
            np.stack([r["vT_out"] for r in res.results]))


def build_l0scan(nbh, seq):
    nc = bass.Bass("TRN2", target_bir_lowering=False, debug=False)
    qT_in = nc.dram_tensor("qT_in", [nbh, 64, seq], BF16, kind="ExternalInput").ap()
    kT_in = nc.dram_tensor("kT_in", [nbh, 64, seq], BF16, kind="ExternalInput").ap()
    vT_in = nc.dram_tensor("vT_in", [nbh, 64, seq], BF16, kind="ExternalInput").ap()
    wgT_in = nc.dram_tensor("wgT_in", [64, 64], F32, kind="ExternalInput").ap()
    kvs_in = nc.dram_tensor("kvs_in", [1, nbh], F32, kind="ExternalInput").ap()
    wgb_in = nc.dram_tensor("wgb_in", [1, nbh], F32, kind="ExternalInput").ap()
    ident_in = nc.dram_tensor("ident_in", [128, 128], F32, kind="ExternalInput").ap()
    triu_in = nc.dram_tensor("triu_in", [128, 128], F32, kind="ExternalInput").ap()
    y_out = nc.dram_tensor("y_out", [nbh, seq, 64], F32, kind="ExternalOutput").ap()

    SEG = 1024
    nseg = seq // SEG
    EPS = 1e-5

    p = Prog()
    with contextlib.ExitStack() as es:
        sb = lambda name, shape, dt=F32: es.enter_context(nc.sbuf_tensor(name, shape, dt))[:]
        sQT = sb("sQT", [64, seq], BF16)
        sKT = sb("sKT", [64, seq], BF16)
        sVT = sb("sVT", [64, seq], BF16)
        sWgb = sb("sWgb", [64, 64], BF16)
        sKvs = sb("sKvs", [1, nbh])
        sWgbias = sb("sWgbias", [1, nbh])
        sIdF = sb("sIdF", [128, 128])
        sIdB = sb("sIdB", [128, 128], BF16)
        sTriu = sb("sTriu", [128, 128])
        sOnes64b = sb("sOnes64b", [64, 1], BF16)
        sOnesF = sb("sOnesF", [1, 64])
        sOneF = sb("sOneF", [1, 1])
        sE = sb("sE", [64, SEG], BF16)
        L = {nm: sb("L_" + nm, [1, SEG]) for nm in
             ["qk", "ksq", "vsq", "vkg", "qsum", "gcum", "m", "msh", "nrm",
              "w", "ones"]}
        sLast = sb("sLast", [1, 8])
        vc_last = sb("vc_last", [64, 1])
        dkb = sb("dkb", [64, SEG])
        w0b = sb("w0b", [64, SEG])
        vcT = sb("vcT", [64, SEG])
        k_tm = sb("k_tm", [128, 64], BF16)
        Va = sb("Va", [128, 64], BF16)
        VaT = sb("VaT", [64, 128], BF16)
        PmT = sb("PmT", [128, 128], BF16)
        S_sb = sb("S_sb", [64, 64], BF16)
        sCols = sb("sCols", [128, 3])
        sY = sb("sY", [128, 64])
        pA = es.enter_context(nc.psum_tensor("pA", [64, 512], F32))[:]
        pL = es.enter_context(nc.psum_tensor("pL", [1, 512], F32))[:]
        pT = es.enter_context(nc.psum_tensor("pT", [128, 128], BF16))[:]
        pcols = es.enter_context(nc.psum_tensor("pcols", [128, 3], F32))[:]
        pYb = es.enter_context(nc.psum_tensor("pYb", [128, 64], F32))[:]
        pP = es.enter_context(nc.psum_tensor("pP", [128, 128], F32))[:]
        pS64 = es.enter_context(nc.psum_tensor("pS64", [64, 64], F32))[:]
        pVc = es.enter_context(nc.psum_tensor("pVc", [128, 64], F32))[:]
        sem = es.enter_context(nc.semaphore("sem"))
        block = es.enter_context(nc.Block())

        p.sync(lambda e: e.dma_start(out=sIdF, in_=ident_in))
        p.sync(lambda e: e.dma_start(out=sTriu, in_=triu_in))
        p.sync(lambda e: e.dma_start(out=sKvs, in_=kvs_in))
        p.sync(lambda e: e.dma_start(out=sWgbias, in_=wgb_in))
        p.gps_dma(lambda e: e.dma_start(out=sWgb, in_=wgT_in))
        p.vector(lambda e: e.tensor_copy(sIdB, sIdF))
        p.vector(lambda e: e.memset(sOnes64b, 1.0))
        p.vector(lambda e: e.memset(sOnesF, 1.0))
        p.vector(lambda e: e.memset(sOneF, 1.0))
        p.vector(lambda e: e.memset(L["ones"], 1.0))

        def lane_sum(dst, mov, width=SEG):
            for s0 in range(0, width, 512):
                p.tensor(lambda e, s0=s0, mov=mov: e.matmul(
                    pL, sOnes64b, mov[:, s0:s0 + 512], start=True, stop=True))
                p.vector(lambda e, s0=s0, dst=dst: e.tensor_copy(
                    dst[:, s0:s0 + 512], pL))

        for u in range(nbh):
            p.sync(lambda e, u=u: e.dma_start(out=sQT, in_=qT_in[u]))
            p.sync(lambda e, u=u: e.dma_start(out=sKT, in_=kT_in[u]))
            p.sync(lambda e, u=u: e.dma_start(out=sVT, in_=vT_in[u]))
            p.vector(lambda e: e.memset(S_sb, 0.0))
            p.vector(lambda e: e.memset(vc_last, 0.0))
            p.vector(lambda e: e.memset(sLast[:, 0:1], -3.0e38))
            p.vector(lambda e: e.memset(sLast[:, 1:3], 0.0))
            kv_ap = sKvs[0:1, u:u + 1]
            wgb_ap = sWgbias[0:1, u:u + 1]
            for g in range(nseg):
                gs = slice(g * SEG, (g + 1) * SEG)
                qTg, kTg, vTg = sQT[:, gs], sKT[:, gs], sVT[:, gs]
                p.vector(lambda e, qTg=qTg, kTg=kTg: e.tensor_tensor(sE, qTg, kTg, ALU.mult))
                lane_sum(L["qk"], sE)
                p.scalar(lambda e, kTg=kTg: e.activation(sE, kTg, AF.Square))
                lane_sum(L["ksq"], sE)
                p.scalar(lambda e, vTg=vTg: e.activation(sE, vTg, AF.Square))
                lane_sum(L["vsq"], sE)
                lane_sum(L["qsum"], qTg)
                for s0 in range(0, SEG, 512):
                    p.tensor(lambda e, s0=s0, kTg=kTg: e.matmul(
                        pA, sWgb, kTg[:, s0:s0 + 512], start=True, stop=True))
                    p.vector(lambda e, s0=s0, vTg=vTg: e.tensor_tensor(
                        sE[:, s0:s0 + 512], vTg[:, s0:s0 + 512], pA, ALU.mult))
                lane_sum(L["vkg"], sE)
                for nm in ("ksq", "vsq"):
                    p.scalar(lambda e, nm=nm: e.activation(L[nm], L[nm], AF.Sqrt))
                    p.vector(lambda e, nm=nm: e.tensor_scalar(L[nm], L[nm], 1e-12, None, ALU.max))
                    p.vector(lambda e, nm=nm: e.reciprocal(L[nm], L[nm]))
                p.vector(lambda e: e.tensor_tensor(L["qk"], L["qk"], L["ksq"], ALU.mult))
                p.vector(lambda e: e.tensor_tensor(L["ksq"], L["ksq"], L["vsq"], ALU.mult))
                p.vector(lambda e, kv_ap=kv_ap: e.tensor_scalar(L["ksq"], L["ksq"], kv_ap, None, ALU.mult))
                p.vector(lambda e: e.tensor_tensor(L["vkg"], L["vkg"], L["ksq"], ALU.mult))
                p.vector(lambda e, wgb_ap=wgb_ap: e.tensor_scalar(L["vkg"], L["vkg"], wgb_ap, None, ALU.add))
                p.scalar(lambda e: e.activation(L["vkg"], L["vkg"], AF.Lrelu, alpha=0.01))
                p.scalar(lambda e: e.activation(L["vkg"], L["vkg"], AF.Square))
                p.vector(lambda e: e.tensor_scalar(L["vkg"], L["vkg"], EPS, None, ALU.add))
                p.vector(lambda e: e.tensor_tensor_scan(
                    L["gcum"], L["ones"], L["vkg"], sLast[0:1, 2:3], ALU.mult, ALU.add))
                p.vector(lambda e: e.tensor_copy(sLast[:, 2:3], L["gcum"][:, SEG - 1:SEG]))
                p.vector(lambda e: e.tensor_tensor(L["vkg"], L["vkg"], L["ksq"], ALU.mult))
                p.vector(lambda e: e.tensor_tensor_scan(
                    L["m"], L["qk"], L["qk"], sLast[0:1, 0:1], ALU.max, ALU.bypass))
                p.vector(lambda e: e.tensor_copy(L["msh"][:, 1:SEG], L["m"][:, 0:SEG - 1]))
                p.vector(lambda e: e.tensor_copy(L["msh"][:, 0:1], sLast[:, 0:1]))
                p.vector(lambda e: e.tensor_copy(sLast[:, 0:1], L["m"][:, SEG - 1:SEG]))
                p.vector(lambda e: e.tensor_tensor(L["msh"], L["msh"], L["m"], ALU.subtract))
                p.scalar(lambda e: e.activation(L["msh"], L["msh"], AF.Exp))
                p.vector(lambda e: e.tensor_tensor(L["qk"], L["qk"], L["m"], ALU.subtract))
                p.scalar(lambda e: e.activation(L["qk"], L["qk"], AF.Exp))
                p.vector(lambda e: e.tensor_tensor_scan(
                    L["nrm"], L["msh"], L["qk"], sLast[0:1, 1:2], ALU.mult, ALU.add))
                p.vector(lambda e: e.tensor_copy(sLast[:, 1:2], L["nrm"][:, SEG - 1:SEG]))
                p.vector(lambda e: e.tensor_scalar(L["nrm"], L["nrm"], EPS, None, ALU.add))
                p.vector(lambda e: e.reciprocal(L["nrm"], L["nrm"]))
                p.vector(lambda e: e.tensor_tensor(L["w"], L["qk"], L["nrm"], ALU.mult))
                p.vector(lambda e: e.tensor_tensor(L["qsum"], L["qsum"], L["w"], ALU.mult))
                p.vector(lambda e: e.tensor_tensor(L["qsum"], L["qsum"], L["nrm"], ALU.mult))
                p.vector(lambda e: e.tensor_tensor(L["qk"], L["qk"], L["vsq"], ALU.mult))
                p.vector(lambda e: e.tensor_scalar(L["gcum"], L["gcum"], EPS, None, ALU.add))
                p.vector(lambda e: e.reciprocal(L["gcum"], L["gcum"]))
                p.vector(lambda e: e.tensor_scalar(L["w"], L["w"], -1.0, 1.0, ALU.mult, ALU.add))
                p.vector(lambda e: e.tensor_tensor(L["w"], L["w"], L["gcum"], ALU.mult))
                for s0 in range(0, SEG, 512):
                    p.tensor(lambda e, s0=s0: e.matmul(
                        pA, sOnesF, L["msh"][:, s0:s0 + 512], start=True, stop=True))
                    p.vector(lambda e, s0=s0: e.tensor_copy(dkb[:, s0:s0 + 512], pA))
                for s0 in range(0, SEG, 512):
                    p.tensor(lambda e, s0=s0: e.matmul(
                        pA, sOnesF, L["qk"][:, s0:s0 + 512], start=True, stop=True))
                    p.vector(lambda e, s0=s0: e.tensor_copy(w0b[:, s0:s0 + 512], pA))
                p.vector(lambda e, vTg=vTg: e.tensor_tensor(w0b, vTg, w0b, ALU.mult))
                p.vector(lambda e: e.tensor_tensor_scan(
                    vcT, dkb, w0b, vc_last, ALU.mult, ALU.add))
                p.vector(lambda e: e.tensor_copy(vc_last, vcT[:, SEG - 1:SEG]))
                for cc in range(SEG // 128):
                    c = g * (SEG // 128) + cc
                    csl = slice(cc * 128, (cc + 1) * 128)
                    qTc = qTg[:, csl]
                    p.tensor(lambda e, kTg=kTg, csl=csl: e.transpose(
                        pT[:, 0:64], kTg[:, csl], sIdB[0:64, 0:64]))
                    p.vector(lambda e: e.tensor_copy(k_tm, pT[:, 0:64]))
                    p.tensor(lambda e, csl=csl: e.matmul(
                        pcols[:, 0:1], L["vkg"][:, csl], sOneF, start=True, stop=True))
                    p.tensor(lambda e, csl=csl: e.matmul(
                        pcols[:, 1:2], L["w"][:, csl], sOneF, start=True, stop=True))
                    p.tensor(lambda e, csl=csl: e.matmul(
                        pcols[:, 2:3], L["qsum"][:, csl], sOneF, start=True, stop=True))
                    p.vector(lambda e: e.tensor_copy(sCols, pcols))
                    p.tensor(lambda e, vTg=vTg, csl=csl: e.transpose(
                        pT[:, 0:64], vTg[:, csl], sIdB[0:64, 0:64]))
                    p.vector(lambda e: e.tensor_scalar(
                        Va, pT[:, 0:64], sCols[:, 0:1], None, ALU.mult))
                    p.tensor(lambda e: e.transpose(pT[0:64, :], Va, sIdB))
                    p.vector(lambda e: e.tensor_copy(VaT, pT[0:64, :]))
                    p.tensor(lambda e, qTc=qTc: e.matmul(
                        pYb, qTc, S_sb, start=True, stop=False))
                    p.tensor(lambda e, qTc=qTc: e.matmul(
                        pP, VaT, qTc, start=True, stop=True))
                    p.vector(lambda e: e.tensor_tensor(PmT, pP, sTriu, ALU.mult))
                    p.tensor(lambda e: e.matmul(
                        pYb, PmT, k_tm, start=False, stop=True))
                    p.tensor(lambda e, c=c: e.matmul(
                        pS64, Va, k_tm, start=(c == 0), stop=(c == seq // 128 - 1)))
                    p.vector(lambda e: e.tensor_copy(S_sb, pS64))
                    p.tensor(lambda e, csl=csl: e.transpose(
                        pVc[:, 0:64], vcT[:, csl], sIdF[0:64, 0:64]))
                    p.vector(lambda e: e.tensor_scalar(
                        sY, pYb, sCols[:, 1:2], None, ALU.mult))
                    p.vector(lambda e: e.scalar_tensor_tensor(
                        sY, pVc[:, 0:64], sCols[:, 2:3], sY, ALU.mult, ALU.add))
                    p.sync(lambda e, u=u, c=c: e.dma_start(
                        out=y_out[u][c * 128:(c + 1) * 128, :], in_=sY))
        p.emit(block, sem)
    return nc


def run_l0scan(qT, kT, vT, wg_z_w, wg_z_b, kv_norm_scale, bh_assign):
    ncore, nbh, _, seq = qT.shape
    nc = build_l0scan(nbh, seq)
    wgT = np.ascontiguousarray(wg_z_w.reshape(64, 64).T)
    kvs_flat = kv_norm_scale.reshape(16)
    ident = np.eye(128, dtype=np.float32)
    triu = np.triu(np.ones((128, 128), np.float32))
    in_maps = []
    for i in range(ncore):
        kvs = np.ascontiguousarray(kvs_flat[bh_assign[i]].reshape(1, nbh))
        wgb = np.full((1, nbh), float(wg_z_b[0]), np.float32)
        in_maps.append({"qT_in": np.ascontiguousarray(qT[i]),
                        "kT_in": np.ascontiguousarray(kT[i]),
                        "vT_in": np.ascontiguousarray(vT[i]),
                        "wgT_in": wgT, "kvs_in": kvs, "wgb_in": wgb,
                        "ident_in": ident, "triu_in": triu})
    res = run_bass_kernel_spmd(nc, in_maps, core_ids=list(range(ncore)))
    return np.stack([r["y_out"] for r in res.results])


def build_l0out(ntok):
    nc = bass.Bass("TRN2", target_bir_lowering=False, debug=False)
    x_in = nc.dram_tensor("x_in", [ntok, D], F32, kind="ExternalInput").ap()
    y_in = nc.dram_tensor("y_in", [ntok, D], F32, kind="ExternalInput").ap()
    woT_in = nc.dram_tensor("woT_in", [D, D], F32, kind="ExternalInput").ap()
    wob_in = nc.dram_tensor("wob_in", [1, D], F32, kind="ExternalInput").ap()
    x_out = nc.dram_tensor("x_out", [ntok, D], F32, kind="ExternalOutput").ap()

    ntile = ntok // 128
    p = Prog()
    with contextlib.ExitStack() as es:
        sb = lambda name, shape, dt=F32: es.enter_context(nc.sbuf_tensor(name, shape, dt))[:]
        sX = sb("sX", [128, D])
        sY = sb("sY", [128, D])
        sYb = sb("sYb", [128, D], BF16)
        sYbT = sb("sYbT", [128, 8, 128], BF16)
        sWoT = sb("sWoT", [128, 8, D], BF16)
        sBo = sb("sBo", [1, D])
        sBob = sb("sBob", [1, D], BF16)
        sOnes = sb("sOnes", [1, 128], BF16)
        sBiasTile = sb("sBiasTile", [128, D])
        stat = sb("stat", [128, 1])
        pP = es.enter_context(nc.psum_tensor("pP", [128, 1024], F32))[:]
        sem = es.enter_context(nc.semaphore("sem"))
        block = es.enter_context(nc.Block())

        w_r = woT_in.rearrange("(kc q) o -> kc q o", q=128)
        for kc in range(8):
            p.gps_dma(lambda e, kc=kc: e.dma_start(out=sWoT[:, kc, :], in_=w_r[kc]))
        p.sync(lambda e: e.dma_start(out=sBo, in_=wob_in))
        p.vector(lambda e: e.tensor_copy(sBob, sBo))
        p.vector(lambda e: e.memset(sOnes, 1.0))
        for half in range(2):
            osl = slice(half * 512, (half + 1) * 512)
            p.tensor(lambda e, osl=osl: e.matmul(
                pP[:, osl], sOnes, sBob[:, osl], start=True, stop=True))
        p.vector(lambda e: e.tensor_copy(sBiasTile, pP))

        x_r = x_in.rearrange("(c q) f -> c q f", q=128)
        y_r = y_in.rearrange("(c q) f -> c q f", q=128)
        xo_r = x_out.rearrange("(c q) f -> c q f", q=128)
        sSq = sb("sSq", [128, D])
        for c in range(ntile):
            p.sync(lambda e, c=c: e.dma_start(out=sX, in_=x_r[c]))
            p.sync(lambda e, c=c: e.dma_start(out=sY, in_=y_r[c]))
            p.scalar(lambda e: e.activation(sSq, sY, AF.Square, accum_out=stat))
            p.scalar(lambda e: e.activation(stat, stat, AF.Sqrt))
            p.vector(lambda e: e.tensor_scalar(stat, stat, 1e-12, None, ALU.max))
            p.vector(lambda e: e.reciprocal(stat, stat))
            p.vector(lambda e: e.tensor_scalar(sYb, sY, stat, None, ALU.mult))
            emit_transpose_blocks(p, sYb, sYbT.rearrange("q a b -> q (a b)"), 8)
            for half in range(2):
                osl = slice(half * 512, (half + 1) * 512)
                for kc in range(8):
                    p.tensor(lambda e, kc=kc, osl=osl: e.matmul(
                        pP[:, osl], sYbT[:, kc, :], sWoT[:, kc, osl],
                        start=(kc == 0), stop=(kc == 7)))
            p.vector(lambda e: e.tensor_add(sX, sX, sBiasTile))
            p.vector(lambda e: e.tensor_tensor(sX, pP, sX, ALU.add))
            p.sync(lambda e, c=c: e.dma_start(out=xo_r[c], in_=sX))
        p.emit(block, sem)
    return nc


def run_l0out(x, y, wo, wob):
    ntok = x.shape[1]
    nc = build_l0out(ntok)
    woT = np.ascontiguousarray(wo.T)
    in_maps = [{"x_in": np.ascontiguousarray(x[i]), "y_in": np.ascontiguousarray(y[i]),
                "woT_in": woT, "wob_in": wob.reshape(1, D)}
               for i in range(x.shape[0])]
    res = run_bass_kernel_spmd(nc, in_maps, core_ids=list(range(x.shape[0])))
    return np.stack([r["x_out"] for r in res.results])


def kernel(L0_attn_x, L0_attn_spectral_basis, L0_attn_M_phi_w, L0_attn_M_phi_b,
           L0_attn_wq_w, L0_attn_wq_b, L0_attn_wk_w, L0_attn_wk_b,
           L0_attn_wv_w, L0_attn_wv_b, L0_attn_wo_w, L0_attn_wo_b,
           L0_attn_wg_z_w, L0_attn_wg_z_b, L0_attn_kv_norm_scale,
           L0_ffn_weight, L0_ffn_bias,
           L1_attn_Wq, L1_attn_Wk, L1_attn_Wv, L1_attn_Wu, L1_attn_bu,
           L1_ffn_w_gate, L1_ffn_w_up, L1_ffn_w_down):
    x0 = np.asarray(L0_attn_x)
    B, L, Dm = x0.shape
    ncore = NCORE
    if FUSED:
        inp = dict(
            L0_attn_spectral_basis=L0_attn_spectral_basis,
            L0_attn_M_phi_w=L0_attn_M_phi_w, L0_attn_M_phi_b=L0_attn_M_phi_b,
            L0_attn_wq_w=L0_attn_wq_w, L0_attn_wq_b=L0_attn_wq_b,
            L0_attn_wk_w=L0_attn_wk_w, L0_attn_wk_b=L0_attn_wk_b,
            L0_attn_wv_w=L0_attn_wv_w, L0_attn_wv_b=L0_attn_wv_b,
            L0_attn_wo_w=L0_attn_wo_w, L0_attn_wo_b=L0_attn_wo_b,
            L0_attn_wg_z_w=L0_attn_wg_z_w, L0_attn_wg_z_b=L0_attn_wg_z_b,
            L0_attn_kv_norm_scale=L0_attn_kv_norm_scale,
            L0_ffn_weight=L0_ffn_weight, L0_ffn_bias=L0_ffn_bias,
            L1_attn_Wq=L1_attn_Wq, L1_attn_Wk=L1_attn_Wk, L1_attn_Wv=L1_attn_Wv,
            L1_attn_Wu=L1_attn_Wu, L1_attn_bu=L1_attn_bu,
            L1_ffn_w_gate=L1_ffn_w_gate, L1_ffn_w_up=L1_ffn_w_up,
            L1_ffn_w_down=L1_ffn_w_down,
        )
        return np.ascontiguousarray(run_fused(x0, inp, ncore)).astype(np.float32)
    ntok = B * L // ncore
    seq = L
    cores_per_b = ncore // B
    nbh = 16 // cores_per_b

    def shard(x):
        return x.reshape(ncore, ntok, Dm)

    def unshard(xs):
        return xs.reshape(-1, Dm)

    def to_bh(aT):
        out = np.empty((ncore, nbh, 64, seq), aT.dtype)
        for c in range(ncore):
            b = c // cores_per_b
            full = np.concatenate([aT[b * cores_per_b + i] for i in range(cores_per_b)], axis=1)
            for j in range(nbh):
                h = (c % cores_per_b) * nbh + j
                out[c, j] = full[h * 64:(h + 1) * 64]
        return out

    x = np.ascontiguousarray(x0.reshape(-1, Dm))
    x = unshard(run_ffn0(shard(x), L0_ffn_weight, L0_ffn_bias))

    xb = x.reshape(B, L, Dm)
    xh = []
    for c in range(ncore):
        b = c // cores_per_b
        t0 = (c % cores_per_b) * ntok
        pad = np.concatenate([np.zeros((1024, Dm), np.float32), xb[b]], 0)
        xh.append(pad[t0:t0 + ntok + 1024])
    xh = np.stack(xh)
    qT, kT, vT = run_l0mix(xh, L0_attn_spectral_basis, L0_attn_M_phi_w,
                           L0_attn_M_phi_b, L0_attn_wq_w, L0_attn_wq_b,
                           L0_attn_wk_w, L0_attn_wk_b, L0_attn_wv_w, L0_attn_wv_b)
    bh_assign = np.array([[(c % cores_per_b) * nbh + j for j in range(nbh)]
                          for c in range(ncore)])
    y = run_l0scan(to_bh(qT), to_bh(kT), to_bh(vT), L0_attn_wg_z_w,
                   L0_attn_wg_z_b, L0_attn_kv_norm_scale, bh_assign)
    Y = np.empty((B, L, 16, 64), np.float32)
    for c in range(ncore):
        b = c // cores_per_b
        for j in range(nbh):
            h = (c % cores_per_b) * nbh + j
            Y[b, :, h] = y[c, j]
    Y = Y.reshape(-1, Dm)
    x = unshard(run_l0out(shard(x), shard(Y), L0_attn_wo_w, L0_attn_wo_b))
    x = unshard(run_ffn0(shard(x), L0_ffn_weight, L0_ffn_bias))
    x = unshard(run_ffn1(shard(x), L1_ffn_w_gate, L1_ffn_w_up, L1_ffn_w_down))

    qT1, kT1, v1 = run_l1qkv(shard(x), L1_attn_Wq, L1_attn_Wk, L1_attn_Wv)
    vs = np.empty((ncore, nbh, seq, 64), v1.dtype)
    vfull = v1.reshape(B, L, Dm)
    for c in range(ncore):
        b = c // cores_per_b
        for j in range(nbh):
            h = (c % cores_per_b) * nbh + j
            vs[c, j] = vfull[b, :, h * 64:(h + 1) * 64]
    ot = run_l1flash(to_bh(qT1), to_bh(kT1), vs)
    OT = np.empty((B, 16, 64, L), np.float32)
    for c in range(ncore):
        b = c // cores_per_b
        for j in range(nbh):
            h = (c % cores_per_b) * nbh + j
            OT[b, h] = ot[c, j, 0:64]
    ot_sh = np.empty((ncore, Dm, ntok), np.float32)
    for c in range(ncore):
        b = c // cores_per_b
        t0 = (c % cores_per_b) * ntok
        ot_sh[c] = OT[b].reshape(Dm, L)[:, t0:t0 + ntok]
    x = unshard(run_l1out(shard(x), ot_sh, L1_attn_Wu, L1_attn_bu))
    x = unshard(run_ffn1(shard(x), L1_ffn_w_gate, L1_ffn_w_up, L1_ffn_w_down))
    return np.ascontiguousarray(x.reshape(B, L, Dm)).astype(np.float32)


def build_fused(ntok):
    seq = 2 * ntok
    nch128 = ntok // 128
    nc = bass.Bass("TRN2", target_bir_lowering=False, debug=False)
    x_in = nc.dram_tensor("x_in", [ntok, D], F32, kind="ExternalInput").ap()
    nonce_in = nc.dram_tensor("nonce_in", [1, 128], I32, kind="ExternalInput").ap()
    w0_in = nc.dram_tensor("w0_in", [D, D], F32, kind="ExternalInput").ap()
    b0_in = nc.dram_tensor("b0_in", [D], F32, kind="ExternalInput").ap()
    tt_in = nc.dram_tensor("tt_in", [128, 16 * 9 * 128], F32, kind="ExternalInput").ap()
    mphiT_in = nc.dram_tensor("mphiT_in", [16384, D], F32, kind="ExternalInput").ap()
    mb_in = nc.dram_tensor("mb_in", [128, 8], F32, kind="ExternalInput").ap()
    wqT_in = nc.dram_tensor("wqT_in", [D, D], F32, kind="ExternalInput").ap()
    wkT_in = nc.dram_tensor("wkT_in", [D, D], F32, kind="ExternalInput").ap()
    wvT_in = nc.dram_tensor("wvT_in", [D, D], F32, kind="ExternalInput").ap()
    qb_in = nc.dram_tensor("qb_in", [128, 8], F32, kind="ExternalInput").ap()
    kb_in = nc.dram_tensor("kb_in", [128, 8], F32, kind="ExternalInput").ap()
    vb_in = nc.dram_tensor("vb_in", [128, 8], F32, kind="ExternalInput").ap()
    wgT_in = nc.dram_tensor("wgT_in", [64, 64], F32, kind="ExternalInput").ap()
    kvs_in = nc.dram_tensor("kvs_in", [1, 16], F32, kind="ExternalInput").ap()
    wgb_in = nc.dram_tensor("wgb_in", [1, 16], F32, kind="ExternalInput").ap()
    ident_in = nc.dram_tensor("ident_in", [128, 128], F32, kind="ExternalInput").ap()
    triu_in = nc.dram_tensor("triu_in", [128, 128], F32, kind="ExternalInput").ap()
    woT_in = nc.dram_tensor("woT_in", [D, D], F32, kind="ExternalInput").ap()
    wob_in = nc.dram_tensor("wob_in", [1, D], F32, kind="ExternalInput").ap()
    wg1_in = nc.dram_tensor("wg1_in", [IFF, D], F32, kind="ExternalInput").ap()
    wu1_in = nc.dram_tensor("wu1_in", [IFF, D], F32, kind="ExternalInput").ap()
    wd1_in = nc.dram_tensor("wd1_in", [D, IFF], F32, kind="ExternalInput").ap()
    waqT_in = nc.dram_tensor("waqT_in", [D, D], F32, kind="ExternalInput").ap()
    wakT_in = nc.dram_tensor("wakT_in", [D, D], F32, kind="ExternalInput").ap()
    wavT_in = nc.dram_tensor("wavT_in", [D, D], F32, kind="ExternalInput").ap()
    wauT_in = nc.dram_tensor("wauT_in", [D, D], F32, kind="ExternalInput").ap()
    bau_in = nc.dram_tensor("bau_in", [1, D], F32, kind="ExternalInput").ap()
    x_out = nc.dram_tensor("x_out", [ntok, D], F32, kind="ExternalOutput").ap()
    xres = nc.dram_tensor("xres", [ntok, D], F32).ap()
    hscr = nc.dram_tensor("hscr", [ntok // 128, 128, IFF], BF16).ap()
    mphi_bf = nc.dram_tensor("mphi_bf", [16384, D], BF16).ap()
    uqscr = nc.dram_tensor("uqscr", [ntok // 128, 128, D], BF16).ap()
    xtscr = nc.dram_tensor("xtscr", [ntok // 256, 128, 8 * 256], BF16).ap()
    gyscr = nc.dram_tensor("gyscr", [ntok // 128, 128, 2 * IFF], F32).ap()
    halo_sh = nc.dram_tensor("halo_sh", [3 * 1024, D], F32, addr_space="Shared").ap()
    q0_sh = nc.dram_tensor("q0_sh", [D, seq], BF16, addr_space="Shared").ap()
    k0_sh = nc.dram_tensor("k0_sh", [D, seq], BF16, addr_space="Shared").ap()
    v0_sh = nc.dram_tensor("v0_sh", [D, seq], BF16, addr_space="Shared").ap()
    y_sh = nc.dram_tensor("y_sh", [seq, D], F32, addr_space="Shared").ap()
    q1_sh = nc.dram_tensor("q1_sh", [D, seq], BF16, addr_space="Shared").ap()
    k1_sh = nc.dram_tensor("k1_sh", [D, seq], BF16, addr_space="Shared").ap()
    v1_sh = nc.dram_tensor("v1_sh", [seq, D], BF16, addr_space="Shared").ap()
    ot_sh = nc.dram_tensor("ot_sh", [D, seq], F32, addr_space="Shared").ap()
    flg = nc.dram_tensor("flg", [2, 16], I32, addr_space="Shared").ap()

    p = Prog()
    es = contextlib.ExitStack()
    with es:
        sb = lambda name, shape, dt=F32: es.enter_context(nc.sbuf_tensor(name, shape, dt))[:]
        sNon = sb("sNon", [1, 128], I32)
        sFw = sb("sFw", [1, 1], I32)
        sFr = sb("sFr", [1, 1], I32)
        sem = es.enter_context(nc.semaphore("sem"))
        bsem = es.enter_context(nc.semaphore("bsem"))
        psem = es.enter_context(nc.semaphore("psem"))
        tsem = es.enter_context(nc.semaphore("tsem"))
        msem = es.enter_context(nc.semaphore("msem"))
        esem = es.enter_context(nc.semaphore("esem"))
        qsem = es.enter_context(nc.semaphore("qsem"))
        vsem = es.enter_context(nc.semaphore("vsem"))
        block = es.enter_context(nc.Block())

        sIdF2 = es.enter_context(nc.sbuf_tensor("sIdF2", [128, 128], F32))[:]
        sIdB2 = es.enter_context(nc.sbuf_tensor("sIdB2", [128, 128], BF16))[:]
        p.sync(lambda e: e.dma_start(out=sNon, in_=nonce_in))
        p.sync(lambda e: e.dma_start(out=sIdF2, in_=ident_in))
        p.vector(lambda e: e.tensor_copy(sIdB2, sIdF2))

        bar_state = p.ctx

        def gps_pre(g):
            bar_state["b_par"] = g.alloc_register("b_par")
            g.reg_alu(bar_state["b_par"], g.partition_id(), 1, ALU.bitwise_and)
            bar_state["rn"] = g.alloc_register("b_rn")
            bar_state["rt"] = g.alloc_register("b_rt")
            bar_state["rv"] = g.alloc_register("b_rv")
            bar_state["rf"] = g.alloc_register("b_rf")
            bar_state["pc"] = g.alloc_register("b_pc")
            g.reg_mov(bar_state["pc"], 0)

        p.pre["gpsimd"] = gps_pre
        bar_n = [0]
        tcount = [0]
        mcount = [0]
        qcount = [0]
        vcount = [0]

        def barrier():
            bar_n[0] += 1
            k = bar_n[0]

            def fn(g, t0, sem_):
                if t0 > 0:
                    g.wait_ge(sem_, t0)
                rn, rt, rv, rf, pc = (bar_state[n] for n in ("rn", "rt", "rv", "rf", "pc"))
                if k == 1:
                    g.reg_load(rn, sNon[0:1, 0:1])
                g.reg_alu(rt, rn, k, ALU.add)
                g.store(sFw[0:1, 0:1], rt)
                with g.If_eq(bar_state["b_par"], 0):
                    g.dma_start(out=flg[0:1, k:k + 1], in_=sFw[0:1, 0:1]).then_inc(bsem, 16)
                with g.Else():
                    g.dma_start(out=flg[1:2, k:k + 1], in_=sFw[0:1, 0:1]).then_inc(bsem, 16)
                g.wait_ge(bsem, 16 * k)
                g.reg_mov(rv, 0)
                g.reg_alu(rf, rv, rt, ALU.not_equal)
                with g.While(rf):
                    g.reg_add(pc, pc, 16)
                    with g.If_eq(bar_state["b_par"], 0):
                        g.dma_start(out=sFr[0:1, 0:1], in_=flg[1:2, k:k + 1]).then_inc(psem, 16)
                    with g.Else():
                        g.dma_start(out=sFr[0:1, 0:1], in_=flg[0:1, k:k + 1]).then_inc(psem, 16)
                    g.wait_ge(psem, pc)
                    g.reg_load(rv, sFr[0:1, 0:1])
                    g.reg_alu(rf, rv, rt, ALU.not_equal)
                g.nop().then_inc(sem_, 1)

            p.raw("gpsimd", fn, 1)

        es_f0 = contextlib.ExitStack()
        w0T = es_f0.enter_context(nc.sbuf_tensor("r_w0T", [128, 8, D], BF16))[:]
        sB0b = es_f0.enter_context(nc.sbuf_tensor("r_sB0b", [1, D], BF16))[:]
        sOnes1 = es_f0.enter_context(nc.sbuf_tensor("r_sOnes1", [1, 128], BF16))[:]
        f0_init = [False]

        def emit_ffn0(src_x, dst_x, publish_halo, tag):
            CH = 512
            TB = CH // 128
            nchk = ntok // CH
            with contextlib.ExitStack() as s2:
                sb2 = lambda name, shape, dt=F32: s2.enter_context(nc.sbuf_tensor(tag + name, shape, dt))[:]
                if not f0_init[0]:
                    f0_init[0] = True
                    with contextlib.ExitStack() as s3:
                        sb3 = lambda name, shape, dt=F32: s3.enter_context(nc.sbuf_tensor(tag + name, shape, dt))[:]
                        sWrow = sb3("w_sWrow", [128, D])
                        sWd = sb3("w_sWd", [128, D], BF16)
                        wA = sb3("w_A", [128, D])
                        wB = sb3("w_B", [128, D])
                        wC = sb3("w_C", [128, D])
                        wD = sb3("w_D", [128, D])
                        wq_dscr = sb3("w_dscr", [128, D // 32])
                        sBias = sb3("w_sBias", [1, D])
                        w_r = w0_in.rearrange("(oc q) k -> oc q k", q=128)
                        for oc in range(8):
                            p.sync(lambda e, oc=oc: e.dma_start(out=sWrow, in_=w_r[oc]))
                            emit_quant_fp6(p, sWrow, sWd, wA, wB, wC, wD, wq_dscr)
                            for kc in range(8):
                                p.sync(lambda e, oc=oc, kc=kc: e.dma_start_transpose(
                                    w0T[:, kc, oc * 128:(oc + 1) * 128],
                                    sWd[:, kc * 128:(kc + 1) * 128]), fuse=(kc > 0))
                        p.sync(lambda e: e.dma_start(out=sBias, in_=b0_in.rearrange("(o f) -> o f", o=1)))
                        p.vector(lambda e: e.tensor_copy(sB0b, sBias))
                        p.vector(lambda e: e.memset(sOnes1, 1.0))

                sX = sb2("f0_sX", [128, TB, D])
                sU = sb2("f0_sU", [128, TB * D])
                sUd = sb2("f0_sUd", [128, TB * D], BF16)
                sUdT = sb2("f0_sUdT", [128, TB * 8, 128], BF16)
                bA = sb2("f0_bA", [128, TB * D])
                bB = sb2("f0_bB", [128, TB * D])
                bC = sb2("f0_bC", [128, TB * D])
                bD = sb2("f0_bD", [128, TB * D])
                dscr = sb2("f0_dscr", [128, TB * D // 32])
                stat_a = sb2("f0_stat_a", [128, TB])
                stat_b = sb2("f0_stat_b", [128, TB])
                sZ = sb2("f0_sZ", [128, 1024]) if publish_halo else None
                if publish_halo:
                    p.vector(lambda e: e.memset(sZ, 0.0))
                    hv3 = halo_sh.rearrange("(blk q) f -> q blk f", q=128)
                    for zb in range(8):
                        p.sync(lambda e, zb=zb: e.dma_start(
                            out=hv3[:, zb:zb + 1, :], in_=sZ.rearrange("q (a f) -> q a f", a=1)))
                x_r = src_x.rearrange("(c tb q) f -> c q tb f", q=128, tb=TB)
                uq_r = uqscr.rearrange("(c tb) q f -> c q tb f", tb=TB)
                for c in range(nchk):
                    p.sync(lambda e, c=c: e.dma_start(out=sX, in_=x_r[c]))
                    emit_ln(p, sX, sU.rearrange("q (n f) -> q n f", f=D), stat_a, stat_b, bA, TB)
                    emit_quant_fp6(p, sU, sUd, bA, bB, bC, bD, dscr)
                    emit_transpose_blocks(p, sUd, sUdT.rearrange("q a b -> q (a b)"), TB * 8)
                    p.sync(lambda e, c=c: e.dma_start(
                        out=uq_r[c], in_=sUdT.rearrange("q (tb k) b -> q tb (k b)", tb=TB)))

                sX2 = sb2("f0_sX2", [128, 2, TB, D])
                sUdT2 = sb2("f0_sUdT2", [128, 2, TB * 8, 128], BF16)
                pTa = s2.enter_context(nc.psum_tensor(tag + "f0_pTa", [128, D], F32))[:]
                pTb = s2.enter_context(nc.psum_tensor(tag + "f0_pTb", [128, D], F32))[:]
                pT2 = [pTa, pTb]
                xo_r = dst_x.rearrange("(c tb q) f -> c q tb f", q=128, tb=TB)
                mtf = {}
                qtf = {}

                def f0_dma(c):
                    qdep = qtf.get(c - 2, 0)
                    def f(e, t0, sem_, c=c, qdep=qdep):
                        e.wait_ge(sem_, t0)
                        if qdep:
                            e.wait_ge(qsem, qdep)
                        e.dma_start(out=sUdT2[:, c % 2, :, :].rearrange(
                            "q (tb k) b -> q tb (k b)", tb=TB),
                                    in_=uq_r[c]).then_inc(msem, 16)
                        e.dma_start(out=sX2[:, c % 2, :, :], in_=x_r[c]).then_inc(msem, 16)
                    p.raw("gpsimd", f, 0)
                    mcount[0] += 32
                    mtf[c] = mcount[0]

                f0_dma(0)
                f0_dma(1)
                for c in range(nchk):
                    for tb in range(TB):
                        def fgrp(e, t0, sem_, c=c, tb=tb, nm=mtf[c]):
                            if t0 > 0 and tb == 0:
                                e.wait_ge(sem_, t0)
                            e.wait_ge(msem, nm)
                            for half in range(2):
                                osl = slice(half * 512, (half + 1) * 512)
                                for kc in range(8):
                                    e.matmul(pT2[(c * TB + tb) % 2][:, osl],
                                             sUdT2[:, c % 2, tb * 8 + kc, :],
                                             w0T[:, kc, osl],
                                             start=(kc == 0), stop=False).then_inc(qsem, 1)
                                e.matmul(pT2[(c * TB + tb) % 2][:, osl], sOnes1,
                                         sB0b[:, osl], start=False,
                                         stop=True).then_inc(qsem, 1)
                        p.raw("tensor", fgrp, 0)
                        qcount[0] += 18
                        qd = qcount[0]
                        def fcomb(e, t0, sem_, c=c, tb=tb, qd=qd):
                            if t0 > 0:
                                e.wait_ge(sem_, t0)
                            e.wait_ge(qsem, qd)
                            e.scalar_tensor_tensor(
                                sX2[:, c % 2, tb, :], pT2[(c * TB + tb) % 2], 0.5,
                                sX2[:, c % 2, tb, :], ALU.mult, ALU.add).then_inc(sem_, 1)
                        p.raw("vector", fcomb, 1)
                    qtf[c] = qcount[0]
                    p.sync(lambda e, c=c: e.dma_start(out=xo_r[c], in_=sX2[:, c % 2, :, :]))
                    if publish_halo and c >= nchk - 2:
                        hv5 = halo_sh.rearrange("(a blk q) f -> a q blk f", a=3, q=128)
                        cc0 = c - (nchk - 2)
                        def pub(par, cc0=cc0, c=c):
                            return lambda e: e.dma_start(
                                out=hv5[par + 1, :, cc0 * TB:(cc0 + 1) * TB, :],
                                in_=sX2[:, c % 2, :, :])
                        p.par_dma(pub)
                    if c + 2 < nchk:
                        f0_dma(c + 2)

        emit_ffn0(x_in, xres, True, "A")
        barrier()

        def emit_l0mix():
            T = 256
            nchk = ntok // T
            NRING = 10
            NSLAB = 2
            with contextlib.ExitStack() as s2:
                sb2 = lambda name, shape, dt=F32: s2.enter_context(nc.sbuf_tensor(name, shape, dt))[:]
                sXb = sb2("m_sXb", [128, D])
                sU = sb2("m_sU", [128, D])
                bA = sb2("m_bA", [128, D])
                stat_a = sb2("m_stat_a", [128, 1])
                stat_b = sb2("m_stat_b", [128, 1])
                ring = sb2("m_ring", [128, NRING, D], BF16)
                sTT = sb2("m_sTT", [128, 144, 128], BF16)
                sConv = sb2("m_sConv", [128, 2, D], BF16)
                sCT = sb2("m_sCT", [128, 128, T], BF16)
                sMph = sb2("m_sMph", [128, 2, NSLAB, D], BF16)
                sXtT = sb2("m_sXtT", [128, 8, T], BF16)
                sMb = sb2("m_sMb", [128, 8])
                pC = s2.enter_context(nc.psum_tensor("m_pC", [128, D], F32))[:]
                pXT = s2.enter_context(nc.psum_tensor("m_pXT", [128, 8 * T], F32))[:]

                p.gps_dma(lambda e: e.dma_start(
                    out=sTT.rearrange("q a b -> q (a b)"), in_=tt_in))
                for qq in range(8):
                    p.gps_dma(lambda e, qq=qq: e.dma_start(
                        out=mphi_bf[qq * 2048:(qq + 1) * 2048, :],
                        in_=mphiT_in[qq * 2048:(qq + 1) * 2048, :]), fuse=(qq > 0))
                p.sync(lambda e: e.dma_start(out=sMb, in_=mb_in))

                xr_r = xres.rearrange("(ib q) f -> ib q f", q=128)

                def ln_to_ring(ib):
                    if ib < 8:
                        hv4 = halo_sh.rearrange("(a r) f -> a r f", a=3)
                        def ld(par, ib=ib):
                            return lambda e: e.dma_start(
                                out=sXb, in_=hv4[par, ib * 128:(ib + 1) * 128, :])
                        p.par_dma(ld)
                    else:
                        p.sync(lambda e, ib=ib: e.dma_start(out=sXb, in_=xr_r[ib - 8]))
                    emit_ln(p, sXb.rearrange("q (n f) -> q n f", n=1),
                            sU.rearrange("q (n f) -> q n f", n=1), stat_a, stat_b, bA, 1)
                    p.vector(lambda e, ib=ib: e.tensor_copy(ring[:, ib % NRING, :], sU))

                for ib in range(8):
                    ln_to_ring(ib)

                mphi_r = mphi_bf.rearrange("(sl j q) o -> sl q j o", q=128, j=NSLAB)
                for c in range(nchk):
                    for bi in range(2):
                        ln_to_ring(8 + 2 * c + bi)
                    for bi in range(2):
                        ob = 2 * c + bi
                        for kf in range(16):
                            kk = bi * 16 + kf
                            for d in range(9):
                                ib = ob + 8 - d
                                for half in range(2):
                                    osl = slice(half * 512, (half + 1) * 512)
                                    first = (d == 0 and half == 0)
                                    p.tensor(lambda e, kf=kf, d=d, ib=ib, osl=osl: e.matmul(
                                        pC[:, osl], sTT[:, kf * 9 + d, :],
                                        ring[:, ib % NRING, osl],
                                        start=(d == 0), stop=(d == 8)),
                                        fuse=not first)
                            need_t = max(0, tcount[0] - 16 * 8)
                            def evac(e, t0, sem_, kk=kk, need_t=need_t):
                                e.wait_ge(sem_, t0)
                                if need_t:
                                    e.wait_ge(tsem, need_t)
                                e.tensor_copy(sConv[:, kk % 2, :], pC).then_inc(sem_, 1)
                            p.raw("vector", evac, 1)
                            def transp(e, t0, sem_, kf=kf, bi=bi, kk=kk):
                                e.wait_ge(sem_, t0)
                                for cc in range(8):
                                    e.dma_start_transpose(
                                        sCT[:, kf * 8 + cc, bi * 128:(bi + 1) * 128],
                                        sConv[:, kk % 2, cc * 128:(cc + 1) * 128]
                                    ).then_inc(tsem, 16)
                            p.raw("sync", transp, 0)
                            tcount[0] += 16 * 8
                    NSL = 128 // NSLAB
                    mtarget0 = mcount[0]

                    def slab_dma(sl):
                        need_t = tcount[0] if sl < 2 else 0
                        def f(e, t0, sem_, sl=sl, need_t=need_t):
                            e.wait_ge(sem_, t0)
                            if need_t:
                                e.wait_ge(tsem, need_t)
                            e.dma_start(out=sMph[:, sl % 2, :, :],
                                        in_=mphi_r[sl]).then_inc(msem, 16)
                        p.raw("gpsimd", f, 0)
                        mcount[0] += 16

                    slab_dma(0)
                    slab_dma(1)
                    for sl in range(NSL):
                        need_m = mtarget0 + 16 * (sl + 1)
                        need_t = tcount[0]
                        for j in range(NSLAB):
                            kc = sl * NSLAB + j
                            for oc in range(8):
                                first = (j == 0 and oc == 0)
                                if first:
                                    def fmm(e, t0, sem_, sl=sl, j=j, oc=oc, kc=kc,
                                            need_m=need_m, need_t=need_t):
                                        e.wait_ge(sem_, t0)
                                        e.wait_ge(msem, need_m)
                                        if sl == 0:
                                            e.wait_ge(tsem, need_t)
                                        e.matmul(
                                            pXT[:, oc * T:(oc + 1) * T],
                                            sMph[:, sl % 2, j, oc * 128:(oc + 1) * 128],
                                            sCT[:, kc, :],
                                            start=(kc == 0 and oc % 2 == 0),
                                            stop=(kc == 127)).then_inc(sem_, 1)
                                    p.raw("tensor", fmm, 1)
                                else:
                                    p.tensor(lambda e, sl=sl, j=j, oc=oc, kc=kc: e.matmul(
                                        pXT[:, oc * T:(oc + 1) * T],
                                        sMph[:, sl % 2, j, oc * 128:(oc + 1) * 128],
                                        sCT[:, kc, :],
                                        start=(kc == 0 and oc % 2 == 0), stop=(kc == 127)),
                                        fuse=True)
                        if sl + 2 < NSL:
                            slab_dma(sl + 2)
                    for oc in range(8):
                        p.scalar(lambda e, oc=oc: e.activation(
                            sXtT[:, oc, :], pXT[:, oc * T:(oc + 1) * T], AF.Identity,
                            bias=sMb[:, oc:oc + 1]))
                    p.sync(lambda e, c=c: e.dma_start(
                        out=xtscr[c], in_=sXtT.rearrange("q a b -> q (a b)")))

        emit_l0mix()

        def emit_l0qkv():
            T = 256
            nchk = ntok // T
            with contextlib.ExitStack() as s2:
                sb2 = lambda name, shape, dt=F32: s2.enter_context(nc.sbuf_tensor(name, shape, dt))[:]
                sWqT = sb2("k_sWqT", [128, 8, D], BF16)
                sWkT = sb2("k_sWkT", [128, 8, D], BF16)
                sWvT = sb2("k_sWvT", [128, 8, D], BF16)
                sQb = sb2("k_sQb", [128, 8])
                sKb = sb2("k_sKb", [128, 8])
                sVb = sb2("k_sVb", [128, 8])
                sXt2 = sb2("k_sXt2", [128, 2, 8, T], BF16)
                sQo = sb2("k_sQo", [128, 8, T], BF16)
                pQa = s2.enter_context(nc.psum_tensor("k_pQa", [128, T], F32))[:]
                pQb = s2.enter_context(nc.psum_tensor("k_pQb", [128, T], F32))[:]
                pQ2 = [pQa, pQb]
                for w_in, sW in ((wqT_in, sWqT), (wkT_in, sWkT), (wvT_in, sWvT)):
                    w_r = w_in.rearrange("(kc q) o -> kc q o", q=128)
                    for kc in range(8):
                        p.gps_dma(lambda e, kc=kc, w_r=w_r, sW=sW: e.dma_start(
                            out=sW[:, kc, :], in_=w_r[kc]))
                for b_in_, sB in ((qb_in, sQb), (kb_in, sKb), (vb_in, sVb)):
                    p.sync(lambda e, b_in_=b_in_, sB=sB: e.dma_start(out=sB, in_=b_in_))
                q_r = q0_sh.rearrange("(oc p) (h t) -> p oc h t", p=128, h=2)
                k_r = k0_sh.rearrange("(oc p) (h t) -> p oc h t", p=128, h=2)
                v_r = v0_sh.rearrange("(oc p) (h t) -> p oc h t", p=128, h=2)
                mtq = {}
                qtq = {}

                def xt_dma(c):
                    qdep = qtq.get(c - 2, 0)
                    def f(e, t0, sem_, c=c, qdep=qdep):
                        e.wait_ge(sem_, t0)
                        if qdep:
                            e.wait_ge(qsem, qdep)
                        e.dma_start(out=sXt2[:, c % 2, :, :].rearrange("q a b -> q (a b)"),
                                    in_=xtscr[c]).then_inc(msem, 16)
                    p.raw("gpsimd", f, 0)
                    mcount[0] += 16
                    mtq[c] = mcount[0]

                xt_dma(0)
                xt_dma(1)
                gq = [0]
                for c in range(nchk):
                    for sW, sBv, out_r in ((sWqT, sQb, q_r), (sWkT, sKb, k_r),
                                           (sWvT, sVb, v_r)):
                        for oc in range(8):
                            gg = gq[0]
                            vdep = max(0, vcount[0] - 1)
                            def fgrp(e, t0, sem_, c=c, oc=oc, sW=sW, gg=gg,
                                     nm=mtq[c], vdep=vdep):
                                e.wait_ge(msem, nm)
                                if vdep:
                                    e.wait_ge(vsem, vdep)
                                for kc in range(8):
                                    e.matmul(pQ2[gg % 2], sW[:, kc, oc * 128:(oc + 1) * 128],
                                             sXt2[:, c % 2, kc, :],
                                             start=(kc == 0), stop=(kc == 7)).then_inc(qsem, 1)
                            p.raw("tensor", fgrp, 0)
                            qcount[0] += 8
                            qd = qcount[0]
                            foc = (oc == 0 and sW is sWqT)
                            def fev(e, t0, sem_, oc=oc, sBv=sBv, gg=gg, qd=qd, foc=foc):
                                if foc and t0 > 0:
                                    e.wait_ge(sem_, t0)
                                e.wait_ge(qsem, qd)
                                e.activation(sQo[:, oc, :], pQ2[gg % 2], AF.Identity,
                                             bias=sBv[:, oc:oc + 1]).then_inc(vsem, 1)
                            p.raw("scalar", fev, 0)
                            vcount[0] += 1
                            gq[0] += 1
                        vd = vcount[0]
                        def wrout(e, t0, sem_, c=c, out_r=out_r, vd=vd):
                            if t0 > 0:
                                e.wait_ge(sem_, t0)
                            e.wait_ge(vsem, vd)
                            with e.If_eq(p.ctx["b_par"], 0):
                                e.dma_start(out=out_r[:, :, 0, c * T:(c + 1) * T],
                                            in_=sQo).then_inc(sem_, 16)
                            with e.Else():
                                e.dma_start(out=out_r[:, :, 1, c * T:(c + 1) * T],
                                            in_=sQo).then_inc(sem_, 16)
                        p.raw("gpsimd", wrout, 16)
                    qtq[c] = qcount[0]
                    if c + 2 < nchk:
                        xt_dma(c + 2)

        emit_l0qkv()
        barrier()
        def emit_l0scan():
            SEG = 1024
            nseg = seq // SEG
            EPS = 1e-5
            nbh = 8
            with contextlib.ExitStack() as s2:
                sb2 = lambda name, shape, dt=F32: s2.enter_context(nc.sbuf_tensor(name, shape, dt))[:]
                sQT = sb2("s_sQT", [64, seq], BF16)
                sKT = sb2("s_sKT", [64, seq], BF16)
                sVT = sb2("s_sVT", [64, seq], BF16)
                sWgb = sb2("s_sWgb", [64, 64], BF16)
                sKvsU = sb2("s_sKvsU", [1, 1])
                sWgbU = sb2("s_sWgbU", [1, 1])
                sIdF = sb2("s_sIdF", [128, 128])
                sIdB = sb2("s_sIdB", [128, 128], BF16)
                sTriu = sb2("s_sTriu", [128, 128])
                sOnes64b = sb2("s_sOnes64b", [64, 1], BF16)
                sOnesF = sb2("s_sOnesF", [1, 64])
                sOneF = sb2("s_sOneF", [1, 1])
                sE = sb2("s_sE", [64, SEG], BF16)
                L = {nm: sb2("s_L_" + nm, [1, SEG]) for nm in
                     ["qk", "ksq", "vsq", "vkg", "qsum", "gcum", "m", "msh", "nrm",
                      "w", "ones"]}
                sLast = sb2("s_sLast", [1, 8])
                vc_last = sb2("s_vc_last", [64, 1])
                dkb = sb2("s_dkb", [64, SEG])
                w0b = sb2("s_w0b", [64, SEG])
                vcT = sb2("s_vcT", [64, SEG])
                k_tm8 = sb2("s_k_tm8", [128, 8, 64], BF16)
                v_tm8 = sb2("s_v_tm8", [128, 8, 64], BF16)
                Va8 = sb2("s_Va8", [128, 8, 64], BF16)
                VaT8 = sb2("s_VaT8", [64, 8, 128], BF16)
                PmT = sb2("s_PmT", [128, 128], BF16)
                S_sb = sb2("s_S_sb", [64, 64], BF16)
                sCols8 = sb2("s_sCols8", [128, 24])
                sY8 = sb2("s_sY8", [128, 8, 64])
                pA = s2.enter_context(nc.psum_tensor("s_pA", [64, 512], F32))[:]
                pTK = s2.enter_context(nc.psum_tensor("s_pTK", [128, 512], BF16))[:]
                pVaT = s2.enter_context(nc.psum_tensor("s_pVaT", [64, 1024], BF16))[:]
                pcols = s2.enter_context(nc.psum_tensor("s_pcols", [128, 24], F32))[:]
                pYb = s2.enter_context(nc.psum_tensor("s_pYb", [128, 64], F32))[:]
                pP = s2.enter_context(nc.psum_tensor("s_pP", [128, 128], F32))[:]
                pS64 = s2.enter_context(nc.psum_tensor("s_pS64", [64, 64], F32))[:]
                pVc8 = s2.enter_context(nc.psum_tensor("s_pVc8", [128, 512], F32))[:]

                p.sync(lambda e: e.dma_start(out=sIdF, in_=ident_in))
                p.sync(lambda e: e.dma_start(out=sTriu, in_=triu_in))
                p.gps_dma(lambda e: e.dma_start(out=sWgb, in_=wgT_in))
                p.vector(lambda e: e.tensor_copy(sIdB, sIdF))
                p.vector(lambda e: e.memset(sOnes64b, 1.0))
                p.vector(lambda e: e.memset(sOnesF, 1.0))
                p.vector(lambda e: e.memset(sOneF, 1.0))
                p.vector(lambda e: e.memset(L["ones"], 1.0))

                def lane_sum(dst, mov, width=SEG):
                    for s0 in range(0, width, 512):
                        p.tensor(lambda e, s0=s0, mov=mov: e.matmul(
                            pA[0:1, :], sOnes64b, mov[:, s0:s0 + 512], start=True, stop=True))
                        p.vector(lambda e, s0=s0, dst=dst: e.tensor_copy(
                            dst[:, s0:s0 + 512], pA[0:1, :]))

                for u in range(nbh):
                    def mk_ld(dst, src, u=u):
                        sv = src.rearrange("(h r) t -> h r t", h=2)
                        def f(par, sv=sv, dst=dst):
                            return lambda e: e.dma_start(
                                out=dst, in_=sv[par, u * 64:(u + 1) * 64, :])
                        return f
                    p.par_dma(mk_ld(sQT, q0_sh))
                    p.par_dma(mk_ld(sKT, k0_sh))
                    p.par_dma(mk_ld(sVT, v0_sh))
                    p.vector(lambda e: e.memset(S_sb, 0.0))
                    p.vector(lambda e: e.memset(vc_last, 0.0))
                    p.vector(lambda e: e.memset(sLast[:, 0:1], -3.0e38))
                    p.vector(lambda e: e.memset(sLast[:, 1:3], 0.0))

                    def mk_scal_ld(dst, src_d, u=u):
                        sv = src_d.rearrange("o (h u2) -> o h u2", h=2)
                        def f(par, sv=sv, dst=dst):
                            return lambda e: e.dma_start(
                                out=dst, in_=sv[0:1, par, u:u + 1])
                        return f
                    p.par_dma(mk_scal_ld(sKvsU, kvs_in))
                    p.par_dma(mk_scal_ld(sWgbU, wgb_in))

                    for g in range(nseg):
                        gs = slice(g * SEG, (g + 1) * SEG)
                        qTg, kTg, vTg = sQT[:, gs], sKT[:, gs], sVT[:, gs]
                        p.vector(lambda e, qTg=qTg, kTg=kTg: e.tensor_tensor(sE, qTg, kTg, ALU.mult))
                        lane_sum(L["qk"], sE)
                        p.scalar(lambda e, kTg=kTg: e.activation(sE, kTg, AF.Square))
                        lane_sum(L["ksq"], sE)
                        p.scalar(lambda e, vTg=vTg: e.activation(sE, vTg, AF.Square))
                        lane_sum(L["vsq"], sE)
                        lane_sum(L["qsum"], qTg)
                        for s0 in range(0, SEG, 512):
                            p.tensor(lambda e, s0=s0, kTg=kTg: e.matmul(
                                pA, sWgb, kTg[:, s0:s0 + 512], start=True, stop=True))
                            p.vector(lambda e, s0=s0, vTg=vTg: e.tensor_tensor(
                                sE[:, s0:s0 + 512], vTg[:, s0:s0 + 512], pA, ALU.mult))
                        lane_sum(L["vkg"], sE)
                        for nm in ("ksq", "vsq"):
                            p.scalar(lambda e, nm=nm: e.activation(L[nm], L[nm], AF.Sqrt))
                            p.vector(lambda e, nm=nm: e.tensor_scalar(L[nm], L[nm], 1e-12, None, ALU.max))
                            p.vector(lambda e, nm=nm: e.reciprocal(L[nm], L[nm]))
                        p.vector(lambda e: e.tensor_tensor(L["qk"], L["qk"], L["ksq"], ALU.mult))
                        p.vector(lambda e: e.tensor_tensor(L["ksq"], L["ksq"], L["vsq"], ALU.mult))
                        p.vector(lambda e: e.tensor_scalar(L["ksq"], L["ksq"], sKvsU, None, ALU.mult))
                        p.vector(lambda e: e.tensor_tensor(L["vkg"], L["vkg"], L["ksq"], ALU.mult))
                        p.vector(lambda e: e.tensor_scalar(L["vkg"], L["vkg"], sWgbU, None, ALU.add))
                        p.scalar(lambda e: e.activation(L["vkg"], L["vkg"], AF.Lrelu, alpha=0.01))
                        p.scalar(lambda e: e.activation(L["vkg"], L["vkg"], AF.Square))
                        p.vector(lambda e: e.tensor_scalar(L["vkg"], L["vkg"], EPS, None, ALU.add))
                        p.vector(lambda e: e.tensor_tensor_scan(
                            L["gcum"], L["ones"], L["vkg"], sLast[0:1, 2:3], ALU.mult, ALU.add))
                        p.vector(lambda e: e.tensor_copy(sLast[:, 2:3], L["gcum"][:, SEG - 1:SEG]))
                        p.vector(lambda e: e.tensor_tensor(L["vkg"], L["vkg"], L["ksq"], ALU.mult))
                        p.vector(lambda e: e.tensor_tensor_scan(
                            L["m"], L["qk"], L["qk"], sLast[0:1, 0:1], ALU.max, ALU.bypass))
                        p.vector(lambda e: e.tensor_copy(L["msh"][:, 1:SEG], L["m"][:, 0:SEG - 1]))
                        p.vector(lambda e: e.tensor_copy(L["msh"][:, 0:1], sLast[:, 0:1]))
                        p.vector(lambda e: e.tensor_copy(sLast[:, 0:1], L["m"][:, SEG - 1:SEG]))
                        p.vector(lambda e: e.tensor_tensor(L["msh"], L["msh"], L["m"], ALU.subtract))
                        p.scalar(lambda e: e.activation(L["msh"], L["msh"], AF.Exp))
                        p.vector(lambda e: e.tensor_tensor(L["qk"], L["qk"], L["m"], ALU.subtract))
                        p.scalar(lambda e: e.activation(L["qk"], L["qk"], AF.Exp))
                        p.vector(lambda e: e.tensor_tensor_scan(
                            L["nrm"], L["msh"], L["qk"], sLast[0:1, 1:2], ALU.mult, ALU.add))
                        p.vector(lambda e: e.tensor_copy(sLast[:, 1:2], L["nrm"][:, SEG - 1:SEG]))
                        p.vector(lambda e: e.tensor_scalar(L["nrm"], L["nrm"], EPS, None, ALU.add))
                        p.vector(lambda e: e.reciprocal(L["nrm"], L["nrm"]))
                        p.vector(lambda e: e.tensor_tensor(L["w"], L["qk"], L["nrm"], ALU.mult))
                        p.vector(lambda e: e.tensor_tensor(L["qsum"], L["qsum"], L["w"], ALU.mult))
                        p.vector(lambda e: e.tensor_tensor(L["qsum"], L["qsum"], L["nrm"], ALU.mult))
                        p.vector(lambda e: e.tensor_tensor(L["qk"], L["qk"], L["vsq"], ALU.mult))
                        p.vector(lambda e: e.tensor_scalar(L["gcum"], L["gcum"], EPS, None, ALU.add))
                        p.vector(lambda e: e.reciprocal(L["gcum"], L["gcum"]))
                        p.vector(lambda e: e.tensor_scalar(L["w"], L["w"], -1.0, 1.0, ALU.mult, ALU.add))
                        p.vector(lambda e: e.tensor_tensor(L["w"], L["w"], L["gcum"], ALU.mult))
                        for s0 in range(0, SEG, 512):
                            p.tensor(lambda e, s0=s0: e.matmul(
                                pA, sOnesF, L["msh"][:, s0:s0 + 512], start=True, stop=True))
                            p.vector(lambda e, s0=s0: e.tensor_copy(dkb[:, s0:s0 + 512], pA))
                        for s0 in range(0, SEG, 512):
                            p.tensor(lambda e, s0=s0: e.matmul(
                                pA, sOnesF, L["qk"][:, s0:s0 + 512], start=True, stop=True))
                            p.vector(lambda e, s0=s0: e.tensor_copy(w0b[:, s0:s0 + 512], pA))
                        p.vector(lambda e, vTg=vTg: e.tensor_tensor(w0b, vTg, w0b, ALU.mult))
                        p.vector(lambda e: e.tensor_tensor_scan(
                            vcT, dkb, w0b, vc_last, ALU.mult, ALU.add))
                        p.vector(lambda e: e.tensor_copy(vc_last, vcT[:, SEG - 1:SEG]))
                        for cb in range(8):
                            p.tensor(lambda e, g=g, cb=cb: e.transpose(
                                pTK[:, cb * 64:(cb + 1) * 64],
                                sKT[:, g * SEG + cb * 128:g * SEG + (cb + 1) * 128],
                                sIdB[0:64, 0:64]), fuse=(cb > 0))
                        p.vector(lambda e: e.tensor_copy(
                            k_tm8.rearrange("q a b -> q (a b)"), pTK))
                        for cb in range(8):
                            p.tensor(lambda e, g=g, cb=cb: e.transpose(
                                pTK[:, cb * 64:(cb + 1) * 64],
                                sVT[:, g * SEG + cb * 128:g * SEG + (cb + 1) * 128],
                                sIdB[0:64, 0:64]), fuse=(cb > 0))
                        p.vector(lambda e: e.tensor_copy(
                            v_tm8.rearrange("q a b -> q (a b)"), pTK))
                        for cb in range(8):
                            for j3, lane in ((0, "vkg"), (1, "w"), (2, "qsum")):
                                p.tensor(lambda e, cb=cb, j3=j3, lane=lane: e.matmul(
                                    pcols[:, cb * 3 + j3:cb * 3 + j3 + 1],
                                    L[lane][:, cb * 128:(cb + 1) * 128], sOneF,
                                    start=True, stop=True), fuse=(cb + j3 > 0))
                        p.vector(lambda e: e.tensor_copy(sCols8, pcols))
                        c3 = sCols8.rearrange("q (cb three) -> q cb three", three=3)
                        p.vector(lambda e: e.tensor_tensor(
                            Va8, v_tm8,
                            c3[:, :, 0:1].broadcast_to((128, 8, 64)), ALU.mult))
                        for cb in range(8):
                            p.tensor(lambda e, cb=cb: e.transpose(
                                pVaT[:, cb * 128:(cb + 1) * 128], Va8[:, cb, :], sIdB),
                                fuse=(cb > 0))
                        p.vector(lambda e: e.tensor_copy(
                            VaT8.rearrange("q a b -> q (a b)"), pVaT))
                        for cb in range(8):
                            p.tensor(lambda e, cb=cb: e.transpose(
                                pVc8[:, cb * 64:(cb + 1) * 64], vcT[:, cb * 128:(cb + 1) * 128],
                                sIdF[0:64, 0:64]), fuse=(cb > 0))
                        for cb in range(8):
                            c = g * 8 + cb
                            qTc = qTg[:, cb * 128:(cb + 1) * 128]
                            p.tensor(lambda e, qTc=qTc: e.matmul(
                                pYb, qTc, S_sb, start=True, stop=False))
                            p.tensor(lambda e, qTc=qTc, cb=cb: e.matmul(
                                pP, VaT8[:, cb, :], qTc, start=True, stop=True), fuse=True)
                            p.vector(lambda e: e.tensor_tensor(PmT, pP, sTriu, ALU.mult))
                            p.tensor(lambda e, cb=cb: e.matmul(
                                pYb, PmT, k_tm8[:, cb, :], start=False, stop=True))
                            p.tensor(lambda e, c=c, cb=cb: e.matmul(
                                pS64, Va8[:, cb, :], k_tm8[:, cb, :],
                                start=(c == 0), stop=(c == seq // 128 - 1)), fuse=True)
                            p.vector(lambda e: e.tensor_copy(S_sb, pS64))
                            p.vector(lambda e, cb=cb: e.tensor_scalar(
                                sY8[:, cb, :], pYb, c3[:, cb, 1:2], None, ALU.mult))
                            p.vector(lambda e, cb=cb: e.scalar_tensor_tensor(
                                sY8[:, cb, :], pVc8[:, cb * 64:(cb + 1) * 64],
                                c3[:, cb, 2:3], sY8[:, cb, :], ALU.mult, ALU.add))
                        yv = y_sh.rearrange("(cb q) (h cc2) -> q cb h cc2", q=128, h=2)
                        def ywr(par, u=u, g=g, yv=yv):
                            return lambda e: e.dma_start(
                                out=yv[:, g * 8:(g + 1) * 8, par, u * 64:(u + 1) * 64],
                                in_=sY8)
                        p.par_dma(ywr)
        emit_l0scan()
        barrier()

        def emit_l0out():
            with contextlib.ExitStack() as s2:
                sb2 = lambda name, shape, dt=F32: s2.enter_context(nc.sbuf_tensor(name, shape, dt))[:]
                sX = sb2("o_sX", [128, D])
                sY = sb2("o_sY", [128, D])
                sYb = sb2("o_sYb", [128, D], BF16)
                sYbT = sb2("o_sYbT", [128, 8, 128], BF16)
                sWoT = sb2("o_sWoT", [128, 8, D], BF16)
                sBo = sb2("o_sBo", [1, D])
                sBob = sb2("o_sBob", [1, D], BF16)
                sOnes = sb2("o_sOnes", [1, 128], BF16)
                sBiasTile = sb2("o_sBiasTile", [128, D])
                stat = sb2("o_stat", [128, 1])
                sSq = sb2("o_sSq", [128, D])
                pP = s2.enter_context(nc.psum_tensor("o_pP", [128, 1024], F32))[:]

                w_r = woT_in.rearrange("(kc q) o -> kc q o", q=128)
                for kc in range(8):
                    p.gps_dma(lambda e, kc=kc: e.dma_start(out=sWoT[:, kc, :], in_=w_r[kc]))
                p.sync(lambda e: e.dma_start(out=sBo, in_=wob_in))
                p.vector(lambda e: e.tensor_copy(sBob, sBo))
                p.vector(lambda e: e.memset(sOnes, 1.0))
                for half in range(2):
                    osl = slice(half * 512, (half + 1) * 512)
                    p.tensor(lambda e, osl=osl: e.matmul(
                        pP[:, osl], sOnes, sBob[:, osl], start=True, stop=True))
                p.vector(lambda e: e.tensor_copy(sBiasTile, pP))

                x_r = xres.rearrange("(c q) f -> c q f", q=128)
                for c in range(nch128):
                    p.sync(lambda e, c=c: e.dma_start(out=sX, in_=x_r[c]))
                    yv2 = y_sh.rearrange("(h r) f -> h r f", h=2)
                    def yld(par, c=c, yv2=yv2):
                        return lambda e: e.dma_start(
                            out=sY, in_=yv2[par, c * 128:(c + 1) * 128, :])
                    p.par_dma(yld)
                    p.scalar(lambda e: e.activation(sSq, sY, AF.Square, accum_out=stat))
                    p.scalar(lambda e: e.activation(stat, stat, AF.Sqrt))
                    p.vector(lambda e: e.tensor_scalar(stat, stat, 1e-12, None, ALU.max))
                    p.vector(lambda e: e.reciprocal(stat, stat))
                    p.vector(lambda e: e.tensor_scalar(sYb, sY, stat, None, ALU.mult))
                    emit_transpose_blocks(p, sYb, sYbT.rearrange("q a b -> q (a b)"), 8)
                    for half in range(2):
                        osl = slice(half * 512, (half + 1) * 512)
                        for kc in range(8):
                            p.tensor(lambda e, kc=kc, osl=osl: e.matmul(
                                pP[:, osl], sYbT[:, kc, :], sWoT[:, kc, osl],
                                start=(kc == 0), stop=(kc == 7)))
                    p.vector(lambda e: e.tensor_add(sX, sX, sBiasTile))
                    p.vector(lambda e: e.tensor_tensor(sX, pP, sX, ALU.add))
                    p.sync(lambda e, c=c: e.dma_start(out=x_r[c], in_=sX))

        emit_l0out()
        emit_ffn0(xres, xres, False, "B")
        es_f0.close()

        wgT = es.enter_context(nc.sbuf_tensor("r_wgT", [128, 8, IFF], BF16))[:]
        wuT = es.enter_context(nc.sbuf_tensor("r_wuT", [128, 8, IFF], BF16))[:]
        wdT = es.enter_context(nc.sbuf_tensor("r_wdT", [128, 22, D], BF16))[:]

        def emit_ffn1(dst_x, tag):
            OSEG = [(s, min(s + 512, IFF)) for s in range(0, IFF, 512)]
            with contextlib.ExitStack() as s2:
                x_r = dst_x.rearrange("(c q) f -> c q f", q=128)

                if tag == "C":
                    with contextlib.ExitStack() as s3:
                        sb3 = lambda name, shape, dt=F32: s3.enter_context(nc.sbuf_tensor(tag + name, shape, dt))[:]
                        sWrow = sb3("f1_sWrow", [128, IFF])
                        sWq = sb3("f1_sWq", [128, IFF], BF16)
                        qA = sb3("f1_qA", [128, IFF])
                        qB = sb3("f1_qB", [128, IFF])
                        qC = sb3("f1_qC", [128, IFF])
                        qD = sb3("f1_qD", [128, IFF])
                        dscrH = sb3("f1_dscrH", [128, IFF // 16])

                        def quant_w(w_ap, n_oc, kdim, wT):
                            w_r = w_ap.rearrange("(oc q) k -> oc q k", q=128)
                            for oc in range(n_oc):
                                p.sync(lambda e, oc=oc, w_r=w_r: e.dma_start(
                                    out=sWrow[:, 0:kdim], in_=w_r[oc]))
                                emit_quant_fp4(p, sWrow[:, 0:kdim], sWq[:, 0:kdim],
                                               qA[:, 0:kdim], qB[:, 0:kdim], qC[:, 0:kdim],
                                               qD[:, 0:kdim], dscrH[:, 0:kdim // 16])
                                for kc in range(kdim // 128):
                                    p.sync(lambda e, oc=oc, kc=kc, wT=wT: e.dma_start_transpose(
                                        wT[:, kc, oc * 128:(oc + 1) * 128],
                                        sWq[:, kc * 128:(kc + 1) * 128]), fuse=(kc > 0))

                        quant_w(wg1_in, 22, D, wgT)
                        quant_w(wu1_in, 22, D, wuT)
                        quant_w(wd1_in, 8, IFF, wdT)

                with contextlib.ExitStack() as s3:
                    sb3 = lambda name, shape, dt=F32: s3.enter_context(nc.sbuf_tensor(tag + name, shape, dt))[:]
                    sX = sb3("p1_sX", [128, D])
                    sU = sb3("p1_sU", [128, D])
                    sUd = sb3("p1_sUd", [128, D], BF16)
                    sUdT = sb3("p1_sUdT", [128, 8, 128], BF16)
                    qA = sb3("p1_qA", [128, D])
                    qB = sb3("p1_qB", [128, D])
                    qC = sb3("p1_qC", [128, D])
                    qD = sb3("p1_qD", [128, D])
                    dscr6 = sb3("p1_dscr6", [128, D // 16])
                    stat_a = sb3("p1_stat_a", [128, 1])
                    stat_b = sb3("p1_stat_b", [128, 1])
                    for c in range(nch128):
                        p.sync(lambda e, c=c: e.dma_start(out=sX, in_=x_r[c]))
                        emit_ln(p, sX.rearrange("q (n f) -> q n f", n=1),
                                sU.rearrange("q (n f) -> q n f", n=1),
                                stat_a, stat_b, qA[:, 0:D], 1)
                        emit_quant_fp4(p, sU, sUd, qA, qB, qC, qD, dscr6)
                        emit_transpose_blocks(p, sUd, sUdT.rearrange("q a b -> q (a b)"), 8)
                        p.sync(lambda e, c=c: e.dma_start(
                            out=uqscr[c], in_=sUdT.rearrange("q a b -> q (a b)")))

                with contextlib.ExitStack() as s3:
                    sb3 = lambda name, shape, dt=F32: s3.enter_context(nc.sbuf_tensor(tag + name, shape, dt))[:]
                    sUdT2 = sb3("p2_sUdT", [128, 2, 8, 128], BF16)
                    sGY = sb3("p2_sGY", [128, 2 * IFF])
                    pPa = s3.enter_context(nc.psum_tensor(tag + "p2_pPa", [128, 512], F32))[:]
                    pPb = s3.enter_context(nc.psum_tensor(tag + "p2_pPb", [128, 512], F32))[:]
                    pP2 = [pPa, pPb]

                    mtarg = {}
                    qtarg = {}

                    def uq_dma(c, qdep):
                        def f(e, t0, sem_, c=c, qdep=qdep):
                            e.wait_ge(sem_, t0)
                            if qdep:
                                e.wait_ge(qsem, qdep)
                            e.dma_start(out=sUdT2[:, c % 2, :, :].rearrange("q a b -> q (a b)"),
                                        in_=uqscr[c]).then_inc(msem, 16)
                        p.raw("gpsimd", f, 0)
                        mcount[0] += 16
                        mtarg[c] = mcount[0]

                    uq_dma(0, 0)
                    uq_dma(1, 0)
                    gidx_all = [0]
                    vtarg = {}
                    for c in range(nch128):
                        for si, (o0, o1) in enumerate(OSEG):
                            for wi, wT in enumerate((wgT, wuT)):
                                gg = gidx_all[0]
                                vdep = vtarg.get(gg - 2, 0)
                                def fgrp(e, t0, sem_, c=c, o0=o0, o1=o1, wT=wT,
                                         gg=gg, vdep=vdep, nm=mtarg[c]):
                                    e.wait_ge(msem, nm)
                                    if vdep:
                                        e.wait_ge(vsem, vdep)
                                    for kc in range(8):
                                        e.matmul(pP2[gg % 2][:, 0:o1 - o0],
                                                 sUdT2[:, c % 2, kc, :], wT[:, kc, o0:o1],
                                                 start=(kc == 0),
                                                 stop=(kc == 7)).then_inc(qsem, 1)
                                p.raw("tensor", fgrp, 0)
                                qcount[0] += 8
                                qtarg[gg] = qcount[0]
                                doff = wi * IFF + o0
                                first_of_chunk = (si == 0 and wi == 0)
                                def fev(e, t0, sem_, gg=gg, o0=o0, o1=o1, doff=doff,
                                        foc=first_of_chunk):
                                    if foc and t0 > 0:
                                        e.wait_ge(sem_, t0)
                                    e.wait_ge(qsem, qtarg[gg])
                                    e.tensor_copy(sGY[:, doff:doff + (o1 - o0)],
                                                  pP2[gg % 2][:, 0:o1 - o0]).then_inc(vsem, 1)
                                p.raw("vector", fev, 0)
                                vcount[0] += 1
                                vtarg[gg] = vcount[0]
                                gidx_all[0] += 1
                        vd = vcount[0]
                        def fst(e, t0, sem_, c=c, vd=vd):
                            if t0 > 0:
                                e.wait_ge(sem_, t0)
                            e.wait_ge(vsem, vd)
                            e.dma_start(out=gyscr[c], in_=sGY).then_inc(sem_, 16)
                        p.raw("sync", fst, 16)
                        if c + 2 < nch128:
                            uq_dma(c + 2, qtarg.get(gidx_all[0] - 1, 0))

                with contextlib.ExitStack() as s3:
                    sb3 = lambda name, shape, dt=F32: s3.enter_context(nc.sbuf_tensor(tag + name, shape, dt))[:]
                    sH = sb3("p3_sH", [128, 2 * IFF])
                    sHq = sb3("p3_sHq", [128, IFF], BF16)
                    sHqT = sb3("p3_sHqT", [128, 22, 128], BF16)
                    qA = sb3("p3_qA", [128, IFF])
                    qB = sb3("p3_qB", [128, IFF])
                    qC = sb3("p3_qC", [128, IFF])
                    qD = sH[:, IFF:2 * IFF]
                    dscrH = sb3("p3_dscrH", [128, IFF // 16])
                    pTT = s3.enter_context(nc.psum_tensor(tag + "p3_pTT", [128, 1024], BF16))[:]
                    for c in range(nch128):
                        p.sync(lambda e, c=c: e.dma_start(out=sH, in_=gyscr[c]))
                        p.scalar(lambda e: e.activation(sH[:, 0:IFF], sH[:, 0:IFF], AF.Silu))
                        p.vector(lambda e: e.tensor_mul(sH[:, 0:IFF], sH[:, 0:IFF], sH[:, IFF:2 * IFF]))
                        emit_quant_fp4(p, sH[:, 0:IFF], sHq, qA, qB, qC, qD, dscrH)
                        for grp in range(3):
                            nblk = min(8, 22 - grp * 8)
                            qd = qcount[0]
                            def ftr(e, t0, sem_, grp=grp, nblk=nblk, qd=qd):
                                e.wait_ge(sem_, t0)
                                for j in range(nblk):
                                    kc = grp * 8 + j
                                    e.transpose(pTT[:, j * 128:(j + 1) * 128],
                                                sHq[:, kc * 128:(kc + 1) * 128],
                                                sIdB2).then_inc(qsem, 1)
                            p.raw("tensor", ftr, 0)
                            qcount[0] += nblk
                            qd2 = qcount[0]
                            def fcp(e, t0, sem_, grp=grp, nblk=nblk, qd2=qd2):
                                if t0 > 0:
                                    e.wait_ge(sem_, t0)
                                e.wait_ge(qsem, qd2)
                                e.tensor_copy(
                                    sHqT.rearrange("q a b -> q (a b)")[:, grp * 1024:grp * 1024 + nblk * 128],
                                    pTT[:, 0:nblk * 128]).then_inc(sem_, 1)
                            p.raw("vector", fcp, 1)
                        p.sync(lambda e, c=c: e.dma_start(
                            out=hscr[c], in_=sHqT.rearrange("q a b -> q (a b)")))

                with contextlib.ExitStack() as s3:
                    sb3 = lambda name, shape, dt=F32: s3.enter_context(nc.sbuf_tensor(tag + name, shape, dt))[:]
                    sX = sb3("p4_sX", [128, D])
                    sHqT2 = sb3("p4_sHqT", [128, 2, 22, 128], BF16)
                    pP4 = s3.enter_context(nc.psum_tensor(tag + "p4_pP", [128, 1024], F32))[:]

                    mtarg4 = {}
                    qtarg4 = {}

                    def hq_dma(c):
                        qdep = qtarg4.get(c - 2, 0)
                        def f(e, t0, sem_, c=c, qdep=qdep):
                            e.wait_ge(sem_, t0)
                            if qdep:
                                e.wait_ge(qsem, qdep)
                            e.dma_start(out=sHqT2[:, c % 2, :, :].rearrange("q a b -> q (a b)"),
                                        in_=hscr[c]).then_inc(msem, 16)
                        p.raw("gpsimd", f, 0)
                        mcount[0] += 16
                        mtarg4[c] = mcount[0]

                    hq_dma(0)
                    hq_dma(1)
                    for c in range(nch128):
                        p.sync(lambda e, c=c: e.dma_start(out=sX, in_=x_r[c]))
                        def fgrp(e, t0, sem_, c=c, nm=mtarg4[c]):
                            if t0 > 0:
                                e.wait_ge(sem_, t0)
                            e.wait_ge(msem, nm)
                            for half in range(2):
                                osl = slice(half * 512, (half + 1) * 512)
                                for kc in range(22):
                                    e.matmul(pP4[:, osl], sHqT2[:, c % 2, kc, :],
                                             wdT[:, kc, osl], start=(kc == 0),
                                             stop=(kc == 21)).then_inc(qsem, 1)
                        p.raw("tensor", fgrp, 0)
                        qcount[0] += 44
                        qtarg4[c] = qcount[0]
                        qd = qcount[0]
                        def fcomb(e, t0, sem_, c=c, qd=qd):
                            if t0 > 0:
                                e.wait_ge(sem_, t0)
                            e.wait_ge(qsem, qd)
                            e.scalar_tensor_tensor(
                                sX, pP4, 0.5, sX, ALU.mult, ALU.add).then_inc(sem_, 1)
                        p.raw("vector", fcomb, 1)
                        p.sync(lambda e, c=c: e.dma_start(out=x_r[c], in_=sX))
                        if c + 2 < nch128:
                            hq_dma(c + 2)

        emit_ffn1(xres, "C")

        def emit_l1qkv():
            with contextlib.ExitStack() as s2:
                sb2 = lambda name, shape, dt=F32: s2.enter_context(nc.sbuf_tensor(name, shape, dt))[:]
                sX = sb2("a_sX", [128, D])
                sU = sb2("a_sU", [128, D])
                sUb = sb2("a_sUb", [128, D], BF16)
                sUbT = sb2("a_sUbT", [128, 8, 128], BF16)
                bA = sb2("a_bA", [128, D])
                stat_a = sb2("a_stat_a", [128, 1])
                stat_b = sb2("a_stat_b", [128, 1])
                sWqT = sb2("a_sWqT", [128, 8, D], BF16)
                sWkT = sb2("a_sWkT", [128, 8, D], BF16)
                sWvT = sb2("a_sWvT", [128, 8, D], BF16)
                sQTt = sb2("a_sQTt", [128, 8, 128], BF16)
                sKTt = sb2("a_sKTt", [128, 8, 128], BF16)
                sV = sb2("a_sV", [128, D], BF16)
                pF = s2.enter_context(nc.psum_tensor("a_pF", [128, 128], F32))[:]
                pV = s2.enter_context(nc.psum_tensor("a_pV", [128, 1024], F32))[:]

                for w_in, sW in ((waqT_in, sWqT), (wakT_in, sWkT), (wavT_in, sWvT)):
                    w_r = w_in.rearrange("(kc q) o -> kc q o", q=128)
                    for kc in range(8):
                        p.gps_dma(lambda e, kc=kc, w_r=w_r, sW=sW: e.dma_start(
                            out=sW[:, kc, :], in_=w_r[kc]))

                x_r = xres.rearrange("(c q) f -> c q f", q=128)
                q_r = q1_sh.rearrange("(oc p) (h t) -> p oc h t", p=128, h=2)
                k_r = k1_sh.rearrange("(oc p) (h t) -> p oc h t", p=128, h=2)
                v1v = v1_sh.rearrange("(h r) f -> h r f", h=2)
                for c in range(nch128):
                    p.sync(lambda e, c=c: e.dma_start(out=sX, in_=x_r[c]))
                    emit_ln(p, sX.rearrange("q (n f) -> q n f", n=1),
                            sU.rearrange("q (n f) -> q n f", n=1), stat_a, stat_b, bA, 1)
                    p.vector(lambda e: e.tensor_copy(sUb, sU))
                    emit_transpose_blocks(p, sUb, sUbT.rearrange("q a b -> q (a b)"), 8)
                    for sW, sOut in ((sWqT, sQTt), (sWkT, sKTt)):
                        for oc in range(8):
                            for kc in range(8):
                                p.tensor(lambda e, oc=oc, kc=kc, sW=sW: e.matmul(
                                    pF, sW[:, kc, oc * 128:(oc + 1) * 128], sUbT[:, kc, :],
                                    start=(kc == 0), stop=(kc == 7)))
                            p.vector(lambda e, oc=oc, sOut=sOut: e.tensor_copy(sOut[:, oc, :], pF))
                    for half in range(2):
                        osl = slice(half * 512, (half + 1) * 512)
                        for kc in range(8):
                            p.tensor(lambda e, kc=kc, osl=osl: e.matmul(
                                pV[:, osl], sUbT[:, kc, :], sWvT[:, kc, osl],
                                start=(kc == 0), stop=(kc == 7)))
                    p.vector(lambda e: e.tensor_copy(sV, pV))
                    def qwr(par, c=c, r=q_r, src=sQTt):
                        return lambda e: e.dma_start(
                            out=r[:, :, par, c * 128:(c + 1) * 128], in_=src)
                    def kwr(par, c=c, r=k_r, src=sKTt):
                        return lambda e: e.dma_start(
                            out=r[:, :, par, c * 128:(c + 1) * 128], in_=src)
                    def vwr(par, c=c):
                        return lambda e: e.dma_start(
                            out=v1v[par, c * 128:(c + 1) * 128, :], in_=sV)
                    p.par_dma(qwr)
                    p.par_dma(kwr)
                    p.par_dma(vwr)

        emit_l1qkv()
        barrier()

        def emit_l1flash():
            nbh = 8
            nsc = seq // 128
            nts = seq // 512
            GRP = 3
            ngrp = (nsc + GRP - 1) // GRP
            assert ngrp % 2 == 0, "pS parity scheme needs even group count"
            ecount = [0]
            with contextlib.ExitStack() as s2:
                sb2 = lambda name, shape, dt=F32: s2.enter_context(nc.sbuf_tensor(name, shape, dt))[:]
                sKT = sb2("fl_sKT", [64, seq], BF16)
                sQT = sb2("fl_sQT", [64, seq], BF16)
                sVext = sb2("fl_sVext", [128, nsc, 65], BF16)
                sP = sb2("fl_sP", [128, 2, GRP * 512], BF16)
                sOT = sb2("fl_sOT", [65, 512])
                sInv = sb2("fl_sInv", [1, 512])
                sOnes64 = sb2("fl_sOnes64", [1, 64])
                pSa = s2.enter_context(nc.psum_tensor("fl_pSa", [128, GRP * 512], F32))[:]
                pSb = s2.enter_context(nc.psum_tensor("fl_pSb", [128, GRP * 512], F32))[:]
                pO = s2.enter_context(nc.psum_tensor("fl_pO", [65, 512], F32))[:]
                pB = s2.enter_context(nc.psum_tensor("fl_pB", [64, 512], F32))[:]
                pS2 = [pSa, pSb]

                p.vector(lambda e: e.memset(sOnes64, 1.0))
                p.vector(lambda e: e.memset(sVext[:, :, 64:65], 1.0))
                v_r = v1_sh.rearrange("(c q) (h d2) -> q c h d2", q=128, h=2)
                for u in range(nbh):
                    def mk_ld(dst, src, u=u):
                        sv = src.rearrange("(h r) t -> h r t", h=2)
                        def f(par, sv=sv, dst=dst):
                            return lambda e: e.dma_start(
                                out=dst, in_=sv[par, u * 64:(u + 1) * 64, :])
                        return f
                    p.par_dma(mk_ld(sKT, k1_sh))
                    p.par_dma(mk_ld(sQT, q1_sh))
                    def vld(par, u=u):
                        return lambda e: e.dma_start(
                            out=sVext[:, :, 0:64],
                            in_=v_r[:, :, par, u * 64:(u + 1) * 64])
                    p.par_dma(vld)
                    for ts in range(nts):
                        tsl = slice(ts * 512, (ts + 1) * 512)
                        e_base = ecount[0]

                        def scores(g):
                            sc0 = g * GRP
                            n = min(GRP, nsc - sc0)
                            for i in range(n):
                                sc = sc0 + i
                                if i == 0:
                                    need_e = max(0, e_base + g - 1)
                                    def fsc(e, t0, sem_, sc=sc, g=g, tsl=tsl,
                                            need_e=need_e, gw=(g == 0)):
                                        if gw:
                                            e.wait_ge(sem_, t0)
                                        if need_e:
                                            e.wait_ge(esem, need_e)
                                        e.matmul(pS2[g % 2][:, 0:512],
                                                 sKT[:, sc * 128:(sc + 1) * 128],
                                                 sQT[:, tsl],
                                                 start=True, stop=True).then_inc(sem_, 1)
                                    p.raw("tensor", fsc, 1)
                                else:
                                    p.tensor(lambda e, sc=sc, i=i, g=g, tsl=tsl: e.matmul(
                                        pS2[g % 2][:, i * 512:(i + 1) * 512],
                                        sKT[:, sc * 128:(sc + 1) * 128], sQT[:, tsl],
                                        start=True, stop=True), fuse=True)
                            return n

                        def expg(g, n):
                            def fex(e, t0, sem_, g=g, n=n):
                                e.wait_ge(sem_, t0)
                                e.activation(sP[:, g % 2, 0:n * 512],
                                             pS2[g % 2][:, 0:n * 512], AF.Exp,
                                             scale=1.0 / 32.0).then_inc(esem, 1)
                            p.raw("scalar", fex, 0)
                            ecount[0] += 1

                        def pvg(g, n):
                            for i in range(n):
                                sc = g * GRP + i
                                if i == 0:
                                    need_e = e_base + g + 1
                                    def fpv(e, t0, sem_, sc=sc, g=g, need_e=need_e,
                                            gw=(g == 0)):
                                        if gw:
                                            e.wait_ge(sem_, t0)
                                        e.wait_ge(esem, need_e)
                                        e.matmul(pO, sVext[:, sc, :],
                                                 sP[:, g % 2, 0:512],
                                                 start=(sc == 0),
                                                 stop=(sc == nsc - 1)).then_inc(sem_, 1)
                                    p.raw("tensor", fpv, 1)
                                else:
                                    p.tensor(lambda e, sc=sc, i=i, g=g: e.matmul(
                                        pO, sVext[:, sc, :],
                                        sP[:, g % 2, i * 512:(i + 1) * 512],
                                        start=(sc == 0), stop=(sc == nsc - 1)),
                                        fuse=True)

                        nlist = [None] * ngrp
                        nlist[0] = scores(0)
                        expg(0, nlist[0])
                        for g in range(1, ngrp):
                            nlist[g] = scores(g)
                            expg(g, nlist[g])
                            pvg(g - 1, nlist[g - 1])
                        pvg(ngrp - 1, nlist[ngrp - 1])

                        p.vector(lambda e: e.tensor_copy(sOT, pO))
                        p.vector(lambda e: e.reciprocal(sInv, sOT[64:65, :]))
                        p.tensor(lambda e: e.matmul(pB, sOnes64, sInv, start=True, stop=True))
                        p.vector(lambda e: e.tensor_tensor(sOT[0:64, :], sOT[0:64, :], pB, ALU.mult))
                        ov = ot_sh.rearrange("(h r) t -> h r t", h=2)
                        def owr(par, u=u, tsl=tsl, ov=ov):
                            return lambda e: e.dma_start(
                                out=ov[par, u * 64:(u + 1) * 64, tsl], in_=sOT[0:64, :])
                        p.par_dma(owr)

        emit_l1flash()
        barrier()

        def emit_l1out():
            with contextlib.ExitStack() as s2:
                sb2 = lambda name, shape, dt=F32: s2.enter_context(nc.sbuf_tensor(name, shape, dt))[:]
                sX = sb2("u_sX", [128, D])
                QT4 = min(ntok, 1024)
                sOTb = sb2("u_sOTb", [128, 8, QT4], BF16)
                sWuT = sb2("u_sWuT", [128, 8, D], BF16)
                sBu = sb2("u_sBu", [1, D])
                sBub = sb2("u_sBub", [1, D], BF16)
                sOnes = sb2("u_sOnes", [1, 128], BF16)
                sBiasTile = sb2("u_sBiasTile", [128, D])
                pP = s2.enter_context(nc.psum_tensor("u_pP", [128, 1024], F32))[:]

                otv = ot_sh.rearrange("r (h t) -> r h t", h=2)

                def load_quarter(qt):
                    for kc in range(8):
                        def otld(par, kc=kc, qt=qt):
                            return lambda e: e.dma_start(
                                out=sOTb[:, kc, :],
                                in_=otv[kc * 128:(kc + 1) * 128, par,
                                        qt * QT4:(qt + 1) * QT4])
                        p.par_dma(otld)
                w_r = wauT_in.rearrange("(kc q) o -> kc q o", q=128)
                for kc in range(8):
                    p.gps_dma(lambda e, kc=kc: e.dma_start(out=sWuT[:, kc, :], in_=w_r[kc]))
                p.sync(lambda e: e.dma_start(out=sBu, in_=bau_in))
                p.vector(lambda e: e.tensor_copy(sBub, sBu))
                p.vector(lambda e: e.memset(sOnes, 1.0))
                for half in range(2):
                    osl = slice(half * 512, (half + 1) * 512)
                    p.tensor(lambda e, osl=osl: e.matmul(
                        pP[:, osl], sOnes, sBub[:, osl], start=True, stop=True))
                p.vector(lambda e: e.tensor_copy(sBiasTile, pP))

                x_r = xres.rearrange("(c q) f -> c q f", q=128)
                for c in range(nch128):
                    if c % (QT4 // 128) == 0:
                        load_quarter(c // (QT4 // 128))
                    cq = c % (QT4 // 128)
                    p.sync(lambda e, c=c: e.dma_start(out=sX, in_=x_r[c]))
                    p.vector(lambda e: e.tensor_add(sX, sX, sBiasTile))
                    for half in range(2):
                        osl = slice(half * 512, (half + 1) * 512)
                        for kc in range(8):
                            p.tensor(lambda e, kc=kc, osl=osl, cq=cq: e.matmul(
                                pP[:, osl], sOTb[:, kc, cq * 128:(cq + 1) * 128], sWuT[:, kc, osl],
                                start=(kc == 0), stop=(kc == 7)))
                    p.vector(lambda e: e.tensor_tensor(sX, pP, sX, ALU.add))
                    p.sync(lambda e, c=c: e.dma_start(out=x_r[c], in_=sX))

        emit_l1out()
        emit_ffn1(xres, "D")
        with contextlib.ExitStack() as s2:
            sFin = s2.enter_context(nc.sbuf_tensor("sFin", [128, D], F32))[:]
            xi_r = xres.rearrange("(c q) f -> c q f", q=128)
            xo_r = x_out.rearrange("(c q) f -> c q f", q=128)
            for c in range(nch128):
                p.sync(lambda e, c=c: e.dma_start(out=sFin, in_=xi_r[c]))
                p.sync(lambda e, c=c: e.dma_start(out=xo_r[c], in_=sFin))

        p.emit(block, sem)
    return nc


_fused_cache = {}


def run_fused(x0, inp, ncore):
    B, L, Dm = x0.shape
    ntok = B * L // ncore
    if ntok not in _fused_cache:
        _fused_cache[ntok] = build_fused(ntok)
    nc = _fused_cache[ntok]
    tt = make_toeplitz(inp["L0_attn_spectral_basis"])
    rb = lambda b: np.ascontiguousarray(b.reshape(8, 128).T)
    tr = lambda w: np.ascontiguousarray(w.T)
    nonce = np.full((1, 128), np.random.randint(1, 1 << 28), np.int32)
    common = {
        "nonce_in": nonce,
        "w0_in": inp["L0_ffn_weight"], "b0_in": inp["L0_ffn_bias"],
        "tt_in": tt, "mphiT_in": tr(inp["L0_attn_M_phi_w"]),
        "mb_in": rb(inp["L0_attn_M_phi_b"]),
        "wqT_in": tr(inp["L0_attn_wq_w"]), "qb_in": rb(inp["L0_attn_wq_b"]),
        "wkT_in": tr(inp["L0_attn_wk_w"]), "kb_in": rb(inp["L0_attn_wk_b"]),
        "wvT_in": tr(inp["L0_attn_wv_w"]), "vb_in": rb(inp["L0_attn_wv_b"]),
        "wgT_in": np.ascontiguousarray(inp["L0_attn_wg_z_w"].reshape(64, 64).T),
        "kvs_in": np.ascontiguousarray(inp["L0_attn_kv_norm_scale"].reshape(1, 16)),
        "wgb_in": np.full((1, 16), float(np.asarray(inp["L0_attn_wg_z_b"]).reshape(-1)[0]), np.float32),
        "ident_in": np.eye(128, dtype=np.float32),
        "triu_in": np.triu(np.ones((128, 128), np.float32)),
        "woT_in": tr(inp["L0_attn_wo_w"]), "wob_in": inp["L0_attn_wo_b"].reshape(1, D),
        "wg1_in": inp["L1_ffn_w_gate"], "wu1_in": inp["L1_ffn_w_up"],
        "wd1_in": inp["L1_ffn_w_down"],
        "waqT_in": tr(inp["L1_attn_Wq"]), "wakT_in": tr(inp["L1_attn_Wk"]),
        "wavT_in": tr(inp["L1_attn_Wv"]), "wauT_in": tr(inp["L1_attn_Wu"]),
        "bau_in": inp["L1_attn_bu"].reshape(1, D),
    }
    xs = x0.reshape(ncore, ntok, Dm)
    in_maps = [{"x_in": np.ascontiguousarray(xs[c]), **common} for c in range(ncore)]
    res = run_bass_kernel_spmd(nc, in_maps, core_ids=list(range(ncore)))
    return np.stack([r["x_out"] for r in res.results]).reshape(B, L, Dm)


# version 11
# speedup vs baseline: 1.0340x; 1.0340x over previous
import contextlib
import numpy as np
import ml_dtypes
import concourse.bass as bass
import concourse.mybir as mybir
from concourse.bass_utils import run_bass_kernel_spmd

F32 = mybir.dt.float32
BF16 = mybir.dt.bfloat16
I32 = mybir.dt.int32
AF = mybir.ActivationFunctionType
ALU = mybir.AluOpType
AX = mybir.AxisListType

NCORE = 8
D = 1024
FUSED = True
RNE_C = 12582912.0


class Prog:

    def __init__(self):
        self.steps = []
        self.pre = {}
        self.ctx = {}

    def sync(self, fn, fuse=False, wait=None):
        self.steps.append(("sync", fn, 16, fuse, wait))

    def gps_dma(self, fn, fuse=False, wait=None):
        self.steps.append(("gpsimd", fn, 16, fuse, wait))

    def tensor(self, fn, fuse=False, wait=None):
        self.steps.append(("tensor", fn, 1, fuse, wait))

    def vector(self, fn, fuse=False, wait=None):
        self.steps.append(("vector", fn, 1, fuse, wait))

    def scalar(self, fn, fuse=False, wait=None):
        self.steps.append(("scalar", fn, 1, fuse, wait))

    def mark(self):
        return ("mark", len(self.steps))

    def raw(self, engine, fn, inc):
        self.steps.append((engine, ("raw", fn), inc, False, None))

    def par_dma(self, mk):
        def fn(e, t0, sem):
            if t0 > 0:
                e.wait_ge(sem, t0)
            with e.If_eq(self.ctx["b_par"], 0):
                mk(0)(e).then_inc(sem, 16)
            with e.Else():
                mk(1)(e).then_inc(sem, 16)
        self.raw("gpsimd", fn, 16)

    def resolve(self, tok):
        return self._cum[tok[1]]

    def emit(self, block, sem):
        cum = [0] * (len(self.steps) + 1)
        for i, st in enumerate(self.steps):
            cum[i + 1] = cum[i] + st[2]
        self._cum = cum
        slots = []
        prev_eng = None
        for i, (eng, fn, inc, fuse, wait) in enumerate(self.steps):
            t0 = cum[i]
            if wait is not None:
                wv = cum[wait[1]]
            else:
                skip_wait = (prev_eng == eng) and (fuse or eng == "tensor")
                wv = 0 if skip_wait else t0
            slots.append((eng, fn, inc, t0, wv))
            prev_eng = eng
        total = cum[-1]

        def make_body(engname):
            def body(eng):
                if engname in self.pre:
                    self.pre[engname](eng)
                for name, fn, inc, t0, wv in slots:
                    if name != engname:
                        continue
                    if isinstance(fn, tuple) and fn[0] == "raw":
                        fn[1](eng, t0, sem)
                        continue
                    if wv > 0:
                        eng.wait_ge(sem, wv)
                    fn(eng).then_inc(sem, inc)
                if engname == "sync":
                    eng.wait_ge(sem, total)
            return body

        for name in ["sync", "gpsimd", "tensor", "vector", "scalar"]:
            getattr(block, name)(make_body(name))



def emit_ln(p, x3, out, stat_a, stat_b, big_a, nt, feat=D, eps=1e-5):
    big3 = big_a.rearrange("q (n f) -> q n f", f=feat)
    p.vector(lambda e: e.reduce_sum(stat_a, x3, axis=AX.X))
    p.vector(lambda e: e.tensor_scalar(stat_a, stat_a, 1.0 / feat, None, ALU.mult))
    p.vector(lambda e: e.tensor_tensor(
        out.rearrange("q n f -> q n f") if False else out, x3,
        stat_a.unsqueeze(2).broadcast_to(x3.shape), ALU.subtract))
    p.scalar(lambda e: e.activation(big3, out, AF.Square))
    p.vector(lambda e: e.reduce_sum(stat_b, big3, axis=AX.X))
    p.vector(lambda e: e.tensor_scalar(stat_b, stat_b, 1.0 / feat, eps, ALU.mult, ALU.add))
    p.scalar(lambda e: e.activation(stat_b, stat_b, AF.Sqrt))
    p.vector(lambda e: e.reciprocal(stat_b, stat_b))
    p.vector(lambda e: e.tensor_tensor(
        out, out, stat_b.unsqueeze(2).broadcast_to(x3.shape), ALU.mult))


def emit_quant_fp6(p, u, out_bf, bA, bB, bC, bD, dscr):
    F = u.shape[1]
    nb = F // 32
    u3 = u.rearrange("q (n k) -> q n k", k=32)
    p.vector(lambda e: e.reduce_max(dscr, u3, axis=AX.X, apply_absolute_value=True))
    p.vector(lambda e: e.tensor_scalar(dscr, dscr, 1.0 / 60.0, 1.25e-31, ALU.mult, ALU.max))
    ivs = bB[:, 0:nb]
    p.vector(lambda e: e.reciprocal(ivs, dscr))
    bA3 = bA.rearrange("q (n k) -> q n k", k=32)
    p.vector(lambda e: e.tensor_tensor(
        bA3, u3, ivs.unsqueeze(2).broadcast_to(u3.shape), ALU.mult))
    p.scalar(lambda e: e.activation(bC, bA, AF.Abs))
    p.vector(lambda e: e.tensor_scalar(bD, bC, 16.0, 1.0, ALU.is_ge, ALU.add))
    p.vector(lambda e: e.tensor_scalar(bB, bC, 32.0, 1.0, ALU.is_ge, ALU.add))
    p.vector(lambda e: e.tensor_tensor(bC, bD, bB, ALU.mult))
    p.vector(lambda e: e.tensor_scalar(bD, bD, -0.5, 1.5, ALU.mult, ALU.add))
    p.vector(lambda e: e.tensor_scalar(bB, bB, -0.5, 1.5, ALU.mult, ALU.add))
    p.vector(lambda e: e.tensor_tensor(bD, bD, bB, ALU.mult))
    p.vector(lambda e: e.tensor_tensor(bD, bA, bD, ALU.mult))
    p.vector(lambda e: e.tensor_scalar(bD, bD, RNE_C, RNE_C, ALU.add, ALU.subtract))
    p.vector(lambda e: e.tensor_tensor(bD, bD, bC, ALU.mult))
    bD3 = bD.rearrange("q (n k) -> q n k", k=32)
    out3 = out_bf.rearrange("q (n k) -> q n k", k=32)
    p.vector(lambda e: e.tensor_tensor(
        out3, bD3, dscr.unsqueeze(2).broadcast_to(bD3.shape), ALU.mult))


def emit_quant_fp4(p, u, out_bf, bA, bB, bC, bD, dscr):
    F = u.shape[1]
    nb = F // 16
    u3 = u.rearrange("q (n k) -> q n k", k=16)
    p.vector(lambda e: e.reduce_max(dscr, u3, axis=AX.X, apply_absolute_value=True))
    p.vector(lambda e: e.tensor_scalar(dscr, dscr, 1.0 / 12.0, 5e-13, ALU.mult, ALU.max))
    ivs = bB[:, 0:nb]
    p.vector(lambda e: e.reciprocal(ivs, dscr))
    bA3 = bA.rearrange("q (n k) -> q n k", k=16)
    p.vector(lambda e: e.tensor_tensor(
        bA3, u3, ivs.unsqueeze(2).broadcast_to(u3.shape), ALU.mult))
    p.scalar(lambda e: e.activation(bC, bA, AF.Abs))
    p.vector(lambda e: e.tensor_scalar(bD, bC, 4.0, 1.0, ALU.is_ge, ALU.add))
    p.vector(lambda e: e.tensor_scalar(bB, bC, 8.0, 1.0, ALU.is_ge, ALU.add))
    p.vector(lambda e: e.tensor_tensor(bA, bD, bB, ALU.mult))
    p.vector(lambda e: e.tensor_scalar(bD, bD, -0.5, 1.5, ALU.mult, ALU.add))
    p.vector(lambda e: e.tensor_scalar(bB, bB, -0.5, 1.5, ALU.mult, ALU.add))
    p.vector(lambda e: e.tensor_tensor(bD, bD, bB, ALU.mult))
    p.vector(lambda e: e.tensor_tensor(bD, bC, bD, ALU.mult))
    p.vector(lambda e: e.tensor_scalar(bC, bD, RNE_C, RNE_C, ALU.add, ALU.subtract))
    p.vector(lambda e: e.tensor_tensor(bB, bC, bD, ALU.subtract))
    p.vector(lambda e: e.tensor_scalar(bB, bB, 0.5, None, ALU.is_equal))
    p.vector(lambda e: e.tensor_tensor(bC, bC, bB, ALU.subtract))
    p.vector(lambda e: e.tensor_tensor(bC, bC, bA, ALU.mult))
    p.scalar(lambda e: e.activation(bD, u, AF.Sign))
    p.vector(lambda e: e.tensor_tensor(bC, bC, bD, ALU.mult))
    bC3 = bC.rearrange("q (n k) -> q n k", k=16)
    out3 = out_bf.rearrange("q (n k) -> q n k", k=16)
    p.vector(lambda e: e.tensor_tensor(
        out3, bC3, dscr.unsqueeze(2).broadcast_to(bC3.shape), ALU.mult))


def emit_transpose_blocks(p, src_bf, dst_bf, nblk):
    for j in range(nblk):
        sl = slice(j * 128, (j + 1) * 128)
        p.sync(lambda e, sl=sl: e.dma_start_transpose(dst_bf[:, sl], src_bf[:, sl]),
               fuse=(j > 0))



def build_ffn0(ntok):
    nc = bass.Bass("TRN2", target_bir_lowering=False, debug=False)
    x_in = nc.dram_tensor("x_in", [ntok, D], F32, kind="ExternalInput").ap()
    w_in = nc.dram_tensor("w_in", [D, D], F32, kind="ExternalInput").ap()
    b_in = nc.dram_tensor("b_in", [D], F32, kind="ExternalInput").ap()
    x_out = nc.dram_tensor("x_out", [ntok, D], F32, kind="ExternalOutput").ap()

    CH = 512
    assert ntok % CH == 0
    nch = ntok // CH
    TB = CH // 128

    p = Prog()
    es = contextlib.ExitStack()
    with es:
        sb = lambda name, shape, dt=F32: es.enter_context(nc.sbuf_tensor(name, shape, dt))[:]
        sX = sb("sX", [128, TB, D])
        sU = sb("sU", [128, TB * D])
        sUd = sb("sUd", [128, TB * D], BF16)
        sUdT = sb("sUdT", [128, TB * 8, 128], BF16)
        bA = sb("bA", [128, TB * D])
        bB = sb("bB", [128, TB * D])
        bC = sb("bC", [128, TB * D])
        bD = sb("bD", [128, TB * D])
        dscr = sb("dscr", [128, TB * D // 32])
        stat_a = sb("stat_a", [128, TB])
        stat_b = sb("stat_b", [128, TB])
        sWrow = sb("sWrow", [128, D])
        sWd = sb("sWd", [128, D], BF16)
        sWT = sb("sWT", [128, 8, D], BF16)
        sBias = sb("sBias", [1, D])
        sBiasb = sb("sBiasb", [1, D], BF16)
        sOnes = sb("sOnes", [1, 128], BF16)
        wq_dscr = sb("wq_dscr", [128, D // 32])
        pT = es.enter_context(nc.psum_tensor("pT", [128, D], F32))[:]
        sem = es.enter_context(nc.semaphore("sem"))
        block = es.enter_context(nc.Block())

        w_r = w_in.rearrange("(oc q) k -> oc q k", q=128)
        for oc in range(8):
            p.sync(lambda e, oc=oc: e.dma_start(out=sWrow, in_=w_r[oc]))
            emit_quant_fp6(p, sWrow, sWd, bA[:, 0:D], bB[:, 0:D], bC[:, 0:D],
                           bD[:, 0:D], wq_dscr)
            for kc in range(8):
                p.sync(lambda e, oc=oc, kc=kc: e.dma_start_transpose(
                    sWT[:, kc, oc * 128:(oc + 1) * 128],
                    sWd[:, kc * 128:(kc + 1) * 128]))
        p.sync(lambda e: e.dma_start(out=sBias, in_=b_in.rearrange("(o f) -> o f", o=1)))
        p.vector(lambda e: e.tensor_copy(sBiasb, sBias))
        p.vector(lambda e: e.memset(sOnes, 1.0))

        x_r = x_in.rearrange("(c tb q) f -> c q tb f", q=128, tb=TB)
        xo_r = x_out.rearrange("(c tb q) f -> c q tb f", q=128, tb=TB)
        for c in range(nch):
            p.sync(lambda e, c=c: e.dma_start(out=sX, in_=x_r[c]))
            emit_ln(p, sX, sU.rearrange("q (n f) -> q n f", f=D), stat_a, stat_b, bA, TB)
            emit_quant_fp6(p, sU, sUd, bA, bB, bC, bD, dscr)
            emit_transpose_blocks(p, sUd, sUdT.rearrange("q a b -> q (a b)"), TB * 8)
            for tb in range(TB):
                for half in range(2):
                    osl = slice(half * 512, (half + 1) * 512)
                    for kc in range(8):
                        p.tensor(lambda e, tb=tb, kc=kc, osl=osl: e.matmul(
                            pT[:, osl], sUdT[:, tb * 8 + kc, :], sWT[:, kc, osl],
                            start=(kc == 0), stop=False))
                    p.tensor(lambda e, osl=osl: e.matmul(
                        pT[:, osl], sOnes, sBiasb[:, osl], start=False, stop=True))
                p.vector(lambda e, tb=tb: e.scalar_tensor_tensor(
                    sX[:, tb, :], pT, 0.5, sX[:, tb, :], ALU.mult, ALU.add))
            p.sync(lambda e, c=c: e.dma_start(out=xo_r[c], in_=sX))

        p.emit(block, sem)
    return nc


def run_ffn0(x, w, b):
    ntok = x.shape[1]
    nc = build_ffn0(ntok)
    in_maps = [{"x_in": np.ascontiguousarray(x[i]), "w_in": w, "b_in": b}
               for i in range(x.shape[0])]
    res = run_bass_kernel_spmd(nc, in_maps, core_ids=list(range(x.shape[0])))
    return np.stack([r["x_out"] for r in res.results])


IFF = 2816


def build_ffn1(ntok):
    nc = bass.Bass("TRN2", target_bir_lowering=False, debug=False)
    x_in = nc.dram_tensor("x_in", [ntok, D], F32, kind="ExternalInput").ap()
    wg_in = nc.dram_tensor("wg_in", [IFF, D], F32, kind="ExternalInput").ap()
    wu_in = nc.dram_tensor("wu_in", [IFF, D], F32, kind="ExternalInput").ap()
    wd_in = nc.dram_tensor("wd_in", [D, IFF], F32, kind="ExternalInput").ap()
    x_out = nc.dram_tensor("x_out", [ntok, D], F32, kind="ExternalOutput").ap()
    hscr = nc.dram_tensor("hscr", [ntok // 128, 128, IFF], BF16).ap()

    nch = ntok // 128
    OSEG = [(s, min(s + 512, IFF)) for s in range(0, IFF, 512)]

    p = Prog()
    with contextlib.ExitStack() as es:
        sb = lambda name, shape, dt=F32: es.enter_context(nc.sbuf_tensor(name, shape, dt))[:]
        sX = sb("sX", [128, D])
        sU = sb("sU", [128, D])
        sUd = sb("sUd", [128, D], BF16)
        sUdT = sb("sUdT", [128, 8, 128], BF16)
        bA = sb("bA", [128, IFF])
        bB = sb("bB", [128, IFF])
        bC = sb("bC", [128, IFF])
        bD = sb("bD", [128, IFF])
        dscr6 = sb("dscr6", [128, D // 16])
        dscrH = sb("dscrH", [128, IFF // 16])
        stat_a = sb("stat_a", [128, 1])
        stat_b = sb("stat_b", [128, 1])
        sWrow = sb("sWrow", [128, IFF])
        sWq = sb("sWq", [128, IFF], BF16)
        sGate = sb("sGate", [128, IFF])
        sY = sb("sY", [128, IFF])
        sHq = sb("sHq", [128, IFF], BF16)
        sHqT = sb("sHqT", [128, 22, 128], BF16)
        pP = es.enter_context(nc.psum_tensor("pP", [128, 1024], F32))[:]
        sem = es.enter_context(nc.semaphore("sem"))
        block = es.enter_context(nc.Block())

        x_r = x_in.rearrange("(c q) f -> c q f", q=128)
        xo_r = x_out.rearrange("(c q) f -> c q f", q=128)

        def quant_weight_and_transpose(w_ap, n_oc, kdim, wT):
            w_r = w_ap.rearrange("(oc q) k -> oc q k", q=128)
            for oc in range(n_oc):
                p.sync(lambda e, oc=oc: e.dma_start(out=sWrow[:, 0:kdim], in_=w_r[oc]))
                emit_quant_fp4(p, sWrow[:, 0:kdim], sWq[:, 0:kdim],
                               bA[:, 0:kdim], bB[:, 0:kdim], bC[:, 0:kdim],
                               bD[:, 0:kdim], dscrH[:, 0:kdim // 16])
                for kc in range(kdim // 128):
                    p.sync(lambda e, oc=oc, kc=kc: e.dma_start_transpose(
                        wT[:, kc, oc * 128:(oc + 1) * 128],
                        sWq[:, kc * 128:(kc + 1) * 128]))

        with contextlib.ExitStack() as es2:
            wgT = es2.enter_context(nc.sbuf_tensor("wgT", [128, 8, IFF], BF16))[:]
            wuT = es2.enter_context(nc.sbuf_tensor("wuT", [128, 8, IFF], BF16))[:]
            quant_weight_and_transpose(wg_in, 22, D, wgT)
            quant_weight_and_transpose(wu_in, 22, D, wuT)

            for c in range(nch):
                p.sync(lambda e, c=c: e.dma_start(out=sX, in_=x_r[c]))
                emit_ln(p, sX.rearrange("q (n f) -> q n f", n=1), sU.rearrange("q (n f) -> q n f", n=1),
                        stat_a, stat_b, bA[:, 0:D], 1)
                emit_quant_fp4(p, sU, sUd, bA[:, 0:D], bB[:, 0:D], bC[:, 0:D],
                               bD[:, 0:D], dscr6)
                emit_transpose_blocks(p, sUd, sUdT.rearrange("q a b -> q (a b)"), 8)
                for dst, wT in ((sGate, wgT), (sY, wuT)):
                    for (o0, o1) in OSEG:
                        for kc in range(8):
                            p.tensor(lambda e, kc=kc, o0=o0, o1=o1, wT=wT: e.matmul(
                                pP[:, 0:o1 - o0], sUdT[:, kc, :], wT[:, kc, o0:o1],
                                start=(kc == 0), stop=(kc == 7)))
                        p.vector(lambda e, dst=dst, o0=o0, o1=o1: e.tensor_copy(
                            dst[:, o0:o1], pP[:, 0:o1 - o0]))
                p.scalar(lambda e: e.activation(sGate, sGate, AF.Silu))
                p.vector(lambda e: e.tensor_mul(sY, sY, sGate))
                emit_quant_fp4(p, sY, sHq, bA, bB, bC, bD, dscrH)
                emit_transpose_blocks(p, sHq, sHqT.rearrange("q a b -> q (a b)"), 22)
                p.sync(lambda e, c=c: e.dma_start(
                    out=hscr.rearrange("c q f -> c q f")[c],
                    in_=sHqT.rearrange("q a b -> q (a b)")))

        wdT = es.enter_context(nc.sbuf_tensor("wdT", [128, 22, D], BF16))[:]
        quant_weight_and_transpose(wd_in, 8, IFF, wdT)
        for c in range(nch):
            p.sync(lambda e, c=c: e.dma_start(
                out=sHqT.rearrange("q a b -> q (a b)"), in_=hscr[c]))
            p.sync(lambda e, c=c: e.dma_start(out=sX, in_=x_r[c]))
            for half in range(2):
                osl = slice(half * 512, (half + 1) * 512)
                for kc in range(22):
                    p.tensor(lambda e, kc=kc, osl=osl: e.matmul(
                        pP[:, osl], sHqT[:, kc, :], wdT[:, kc, osl],
                        start=(kc == 0), stop=(kc == 21)))
            p.vector(lambda e: e.scalar_tensor_tensor(
                sX, pP, 0.5, sX, ALU.mult, ALU.add))
            p.sync(lambda e, c=c: e.dma_start(out=xo_r[c], in_=sX))

        p.emit(block, sem)
    return nc


def run_ffn1(x, wg, wu, wd):
    ntok = x.shape[1]
    nc = build_ffn1(ntok)
    in_maps = [{"x_in": np.ascontiguousarray(x[i]), "wg_in": wg, "wu_in": wu,
                "wd_in": wd} for i in range(x.shape[0])]
    res = run_bass_kernel_spmd(nc, in_maps, core_ids=list(range(x.shape[0])))
    return np.stack([r["x_out"] for r in res.results])


def build_l1qkv(ntok):
    nc = bass.Bass("TRN2", target_bir_lowering=False, debug=False)
    x_in = nc.dram_tensor("x_in", [ntok, D], F32, kind="ExternalInput").ap()
    wqT_in = nc.dram_tensor("wqT_in", [D, D], F32, kind="ExternalInput").ap()
    wkT_in = nc.dram_tensor("wkT_in", [D, D], F32, kind="ExternalInput").ap()
    wvT_in = nc.dram_tensor("wvT_in", [D, D], F32, kind="ExternalInput").ap()
    qT_out = nc.dram_tensor("qT_out", [D, ntok], BF16, kind="ExternalOutput").ap()
    kT_out = nc.dram_tensor("kT_out", [D, ntok], BF16, kind="ExternalOutput").ap()
    v_out = nc.dram_tensor("v_out", [ntok, D], BF16, kind="ExternalOutput").ap()

    ntile = ntok // 128
    p = Prog()
    with contextlib.ExitStack() as es:
        sb = lambda name, shape, dt=F32: es.enter_context(nc.sbuf_tensor(name, shape, dt))[:]
        sX = sb("sX", [128, D])
        sU = sb("sU", [128, D])
        sUb = sb("sUb", [128, D], BF16)
        sUbT = sb("sUbT", [128, 8, 128], BF16)
        bA = sb("bA", [128, D])
        stat_a = sb("stat_a", [128, 1])
        stat_b = sb("stat_b", [128, 1])
        sWqT = sb("sWqT", [128, 8, D], BF16)
        sWkT = sb("sWkT", [128, 8, D], BF16)
        sWvT = sb("sWvT", [128, 8, D], BF16)
        sQTt = sb("sQTt", [128, 8, 128], BF16)
        sKTt = sb("sKTt", [128, 8, 128], BF16)
        sV = sb("sV", [128, D], BF16)
        pF = es.enter_context(nc.psum_tensor("pF", [128, 128], F32))[:]
        pV = es.enter_context(nc.psum_tensor("pV", [128, 1024], F32))[:]
        sem = es.enter_context(nc.semaphore("sem"))
        block = es.enter_context(nc.Block())

        for w_in, sW in ((wqT_in, sWqT), (wkT_in, sWkT), (wvT_in, sWvT)):
            w_r = w_in.rearrange("(kc q) o -> kc q o", q=128)
            for kc in range(8):
                p.gps_dma(lambda e, kc=kc, w_r=w_r, sW=sW: e.dma_start(
                    out=sW[:, kc, :], in_=w_r[kc]))

        x_r = x_in.rearrange("(c q) f -> c q f", q=128)
        qT_r = qT_out.rearrange("(oc p) t -> p oc t", p=128)
        kT_r = kT_out.rearrange("(oc p) t -> p oc t", p=128)
        v_r = v_out.rearrange("(c q) f -> c q f", q=128)
        for c in range(ntile):
            tsl = slice(c * 128, (c + 1) * 128)
            p.sync(lambda e, c=c: e.dma_start(out=sX, in_=x_r[c]))
            emit_ln(p, sX.rearrange("q (n f) -> q n f", n=1),
                    sU.rearrange("q (n f) -> q n f", n=1), stat_a, stat_b, bA, 1)
            p.vector(lambda e: e.tensor_copy(sUb, sU))
            emit_transpose_blocks(p, sUb, sUbT.rearrange("q a b -> q (a b)"), 8)
            for sW, sOut in ((sWqT, sQTt), (sWkT, sKTt)):
                for oc in range(8):
                    for kc in range(8):
                        p.tensor(lambda e, oc=oc, kc=kc, sW=sW: e.matmul(
                            pF, sW[:, kc, oc * 128:(oc + 1) * 128], sUbT[:, kc, :],
                            start=(kc == 0), stop=(kc == 7)))
                    p.vector(lambda e, oc=oc, sOut=sOut: e.tensor_copy(sOut[:, oc, :], pF))
            for half in range(2):
                osl = slice(half * 512, (half + 1) * 512)
                for kc in range(8):
                    p.tensor(lambda e, kc=kc, osl=osl: e.matmul(
                        pV[:, osl], sUbT[:, kc, :], sWvT[:, kc, osl],
                        start=(kc == 0), stop=(kc == 7)))
            p.vector(lambda e: e.tensor_copy(sV, pV))
            p.sync(lambda e, tsl=tsl: e.dma_start(out=qT_r[:, :, tsl], in_=sQTt))
            p.sync(lambda e, tsl=tsl: e.dma_start(out=kT_r[:, :, tsl], in_=sKTt))
            p.sync(lambda e, c=c: e.dma_start(out=v_r[c], in_=sV))
        p.emit(block, sem)
    return nc


def run_l1qkv(x, wq, wk, wv):
    ntok = x.shape[1]
    nc = build_l1qkv(ntok)
    wqT = np.ascontiguousarray(wq.T)
    wkT = np.ascontiguousarray(wk.T)
    wvT = np.ascontiguousarray(wv.T)
    in_maps = [{"x_in": np.ascontiguousarray(x[i]), "wqT_in": wqT,
                "wkT_in": wkT, "wvT_in": wvT} for i in range(x.shape[0])]
    res = run_bass_kernel_spmd(nc, in_maps, core_ids=list(range(x.shape[0])))
    return (np.stack([r["qT_out"] for r in res.results]),
            np.stack([r["kT_out"] for r in res.results]),
            np.stack([r["v_out"] for r in res.results]))



def build_l1flash(nbh, seq):
    nc = bass.Bass("TRN2", target_bir_lowering=False, debug=False)
    qT_in = nc.dram_tensor("qT_in", [nbh, 64, seq], BF16, kind="ExternalInput").ap()
    kT_in = nc.dram_tensor("kT_in", [nbh, 64, seq], BF16, kind="ExternalInput").ap()
    v_in = nc.dram_tensor("v_in", [nbh, seq, 64], BF16, kind="ExternalInput").ap()
    ot_out = nc.dram_tensor("ot_out", [nbh, 65, seq], F32, kind="ExternalOutput").ap()

    nsc = seq // 128
    nts = seq // 512
    GRP = 6
    p = Prog()
    with contextlib.ExitStack() as es:
        sb = lambda name, shape, dt=F32: es.enter_context(nc.sbuf_tensor(name, shape, dt))[:]
        sKT = sb("sKT", [64, seq], BF16)
        sQT = sb("sQT", [64, seq], BF16)
        sVext = sb("sVext", [128, nsc, 65], BF16)
        sP = sb("sP", [128, GRP * 512], BF16)
        sOT = sb("sOT", [65, 512])
        sInv = sb("sInv", [1, 512])
        sOnes64 = sb("sOnes64", [1, 64])
        pS = es.enter_context(nc.psum_tensor("pS", [128, GRP * 512], F32))[:]
        pO = es.enter_context(nc.psum_tensor("pO", [65, 512], F32))[:]
        pB = es.enter_context(nc.psum_tensor("pB", [64, 512], F32))[:]
        sem = es.enter_context(nc.semaphore("sem"))
        block = es.enter_context(nc.Block())

        p.vector(lambda e: e.memset(sOnes64, 1.0))

        p.vector(lambda e: e.memset(sVext[:, :, 64:65], 1.0))
        v_r = v_in.rearrange("u (c q) d -> u q c d", q=128)
        for u in range(nbh):
            p.sync(lambda e, u=u: e.dma_start(out=sKT, in_=kT_in[u]))
            p.sync(lambda e, u=u: e.dma_start(out=sQT, in_=qT_in[u]))
            p.sync(lambda e, u=u: e.dma_start(out=sVext[:, :, 0:64], in_=v_r[u]))
            for ts in range(nts):
                tsl = slice(ts * 512, (ts + 1) * 512)
                sc0 = 0
                while sc0 < nsc:
                    n = min(GRP, nsc - sc0)
                    for i in range(n):
                        sc = sc0 + i
                        p.tensor(lambda e, sc=sc, i=i, tsl=tsl: e.matmul(
                            pS[:, i * 512:(i + 1) * 512],
                            sKT[:, sc * 128:(sc + 1) * 128], sQT[:, tsl],
                            start=True, stop=True))
                    p.scalar(lambda e, n=n: e.activation(
                        sP[:, 0:n * 512], pS[:, 0:n * 512], AF.Exp, scale=1.0 / 32.0))
                    for i in range(n):
                        sc = sc0 + i
                        p.tensor(lambda e, sc=sc, i=i: e.matmul(
                            pO, sVext[:, sc, :], sP[:, i * 512:(i + 1) * 512],
                            start=(sc == 0), stop=(sc == nsc - 1)))
                    sc0 += n
                p.vector(lambda e: e.tensor_copy(sOT, pO))
                p.vector(lambda e: e.reciprocal(sInv, sOT[64:65, :]))
                p.tensor(lambda e: e.matmul(pB, sOnes64, sInv, start=True, stop=True))
                p.vector(lambda e: e.tensor_tensor(sOT[0:64, :], sOT[0:64, :], pB, ALU.mult))
                p.sync(lambda e, u=u, tsl=tsl: e.dma_start(
                    out=ot_out[u][:, tsl], in_=sOT))
        p.emit(block, sem)
    return nc


def run_l1flash(qT, kT, v):
    ncore, nbh, _, seq = qT.shape
    nc = build_l1flash(nbh, seq)
    in_maps = [{"qT_in": np.ascontiguousarray(qT[i]),
                "kT_in": np.ascontiguousarray(kT[i]),
                "v_in": np.ascontiguousarray(v[i])} for i in range(ncore)]
    res = run_bass_kernel_spmd(nc, in_maps, core_ids=list(range(ncore)))
    return np.stack([r["ot_out"] for r in res.results])



def build_l1out(ntok):
    nc = bass.Bass("TRN2", target_bir_lowering=False, debug=False)
    x_in = nc.dram_tensor("x_in", [ntok, D], F32, kind="ExternalInput").ap()
    ot_in = nc.dram_tensor("ot_in", [D, ntok], F32, kind="ExternalInput").ap()
    wuT_in = nc.dram_tensor("wuT_in", [D, D], F32, kind="ExternalInput").ap()
    bu_in = nc.dram_tensor("bu_in", [D], F32, kind="ExternalInput").ap()
    x_out = nc.dram_tensor("x_out", [ntok, D], F32, kind="ExternalOutput").ap()

    ntile = ntok // 128
    p = Prog()
    with contextlib.ExitStack() as es:
        sb = lambda name, shape, dt=F32: es.enter_context(nc.sbuf_tensor(name, shape, dt))[:]
        sX = sb("sX", [128, D])
        sOTb = sb("sOTb", [128, 8, ntok], BF16)
        sWuT = sb("sWuT", [128, 8, D], BF16)
        sBu = sb("sBu", [1, D])
        sBub = sb("sBub", [1, D], BF16)
        sOnes = sb("sOnes", [1, 128], BF16)
        pP = es.enter_context(nc.psum_tensor("pP", [128, 1024], F32))[:]
        sem = es.enter_context(nc.semaphore("sem"))
        block = es.enter_context(nc.Block())

        ot_r = ot_in.rearrange("(kc q) t -> kc q t", q=128)
        for kc in range(8):
            p.gps_dma(lambda e, kc=kc: e.dma_start(out=sOTb[:, kc, :], in_=ot_r[kc]))
        w_r = wuT_in.rearrange("(kc q) o -> kc q o", q=128)
        for kc in range(8):
            p.gps_dma(lambda e, kc=kc: e.dma_start(out=sWuT[:, kc, :], in_=w_r[kc]))
        p.sync(lambda e: e.dma_start(out=sBu, in_=bu_in.rearrange("(o f) -> o f", o=1)))
        p.vector(lambda e: e.tensor_copy(sBub, sBu))
        p.vector(lambda e: e.memset(sOnes, 1.0))

        sBiasTile = sb("sBiasTile", [128, D])
        for half in range(2):
            osl = slice(half * 512, (half + 1) * 512)
            p.tensor(lambda e, osl=osl: e.matmul(
                pP[:, osl], sOnes, sBub[:, osl], start=True, stop=True))
        p.vector(lambda e: e.tensor_copy(sBiasTile, pP))

        x_r = x_in.rearrange("(c q) f -> c q f", q=128)
        xo_r = x_out.rearrange("(c q) f -> c q f", q=128)
        for c in range(ntile):
            p.sync(lambda e, c=c: e.dma_start(out=sX, in_=x_r[c]))
            p.vector(lambda e: e.tensor_add(sX, sX, sBiasTile))
            for half in range(2):
                osl = slice(half * 512, (half + 1) * 512)
                for kc in range(8):
                    p.tensor(lambda e, kc=kc, osl=osl, c=c: e.matmul(
                        pP[:, osl], sOTb[:, kc, c * 128:(c + 1) * 128], sWuT[:, kc, osl],
                        start=(kc == 0), stop=(kc == 7)))
            p.vector(lambda e: e.tensor_tensor(sX, pP, sX, ALU.add))
            p.sync(lambda e, c=c: e.dma_start(out=xo_r[c], in_=sX))
        p.emit(block, sem)
    return nc


def run_l1out(x, ot, wu, bu):
    ntok = x.shape[1]
    nc = build_l1out(ntok)
    wuT = np.ascontiguousarray(wu.T)
    in_maps = [{"x_in": np.ascontiguousarray(x[i]),
                "ot_in": np.ascontiguousarray(ot[i]),
                "wuT_in": wuT, "bu_in": bu} for i in range(x.shape[0])]
    res = run_bass_kernel_spmd(nc, in_maps, core_ids=list(range(x.shape[0])))
    return np.stack([r["x_out"] for r in res.results])


def build_l0mix(ntok):
    nc = bass.Bass("TRN2", target_bir_lowering=False, debug=False)
    xh_in = nc.dram_tensor("xh_in", [1024 + ntok, D], F32, kind="ExternalInput").ap()
    tt_in = nc.dram_tensor("tt_in", [128, 16 * 9 * 128], F32, kind="ExternalInput").ap()
    mphiT_in = nc.dram_tensor("mphiT_in", [16384, D], F32, kind="ExternalInput").ap()
    mb_in = nc.dram_tensor("mb_in", [128, 8], F32, kind="ExternalInput").ap()
    wqT_in = nc.dram_tensor("wqT_in", [D, D], F32, kind="ExternalInput").ap()
    wkT_in = nc.dram_tensor("wkT_in", [D, D], F32, kind="ExternalInput").ap()
    wvT_in = nc.dram_tensor("wvT_in", [D, D], F32, kind="ExternalInput").ap()
    qb_in = nc.dram_tensor("qb_in", [128, 8], F32, kind="ExternalInput").ap()
    kb_in = nc.dram_tensor("kb_in", [128, 8], F32, kind="ExternalInput").ap()
    vb_in = nc.dram_tensor("vb_in", [128, 8], F32, kind="ExternalInput").ap()
    qT_out = nc.dram_tensor("qT_out", [D, ntok], BF16, kind="ExternalOutput").ap()
    kT_out = nc.dram_tensor("kT_out", [D, ntok], BF16, kind="ExternalOutput").ap()
    vT_out = nc.dram_tensor("vT_out", [D, ntok], BF16, kind="ExternalOutput").ap()

    T = 256
    nch = ntok // T
    NRING = 10
    NSLAB = 4

    p = Prog()
    with contextlib.ExitStack() as es:
        sb = lambda name, shape, dt=F32: es.enter_context(nc.sbuf_tensor(name, shape, dt))[:]
        sXb = sb("sXb", [128, D])
        sU = sb("sU", [128, D])
        bA = sb("bA", [128, D])
        stat_a = sb("stat_a", [128, 1])
        stat_b = sb("stat_b", [128, 1])
        ring = sb("ring", [128, NRING, D], BF16)
        sTT = sb("sTT", [128, 144, 128], BF16)
        sConv = sb("sConv", [128, D], BF16)
        sCT = sb("sCT", [128, 128, T], BF16)
        sMph = sb("sMph", [128, NSLAB, D], BF16)
        sXtT = sb("sXtT", [128, 8, T], BF16)
        sMb = sb("sMb", [128, 8])
        sQb = sb("sQb", [128, 8])
        sKb = sb("sKb", [128, 8])
        sVb = sb("sVb", [128, 8])
        sWqT = sb("sWqT", [128, 8, D], BF16)
        sWkT = sb("sWkT", [128, 8, D], BF16)
        sWvT = sb("sWvT", [128, 8, D], BF16)
        sQo = sb("sQo", [128, 8, T], BF16)
        pC = es.enter_context(nc.psum_tensor("pC", [128, D], F32))[:]
        pXT = es.enter_context(nc.psum_tensor("pXT", [128, 8 * T], F32))[:]
        pQ = es.enter_context(nc.psum_tensor("pQ", [128, T], F32))[:]
        sem = es.enter_context(nc.semaphore("sem"))
        block = es.enter_context(nc.Block())

        p.gps_dma(lambda e: e.dma_start(
            out=sTT.rearrange("q a b -> q (a b)"), in_=tt_in))
        for w_in, sW in ((wqT_in, sWqT), (wkT_in, sWkT), (wvT_in, sWvT)):
            w_r = w_in.rearrange("(kc q) o -> kc q o", q=128)
            for kc in range(8):
                p.gps_dma(lambda e, kc=kc, w_r=w_r, sW=sW: e.dma_start(
                    out=sW[:, kc, :], in_=w_r[kc]))
        for b_in_, sB in ((mb_in, sMb), (qb_in, sQb), (kb_in, sKb), (vb_in, sVb)):
            p.sync(lambda e, b_in_=b_in_, sB=sB: e.dma_start(out=sB, in_=b_in_))

        xh_r = xh_in.rearrange("(ib q) f -> ib q f", q=128)

        def ln_to_ring(ib):
            p.sync(lambda e, ib=ib: e.dma_start(out=sXb, in_=xh_r[ib]))
            emit_ln(p, sXb.rearrange("q (n f) -> q n f", n=1),
                    sU.rearrange("q (n f) -> q n f", n=1), stat_a, stat_b, bA, 1)
            p.vector(lambda e, ib=ib: e.tensor_copy(ring[:, ib % NRING, :], sU))

        for ib in range(8):
            ln_to_ring(ib)

        qT_r = qT_out.rearrange("(oc p) t -> p oc t", p=128)
        kT_r = kT_out.rearrange("(oc p) t -> p oc t", p=128)
        vT_r = vT_out.rearrange("(oc p) t -> p oc t", p=128)
        mphi_r = mphiT_in.rearrange("(sl j q) o -> sl q j o", q=128, j=NSLAB)

        for c in range(nch):
            for bi in range(2):
                ln_to_ring(8 + 2 * c + bi)
            for bi in range(2):
                ob = 2 * c + bi
                for kf in range(16):
                    for d in range(9):
                        ib = ob + 8 - d
                        for half in range(2):
                            osl = slice(half * 512, (half + 1) * 512)
                            p.tensor(lambda e, kf=kf, d=d, ib=ib, osl=osl: e.matmul(
                                pC[:, osl], sTT[:, kf * 9 + d, :],
                                ring[:, ib % NRING, osl],
                                start=(d == 0), stop=(d == 8)))
                    p.vector(lambda e: e.tensor_copy(sConv, pC))
                    for cc in range(8):
                        p.sync(lambda e, kf=kf, cc=cc, bi=bi: e.dma_start_transpose(
                            sCT[:, kf * 8 + cc, bi * 128:(bi + 1) * 128],
                            sConv[:, cc * 128:(cc + 1) * 128]))
            for sl in range(128 // NSLAB):
                p.gps_dma(lambda e, sl=sl: e.dma_start(out=sMph, in_=mphi_r[sl]))
                for j in range(NSLAB):
                    kc = sl * NSLAB + j
                    slot = (kc // 8) * 8 + (kc % 8)
                    for oc in range(8):
                        p.tensor(lambda e, j=j, oc=oc, slot=slot, kc=kc: e.matmul(
                            pXT[:, oc * T:(oc + 1) * T],
                            sMph[:, j, oc * 128:(oc + 1) * 128],
                            sCT[:, slot, :],
                            start=(kc == 0 and oc % 2 == 0), stop=(kc == 127)))
            for oc in range(8):
                p.scalar(lambda e, oc=oc: e.activation(
                    sXtT[:, oc, :], pXT[:, oc * T:(oc + 1) * T], AF.Identity,
                    bias=sMb[:, oc:oc + 1]))
            for sW, sBv, out_r in ((sWqT, sQb, qT_r), (sWkT, sKb, kT_r),
                                   (sWvT, sVb, vT_r)):
                for oc in range(8):
                    for kc in range(8):
                        p.tensor(lambda e, oc=oc, kc=kc, sW=sW: e.matmul(
                            pQ, sW[:, kc, oc * 128:(oc + 1) * 128], sXtT[:, kc, :],
                            start=(kc == 0), stop=(kc == 7)))
                    p.scalar(lambda e, oc=oc, sBv=sBv: e.activation(
                        sQo[:, oc, :], pQ, AF.Identity, bias=sBv[:, oc:oc + 1]))
                p.sync(lambda e, c=c, out_r=out_r: e.dma_start(
                    out=out_r[:, :, c * T:(c + 1) * T], in_=sQo))
        p.emit(block, sem)
    return nc


def make_toeplitz(basis):
    TT = np.zeros((16, 9, 128, 128), dtype=np.float32)
    l_idx = np.arange(128)[None, :]
    m_idx = np.arange(128)[:, None]
    for kf in range(16):
        fpad = np.zeros(1280, np.float32)
        fpad[128:1152] = basis[:, kf]
        for d in range(9):
            TT[kf, d] = fpad[128 + 128 * d + l_idx - m_idx]
    return np.ascontiguousarray(TT.transpose(2, 0, 1, 3).reshape(128, 16 * 9 * 128))


def run_l0mix(xh, basis, mphi, mb, wq, qb, wk, kb, wv, vb):
    ntok = xh.shape[1] - 1024
    nc = build_l0mix(ntok)
    tt = make_toeplitz(basis)
    mphiT = np.ascontiguousarray(mphi.T)
    wqT = np.ascontiguousarray(wq.T)
    wkT = np.ascontiguousarray(wk.T)
    wvT = np.ascontiguousarray(wv.T)
    rb = lambda b: np.ascontiguousarray(b.reshape(8, 128).T)
    in_maps = [{"xh_in": np.ascontiguousarray(xh[i]), "tt_in": tt,
                "mphiT_in": mphiT, "mb_in": rb(mb), "wqT_in": wqT, "qb_in": rb(qb),
                "wkT_in": wkT, "kb_in": rb(kb), "wvT_in": wvT, "vb_in": rb(vb)}
               for i in range(xh.shape[0])]
    res = run_bass_kernel_spmd(nc, in_maps, core_ids=list(range(xh.shape[0])))
    return (np.stack([r["qT_out"] for r in res.results]),
            np.stack([r["kT_out"] for r in res.results]),
            np.stack([r["vT_out"] for r in res.results]))


def build_l0scan(nbh, seq):
    nc = bass.Bass("TRN2", target_bir_lowering=False, debug=False)
    qT_in = nc.dram_tensor("qT_in", [nbh, 64, seq], BF16, kind="ExternalInput").ap()
    kT_in = nc.dram_tensor("kT_in", [nbh, 64, seq], BF16, kind="ExternalInput").ap()
    vT_in = nc.dram_tensor("vT_in", [nbh, 64, seq], BF16, kind="ExternalInput").ap()
    wgT_in = nc.dram_tensor("wgT_in", [64, 64], F32, kind="ExternalInput").ap()
    kvs_in = nc.dram_tensor("kvs_in", [1, nbh], F32, kind="ExternalInput").ap()
    wgb_in = nc.dram_tensor("wgb_in", [1, nbh], F32, kind="ExternalInput").ap()
    ident_in = nc.dram_tensor("ident_in", [128, 128], F32, kind="ExternalInput").ap()
    triu_in = nc.dram_tensor("triu_in", [128, 128], F32, kind="ExternalInput").ap()
    y_out = nc.dram_tensor("y_out", [nbh, seq, 64], F32, kind="ExternalOutput").ap()

    SEG = 1024
    nseg = seq // SEG
    EPS = 1e-5

    p = Prog()
    with contextlib.ExitStack() as es:
        sb = lambda name, shape, dt=F32: es.enter_context(nc.sbuf_tensor(name, shape, dt))[:]
        sQT = sb("sQT", [64, seq], BF16)
        sKT = sb("sKT", [64, seq], BF16)
        sVT = sb("sVT", [64, seq], BF16)
        sWgb = sb("sWgb", [64, 64], BF16)
        sKvs = sb("sKvs", [1, nbh])
        sWgbias = sb("sWgbias", [1, nbh])
        sIdF = sb("sIdF", [128, 128])
        sIdB = sb("sIdB", [128, 128], BF16)
        sTriu = sb("sTriu", [128, 128])
        sOnes64b = sb("sOnes64b", [64, 1], BF16)
        sOnesF = sb("sOnesF", [1, 64])
        sOneF = sb("sOneF", [1, 1])
        sE = sb("sE", [64, SEG], BF16)
        L = {nm: sb("L_" + nm, [1, SEG]) for nm in
             ["qk", "ksq", "vsq", "vkg", "qsum", "gcum", "m", "msh", "nrm",
              "w", "ones"]}
        sLast = sb("sLast", [1, 8])
        vc_last = sb("vc_last", [64, 1])
        dkb = sb("dkb", [64, SEG])
        w0b = sb("w0b", [64, SEG])
        vcT = sb("vcT", [64, SEG])
        k_tm = sb("k_tm", [128, 64], BF16)
        Va = sb("Va", [128, 64], BF16)
        VaT = sb("VaT", [64, 128], BF16)
        PmT = sb("PmT", [128, 128], BF16)
        S_sb = sb("S_sb", [64, 64], BF16)
        sCols = sb("sCols", [128, 3])
        sY = sb("sY", [128, 64])
        pA = es.enter_context(nc.psum_tensor("pA", [64, 512], F32))[:]
        pL = es.enter_context(nc.psum_tensor("pL", [1, 512], F32))[:]
        pT = es.enter_context(nc.psum_tensor("pT", [128, 128], BF16))[:]
        pcols = es.enter_context(nc.psum_tensor("pcols", [128, 3], F32))[:]
        pYb = es.enter_context(nc.psum_tensor("pYb", [128, 64], F32))[:]
        pP = es.enter_context(nc.psum_tensor("pP", [128, 128], F32))[:]
        pS64 = es.enter_context(nc.psum_tensor("pS64", [64, 64], F32))[:]
        pVc = es.enter_context(nc.psum_tensor("pVc", [128, 64], F32))[:]
        sem = es.enter_context(nc.semaphore("sem"))
        block = es.enter_context(nc.Block())

        p.sync(lambda e: e.dma_start(out=sIdF, in_=ident_in))
        p.sync(lambda e: e.dma_start(out=sTriu, in_=triu_in))
        p.sync(lambda e: e.dma_start(out=sKvs, in_=kvs_in))
        p.sync(lambda e: e.dma_start(out=sWgbias, in_=wgb_in))
        p.gps_dma(lambda e: e.dma_start(out=sWgb, in_=wgT_in))
        p.vector(lambda e: e.tensor_copy(sIdB, sIdF))
        p.vector(lambda e: e.memset(sOnes64b, 1.0))
        p.vector(lambda e: e.memset(sOnesF, 1.0))
        p.vector(lambda e: e.memset(sOneF, 1.0))
        p.vector(lambda e: e.memset(L["ones"], 1.0))

        def lane_sum(dst, mov, width=SEG):
            for s0 in range(0, width, 512):
                p.tensor(lambda e, s0=s0, mov=mov: e.matmul(
                    pL, sOnes64b, mov[:, s0:s0 + 512], start=True, stop=True))
                p.vector(lambda e, s0=s0, dst=dst: e.tensor_copy(
                    dst[:, s0:s0 + 512], pL))

        for u in range(nbh):
            p.sync(lambda e, u=u: e.dma_start(out=sQT, in_=qT_in[u]))
            p.sync(lambda e, u=u: e.dma_start(out=sKT, in_=kT_in[u]))
            p.sync(lambda e, u=u: e.dma_start(out=sVT, in_=vT_in[u]))
            p.vector(lambda e: e.memset(S_sb, 0.0))
            p.vector(lambda e: e.memset(vc_last, 0.0))
            p.vector(lambda e: e.memset(sLast[:, 0:1], -3.0e38))
            p.vector(lambda e: e.memset(sLast[:, 1:3], 0.0))
            kv_ap = sKvs[0:1, u:u + 1]
            wgb_ap = sWgbias[0:1, u:u + 1]
            for g in range(nseg):
                gs = slice(g * SEG, (g + 1) * SEG)
                qTg, kTg, vTg = sQT[:, gs], sKT[:, gs], sVT[:, gs]
                p.vector(lambda e, qTg=qTg, kTg=kTg: e.tensor_tensor(sE, qTg, kTg, ALU.mult))
                lane_sum(L["qk"], sE)
                p.scalar(lambda e, kTg=kTg: e.activation(sE, kTg, AF.Square))
                lane_sum(L["ksq"], sE)
                p.scalar(lambda e, vTg=vTg: e.activation(sE, vTg, AF.Square))
                lane_sum(L["vsq"], sE)
                lane_sum(L["qsum"], qTg)
                for s0 in range(0, SEG, 512):
                    p.tensor(lambda e, s0=s0, kTg=kTg: e.matmul(
                        pA, sWgb, kTg[:, s0:s0 + 512], start=True, stop=True))
                    p.vector(lambda e, s0=s0, vTg=vTg: e.tensor_tensor(
                        sE[:, s0:s0 + 512], vTg[:, s0:s0 + 512], pA, ALU.mult))
                lane_sum(L["vkg"], sE)
                for nm in ("ksq", "vsq"):
                    p.scalar(lambda e, nm=nm: e.activation(L[nm], L[nm], AF.Sqrt))
                    p.vector(lambda e, nm=nm: e.tensor_scalar(L[nm], L[nm], 1e-12, None, ALU.max))
                    p.vector(lambda e, nm=nm: e.reciprocal(L[nm], L[nm]))
                p.vector(lambda e: e.tensor_tensor(L["qk"], L["qk"], L["ksq"], ALU.mult))
                p.vector(lambda e: e.tensor_tensor(L["ksq"], L["ksq"], L["vsq"], ALU.mult))
                p.vector(lambda e, kv_ap=kv_ap: e.tensor_scalar(L["ksq"], L["ksq"], kv_ap, None, ALU.mult))
                p.vector(lambda e: e.tensor_tensor(L["vkg"], L["vkg"], L["ksq"], ALU.mult))
                p.vector(lambda e, wgb_ap=wgb_ap: e.tensor_scalar(L["vkg"], L["vkg"], wgb_ap, None, ALU.add))
                p.scalar(lambda e: e.activation(L["vkg"], L["vkg"], AF.Lrelu, alpha=0.01))
                p.scalar(lambda e: e.activation(L["vkg"], L["vkg"], AF.Square))
                p.vector(lambda e: e.tensor_scalar(L["vkg"], L["vkg"], EPS, None, ALU.add))
                p.vector(lambda e: e.tensor_tensor_scan(
                    L["gcum"], L["ones"], L["vkg"], sLast[0:1, 2:3], ALU.mult, ALU.add))
                p.vector(lambda e: e.tensor_copy(sLast[:, 2:3], L["gcum"][:, SEG - 1:SEG]))
                p.vector(lambda e: e.tensor_tensor(L["vkg"], L["vkg"], L["ksq"], ALU.mult))
                p.vector(lambda e: e.tensor_tensor_scan(
                    L["m"], L["qk"], L["qk"], sLast[0:1, 0:1], ALU.max, ALU.bypass))
                p.vector(lambda e: e.tensor_copy(L["msh"][:, 1:SEG], L["m"][:, 0:SEG - 1]))
                p.vector(lambda e: e.tensor_copy(L["msh"][:, 0:1], sLast[:, 0:1]))
                p.vector(lambda e: e.tensor_copy(sLast[:, 0:1], L["m"][:, SEG - 1:SEG]))
                p.vector(lambda e: e.tensor_tensor(L["msh"], L["msh"], L["m"], ALU.subtract))
                p.scalar(lambda e: e.activation(L["msh"], L["msh"], AF.Exp))
                p.vector(lambda e: e.tensor_tensor(L["qk"], L["qk"], L["m"], ALU.subtract))
                p.scalar(lambda e: e.activation(L["qk"], L["qk"], AF.Exp))
                p.vector(lambda e: e.tensor_tensor_scan(
                    L["nrm"], L["msh"], L["qk"], sLast[0:1, 1:2], ALU.mult, ALU.add))
                p.vector(lambda e: e.tensor_copy(sLast[:, 1:2], L["nrm"][:, SEG - 1:SEG]))
                p.vector(lambda e: e.tensor_scalar(L["nrm"], L["nrm"], EPS, None, ALU.add))
                p.vector(lambda e: e.reciprocal(L["nrm"], L["nrm"]))
                p.vector(lambda e: e.tensor_tensor(L["w"], L["qk"], L["nrm"], ALU.mult))
                p.vector(lambda e: e.tensor_tensor(L["qsum"], L["qsum"], L["w"], ALU.mult))
                p.vector(lambda e: e.tensor_tensor(L["qsum"], L["qsum"], L["nrm"], ALU.mult))
                p.vector(lambda e: e.tensor_tensor(L["qk"], L["qk"], L["vsq"], ALU.mult))
                p.vector(lambda e: e.tensor_scalar(L["gcum"], L["gcum"], EPS, None, ALU.add))
                p.vector(lambda e: e.reciprocal(L["gcum"], L["gcum"]))
                p.vector(lambda e: e.tensor_scalar(L["w"], L["w"], -1.0, 1.0, ALU.mult, ALU.add))
                p.vector(lambda e: e.tensor_tensor(L["w"], L["w"], L["gcum"], ALU.mult))
                for s0 in range(0, SEG, 512):
                    p.tensor(lambda e, s0=s0: e.matmul(
                        pA, sOnesF, L["msh"][:, s0:s0 + 512], start=True, stop=True))
                    p.vector(lambda e, s0=s0: e.tensor_copy(dkb[:, s0:s0 + 512], pA))
                for s0 in range(0, SEG, 512):
                    p.tensor(lambda e, s0=s0: e.matmul(
                        pA, sOnesF, L["qk"][:, s0:s0 + 512], start=True, stop=True))
                    p.vector(lambda e, s0=s0: e.tensor_copy(w0b[:, s0:s0 + 512], pA))
                p.vector(lambda e, vTg=vTg: e.tensor_tensor(w0b, vTg, w0b, ALU.mult))
                p.vector(lambda e: e.tensor_tensor_scan(
                    vcT, dkb, w0b, vc_last, ALU.mult, ALU.add))
                p.vector(lambda e: e.tensor_copy(vc_last, vcT[:, SEG - 1:SEG]))
                for cc in range(SEG // 128):
                    c = g * (SEG // 128) + cc
                    csl = slice(cc * 128, (cc + 1) * 128)
                    qTc = qTg[:, csl]
                    p.tensor(lambda e, kTg=kTg, csl=csl: e.transpose(
                        pT[:, 0:64], kTg[:, csl], sIdB[0:64, 0:64]))
                    p.vector(lambda e: e.tensor_copy(k_tm, pT[:, 0:64]))
                    p.tensor(lambda e, csl=csl: e.matmul(
                        pcols[:, 0:1], L["vkg"][:, csl], sOneF, start=True, stop=True))
                    p.tensor(lambda e, csl=csl: e.matmul(
                        pcols[:, 1:2], L["w"][:, csl], sOneF, start=True, stop=True))
                    p.tensor(lambda e, csl=csl: e.matmul(
                        pcols[:, 2:3], L["qsum"][:, csl], sOneF, start=True, stop=True))
                    p.vector(lambda e: e.tensor_copy(sCols, pcols))
                    p.tensor(lambda e, vTg=vTg, csl=csl: e.transpose(
                        pT[:, 0:64], vTg[:, csl], sIdB[0:64, 0:64]))
                    p.vector(lambda e: e.tensor_scalar(
                        Va, pT[:, 0:64], sCols[:, 0:1], None, ALU.mult))
                    p.tensor(lambda e: e.transpose(pT[0:64, :], Va, sIdB))
                    p.vector(lambda e: e.tensor_copy(VaT, pT[0:64, :]))
                    p.tensor(lambda e, qTc=qTc: e.matmul(
                        pYb, qTc, S_sb, start=True, stop=False))
                    p.tensor(lambda e, qTc=qTc: e.matmul(
                        pP, VaT, qTc, start=True, stop=True))
                    p.vector(lambda e: e.tensor_tensor(PmT, pP, sTriu, ALU.mult))
                    p.tensor(lambda e: e.matmul(
                        pYb, PmT, k_tm, start=False, stop=True))
                    p.tensor(lambda e, c=c: e.matmul(
                        pS64, Va, k_tm, start=(c == 0), stop=(c == seq // 128 - 1)))
                    p.vector(lambda e: e.tensor_copy(S_sb, pS64))
                    p.tensor(lambda e, csl=csl: e.transpose(
                        pVc[:, 0:64], vcT[:, csl], sIdF[0:64, 0:64]))
                    p.vector(lambda e: e.tensor_scalar(
                        sY, pYb, sCols[:, 1:2], None, ALU.mult))
                    p.vector(lambda e: e.scalar_tensor_tensor(
                        sY, pVc[:, 0:64], sCols[:, 2:3], sY, ALU.mult, ALU.add))
                    p.sync(lambda e, u=u, c=c: e.dma_start(
                        out=y_out[u][c * 128:(c + 1) * 128, :], in_=sY))
        p.emit(block, sem)
    return nc


def run_l0scan(qT, kT, vT, wg_z_w, wg_z_b, kv_norm_scale, bh_assign):
    ncore, nbh, _, seq = qT.shape
    nc = build_l0scan(nbh, seq)
    wgT = np.ascontiguousarray(wg_z_w.reshape(64, 64).T)
    kvs_flat = kv_norm_scale.reshape(16)
    ident = np.eye(128, dtype=np.float32)
    triu = np.triu(np.ones((128, 128), np.float32))
    in_maps = []
    for i in range(ncore):
        kvs = np.ascontiguousarray(kvs_flat[bh_assign[i]].reshape(1, nbh))
        wgb = np.full((1, nbh), float(wg_z_b[0]), np.float32)
        in_maps.append({"qT_in": np.ascontiguousarray(qT[i]),
                        "kT_in": np.ascontiguousarray(kT[i]),
                        "vT_in": np.ascontiguousarray(vT[i]),
                        "wgT_in": wgT, "kvs_in": kvs, "wgb_in": wgb,
                        "ident_in": ident, "triu_in": triu})
    res = run_bass_kernel_spmd(nc, in_maps, core_ids=list(range(ncore)))
    return np.stack([r["y_out"] for r in res.results])


def build_l0out(ntok):
    nc = bass.Bass("TRN2", target_bir_lowering=False, debug=False)
    x_in = nc.dram_tensor("x_in", [ntok, D], F32, kind="ExternalInput").ap()
    y_in = nc.dram_tensor("y_in", [ntok, D], F32, kind="ExternalInput").ap()
    woT_in = nc.dram_tensor("woT_in", [D, D], F32, kind="ExternalInput").ap()
    wob_in = nc.dram_tensor("wob_in", [1, D], F32, kind="ExternalInput").ap()
    x_out = nc.dram_tensor("x_out", [ntok, D], F32, kind="ExternalOutput").ap()

    ntile = ntok // 128
    p = Prog()
    with contextlib.ExitStack() as es:
        sb = lambda name, shape, dt=F32: es.enter_context(nc.sbuf_tensor(name, shape, dt))[:]
        sX = sb("sX", [128, D])
        sY = sb("sY", [128, D])
        sYb = sb("sYb", [128, D], BF16)
        sYbT = sb("sYbT", [128, 8, 128], BF16)
        sWoT = sb("sWoT", [128, 8, D], BF16)
        sBo = sb("sBo", [1, D])
        sBob = sb("sBob", [1, D], BF16)
        sOnes = sb("sOnes", [1, 128], BF16)
        sBiasTile = sb("sBiasTile", [128, D])
        stat = sb("stat", [128, 1])
        pP = es.enter_context(nc.psum_tensor("pP", [128, 1024], F32))[:]
        sem = es.enter_context(nc.semaphore("sem"))
        block = es.enter_context(nc.Block())

        w_r = woT_in.rearrange("(kc q) o -> kc q o", q=128)
        for kc in range(8):
            p.gps_dma(lambda e, kc=kc: e.dma_start(out=sWoT[:, kc, :], in_=w_r[kc]))
        p.sync(lambda e: e.dma_start(out=sBo, in_=wob_in))
        p.vector(lambda e: e.tensor_copy(sBob, sBo))
        p.vector(lambda e: e.memset(sOnes, 1.0))
        for half in range(2):
            osl = slice(half * 512, (half + 1) * 512)
            p.tensor(lambda e, osl=osl: e.matmul(
                pP[:, osl], sOnes, sBob[:, osl], start=True, stop=True))
        p.vector(lambda e: e.tensor_copy(sBiasTile, pP))

        x_r = x_in.rearrange("(c q) f -> c q f", q=128)
        y_r = y_in.rearrange("(c q) f -> c q f", q=128)
        xo_r = x_out.rearrange("(c q) f -> c q f", q=128)
        sSq = sb("sSq", [128, D])
        for c in range(ntile):
            p.sync(lambda e, c=c: e.dma_start(out=sX, in_=x_r[c]))
            p.sync(lambda e, c=c: e.dma_start(out=sY, in_=y_r[c]))
            p.scalar(lambda e: e.activation(sSq, sY, AF.Square, accum_out=stat))
            p.scalar(lambda e: e.activation(stat, stat, AF.Sqrt))
            p.vector(lambda e: e.tensor_scalar(stat, stat, 1e-12, None, ALU.max))
            p.vector(lambda e: e.reciprocal(stat, stat))
            p.vector(lambda e: e.tensor_scalar(sYb, sY, stat, None, ALU.mult))
            emit_transpose_blocks(p, sYb, sYbT.rearrange("q a b -> q (a b)"), 8)
            for half in range(2):
                osl = slice(half * 512, (half + 1) * 512)
                for kc in range(8):
                    p.tensor(lambda e, kc=kc, osl=osl: e.matmul(
                        pP[:, osl], sYbT[:, kc, :], sWoT[:, kc, osl],
                        start=(kc == 0), stop=(kc == 7)))
            p.vector(lambda e: e.tensor_add(sX, sX, sBiasTile))
            p.vector(lambda e: e.tensor_tensor(sX, pP, sX, ALU.add))
            p.sync(lambda e, c=c: e.dma_start(out=xo_r[c], in_=sX))
        p.emit(block, sem)
    return nc


def run_l0out(x, y, wo, wob):
    ntok = x.shape[1]
    nc = build_l0out(ntok)
    woT = np.ascontiguousarray(wo.T)
    in_maps = [{"x_in": np.ascontiguousarray(x[i]), "y_in": np.ascontiguousarray(y[i]),
                "woT_in": woT, "wob_in": wob.reshape(1, D)}
               for i in range(x.shape[0])]
    res = run_bass_kernel_spmd(nc, in_maps, core_ids=list(range(x.shape[0])))
    return np.stack([r["x_out"] for r in res.results])


def kernel(L0_attn_x, L0_attn_spectral_basis, L0_attn_M_phi_w, L0_attn_M_phi_b,
           L0_attn_wq_w, L0_attn_wq_b, L0_attn_wk_w, L0_attn_wk_b,
           L0_attn_wv_w, L0_attn_wv_b, L0_attn_wo_w, L0_attn_wo_b,
           L0_attn_wg_z_w, L0_attn_wg_z_b, L0_attn_kv_norm_scale,
           L0_ffn_weight, L0_ffn_bias,
           L1_attn_Wq, L1_attn_Wk, L1_attn_Wv, L1_attn_Wu, L1_attn_bu,
           L1_ffn_w_gate, L1_ffn_w_up, L1_ffn_w_down):
    x0 = np.asarray(L0_attn_x)
    B, L, Dm = x0.shape
    ncore = NCORE
    if FUSED:
        inp = dict(
            L0_attn_spectral_basis=L0_attn_spectral_basis,
            L0_attn_M_phi_w=L0_attn_M_phi_w, L0_attn_M_phi_b=L0_attn_M_phi_b,
            L0_attn_wq_w=L0_attn_wq_w, L0_attn_wq_b=L0_attn_wq_b,
            L0_attn_wk_w=L0_attn_wk_w, L0_attn_wk_b=L0_attn_wk_b,
            L0_attn_wv_w=L0_attn_wv_w, L0_attn_wv_b=L0_attn_wv_b,
            L0_attn_wo_w=L0_attn_wo_w, L0_attn_wo_b=L0_attn_wo_b,
            L0_attn_wg_z_w=L0_attn_wg_z_w, L0_attn_wg_z_b=L0_attn_wg_z_b,
            L0_attn_kv_norm_scale=L0_attn_kv_norm_scale,
            L0_ffn_weight=L0_ffn_weight, L0_ffn_bias=L0_ffn_bias,
            L1_attn_Wq=L1_attn_Wq, L1_attn_Wk=L1_attn_Wk, L1_attn_Wv=L1_attn_Wv,
            L1_attn_Wu=L1_attn_Wu, L1_attn_bu=L1_attn_bu,
            L1_ffn_w_gate=L1_ffn_w_gate, L1_ffn_w_up=L1_ffn_w_up,
            L1_ffn_w_down=L1_ffn_w_down,
        )
        return np.ascontiguousarray(run_fused(x0, inp, ncore)).astype(np.float32)
    ntok = B * L // ncore
    seq = L
    cores_per_b = ncore // B
    nbh = 16 // cores_per_b

    def shard(x):
        return x.reshape(ncore, ntok, Dm)

    def unshard(xs):
        return xs.reshape(-1, Dm)

    def to_bh(aT):
        out = np.empty((ncore, nbh, 64, seq), aT.dtype)
        for c in range(ncore):
            b = c // cores_per_b
            full = np.concatenate([aT[b * cores_per_b + i] for i in range(cores_per_b)], axis=1)
            for j in range(nbh):
                h = (c % cores_per_b) * nbh + j
                out[c, j] = full[h * 64:(h + 1) * 64]
        return out

    x = np.ascontiguousarray(x0.reshape(-1, Dm))
    x = unshard(run_ffn0(shard(x), L0_ffn_weight, L0_ffn_bias))

    xb = x.reshape(B, L, Dm)
    xh = []
    for c in range(ncore):
        b = c // cores_per_b
        t0 = (c % cores_per_b) * ntok
        pad = np.concatenate([np.zeros((1024, Dm), np.float32), xb[b]], 0)
        xh.append(pad[t0:t0 + ntok + 1024])
    xh = np.stack(xh)
    qT, kT, vT = run_l0mix(xh, L0_attn_spectral_basis, L0_attn_M_phi_w,
                           L0_attn_M_phi_b, L0_attn_wq_w, L0_attn_wq_b,
                           L0_attn_wk_w, L0_attn_wk_b, L0_attn_wv_w, L0_attn_wv_b)
    bh_assign = np.array([[(c % cores_per_b) * nbh + j for j in range(nbh)]
                          for c in range(ncore)])
    y = run_l0scan(to_bh(qT), to_bh(kT), to_bh(vT), L0_attn_wg_z_w,
                   L0_attn_wg_z_b, L0_attn_kv_norm_scale, bh_assign)
    Y = np.empty((B, L, 16, 64), np.float32)
    for c in range(ncore):
        b = c // cores_per_b
        for j in range(nbh):
            h = (c % cores_per_b) * nbh + j
            Y[b, :, h] = y[c, j]
    Y = Y.reshape(-1, Dm)
    x = unshard(run_l0out(shard(x), shard(Y), L0_attn_wo_w, L0_attn_wo_b))
    x = unshard(run_ffn0(shard(x), L0_ffn_weight, L0_ffn_bias))
    x = unshard(run_ffn1(shard(x), L1_ffn_w_gate, L1_ffn_w_up, L1_ffn_w_down))

    qT1, kT1, v1 = run_l1qkv(shard(x), L1_attn_Wq, L1_attn_Wk, L1_attn_Wv)
    vs = np.empty((ncore, nbh, seq, 64), v1.dtype)
    vfull = v1.reshape(B, L, Dm)
    for c in range(ncore):
        b = c // cores_per_b
        for j in range(nbh):
            h = (c % cores_per_b) * nbh + j
            vs[c, j] = vfull[b, :, h * 64:(h + 1) * 64]
    ot = run_l1flash(to_bh(qT1), to_bh(kT1), vs)
    OT = np.empty((B, 16, 64, L), np.float32)
    for c in range(ncore):
        b = c // cores_per_b
        for j in range(nbh):
            h = (c % cores_per_b) * nbh + j
            OT[b, h] = ot[c, j, 0:64]
    ot_sh = np.empty((ncore, Dm, ntok), np.float32)
    for c in range(ncore):
        b = c // cores_per_b
        t0 = (c % cores_per_b) * ntok
        ot_sh[c] = OT[b].reshape(Dm, L)[:, t0:t0 + ntok]
    x = unshard(run_l1out(shard(x), ot_sh, L1_attn_Wu, L1_attn_bu))
    x = unshard(run_ffn1(shard(x), L1_ffn_w_gate, L1_ffn_w_up, L1_ffn_w_down))
    return np.ascontiguousarray(x.reshape(B, L, Dm)).astype(np.float32)


def build_fused(ntok):
    seq = 2 * ntok
    nch128 = ntok // 128
    nc = bass.Bass("TRN2", target_bir_lowering=False, debug=False)
    x_in = nc.dram_tensor("x_in", [ntok, D], F32, kind="ExternalInput").ap()
    nonce_in = nc.dram_tensor("nonce_in", [1, 128], I32, kind="ExternalInput").ap()
    w0_in = nc.dram_tensor("w0_in", [D, D], F32, kind="ExternalInput").ap()
    b0_in = nc.dram_tensor("b0_in", [D], F32, kind="ExternalInput").ap()
    tt_in = nc.dram_tensor("tt_in", [128, 16 * 9 * 128], F32, kind="ExternalInput").ap()
    mphiT_in = nc.dram_tensor("mphiT_in", [16384, D], F32, kind="ExternalInput").ap()
    mb_in = nc.dram_tensor("mb_in", [128, 8], F32, kind="ExternalInput").ap()
    wqT_in = nc.dram_tensor("wqT_in", [D, D], F32, kind="ExternalInput").ap()
    wkT_in = nc.dram_tensor("wkT_in", [D, D], F32, kind="ExternalInput").ap()
    wvT_in = nc.dram_tensor("wvT_in", [D, D], F32, kind="ExternalInput").ap()
    qb_in = nc.dram_tensor("qb_in", [128, 8], F32, kind="ExternalInput").ap()
    kb_in = nc.dram_tensor("kb_in", [128, 8], F32, kind="ExternalInput").ap()
    vb_in = nc.dram_tensor("vb_in", [128, 8], F32, kind="ExternalInput").ap()
    wgT_in = nc.dram_tensor("wgT_in", [64, 64], F32, kind="ExternalInput").ap()
    kvs_in = nc.dram_tensor("kvs_in", [1, 16], F32, kind="ExternalInput").ap()
    wgb_in = nc.dram_tensor("wgb_in", [1, 16], F32, kind="ExternalInput").ap()
    ident_in = nc.dram_tensor("ident_in", [128, 128], F32, kind="ExternalInput").ap()
    triu_in = nc.dram_tensor("triu_in", [128, 128], F32, kind="ExternalInput").ap()
    woT_in = nc.dram_tensor("woT_in", [D, D], F32, kind="ExternalInput").ap()
    wob_in = nc.dram_tensor("wob_in", [1, D], F32, kind="ExternalInput").ap()
    wg1_in = nc.dram_tensor("wg1_in", [IFF, D], F32, kind="ExternalInput").ap()
    wu1_in = nc.dram_tensor("wu1_in", [IFF, D], F32, kind="ExternalInput").ap()
    wd1_in = nc.dram_tensor("wd1_in", [D, IFF], F32, kind="ExternalInput").ap()
    waqT_in = nc.dram_tensor("waqT_in", [D, D], F32, kind="ExternalInput").ap()
    wakT_in = nc.dram_tensor("wakT_in", [D, D], F32, kind="ExternalInput").ap()
    wavT_in = nc.dram_tensor("wavT_in", [D, D], F32, kind="ExternalInput").ap()
    wauT_in = nc.dram_tensor("wauT_in", [D, D], F32, kind="ExternalInput").ap()
    bau_in = nc.dram_tensor("bau_in", [1, D], F32, kind="ExternalInput").ap()
    x_out = nc.dram_tensor("x_out", [ntok, D], F32, kind="ExternalOutput").ap()
    xres = nc.dram_tensor("xres", [ntok, D], F32).ap()
    hscr = nc.dram_tensor("hscr", [ntok // 128, 128, IFF], BF16).ap()
    mphi_bf = nc.dram_tensor("mphi_bf", [16384, D], BF16).ap()
    uqscr = nc.dram_tensor("uqscr", [ntok // 128, 128, D], BF16).ap()
    xtscr = nc.dram_tensor("xtscr", [ntok // 256, 128, 8 * 256], BF16).ap()
    gyscr = nc.dram_tensor("gyscr", [ntok // 128, 128, 2 * IFF], F32).ap()
    halo_sh = nc.dram_tensor("halo_sh", [3 * 1024, D], F32, addr_space="Shared").ap()
    q0_sh = nc.dram_tensor("q0_sh", [D, seq], BF16, addr_space="Shared").ap()
    k0_sh = nc.dram_tensor("k0_sh", [D, seq], BF16, addr_space="Shared").ap()
    v0_sh = nc.dram_tensor("v0_sh", [D, seq], BF16, addr_space="Shared").ap()
    y_sh = nc.dram_tensor("y_sh", [seq, D], F32, addr_space="Shared").ap()
    q1_sh = nc.dram_tensor("q1_sh", [D, seq], BF16, addr_space="Shared").ap()
    k1_sh = nc.dram_tensor("k1_sh", [D, seq], BF16, addr_space="Shared").ap()
    v1_sh = nc.dram_tensor("v1_sh", [seq, D], BF16, addr_space="Shared").ap()
    ot_sh = nc.dram_tensor("ot_sh", [D, seq], F32, addr_space="Shared").ap()
    flg = nc.dram_tensor("flg", [2, 16], I32, addr_space="Shared").ap()

    p = Prog()
    es = contextlib.ExitStack()
    with es:
        sb = lambda name, shape, dt=F32: es.enter_context(nc.sbuf_tensor(name, shape, dt))[:]
        sNon = sb("sNon", [1, 128], I32)
        sFw = sb("sFw", [1, 1], I32)
        sFr = sb("sFr", [1, 1], I32)
        sem = es.enter_context(nc.semaphore("sem"))
        bsem = es.enter_context(nc.semaphore("bsem"))
        psem = es.enter_context(nc.semaphore("psem"))
        tsem = es.enter_context(nc.semaphore("tsem"))
        msem = es.enter_context(nc.semaphore("msem"))
        esem = es.enter_context(nc.semaphore("esem"))
        qsem = es.enter_context(nc.semaphore("qsem"))
        vsem = es.enter_context(nc.semaphore("vsem"))
        block = es.enter_context(nc.Block())

        sIdF2 = es.enter_context(nc.sbuf_tensor("sIdF2", [128, 128], F32))[:]
        sIdB2 = es.enter_context(nc.sbuf_tensor("sIdB2", [128, 128], BF16))[:]
        p.sync(lambda e: e.dma_start(out=sNon, in_=nonce_in))
        p.sync(lambda e: e.dma_start(out=sIdF2, in_=ident_in))
        p.vector(lambda e: e.tensor_copy(sIdB2, sIdF2))

        bar_state = p.ctx

        def gps_pre(g):
            bar_state["b_par"] = g.alloc_register("b_par")
            g.reg_alu(bar_state["b_par"], g.partition_id(), 1, ALU.bitwise_and)
            bar_state["rn"] = g.alloc_register("b_rn")
            bar_state["rt"] = g.alloc_register("b_rt")
            bar_state["rv"] = g.alloc_register("b_rv")
            bar_state["rf"] = g.alloc_register("b_rf")
            bar_state["pc"] = g.alloc_register("b_pc")
            g.reg_mov(bar_state["pc"], 0)

        p.pre["gpsimd"] = gps_pre
        bar_n = [0]
        tcount = [0]
        mcount = [0]
        qcount = [0]
        vcount = [0]

        def barrier():
            bar_n[0] += 1
            k = bar_n[0]

            def fn(g, t0, sem_):
                if t0 > 0:
                    g.wait_ge(sem_, t0)
                rn, rt, rv, rf, pc = (bar_state[n] for n in ("rn", "rt", "rv", "rf", "pc"))
                if k == 1:
                    g.reg_load(rn, sNon[0:1, 0:1])
                g.reg_alu(rt, rn, k, ALU.add)
                g.store(sFw[0:1, 0:1], rt)
                with g.If_eq(bar_state["b_par"], 0):
                    g.dma_start(out=flg[0:1, k:k + 1], in_=sFw[0:1, 0:1]).then_inc(bsem, 16)
                with g.Else():
                    g.dma_start(out=flg[1:2, k:k + 1], in_=sFw[0:1, 0:1]).then_inc(bsem, 16)
                g.wait_ge(bsem, 16 * k)
                g.reg_mov(rv, 0)
                g.reg_alu(rf, rv, rt, ALU.not_equal)
                with g.While(rf):
                    g.reg_add(pc, pc, 16)
                    with g.If_eq(bar_state["b_par"], 0):
                        g.dma_start(out=sFr[0:1, 0:1], in_=flg[1:2, k:k + 1]).then_inc(psem, 16)
                    with g.Else():
                        g.dma_start(out=sFr[0:1, 0:1], in_=flg[0:1, k:k + 1]).then_inc(psem, 16)
                    g.wait_ge(psem, pc)
                    g.reg_load(rv, sFr[0:1, 0:1])
                    g.reg_alu(rf, rv, rt, ALU.not_equal)
                g.nop().then_inc(sem_, 1)

            p.raw("gpsimd", fn, 1)

        es_f0 = contextlib.ExitStack()
        w0T = es_f0.enter_context(nc.sbuf_tensor("r_w0T", [128, 8, D], BF16))[:]
        sB0b = es_f0.enter_context(nc.sbuf_tensor("r_sB0b", [1, D], BF16))[:]
        sOnes1 = es_f0.enter_context(nc.sbuf_tensor("r_sOnes1", [1, 128], BF16))[:]
        f0_init = [False]

        def emit_ffn0(src_x, dst_x, publish_halo, tag):
            CH = 512
            TB = CH // 128
            nchk = ntok // CH
            with contextlib.ExitStack() as s2:
                sb2 = lambda name, shape, dt=F32: s2.enter_context(nc.sbuf_tensor(tag + name, shape, dt))[:]
                if not f0_init[0]:
                    f0_init[0] = True
                    with contextlib.ExitStack() as s3:
                        sb3 = lambda name, shape, dt=F32: s3.enter_context(nc.sbuf_tensor(tag + name, shape, dt))[:]
                        sWrow = sb3("w_sWrow", [128, D])
                        sWd = sb3("w_sWd", [128, D], BF16)
                        wA = sb3("w_A", [128, D])
                        wB = sb3("w_B", [128, D])
                        wC = sb3("w_C", [128, D])
                        wD = sb3("w_D", [128, D])
                        wq_dscr = sb3("w_dscr", [128, D // 32])
                        sBias = sb3("w_sBias", [1, D])
                        w_r = w0_in.rearrange("(oc q) k -> oc q k", q=128)
                        for oc in range(8):
                            p.sync(lambda e, oc=oc: e.dma_start(out=sWrow, in_=w_r[oc]))
                            emit_quant_fp6(p, sWrow, sWd, wA, wB, wC, wD, wq_dscr)
                            for kc in range(8):
                                p.sync(lambda e, oc=oc, kc=kc: e.dma_start_transpose(
                                    w0T[:, kc, oc * 128:(oc + 1) * 128],
                                    sWd[:, kc * 128:(kc + 1) * 128]), fuse=(kc > 0))
                        p.sync(lambda e: e.dma_start(out=sBias, in_=b0_in.rearrange("(o f) -> o f", o=1)))
                        p.vector(lambda e: e.tensor_copy(sB0b, sBias))
                        p.vector(lambda e: e.memset(sOnes1, 1.0))

                sX = sb2("f0_sX", [128, TB, D])
                sU = sb2("f0_sU", [128, TB * D])
                sUd = sb2("f0_sUd", [128, TB * D], BF16)
                sUdT = sb2("f0_sUdT", [128, TB * 8, 128], BF16)
                bA = sb2("f0_bA", [128, TB * D])
                bB = sb2("f0_bB", [128, TB * D])
                bC = sb2("f0_bC", [128, TB * D])
                bD = sb2("f0_bD", [128, TB * D])
                dscr = sb2("f0_dscr", [128, TB * D // 32])
                stat_a = sb2("f0_stat_a", [128, TB])
                stat_b = sb2("f0_stat_b", [128, TB])
                sZ = sb2("f0_sZ", [128, 1024]) if publish_halo else None
                if publish_halo:
                    p.vector(lambda e: e.memset(sZ, 0.0))
                    hv3 = halo_sh.rearrange("(blk q) f -> q blk f", q=128)
                    for zb in range(8):
                        p.sync(lambda e, zb=zb: e.dma_start(
                            out=hv3[:, zb:zb + 1, :], in_=sZ.rearrange("q (a f) -> q a f", a=1)))
                x_r = src_x.rearrange("(c tb q) f -> c q tb f", q=128, tb=TB)
                uq_r = uqscr.rearrange("(c tb) q f -> c q tb f", tb=TB)
                for c in range(nchk):
                    p.sync(lambda e, c=c: e.dma_start(out=sX, in_=x_r[c]))
                    emit_ln(p, sX, sU.rearrange("q (n f) -> q n f", f=D), stat_a, stat_b, bA, TB)
                    emit_quant_fp6(p, sU, sUd, bA, bB, bC, bD, dscr)
                    emit_transpose_blocks(p, sUd, sUdT.rearrange("q a b -> q (a b)"), TB * 8)
                    p.sync(lambda e, c=c: e.dma_start(
                        out=uq_r[c], in_=sUdT.rearrange("q (tb k) b -> q tb (k b)", tb=TB)))

                sX2 = sb2("f0_sX2", [128, 2, TB, D])
                sUdT2 = sb2("f0_sUdT2", [128, 2, TB * 8, 128], BF16)
                pTa = s2.enter_context(nc.psum_tensor(tag + "f0_pTa", [128, D], F32))[:]
                pTb = s2.enter_context(nc.psum_tensor(tag + "f0_pTb", [128, D], F32))[:]
                pT2 = [pTa, pTb]
                xo_r = dst_x.rearrange("(c tb q) f -> c q tb f", q=128, tb=TB)
                mtf = {}
                qtf = {}
                fmark = {}

                def f0_dma(c):
                    qdep = qtf.get(c - 2, 0)
                    def f(e, t0, sem_, c=c, qdep=qdep):
                        e.wait_ge(sem_, t0)
                        if qdep:
                            e.wait_ge(qsem, qdep)
                        e.dma_start(out=sUdT2[:, c % 2, :, :].rearrange(
                            "q (tb k) b -> q tb (k b)", tb=TB),
                                    in_=uq_r[c]).then_inc(msem, 16)
                        e.dma_start(out=sX2[:, c % 2, :, :], in_=x_r[c]).then_inc(msem, 16)
                    p.raw("gpsimd", f, 0)
                    mcount[0] += 32
                    mtf[c] = mcount[0]

                f0_dma(0)
                f0_dma(1)
                for c in range(nchk):
                    for tb in range(TB):
                        def fgrp(e, t0, sem_, c=c, tb=tb, nm=mtf[c],
                                 wm=fmark.get((c, tb - 2))):
                            if t0 > 0 and tb == 0:
                                e.wait_ge(sem_, t0)
                            if wm is not None:
                                e.wait_ge(sem_, p.resolve(wm))
                            e.wait_ge(msem, nm)
                            for half in range(2):
                                osl = slice(half * 512, (half + 1) * 512)
                                for kc in range(8):
                                    e.matmul(pT2[(c * TB + tb) % 2][:, osl],
                                             sUdT2[:, c % 2, tb * 8 + kc, :],
                                             w0T[:, kc, osl],
                                             start=(kc == 0), stop=False).then_inc(qsem, 1)
                                e.matmul(pT2[(c * TB + tb) % 2][:, osl], sOnes1,
                                         sB0b[:, osl], start=False,
                                         stop=True).then_inc(qsem, 1)
                        p.raw("tensor", fgrp, 0)
                        qcount[0] += 18
                        qd = qcount[0]
                        def fcomb(e, t0, sem_, c=c, tb=tb, qd=qd):
                            if t0 > 0:
                                e.wait_ge(sem_, t0)
                            e.wait_ge(qsem, qd)
                            e.scalar_tensor_tensor(
                                sX2[:, c % 2, tb, :], pT2[(c * TB + tb) % 2], 0.5,
                                sX2[:, c % 2, tb, :], ALU.mult, ALU.add).then_inc(sem_, 1)
                        p.raw("vector", fcomb, 1)
                        fmark[(c, tb)] = p.mark()
                    qtf[c] = qcount[0]
                    p.sync(lambda e, c=c: e.dma_start(out=xo_r[c], in_=sX2[:, c % 2, :, :]))
                    if publish_halo and c >= nchk - 2:
                        hv5 = halo_sh.rearrange("(a blk q) f -> a q blk f", a=3, q=128)
                        cc0 = c - (nchk - 2)
                        def pub(par, cc0=cc0, c=c):
                            return lambda e: e.dma_start(
                                out=hv5[par + 1, :, cc0 * TB:(cc0 + 1) * TB, :],
                                in_=sX2[:, c % 2, :, :])
                        p.par_dma(pub)
                    if c + 2 < nchk:
                        f0_dma(c + 2)

        emit_ffn0(x_in, xres, True, "A")
        barrier()

        def emit_l0mix():
            T = 256
            nchk = ntok // T
            NRING = 10
            NSLAB = 2
            with contextlib.ExitStack() as s2:
                sb2 = lambda name, shape, dt=F32: s2.enter_context(nc.sbuf_tensor(name, shape, dt))[:]
                sXb = sb2("m_sXb", [128, D])
                sU = sb2("m_sU", [128, D])
                bA = sb2("m_bA", [128, D])
                stat_a = sb2("m_stat_a", [128, 1])
                stat_b = sb2("m_stat_b", [128, 1])
                ring = sb2("m_ring", [128, NRING, D], BF16)
                sTT = sb2("m_sTT", [128, 144, 128], BF16)
                sConv = sb2("m_sConv", [128, 2, D], BF16)
                sCT = sb2("m_sCT", [128, 128, T], BF16)
                sMph = sb2("m_sMph", [128, 2, NSLAB, D], BF16)
                sXtT = sb2("m_sXtT", [128, 8, T], BF16)
                sMb = sb2("m_sMb", [128, 8])
                pC = s2.enter_context(nc.psum_tensor("m_pC", [128, D], F32))[:]
                pXT = s2.enter_context(nc.psum_tensor("m_pXT", [128, 8 * T], F32))[:]

                p.gps_dma(lambda e: e.dma_start(
                    out=sTT.rearrange("q a b -> q (a b)"), in_=tt_in))
                for qq in range(8):
                    p.gps_dma(lambda e, qq=qq: e.dma_start(
                        out=mphi_bf[qq * 2048:(qq + 1) * 2048, :],
                        in_=mphiT_in[qq * 2048:(qq + 1) * 2048, :]), fuse=(qq > 0))
                p.sync(lambda e: e.dma_start(out=sMb, in_=mb_in))

                xr_r = xres.rearrange("(ib q) f -> ib q f", q=128)

                def ln_to_ring(ib):
                    if ib < 8:
                        hv4 = halo_sh.rearrange("(a r) f -> a r f", a=3)
                        def ld(par, ib=ib):
                            return lambda e: e.dma_start(
                                out=sXb, in_=hv4[par, ib * 128:(ib + 1) * 128, :])
                        p.par_dma(ld)
                    else:
                        p.sync(lambda e, ib=ib: e.dma_start(out=sXb, in_=xr_r[ib - 8]))
                    emit_ln(p, sXb.rearrange("q (n f) -> q n f", n=1),
                            sU.rearrange("q (n f) -> q n f", n=1), stat_a, stat_b, bA, 1)
                    p.vector(lambda e, ib=ib: e.tensor_copy(ring[:, ib % NRING, :], sU))

                for ib in range(8):
                    ln_to_ring(ib)

                mphi_r = mphi_bf.rearrange("(sl j q) o -> sl q j o", q=128, j=NSLAB)
                for c in range(nchk):
                    for bi in range(2):
                        ln_to_ring(8 + 2 * c + bi)
                    for bi in range(2):
                        ob = 2 * c + bi
                        for kf in range(16):
                            kk = bi * 16 + kf
                            for d in range(9):
                                ib = ob + 8 - d
                                for half in range(2):
                                    osl = slice(half * 512, (half + 1) * 512)
                                    first = (d == 0 and half == 0)
                                    p.tensor(lambda e, kf=kf, d=d, ib=ib, osl=osl: e.matmul(
                                        pC[:, osl], sTT[:, kf * 9 + d, :],
                                        ring[:, ib % NRING, osl],
                                        start=(d == 0), stop=(d == 8)),
                                        fuse=not first)
                            need_t = max(0, tcount[0] - 16 * 8)
                            def evac(e, t0, sem_, kk=kk, need_t=need_t):
                                e.wait_ge(sem_, t0)
                                if need_t:
                                    e.wait_ge(tsem, need_t)
                                e.tensor_copy(sConv[:, kk % 2, :], pC).then_inc(sem_, 1)
                            p.raw("vector", evac, 1)
                            def transp(e, t0, sem_, kf=kf, bi=bi, kk=kk):
                                e.wait_ge(sem_, t0)
                                for cc in range(8):
                                    e.dma_start_transpose(
                                        sCT[:, kf * 8 + cc, bi * 128:(bi + 1) * 128],
                                        sConv[:, kk % 2, cc * 128:(cc + 1) * 128]
                                    ).then_inc(tsem, 16)
                            p.raw("sync", transp, 0)
                            tcount[0] += 16 * 8
                    NSL = 128 // NSLAB
                    mtarget0 = mcount[0]

                    def slab_dma(sl):
                        need_t = tcount[0] if sl < 2 else 0
                        def f(e, t0, sem_, sl=sl, need_t=need_t):
                            e.wait_ge(sem_, t0)
                            if need_t:
                                e.wait_ge(tsem, need_t)
                            e.dma_start(out=sMph[:, sl % 2, :, :],
                                        in_=mphi_r[sl]).then_inc(msem, 16)
                        p.raw("gpsimd", f, 0)
                        mcount[0] += 16

                    slab_dma(0)
                    slab_dma(1)
                    for sl in range(NSL):
                        need_m = mtarget0 + 16 * (sl + 1)
                        need_t = tcount[0]
                        for j in range(NSLAB):
                            kc = sl * NSLAB + j
                            for oc in range(8):
                                first = (j == 0 and oc == 0)
                                if first:
                                    def fmm(e, t0, sem_, sl=sl, j=j, oc=oc, kc=kc,
                                            need_m=need_m, need_t=need_t):
                                        e.wait_ge(sem_, t0)
                                        e.wait_ge(msem, need_m)
                                        if sl == 0:
                                            e.wait_ge(tsem, need_t)
                                        e.matmul(
                                            pXT[:, oc * T:(oc + 1) * T],
                                            sMph[:, sl % 2, j, oc * 128:(oc + 1) * 128],
                                            sCT[:, kc, :],
                                            start=(kc == 0 and oc % 2 == 0),
                                            stop=(kc == 127)).then_inc(sem_, 1)
                                    p.raw("tensor", fmm, 1)
                                else:
                                    p.tensor(lambda e, sl=sl, j=j, oc=oc, kc=kc: e.matmul(
                                        pXT[:, oc * T:(oc + 1) * T],
                                        sMph[:, sl % 2, j, oc * 128:(oc + 1) * 128],
                                        sCT[:, kc, :],
                                        start=(kc == 0 and oc % 2 == 0), stop=(kc == 127)),
                                        fuse=True)
                        if sl + 2 < NSL:
                            slab_dma(sl + 2)
                    for oc in range(8):
                        p.scalar(lambda e, oc=oc: e.activation(
                            sXtT[:, oc, :], pXT[:, oc * T:(oc + 1) * T], AF.Identity,
                            bias=sMb[:, oc:oc + 1]))
                    p.sync(lambda e, c=c: e.dma_start(
                        out=xtscr[c], in_=sXtT.rearrange("q a b -> q (a b)")))

        emit_l0mix()

        def emit_l0qkv():
            T = 256
            nchk = ntok // T
            with contextlib.ExitStack() as s2:
                sb2 = lambda name, shape, dt=F32: s2.enter_context(nc.sbuf_tensor(name, shape, dt))[:]
                sWqT = sb2("k_sWqT", [128, 8, D], BF16)
                sWkT = sb2("k_sWkT", [128, 8, D], BF16)
                sWvT = sb2("k_sWvT", [128, 8, D], BF16)
                sQb = sb2("k_sQb", [128, 8])
                sKb = sb2("k_sKb", [128, 8])
                sVb = sb2("k_sVb", [128, 8])
                sXt2 = sb2("k_sXt2", [128, 2, 8, T], BF16)
                sQo = sb2("k_sQo", [128, 8, T], BF16)
                pQa = s2.enter_context(nc.psum_tensor("k_pQa", [128, T], F32))[:]
                pQb = s2.enter_context(nc.psum_tensor("k_pQb", [128, T], F32))[:]
                pQ2 = [pQa, pQb]
                for w_in, sW in ((wqT_in, sWqT), (wkT_in, sWkT), (wvT_in, sWvT)):
                    w_r = w_in.rearrange("(kc q) o -> kc q o", q=128)
                    for kc in range(8):
                        p.gps_dma(lambda e, kc=kc, w_r=w_r, sW=sW: e.dma_start(
                            out=sW[:, kc, :], in_=w_r[kc]))
                for b_in_, sB in ((qb_in, sQb), (kb_in, sKb), (vb_in, sVb)):
                    p.sync(lambda e, b_in_=b_in_, sB=sB: e.dma_start(out=sB, in_=b_in_))
                q_r = q0_sh.rearrange("(oc p) (h t) -> p oc h t", p=128, h=2)
                k_r = k0_sh.rearrange("(oc p) (h t) -> p oc h t", p=128, h=2)
                v_r = v0_sh.rearrange("(oc p) (h t) -> p oc h t", p=128, h=2)
                mtq = {}
                qtq = {}

                def xt_dma(c):
                    qdep = qtq.get(c - 2, 0)
                    def f(e, t0, sem_, c=c, qdep=qdep):
                        e.wait_ge(sem_, t0)
                        if qdep:
                            e.wait_ge(qsem, qdep)
                        e.dma_start(out=sXt2[:, c % 2, :, :].rearrange("q a b -> q (a b)"),
                                    in_=xtscr[c]).then_inc(msem, 16)
                    p.raw("gpsimd", f, 0)
                    mcount[0] += 16
                    mtq[c] = mcount[0]

                xt_dma(0)
                xt_dma(1)
                gq = [0]
                for c in range(nchk):
                    for sW, sBv, out_r in ((sWqT, sQb, q_r), (sWkT, sKb, k_r),
                                           (sWvT, sVb, v_r)):
                        for oc in range(8):
                            gg = gq[0]
                            vdep = max(0, vcount[0] - 1)
                            def fgrp(e, t0, sem_, c=c, oc=oc, sW=sW, gg=gg,
                                     nm=mtq[c], vdep=vdep):
                                e.wait_ge(msem, nm)
                                if vdep:
                                    e.wait_ge(vsem, vdep)
                                for kc in range(8):
                                    e.matmul(pQ2[gg % 2], sW[:, kc, oc * 128:(oc + 1) * 128],
                                             sXt2[:, c % 2, kc, :],
                                             start=(kc == 0), stop=(kc == 7)).then_inc(qsem, 1)
                            p.raw("tensor", fgrp, 0)
                            qcount[0] += 8
                            qd = qcount[0]
                            foc = (oc == 0)
                            def fev(e, t0, sem_, oc=oc, sBv=sBv, gg=gg, qd=qd, foc=foc):
                                if foc and t0 > 0:
                                    e.wait_ge(sem_, t0)
                                e.wait_ge(qsem, qd)
                                e.activation(sQo[:, oc, :], pQ2[gg % 2], AF.Identity,
                                             bias=sBv[:, oc:oc + 1]).then_inc(vsem, 1)
                            p.raw("scalar", fev, 0)
                            vcount[0] += 1
                            gq[0] += 1
                        vd = vcount[0]
                        def wrout(e, t0, sem_, c=c, out_r=out_r, vd=vd):
                            if t0 > 0:
                                e.wait_ge(sem_, t0)
                            e.wait_ge(vsem, vd)
                            with e.If_eq(p.ctx["b_par"], 0):
                                e.dma_start(out=out_r[:, :, 0, c * T:(c + 1) * T],
                                            in_=sQo).then_inc(sem_, 16)
                            with e.Else():
                                e.dma_start(out=out_r[:, :, 1, c * T:(c + 1) * T],
                                            in_=sQo).then_inc(sem_, 16)
                        p.raw("gpsimd", wrout, 16)
                    qtq[c] = qcount[0]
                    if c + 2 < nchk:
                        xt_dma(c + 2)

        emit_l0qkv()
        barrier()
        def emit_l0scan():
            SEG = 1024
            nseg = seq // SEG
            EPS = 1e-5
            nbh = 8
            with contextlib.ExitStack() as s2:
                sb2 = lambda name, shape, dt=F32: s2.enter_context(nc.sbuf_tensor(name, shape, dt))[:]
                sQT = sb2("s_sQT", [64, seq], BF16)
                sKT = sb2("s_sKT", [64, seq], BF16)
                sVT = sb2("s_sVT", [64, seq], BF16)
                sWgb = sb2("s_sWgb", [64, 64], BF16)
                sKvsU = sb2("s_sKvsU", [1, 1])
                sWgbU = sb2("s_sWgbU", [1, 1])
                sIdF = sb2("s_sIdF", [128, 128])
                sIdB = sb2("s_sIdB", [128, 128], BF16)
                sTriu = sb2("s_sTriu", [128, 128])
                sOnes64b = sb2("s_sOnes64b", [64, 1], BF16)
                sOnesF = sb2("s_sOnesF", [1, 64])
                sOneF = sb2("s_sOneF", [1, 1])
                sE = sb2("s_sE", [64, SEG], BF16)
                L = {nm: sb2("s_L_" + nm, [1, SEG]) for nm in
                     ["qk", "ksq", "vsq", "vkg", "qsum", "gcum", "m", "msh", "nrm",
                      "w", "ones"]}
                sLast = sb2("s_sLast", [1, 8])
                vc_last = sb2("s_vc_last", [64, 1])
                dkb = sb2("s_dkb", [64, SEG])
                w0b = sb2("s_w0b", [64, SEG])
                vcT = sb2("s_vcT", [64, SEG])
                k_tm8 = sb2("s_k_tm8", [128, 8, 64], BF16)
                v_tm8 = sb2("s_v_tm8", [128, 8, 64], BF16)
                Va8 = sb2("s_Va8", [128, 8, 64], BF16)
                VaT8 = sb2("s_VaT8", [64, 8, 128], BF16)
                PmT = sb2("s_PmT", [128, 128], BF16)
                S_sb = sb2("s_S_sb", [64, 64], BF16)
                sCols8 = sb2("s_sCols8", [128, 24])
                sY8 = sb2("s_sY8", [128, 8, 64])
                pA = s2.enter_context(nc.psum_tensor("s_pA", [64, 512], F32))[:]
                pTK = s2.enter_context(nc.psum_tensor("s_pTK", [128, 512], BF16))[:]
                pVaT = s2.enter_context(nc.psum_tensor("s_pVaT", [64, 1024], BF16))[:]
                pcols = s2.enter_context(nc.psum_tensor("s_pcols", [128, 24], F32))[:]
                pYb = s2.enter_context(nc.psum_tensor("s_pYb", [128, 64], F32))[:]
                pP = s2.enter_context(nc.psum_tensor("s_pP", [128, 128], F32))[:]
                pS64 = s2.enter_context(nc.psum_tensor("s_pS64", [64, 64], F32))[:]
                pVc8 = s2.enter_context(nc.psum_tensor("s_pVc8", [128, 512], F32))[:]

                p.sync(lambda e: e.dma_start(out=sIdF, in_=ident_in))
                p.sync(lambda e: e.dma_start(out=sTriu, in_=triu_in))
                p.gps_dma(lambda e: e.dma_start(out=sWgb, in_=wgT_in))
                p.vector(lambda e: e.tensor_copy(sIdB, sIdF))
                p.vector(lambda e: e.memset(sOnes64b, 1.0))
                p.vector(lambda e: e.memset(sOnesF, 1.0))
                p.vector(lambda e: e.memset(sOneF, 1.0))
                p.vector(lambda e: e.memset(L["ones"], 1.0))

                def lane_sum(dst, mov, width=SEG):
                    for s0 in range(0, width, 512):
                        p.tensor(lambda e, s0=s0, mov=mov: e.matmul(
                            pA[0:1, :], sOnes64b, mov[:, s0:s0 + 512], start=True, stop=True))
                        p.vector(lambda e, s0=s0, dst=dst: e.tensor_copy(
                            dst[:, s0:s0 + 512], pA[0:1, :]))

                for u in range(nbh):
                    def mk_ld(dst, src, u=u):
                        sv = src.rearrange("(h r) t -> h r t", h=2)
                        def f(par, sv=sv, dst=dst):
                            return lambda e: e.dma_start(
                                out=dst, in_=sv[par, u * 64:(u + 1) * 64, :])
                        return f
                    p.par_dma(mk_ld(sQT, q0_sh))
                    p.par_dma(mk_ld(sKT, k0_sh))
                    p.par_dma(mk_ld(sVT, v0_sh))
                    p.vector(lambda e: e.memset(S_sb, 0.0))
                    p.vector(lambda e: e.memset(vc_last, 0.0))
                    p.vector(lambda e: e.memset(sLast[:, 0:1], -3.0e38))
                    p.vector(lambda e: e.memset(sLast[:, 1:3], 0.0))

                    def mk_scal_ld(dst, src_d, u=u):
                        sv = src_d.rearrange("o (h u2) -> o h u2", h=2)
                        def f(par, sv=sv, dst=dst):
                            return lambda e: e.dma_start(
                                out=dst, in_=sv[0:1, par, u:u + 1])
                        return f
                    p.par_dma(mk_scal_ld(sKvsU, kvs_in))
                    p.par_dma(mk_scal_ld(sWgbU, wgb_in))

                    for g in range(nseg):
                        gs = slice(g * SEG, (g + 1) * SEG)
                        qTg, kTg, vTg = sQT[:, gs], sKT[:, gs], sVT[:, gs]
                        p.vector(lambda e, qTg=qTg, kTg=kTg: e.tensor_tensor(sE, qTg, kTg, ALU.mult))
                        lane_sum(L["qk"], sE)
                        p.scalar(lambda e, kTg=kTg: e.activation(sE, kTg, AF.Square))
                        lane_sum(L["ksq"], sE)
                        p.scalar(lambda e, vTg=vTg: e.activation(sE, vTg, AF.Square))
                        lane_sum(L["vsq"], sE)
                        lane_sum(L["qsum"], qTg)
                        for s0 in range(0, SEG, 512):
                            p.tensor(lambda e, s0=s0, kTg=kTg: e.matmul(
                                pA, sWgb, kTg[:, s0:s0 + 512], start=True, stop=True))
                            p.vector(lambda e, s0=s0, vTg=vTg: e.tensor_tensor(
                                sE[:, s0:s0 + 512], vTg[:, s0:s0 + 512], pA, ALU.mult))
                        lane_sum(L["vkg"], sE)
                        for nm in ("ksq", "vsq"):
                            p.scalar(lambda e, nm=nm: e.activation(L[nm], L[nm], AF.Sqrt))
                            p.vector(lambda e, nm=nm: e.tensor_scalar(L[nm], L[nm], 1e-12, None, ALU.max))
                            p.vector(lambda e, nm=nm: e.reciprocal(L[nm], L[nm]))
                        p.vector(lambda e: e.tensor_tensor(L["qk"], L["qk"], L["ksq"], ALU.mult))
                        p.vector(lambda e: e.tensor_tensor(L["ksq"], L["ksq"], L["vsq"], ALU.mult))
                        p.vector(lambda e: e.tensor_scalar(L["ksq"], L["ksq"], sKvsU, None, ALU.mult))
                        p.vector(lambda e: e.tensor_tensor(L["vkg"], L["vkg"], L["ksq"], ALU.mult))
                        p.vector(lambda e: e.tensor_scalar(L["vkg"], L["vkg"], sWgbU, None, ALU.add))
                        p.scalar(lambda e: e.activation(L["vkg"], L["vkg"], AF.Lrelu, alpha=0.01))
                        p.scalar(lambda e: e.activation(L["vkg"], L["vkg"], AF.Square))
                        p.vector(lambda e: e.tensor_scalar(L["vkg"], L["vkg"], EPS, None, ALU.add))
                        p.vector(lambda e: e.tensor_tensor_scan(
                            L["gcum"], L["ones"], L["vkg"], sLast[0:1, 2:3], ALU.mult, ALU.add))
                        p.vector(lambda e: e.tensor_copy(sLast[:, 2:3], L["gcum"][:, SEG - 1:SEG]))
                        p.vector(lambda e: e.tensor_tensor(L["vkg"], L["vkg"], L["ksq"], ALU.mult))
                        p.vector(lambda e: e.tensor_tensor_scan(
                            L["m"], L["qk"], L["qk"], sLast[0:1, 0:1], ALU.max, ALU.bypass))
                        p.vector(lambda e: e.tensor_copy(L["msh"][:, 1:SEG], L["m"][:, 0:SEG - 1]))
                        p.vector(lambda e: e.tensor_copy(L["msh"][:, 0:1], sLast[:, 0:1]))
                        p.vector(lambda e: e.tensor_copy(sLast[:, 0:1], L["m"][:, SEG - 1:SEG]))
                        p.vector(lambda e: e.tensor_tensor(L["msh"], L["msh"], L["m"], ALU.subtract))
                        p.scalar(lambda e: e.activation(L["msh"], L["msh"], AF.Exp))
                        p.vector(lambda e: e.tensor_tensor(L["qk"], L["qk"], L["m"], ALU.subtract))
                        p.scalar(lambda e: e.activation(L["qk"], L["qk"], AF.Exp))
                        p.vector(lambda e: e.tensor_tensor_scan(
                            L["nrm"], L["msh"], L["qk"], sLast[0:1, 1:2], ALU.mult, ALU.add))
                        p.vector(lambda e: e.tensor_copy(sLast[:, 1:2], L["nrm"][:, SEG - 1:SEG]))
                        p.vector(lambda e: e.tensor_scalar(L["nrm"], L["nrm"], EPS, None, ALU.add))
                        p.vector(lambda e: e.reciprocal(L["nrm"], L["nrm"]))
                        p.vector(lambda e: e.tensor_tensor(L["w"], L["qk"], L["nrm"], ALU.mult))
                        p.vector(lambda e: e.tensor_tensor(L["qsum"], L["qsum"], L["w"], ALU.mult))
                        p.vector(lambda e: e.tensor_tensor(L["qsum"], L["qsum"], L["nrm"], ALU.mult))
                        p.vector(lambda e: e.tensor_tensor(L["qk"], L["qk"], L["vsq"], ALU.mult))
                        p.vector(lambda e: e.tensor_scalar(L["gcum"], L["gcum"], EPS, None, ALU.add))
                        p.vector(lambda e: e.reciprocal(L["gcum"], L["gcum"]))
                        p.vector(lambda e: e.tensor_scalar(L["w"], L["w"], -1.0, 1.0, ALU.mult, ALU.add))
                        p.vector(lambda e: e.tensor_tensor(L["w"], L["w"], L["gcum"], ALU.mult))
                        for s0 in range(0, SEG, 512):
                            p.tensor(lambda e, s0=s0: e.matmul(
                                pA, sOnesF, L["msh"][:, s0:s0 + 512], start=True, stop=True))
                            p.vector(lambda e, s0=s0: e.tensor_copy(dkb[:, s0:s0 + 512], pA))
                        for s0 in range(0, SEG, 512):
                            p.tensor(lambda e, s0=s0: e.matmul(
                                pA, sOnesF, L["qk"][:, s0:s0 + 512], start=True, stop=True))
                            p.vector(lambda e, s0=s0: e.tensor_copy(w0b[:, s0:s0 + 512], pA))
                        p.vector(lambda e, vTg=vTg: e.tensor_tensor(w0b, vTg, w0b, ALU.mult))
                        p.vector(lambda e: e.tensor_tensor_scan(
                            vcT, dkb, w0b, vc_last, ALU.mult, ALU.add))
                        p.vector(lambda e: e.tensor_copy(vc_last, vcT[:, SEG - 1:SEG]))
                        for cb in range(8):
                            p.tensor(lambda e, g=g, cb=cb: e.transpose(
                                pTK[:, cb * 64:(cb + 1) * 64],
                                sKT[:, g * SEG + cb * 128:g * SEG + (cb + 1) * 128],
                                sIdB[0:64, 0:64]), fuse=(cb > 0))
                        p.vector(lambda e: e.tensor_copy(
                            k_tm8.rearrange("q a b -> q (a b)"), pTK))
                        for cb in range(8):
                            p.tensor(lambda e, g=g, cb=cb: e.transpose(
                                pTK[:, cb * 64:(cb + 1) * 64],
                                sVT[:, g * SEG + cb * 128:g * SEG + (cb + 1) * 128],
                                sIdB[0:64, 0:64]), fuse=(cb > 0))
                        p.vector(lambda e: e.tensor_copy(
                            v_tm8.rearrange("q a b -> q (a b)"), pTK))
                        for cb in range(8):
                            for j3, lane in ((0, "vkg"), (1, "w"), (2, "qsum")):
                                p.tensor(lambda e, cb=cb, j3=j3, lane=lane: e.matmul(
                                    pcols[:, cb * 3 + j3:cb * 3 + j3 + 1],
                                    L[lane][:, cb * 128:(cb + 1) * 128], sOneF,
                                    start=True, stop=True), fuse=(cb + j3 > 0))
                        p.vector(lambda e: e.tensor_copy(sCols8, pcols))
                        c3 = sCols8.rearrange("q (cb three) -> q cb three", three=3)
                        p.vector(lambda e: e.tensor_tensor(
                            Va8, v_tm8,
                            c3[:, :, 0:1].broadcast_to((128, 8, 64)), ALU.mult))
                        for cb in range(8):
                            p.tensor(lambda e, cb=cb: e.transpose(
                                pVaT[:, cb * 128:(cb + 1) * 128], Va8[:, cb, :], sIdB),
                                fuse=(cb > 0))
                        p.vector(lambda e: e.tensor_copy(
                            VaT8.rearrange("q a b -> q (a b)"), pVaT))
                        for cb in range(8):
                            p.tensor(lambda e, cb=cb: e.transpose(
                                pVc8[:, cb * 64:(cb + 1) * 64], vcT[:, cb * 128:(cb + 1) * 128],
                                sIdF[0:64, 0:64]), fuse=(cb > 0))
                        for cb in range(8):
                            c = g * 8 + cb
                            qTc = qTg[:, cb * 128:(cb + 1) * 128]
                            p.tensor(lambda e, qTc=qTc: e.matmul(
                                pYb, qTc, S_sb, start=True, stop=False))
                            p.tensor(lambda e, qTc=qTc, cb=cb: e.matmul(
                                pP, VaT8[:, cb, :], qTc, start=True, stop=True), fuse=True)
                            p.vector(lambda e: e.tensor_tensor(PmT, pP, sTriu, ALU.mult))
                            p.tensor(lambda e, cb=cb: e.matmul(
                                pYb, PmT, k_tm8[:, cb, :], start=False, stop=True))
                            p.tensor(lambda e, c=c, cb=cb: e.matmul(
                                pS64, Va8[:, cb, :], k_tm8[:, cb, :],
                                start=(c == 0), stop=(c == seq // 128 - 1)), fuse=True)
                            p.vector(lambda e: e.tensor_copy(S_sb, pS64))
                            p.vector(lambda e, cb=cb: e.tensor_scalar(
                                sY8[:, cb, :], pYb, c3[:, cb, 1:2], None, ALU.mult))
                            p.vector(lambda e, cb=cb: e.scalar_tensor_tensor(
                                sY8[:, cb, :], pVc8[:, cb * 64:(cb + 1) * 64],
                                c3[:, cb, 2:3], sY8[:, cb, :], ALU.mult, ALU.add))
                        yv = y_sh.rearrange("(cb q) (h cc2) -> q cb h cc2", q=128, h=2)
                        def ywr(par, u=u, g=g, yv=yv):
                            return lambda e: e.dma_start(
                                out=yv[:, g * 8:(g + 1) * 8, par, u * 64:(u + 1) * 64],
                                in_=sY8)
                        p.par_dma(ywr)
        emit_l0scan()
        barrier()

        def emit_l0out():
            with contextlib.ExitStack() as s2:
                sb2 = lambda name, shape, dt=F32: s2.enter_context(nc.sbuf_tensor(name, shape, dt))[:]
                sX = sb2("o_sX", [128, D])
                sY = sb2("o_sY", [128, D])
                sYb = sb2("o_sYb", [128, D], BF16)
                sYbT = sb2("o_sYbT", [128, 8, 128], BF16)
                sWoT = sb2("o_sWoT", [128, 8, D], BF16)
                sBo = sb2("o_sBo", [1, D])
                sBob = sb2("o_sBob", [1, D], BF16)
                sOnes = sb2("o_sOnes", [1, 128], BF16)
                sBiasTile = sb2("o_sBiasTile", [128, D])
                stat = sb2("o_stat", [128, 1])
                sSq = sb2("o_sSq", [128, D])
                pP = s2.enter_context(nc.psum_tensor("o_pP", [128, 1024], F32))[:]

                w_r = woT_in.rearrange("(kc q) o -> kc q o", q=128)
                for kc in range(8):
                    p.gps_dma(lambda e, kc=kc: e.dma_start(out=sWoT[:, kc, :], in_=w_r[kc]))
                p.sync(lambda e: e.dma_start(out=sBo, in_=wob_in))
                p.vector(lambda e: e.tensor_copy(sBob, sBo))
                p.vector(lambda e: e.memset(sOnes, 1.0))
                for half in range(2):
                    osl = slice(half * 512, (half + 1) * 512)
                    p.tensor(lambda e, osl=osl: e.matmul(
                        pP[:, osl], sOnes, sBob[:, osl], start=True, stop=True))
                p.vector(lambda e: e.tensor_copy(sBiasTile, pP))

                x_r = xres.rearrange("(c q) f -> c q f", q=128)
                for c in range(nch128):
                    p.sync(lambda e, c=c: e.dma_start(out=sX, in_=x_r[c]))
                    yv2 = y_sh.rearrange("(h r) f -> h r f", h=2)
                    def yld(par, c=c, yv2=yv2):
                        return lambda e: e.dma_start(
                            out=sY, in_=yv2[par, c * 128:(c + 1) * 128, :])
                    p.par_dma(yld)
                    p.scalar(lambda e: e.activation(sSq, sY, AF.Square, accum_out=stat))
                    p.scalar(lambda e: e.activation(stat, stat, AF.Sqrt))
                    p.vector(lambda e: e.tensor_scalar(stat, stat, 1e-12, None, ALU.max))
                    p.vector(lambda e: e.reciprocal(stat, stat))
                    p.vector(lambda e: e.tensor_scalar(sYb, sY, stat, None, ALU.mult))
                    emit_transpose_blocks(p, sYb, sYbT.rearrange("q a b -> q (a b)"), 8)
                    for half in range(2):
                        osl = slice(half * 512, (half + 1) * 512)
                        for kc in range(8):
                            p.tensor(lambda e, kc=kc, osl=osl: e.matmul(
                                pP[:, osl], sYbT[:, kc, :], sWoT[:, kc, osl],
                                start=(kc == 0), stop=(kc == 7)))
                    p.vector(lambda e: e.tensor_add(sX, sX, sBiasTile))
                    p.vector(lambda e: e.tensor_tensor(sX, pP, sX, ALU.add))
                    p.sync(lambda e, c=c: e.dma_start(out=x_r[c], in_=sX))

        emit_l0out()
        emit_ffn0(xres, xres, False, "B")
        es_f0.close()

        wgT = es.enter_context(nc.sbuf_tensor("r_wgT", [128, 8, IFF], BF16))[:]
        wuT = es.enter_context(nc.sbuf_tensor("r_wuT", [128, 8, IFF], BF16))[:]
        wdT = es.enter_context(nc.sbuf_tensor("r_wdT", [128, 22, D], BF16))[:]

        def emit_ffn1(dst_x, tag):
            OSEG = [(s, min(s + 512, IFF)) for s in range(0, IFF, 512)]
            with contextlib.ExitStack() as s2:
                x_r = dst_x.rearrange("(c q) f -> c q f", q=128)

                if tag == "C":
                    with contextlib.ExitStack() as s3:
                        sb3 = lambda name, shape, dt=F32: s3.enter_context(nc.sbuf_tensor(tag + name, shape, dt))[:]
                        sWrow = sb3("f1_sWrow", [128, IFF])
                        sWq = sb3("f1_sWq", [128, IFF], BF16)
                        qA = sb3("f1_qA", [128, IFF])
                        qB = sb3("f1_qB", [128, IFF])
                        qC = sb3("f1_qC", [128, IFF])
                        qD = sb3("f1_qD", [128, IFF])
                        dscrH = sb3("f1_dscrH", [128, IFF // 16])

                        def quant_w(w_ap, n_oc, kdim, wT):
                            w_r = w_ap.rearrange("(oc q) k -> oc q k", q=128)
                            for oc in range(n_oc):
                                p.sync(lambda e, oc=oc, w_r=w_r: e.dma_start(
                                    out=sWrow[:, 0:kdim], in_=w_r[oc]))
                                emit_quant_fp4(p, sWrow[:, 0:kdim], sWq[:, 0:kdim],
                                               qA[:, 0:kdim], qB[:, 0:kdim], qC[:, 0:kdim],
                                               qD[:, 0:kdim], dscrH[:, 0:kdim // 16])
                                for kc in range(kdim // 128):
                                    p.sync(lambda e, oc=oc, kc=kc, wT=wT: e.dma_start_transpose(
                                        wT[:, kc, oc * 128:(oc + 1) * 128],
                                        sWq[:, kc * 128:(kc + 1) * 128]), fuse=(kc > 0))

                        quant_w(wg1_in, 22, D, wgT)
                        quant_w(wu1_in, 22, D, wuT)
                        quant_w(wd1_in, 8, IFF, wdT)

                with contextlib.ExitStack() as s3:
                    sb3 = lambda name, shape, dt=F32: s3.enter_context(nc.sbuf_tensor(tag + name, shape, dt))[:]
                    sX = sb3("p1_sX", [128, D])
                    sU = sb3("p1_sU", [128, D])
                    sUd = sb3("p1_sUd", [128, D], BF16)
                    sUdT = sb3("p1_sUdT", [128, 8, 128], BF16)
                    qA = sb3("p1_qA", [128, D])
                    qB = sb3("p1_qB", [128, D])
                    qC = sb3("p1_qC", [128, D])
                    qD = sb3("p1_qD", [128, D])
                    dscr6 = sb3("p1_dscr6", [128, D // 16])
                    stat_a = sb3("p1_stat_a", [128, 1])
                    stat_b = sb3("p1_stat_b", [128, 1])
                    for c in range(nch128):
                        p.sync(lambda e, c=c: e.dma_start(out=sX, in_=x_r[c]))
                        emit_ln(p, sX.rearrange("q (n f) -> q n f", n=1),
                                sU.rearrange("q (n f) -> q n f", n=1),
                                stat_a, stat_b, qA[:, 0:D], 1)
                        emit_quant_fp4(p, sU, sUd, qA, qB, qC, qD, dscr6)
                        emit_transpose_blocks(p, sUd, sUdT.rearrange("q a b -> q (a b)"), 8)
                        p.sync(lambda e, c=c: e.dma_start(
                            out=uqscr[c], in_=sUdT.rearrange("q a b -> q (a b)")))

                with contextlib.ExitStack() as s3:
                    sb3 = lambda name, shape, dt=F32: s3.enter_context(nc.sbuf_tensor(tag + name, shape, dt))[:]
                    sUdT2 = sb3("p2_sUdT", [128, 2, 8, 128], BF16)
                    sGY = sb3("p2_sGY", [128, 2 * IFF])
                    pPa = s3.enter_context(nc.psum_tensor(tag + "p2_pPa", [128, 512], F32))[:]
                    pPb = s3.enter_context(nc.psum_tensor(tag + "p2_pPb", [128, 512], F32))[:]
                    pP2 = [pPa, pPb]

                    mtarg = {}
                    qtarg = {}

                    def uq_dma(c, qdep):
                        def f(e, t0, sem_, c=c, qdep=qdep):
                            e.wait_ge(sem_, t0)
                            if qdep:
                                e.wait_ge(qsem, qdep)
                            e.dma_start(out=sUdT2[:, c % 2, :, :].rearrange("q a b -> q (a b)"),
                                        in_=uqscr[c]).then_inc(msem, 16)
                        p.raw("gpsimd", f, 0)
                        mcount[0] += 16
                        mtarg[c] = mcount[0]

                    uq_dma(0, 0)
                    uq_dma(1, 0)
                    gidx_all = [0]
                    vtarg = {}
                    for c in range(nch128):
                        for si, (o0, o1) in enumerate(OSEG):
                            for wi, wT in enumerate((wgT, wuT)):
                                gg = gidx_all[0]
                                vdep = vtarg.get(gg - 2, 0)
                                def fgrp(e, t0, sem_, c=c, o0=o0, o1=o1, wT=wT,
                                         gg=gg, vdep=vdep, nm=mtarg[c]):
                                    e.wait_ge(msem, nm)
                                    if vdep:
                                        e.wait_ge(vsem, vdep)
                                    for kc in range(8):
                                        e.matmul(pP2[gg % 2][:, 0:o1 - o0],
                                                 sUdT2[:, c % 2, kc, :], wT[:, kc, o0:o1],
                                                 start=(kc == 0),
                                                 stop=(kc == 7)).then_inc(qsem, 1)
                                p.raw("tensor", fgrp, 0)
                                qcount[0] += 8
                                qtarg[gg] = qcount[0]
                                doff = wi * IFF + o0
                                first_of_chunk = (si == 0 and wi == 0)
                                def fev(e, t0, sem_, gg=gg, o0=o0, o1=o1, doff=doff,
                                        foc=first_of_chunk):
                                    if foc and t0 > 0:
                                        e.wait_ge(sem_, t0)
                                    e.wait_ge(qsem, qtarg[gg])
                                    e.tensor_copy(sGY[:, doff:doff + (o1 - o0)],
                                                  pP2[gg % 2][:, 0:o1 - o0]).then_inc(vsem, 1)
                                p.raw("vector", fev, 0)
                                vcount[0] += 1
                                vtarg[gg] = vcount[0]
                                gidx_all[0] += 1
                        vd = vcount[0]
                        def fst(e, t0, sem_, c=c, vd=vd):
                            if t0 > 0:
                                e.wait_ge(sem_, t0)
                            e.wait_ge(vsem, vd)
                            e.dma_start(out=gyscr[c], in_=sGY).then_inc(sem_, 16)
                        p.raw("sync", fst, 16)
                        if c + 2 < nch128:
                            uq_dma(c + 2, qtarg.get(gidx_all[0] - 1, 0))

                with contextlib.ExitStack() as s3:
                    sb3 = lambda name, shape, dt=F32: s3.enter_context(nc.sbuf_tensor(tag + name, shape, dt))[:]
                    sH = sb3("p3_sH", [128, 2 * IFF])
                    sHq = sb3("p3_sHq", [128, IFF], BF16)
                    sHqT = sb3("p3_sHqT", [128, 22, 128], BF16)
                    qA = sb3("p3_qA", [128, IFF])
                    qB = sb3("p3_qB", [128, IFF])
                    qC = sb3("p3_qC", [128, IFF])
                    qD = sH[:, IFF:2 * IFF]
                    dscrH = sb3("p3_dscrH", [128, IFF // 16])
                    pTT = s3.enter_context(nc.psum_tensor(tag + "p3_pTT", [128, 1024], BF16))[:]
                    for c in range(nch128):
                        p.sync(lambda e, c=c: e.dma_start(out=sH, in_=gyscr[c]))
                        p.scalar(lambda e: e.activation(sH[:, 0:IFF], sH[:, 0:IFF], AF.Silu))
                        p.vector(lambda e: e.tensor_mul(sH[:, 0:IFF], sH[:, 0:IFF], sH[:, IFF:2 * IFF]))
                        emit_quant_fp4(p, sH[:, 0:IFF], sHq, qA, qB, qC, qD, dscrH)
                        for grp in range(3):
                            nblk = min(8, 22 - grp * 8)
                            qd = qcount[0]
                            def ftr(e, t0, sem_, grp=grp, nblk=nblk, qd=qd):
                                e.wait_ge(sem_, t0)
                                for j in range(nblk):
                                    kc = grp * 8 + j
                                    e.transpose(pTT[:, j * 128:(j + 1) * 128],
                                                sHq[:, kc * 128:(kc + 1) * 128],
                                                sIdB2).then_inc(qsem, 1)
                            p.raw("tensor", ftr, 0)
                            qcount[0] += nblk
                            qd2 = qcount[0]
                            def fcp(e, t0, sem_, grp=grp, nblk=nblk, qd2=qd2):
                                if t0 > 0:
                                    e.wait_ge(sem_, t0)
                                e.wait_ge(qsem, qd2)
                                e.tensor_copy(
                                    sHqT.rearrange("q a b -> q (a b)")[:, grp * 1024:grp * 1024 + nblk * 128],
                                    pTT[:, 0:nblk * 128]).then_inc(sem_, 1)
                            p.raw("vector", fcp, 1)
                        p.sync(lambda e, c=c: e.dma_start(
                            out=hscr[c], in_=sHqT.rearrange("q a b -> q (a b)")))

                with contextlib.ExitStack() as s3:
                    sb3 = lambda name, shape, dt=F32: s3.enter_context(nc.sbuf_tensor(tag + name, shape, dt))[:]
                    sX = sb3("p4_sX", [128, D])
                    sHqT2 = sb3("p4_sHqT", [128, 2, 22, 128], BF16)
                    pP4 = s3.enter_context(nc.psum_tensor(tag + "p4_pP", [128, 1024], F32))[:]

                    mtarg4 = {}
                    qtarg4 = {}

                    def hq_dma(c):
                        qdep = qtarg4.get(c - 2, 0)
                        def f(e, t0, sem_, c=c, qdep=qdep):
                            e.wait_ge(sem_, t0)
                            if qdep:
                                e.wait_ge(qsem, qdep)
                            e.dma_start(out=sHqT2[:, c % 2, :, :].rearrange("q a b -> q (a b)"),
                                        in_=hscr[c]).then_inc(msem, 16)
                        p.raw("gpsimd", f, 0)
                        mcount[0] += 16
                        mtarg4[c] = mcount[0]

                    hq_dma(0)
                    hq_dma(1)
                    for c in range(nch128):
                        p.sync(lambda e, c=c: e.dma_start(out=sX, in_=x_r[c]))
                        def fgrp(e, t0, sem_, c=c, nm=mtarg4[c]):
                            if t0 > 0:
                                e.wait_ge(sem_, t0)
                            e.wait_ge(msem, nm)
                            for half in range(2):
                                osl = slice(half * 512, (half + 1) * 512)
                                for kc in range(22):
                                    e.matmul(pP4[:, osl], sHqT2[:, c % 2, kc, :],
                                             wdT[:, kc, osl], start=(kc == 0),
                                             stop=(kc == 21)).then_inc(qsem, 1)
                        p.raw("tensor", fgrp, 0)
                        qcount[0] += 44
                        qtarg4[c] = qcount[0]
                        qd = qcount[0]
                        def fcomb(e, t0, sem_, c=c, qd=qd):
                            if t0 > 0:
                                e.wait_ge(sem_, t0)
                            e.wait_ge(qsem, qd)
                            e.scalar_tensor_tensor(
                                sX, pP4, 0.5, sX, ALU.mult, ALU.add).then_inc(sem_, 1)
                        p.raw("vector", fcomb, 1)
                        p.sync(lambda e, c=c: e.dma_start(out=x_r[c], in_=sX))
                        if c + 2 < nch128:
                            hq_dma(c + 2)

        emit_ffn1(xres, "C")

        def emit_l1qkv():
            with contextlib.ExitStack() as s2:
                sb2 = lambda name, shape, dt=F32: s2.enter_context(nc.sbuf_tensor(name, shape, dt))[:]
                sX = sb2("a_sX", [128, D])
                sU = sb2("a_sU", [128, D])
                sUb = sb2("a_sUb", [128, D], BF16)
                sUbT = sb2("a_sUbT", [128, 8, 128], BF16)
                bA = sb2("a_bA", [128, D])
                stat_a = sb2("a_stat_a", [128, 1])
                stat_b = sb2("a_stat_b", [128, 1])
                sWqT = sb2("a_sWqT", [128, 8, D], BF16)
                sWkT = sb2("a_sWkT", [128, 8, D], BF16)
                sWvT = sb2("a_sWvT", [128, 8, D], BF16)
                sQTt = sb2("a_sQTt", [128, 8, 128], BF16)
                sKTt = sb2("a_sKTt", [128, 8, 128], BF16)
                sV = sb2("a_sV", [128, D], BF16)
                pF = s2.enter_context(nc.psum_tensor("a_pF", [128, 128], F32))[:]
                pV = s2.enter_context(nc.psum_tensor("a_pV", [128, 1024], F32))[:]

                for w_in, sW in ((waqT_in, sWqT), (wakT_in, sWkT), (wavT_in, sWvT)):
                    w_r = w_in.rearrange("(kc q) o -> kc q o", q=128)
                    for kc in range(8):
                        p.gps_dma(lambda e, kc=kc, w_r=w_r, sW=sW: e.dma_start(
                            out=sW[:, kc, :], in_=w_r[kc]))

                x_r = xres.rearrange("(c q) f -> c q f", q=128)
                q_r = q1_sh.rearrange("(oc p) (h t) -> p oc h t", p=128, h=2)
                k_r = k1_sh.rearrange("(oc p) (h t) -> p oc h t", p=128, h=2)
                v1v = v1_sh.rearrange("(h r) f -> h r f", h=2)
                for c in range(nch128):
                    p.sync(lambda e, c=c: e.dma_start(out=sX, in_=x_r[c]))
                    emit_ln(p, sX.rearrange("q (n f) -> q n f", n=1),
                            sU.rearrange("q (n f) -> q n f", n=1), stat_a, stat_b, bA, 1)
                    p.vector(lambda e: e.tensor_copy(sUb, sU))
                    emit_transpose_blocks(p, sUb, sUbT.rearrange("q a b -> q (a b)"), 8)
                    for sW, sOut in ((sWqT, sQTt), (sWkT, sKTt)):
                        for oc in range(8):
                            for kc in range(8):
                                p.tensor(lambda e, oc=oc, kc=kc, sW=sW: e.matmul(
                                    pF, sW[:, kc, oc * 128:(oc + 1) * 128], sUbT[:, kc, :],
                                    start=(kc == 0), stop=(kc == 7)))
                            p.vector(lambda e, oc=oc, sOut=sOut: e.tensor_copy(sOut[:, oc, :], pF))
                    for half in range(2):
                        osl = slice(half * 512, (half + 1) * 512)
                        for kc in range(8):
                            p.tensor(lambda e, kc=kc, osl=osl: e.matmul(
                                pV[:, osl], sUbT[:, kc, :], sWvT[:, kc, osl],
                                start=(kc == 0), stop=(kc == 7)))
                    p.vector(lambda e: e.tensor_copy(sV, pV))
                    def qwr(par, c=c, r=q_r, src=sQTt):
                        return lambda e: e.dma_start(
                            out=r[:, :, par, c * 128:(c + 1) * 128], in_=src)
                    def kwr(par, c=c, r=k_r, src=sKTt):
                        return lambda e: e.dma_start(
                            out=r[:, :, par, c * 128:(c + 1) * 128], in_=src)
                    def vwr(par, c=c):
                        return lambda e: e.dma_start(
                            out=v1v[par, c * 128:(c + 1) * 128, :], in_=sV)
                    p.par_dma(qwr)
                    p.par_dma(kwr)
                    p.par_dma(vwr)

        emit_l1qkv()
        barrier()

        def emit_l1flash():
            nbh = 8
            nsc = seq // 128
            nts = seq // 512
            GRP = 3
            ngrp = (nsc + GRP - 1) // GRP
            assert ngrp % 2 == 0, "pS parity scheme needs even group count"
            ecount = [0]
            with contextlib.ExitStack() as s2:
                sb2 = lambda name, shape, dt=F32: s2.enter_context(nc.sbuf_tensor(name, shape, dt))[:]
                sKT = sb2("fl_sKT", [64, seq], BF16)
                sQT = sb2("fl_sQT", [64, seq], BF16)
                sVext = sb2("fl_sVext", [128, nsc, 65], BF16)
                sP = sb2("fl_sP", [128, 2, GRP * 512], BF16)
                sOT = sb2("fl_sOT", [65, 512])
                sInv = sb2("fl_sInv", [1, 512])
                sOnes64 = sb2("fl_sOnes64", [1, 64])
                pSa = s2.enter_context(nc.psum_tensor("fl_pSa", [128, GRP * 512], F32))[:]
                pSb = s2.enter_context(nc.psum_tensor("fl_pSb", [128, GRP * 512], F32))[:]
                pO = s2.enter_context(nc.psum_tensor("fl_pO", [65, 512], F32))[:]
                pB = s2.enter_context(nc.psum_tensor("fl_pB", [64, 512], F32))[:]
                pS2 = [pSa, pSb]

                p.vector(lambda e: e.memset(sOnes64, 1.0))
                p.vector(lambda e: e.memset(sVext[:, :, 64:65], 1.0))
                v_r = v1_sh.rearrange("(c q) (h d2) -> q c h d2", q=128, h=2)
                for u in range(nbh):
                    def mk_ld(dst, src, u=u):
                        sv = src.rearrange("(h r) t -> h r t", h=2)
                        def f(par, sv=sv, dst=dst):
                            return lambda e: e.dma_start(
                                out=dst, in_=sv[par, u * 64:(u + 1) * 64, :])
                        return f
                    p.par_dma(mk_ld(sKT, k1_sh))
                    p.par_dma(mk_ld(sQT, q1_sh))
                    def vld(par, u=u):
                        return lambda e: e.dma_start(
                            out=sVext[:, :, 0:64],
                            in_=v_r[:, :, par, u * 64:(u + 1) * 64])
                    p.par_dma(vld)
                    for ts in range(nts):
                        tsl = slice(ts * 512, (ts + 1) * 512)
                        e_base = ecount[0]

                        def scores(g):
                            sc0 = g * GRP
                            n = min(GRP, nsc - sc0)
                            for i in range(n):
                                sc = sc0 + i
                                if i == 0:
                                    need_e = max(0, e_base + g - 1)
                                    def fsc(e, t0, sem_, sc=sc, g=g, tsl=tsl,
                                            need_e=need_e, gw=(g == 0)):
                                        if gw:
                                            e.wait_ge(sem_, t0)
                                        if need_e:
                                            e.wait_ge(esem, need_e)
                                        e.matmul(pS2[g % 2][:, 0:512],
                                                 sKT[:, sc * 128:(sc + 1) * 128],
                                                 sQT[:, tsl],
                                                 start=True, stop=True).then_inc(sem_, 1)
                                    p.raw("tensor", fsc, 1)
                                else:
                                    p.tensor(lambda e, sc=sc, i=i, g=g, tsl=tsl: e.matmul(
                                        pS2[g % 2][:, i * 512:(i + 1) * 512],
                                        sKT[:, sc * 128:(sc + 1) * 128], sQT[:, tsl],
                                        start=True, stop=True), fuse=True)
                            return n

                        def expg(g, n):
                            def fex(e, t0, sem_, g=g, n=n):
                                e.wait_ge(sem_, t0)
                                e.activation(sP[:, g % 2, 0:n * 512],
                                             pS2[g % 2][:, 0:n * 512], AF.Exp,
                                             scale=1.0 / 32.0).then_inc(esem, 1)
                            p.raw("scalar", fex, 0)
                            ecount[0] += 1

                        def pvg(g, n):
                            for i in range(n):
                                sc = g * GRP + i
                                if i == 0:
                                    need_e = e_base + g + 1
                                    def fpv(e, t0, sem_, sc=sc, g=g, need_e=need_e,
                                            gw=(g == 0)):
                                        if gw:
                                            e.wait_ge(sem_, t0)
                                        e.wait_ge(esem, need_e)
                                        e.matmul(pO, sVext[:, sc, :],
                                                 sP[:, g % 2, 0:512],
                                                 start=(sc == 0),
                                                 stop=(sc == nsc - 1)).then_inc(sem_, 1)
                                    p.raw("tensor", fpv, 1)
                                else:
                                    p.tensor(lambda e, sc=sc, i=i, g=g: e.matmul(
                                        pO, sVext[:, sc, :],
                                        sP[:, g % 2, i * 512:(i + 1) * 512],
                                        start=(sc == 0), stop=(sc == nsc - 1)),
                                        fuse=True)

                        nlist = [None] * ngrp
                        nlist[0] = scores(0)
                        expg(0, nlist[0])
                        for g in range(1, ngrp):
                            nlist[g] = scores(g)
                            expg(g, nlist[g])
                            pvg(g - 1, nlist[g - 1])
                        pvg(ngrp - 1, nlist[ngrp - 1])

                        p.vector(lambda e: e.tensor_copy(sOT, pO))
                        p.vector(lambda e: e.reciprocal(sInv, sOT[64:65, :]))
                        p.tensor(lambda e: e.matmul(pB, sOnes64, sInv, start=True, stop=True))
                        p.vector(lambda e: e.tensor_tensor(sOT[0:64, :], sOT[0:64, :], pB, ALU.mult))
                        ov = ot_sh.rearrange("(h r) t -> h r t", h=2)
                        def owr(par, u=u, tsl=tsl, ov=ov):
                            return lambda e: e.dma_start(
                                out=ov[par, u * 64:(u + 1) * 64, tsl], in_=sOT[0:64, :])
                        p.par_dma(owr)

        emit_l1flash()
        barrier()

        def emit_l1out():
            with contextlib.ExitStack() as s2:
                sb2 = lambda name, shape, dt=F32: s2.enter_context(nc.sbuf_tensor(name, shape, dt))[:]
                sX = sb2("u_sX", [128, D])
                QT4 = min(ntok, 1024)
                sOTb = sb2("u_sOTb", [128, 8, QT4], BF16)
                sWuT = sb2("u_sWuT", [128, 8, D], BF16)
                sBu = sb2("u_sBu", [1, D])
                sBub = sb2("u_sBub", [1, D], BF16)
                sOnes = sb2("u_sOnes", [1, 128], BF16)
                sBiasTile = sb2("u_sBiasTile", [128, D])
                pP = s2.enter_context(nc.psum_tensor("u_pP", [128, 1024], F32))[:]

                otv = ot_sh.rearrange("r (h t) -> r h t", h=2)

                def load_quarter(qt):
                    for kc in range(8):
                        def otld(par, kc=kc, qt=qt):
                            return lambda e: e.dma_start(
                                out=sOTb[:, kc, :],
                                in_=otv[kc * 128:(kc + 1) * 128, par,
                                        qt * QT4:(qt + 1) * QT4])
                        p.par_dma(otld)
                w_r = wauT_in.rearrange("(kc q) o -> kc q o", q=128)
                for kc in range(8):
                    p.gps_dma(lambda e, kc=kc: e.dma_start(out=sWuT[:, kc, :], in_=w_r[kc]))
                p.sync(lambda e: e.dma_start(out=sBu, in_=bau_in))
                p.vector(lambda e: e.tensor_copy(sBub, sBu))
                p.vector(lambda e: e.memset(sOnes, 1.0))
                for half in range(2):
                    osl = slice(half * 512, (half + 1) * 512)
                    p.tensor(lambda e, osl=osl: e.matmul(
                        pP[:, osl], sOnes, sBub[:, osl], start=True, stop=True))
                p.vector(lambda e: e.tensor_copy(sBiasTile, pP))

                x_r = xres.rearrange("(c q) f -> c q f", q=128)
                for c in range(nch128):
                    if c % (QT4 // 128) == 0:
                        load_quarter(c // (QT4 // 128))
                    cq = c % (QT4 // 128)
                    p.sync(lambda e, c=c: e.dma_start(out=sX, in_=x_r[c]))
                    p.vector(lambda e: e.tensor_add(sX, sX, sBiasTile))
                    for half in range(2):
                        osl = slice(half * 512, (half + 1) * 512)
                        for kc in range(8):
                            p.tensor(lambda e, kc=kc, osl=osl, cq=cq: e.matmul(
                                pP[:, osl], sOTb[:, kc, cq * 128:(cq + 1) * 128], sWuT[:, kc, osl],
                                start=(kc == 0), stop=(kc == 7)))
                    p.vector(lambda e: e.tensor_tensor(sX, pP, sX, ALU.add))
                    p.sync(lambda e, c=c: e.dma_start(out=x_r[c], in_=sX))

        emit_l1out()
        emit_ffn1(xres, "D")
        with contextlib.ExitStack() as s2:
            sFin = s2.enter_context(nc.sbuf_tensor("sFin", [128, D], F32))[:]
            xi_r = xres.rearrange("(c q) f -> c q f", q=128)
            xo_r = x_out.rearrange("(c q) f -> c q f", q=128)
            for c in range(nch128):
                p.sync(lambda e, c=c: e.dma_start(out=sFin, in_=xi_r[c]))
                p.sync(lambda e, c=c: e.dma_start(out=xo_r[c], in_=sFin))

        p.emit(block, sem)
    return nc


_fused_cache = {}


def run_fused(x0, inp, ncore):
    B, L, Dm = x0.shape
    ntok = B * L // ncore
    if ntok not in _fused_cache:
        _fused_cache[ntok] = build_fused(ntok)
    nc = _fused_cache[ntok]
    tt = make_toeplitz(inp["L0_attn_spectral_basis"])
    rb = lambda b: np.ascontiguousarray(b.reshape(8, 128).T)
    tr = lambda w: np.ascontiguousarray(w.T)
    nonce = np.full((1, 128), np.random.randint(1, 1 << 28), np.int32)
    common = {
        "nonce_in": nonce,
        "w0_in": inp["L0_ffn_weight"], "b0_in": inp["L0_ffn_bias"],
        "tt_in": tt, "mphiT_in": tr(inp["L0_attn_M_phi_w"]),
        "mb_in": rb(inp["L0_attn_M_phi_b"]),
        "wqT_in": tr(inp["L0_attn_wq_w"]), "qb_in": rb(inp["L0_attn_wq_b"]),
        "wkT_in": tr(inp["L0_attn_wk_w"]), "kb_in": rb(inp["L0_attn_wk_b"]),
        "wvT_in": tr(inp["L0_attn_wv_w"]), "vb_in": rb(inp["L0_attn_wv_b"]),
        "wgT_in": np.ascontiguousarray(inp["L0_attn_wg_z_w"].reshape(64, 64).T),
        "kvs_in": np.ascontiguousarray(inp["L0_attn_kv_norm_scale"].reshape(1, 16)),
        "wgb_in": np.full((1, 16), float(np.asarray(inp["L0_attn_wg_z_b"]).reshape(-1)[0]), np.float32),
        "ident_in": np.eye(128, dtype=np.float32),
        "triu_in": np.triu(np.ones((128, 128), np.float32)),
        "woT_in": tr(inp["L0_attn_wo_w"]), "wob_in": inp["L0_attn_wo_b"].reshape(1, D),
        "wg1_in": inp["L1_ffn_w_gate"], "wu1_in": inp["L1_ffn_w_up"],
        "wd1_in": inp["L1_ffn_w_down"],
        "waqT_in": tr(inp["L1_attn_Wq"]), "wakT_in": tr(inp["L1_attn_Wk"]),
        "wavT_in": tr(inp["L1_attn_Wv"]), "wauT_in": tr(inp["L1_attn_Wu"]),
        "bau_in": inp["L1_attn_bu"].reshape(1, D),
    }
    xs = x0.reshape(ncore, ntok, Dm)
    in_maps = [{"x_in": np.ascontiguousarray(xs[c]), **common} for c in range(ncore)]
    res = run_bass_kernel_spmd(nc, in_maps, core_ids=list(range(ncore)))
    return np.stack([r["x_out"] for r in res.results]).reshape(B, L, Dm)


# version 12 (final)
# speedup vs baseline: 1.1027x; 1.0665x over previous
import contextlib
import numpy as np
import ml_dtypes
import concourse.bass as bass
import concourse.mybir as mybir
from concourse.bass_utils import run_bass_kernel_spmd

F32 = mybir.dt.float32
BF16 = mybir.dt.bfloat16
I32 = mybir.dt.int32
AF = mybir.ActivationFunctionType
ALU = mybir.AluOpType
AX = mybir.AxisListType

NCORE = 8
D = 1024
FUSED = True
RNE_C = 12582912.0


class Prog:

    def __init__(self):
        self.steps = []
        self.pre = {}
        self.ctx = {}

    def sync(self, fn, fuse=False, wait=None):
        self.steps.append(("sync", fn, 16, fuse, wait))

    def gps_dma(self, fn, fuse=False, wait=None):
        self.steps.append(("gpsimd", fn, 16, fuse, wait))

    def tensor(self, fn, fuse=False, wait=None):
        self.steps.append(("tensor", fn, 1, fuse, wait))

    def vector(self, fn, fuse=False, wait=None):
        self.steps.append(("vector", fn, 1, fuse, wait))

    def scalar(self, fn, fuse=False, wait=None):
        self.steps.append(("scalar", fn, 1, fuse, wait))

    def mark(self):
        return ("mark", len(self.steps))

    def raw(self, engine, fn, inc):
        self.steps.append((engine, ("raw", fn), inc, False, None))

    def par_dma(self, mk):
        def fn(e, t0, sem):
            if t0 > 0:
                e.wait_ge(sem, t0)
            with e.If_eq(self.ctx["b_par"], 0):
                mk(0)(e).then_inc(sem, 16)
            with e.Else():
                mk(1)(e).then_inc(sem, 16)
        self.raw("gpsimd", fn, 16)

    def resolve(self, tok):
        return self._cum[tok[1]]

    def emit(self, block, sem):
        cum = [0] * (len(self.steps) + 1)
        for i, st in enumerate(self.steps):
            cum[i + 1] = cum[i] + st[2]
        self._cum = cum
        slots = []
        prev_eng = None
        for i, (eng, fn, inc, fuse, wait) in enumerate(self.steps):
            t0 = cum[i]
            if wait is not None:
                wv = cum[wait[1]]
            else:
                skip_wait = (prev_eng == eng) and (fuse or eng == "tensor")
                wv = 0 if skip_wait else t0
            slots.append((eng, fn, inc, t0, wv))
            prev_eng = eng
        total = cum[-1]

        def make_body(engname):
            def body(eng):
                if engname in self.pre:
                    self.pre[engname](eng)
                for name, fn, inc, t0, wv in slots:
                    if name != engname:
                        continue
                    if isinstance(fn, tuple) and fn[0] == "raw":
                        fn[1](eng, t0, sem)
                        continue
                    if wv > 0:
                        eng.wait_ge(sem, wv)
                    fn(eng).then_inc(sem, inc)
                if engname == "sync":
                    eng.wait_ge(sem, total)
            return body

        for name in ["sync", "gpsimd", "tensor", "vector", "scalar"]:
            getattr(block, name)(make_body(name))



def emit_ln(p, x3, out, stat_a, stat_b, big_a, nt, feat=D, eps=1e-5):
    big3 = big_a.rearrange("q (n f) -> q n f", f=feat)
    p.vector(lambda e: e.reduce_sum(stat_a, x3, axis=AX.X))
    p.vector(lambda e: e.tensor_scalar(stat_a, stat_a, 1.0 / feat, None, ALU.mult))
    p.vector(lambda e: e.tensor_tensor(
        out.rearrange("q n f -> q n f") if False else out, x3,
        stat_a.unsqueeze(2).broadcast_to(x3.shape), ALU.subtract))
    p.scalar(lambda e: e.activation(big3, out, AF.Square))
    p.vector(lambda e: e.reduce_sum(stat_b, big3, axis=AX.X))
    p.vector(lambda e: e.tensor_scalar(stat_b, stat_b, 1.0 / feat, eps, ALU.mult, ALU.add))
    p.scalar(lambda e: e.activation(stat_b, stat_b, AF.Sqrt))
    p.vector(lambda e: e.reciprocal(stat_b, stat_b))
    p.vector(lambda e: e.tensor_tensor(
        out, out, stat_b.unsqueeze(2).broadcast_to(x3.shape), ALU.mult))


def emit_quant_fp6(p, u, out_bf, bA, bB, bC, bD, dscr):
    F = u.shape[1]
    nb = F // 32
    u3 = u.rearrange("q (n k) -> q n k", k=32)
    p.vector(lambda e: e.reduce_max(dscr, u3, axis=AX.X, apply_absolute_value=True))
    p.vector(lambda e: e.tensor_scalar(dscr, dscr, 1.0 / 60.0, 1.25e-31, ALU.mult, ALU.max))
    ivs = bB[:, 0:nb]
    p.vector(lambda e: e.reciprocal(ivs, dscr))
    bA3 = bA.rearrange("q (n k) -> q n k", k=32)
    p.vector(lambda e: e.tensor_tensor(
        bA3, u3, ivs.unsqueeze(2).broadcast_to(u3.shape), ALU.mult))
    p.scalar(lambda e: e.activation(bC, bA, AF.Abs))
    p.vector(lambda e: e.tensor_scalar(bD, bC, 16.0, 1.0, ALU.is_ge, ALU.add))
    p.vector(lambda e: e.tensor_scalar(bB, bC, 32.0, 1.0, ALU.is_ge, ALU.add))
    p.vector(lambda e: e.tensor_tensor(bC, bD, bB, ALU.mult))
    p.vector(lambda e: e.tensor_scalar(bD, bD, -0.5, 1.5, ALU.mult, ALU.add))
    p.vector(lambda e: e.tensor_scalar(bB, bB, -0.5, 1.5, ALU.mult, ALU.add))
    p.vector(lambda e: e.tensor_tensor(bD, bD, bB, ALU.mult))
    p.vector(lambda e: e.tensor_tensor(bD, bA, bD, ALU.mult))
    p.vector(lambda e: e.tensor_scalar(bD, bD, RNE_C, RNE_C, ALU.add, ALU.subtract))
    p.vector(lambda e: e.tensor_tensor(bD, bD, bC, ALU.mult))
    bD3 = bD.rearrange("q (n k) -> q n k", k=32)
    out3 = out_bf.rearrange("q (n k) -> q n k", k=32)
    p.vector(lambda e: e.tensor_tensor(
        out3, bD3, dscr.unsqueeze(2).broadcast_to(bD3.shape), ALU.mult))


def emit_quant_fp4(p, u, out_bf, bA, bB, bC, bD, dscr):
    F = u.shape[1]
    nb = F // 16
    u3 = u.rearrange("q (n k) -> q n k", k=16)
    p.vector(lambda e: e.reduce_max(dscr, u3, axis=AX.X, apply_absolute_value=True))
    p.vector(lambda e: e.tensor_scalar(dscr, dscr, 1.0 / 12.0, 5e-13, ALU.mult, ALU.max))
    ivs = bB[:, 0:nb]
    p.vector(lambda e: e.reciprocal(ivs, dscr))
    bA3 = bA.rearrange("q (n k) -> q n k", k=16)
    p.vector(lambda e: e.tensor_tensor(
        bA3, u3, ivs.unsqueeze(2).broadcast_to(u3.shape), ALU.mult))
    p.scalar(lambda e: e.activation(bC, bA, AF.Abs))
    p.vector(lambda e: e.tensor_scalar(bD, bC, 4.0, 1.0, ALU.is_ge, ALU.add))
    p.vector(lambda e: e.tensor_scalar(bB, bC, 8.0, 1.0, ALU.is_ge, ALU.add))
    p.vector(lambda e: e.tensor_tensor(bA, bD, bB, ALU.mult))
    p.vector(lambda e: e.tensor_scalar(bD, bD, -0.5, 1.5, ALU.mult, ALU.add))
    p.vector(lambda e: e.tensor_scalar(bB, bB, -0.5, 1.5, ALU.mult, ALU.add))
    p.vector(lambda e: e.tensor_tensor(bD, bD, bB, ALU.mult))
    p.vector(lambda e: e.tensor_tensor(bD, bC, bD, ALU.mult))
    p.vector(lambda e: e.tensor_scalar(bC, bD, RNE_C, RNE_C, ALU.add, ALU.subtract))
    p.vector(lambda e: e.tensor_tensor(bB, bC, bD, ALU.subtract))
    p.vector(lambda e: e.tensor_scalar(bB, bB, 0.5, None, ALU.is_equal))
    p.vector(lambda e: e.tensor_tensor(bC, bC, bB, ALU.subtract))
    p.vector(lambda e: e.tensor_tensor(bC, bC, bA, ALU.mult))
    p.scalar(lambda e: e.activation(bD, u, AF.Sign))
    p.vector(lambda e: e.tensor_tensor(bC, bC, bD, ALU.mult))
    bC3 = bC.rearrange("q (n k) -> q n k", k=16)
    out3 = out_bf.rearrange("q (n k) -> q n k", k=16)
    p.vector(lambda e: e.tensor_tensor(
        out3, bC3, dscr.unsqueeze(2).broadcast_to(bC3.shape), ALU.mult))


def emit_transpose_blocks(p, src_bf, dst_bf, nblk):
    for j in range(nblk):
        sl = slice(j * 128, (j + 1) * 128)
        p.sync(lambda e, sl=sl: e.dma_start_transpose(dst_bf[:, sl], src_bf[:, sl]),
               fuse=(j > 0))



def build_ffn0(ntok):
    nc = bass.Bass("TRN2", target_bir_lowering=False, debug=False)
    x_in = nc.dram_tensor("x_in", [ntok, D], F32, kind="ExternalInput").ap()
    w_in = nc.dram_tensor("w_in", [D, D], F32, kind="ExternalInput").ap()
    b_in = nc.dram_tensor("b_in", [D], F32, kind="ExternalInput").ap()
    x_out = nc.dram_tensor("x_out", [ntok, D], F32, kind="ExternalOutput").ap()

    CH = 512
    assert ntok % CH == 0
    nch = ntok // CH
    TB = CH // 128

    p = Prog()
    es = contextlib.ExitStack()
    with es:
        sb = lambda name, shape, dt=F32: es.enter_context(nc.sbuf_tensor(name, shape, dt))[:]
        sX = sb("sX", [128, TB, D])
        sU = sb("sU", [128, TB * D])
        sUd = sb("sUd", [128, TB * D], BF16)
        sUdT = sb("sUdT", [128, TB * 8, 128], BF16)
        bA = sb("bA", [128, TB * D])
        bB = sb("bB", [128, TB * D])
        bC = sb("bC", [128, TB * D])
        bD = sb("bD", [128, TB * D])
        dscr = sb("dscr", [128, TB * D // 32])
        stat_a = sb("stat_a", [128, TB])
        stat_b = sb("stat_b", [128, TB])
        sWrow = sb("sWrow", [128, D])
        sWd = sb("sWd", [128, D], BF16)
        sWT = sb("sWT", [128, 8, D], BF16)
        sBias = sb("sBias", [1, D])
        sBiasb = sb("sBiasb", [1, D], BF16)
        sOnes = sb("sOnes", [1, 128], BF16)
        wq_dscr = sb("wq_dscr", [128, D // 32])
        pT = es.enter_context(nc.psum_tensor("pT", [128, D], F32))[:]
        sem = es.enter_context(nc.semaphore("sem"))
        block = es.enter_context(nc.Block())

        w_r = w_in.rearrange("(oc q) k -> oc q k", q=128)
        for oc in range(8):
            p.sync(lambda e, oc=oc: e.dma_start(out=sWrow, in_=w_r[oc]))
            emit_quant_fp6(p, sWrow, sWd, bA[:, 0:D], bB[:, 0:D], bC[:, 0:D],
                           bD[:, 0:D], wq_dscr)
            for kc in range(8):
                p.sync(lambda e, oc=oc, kc=kc: e.dma_start_transpose(
                    sWT[:, kc, oc * 128:(oc + 1) * 128],
                    sWd[:, kc * 128:(kc + 1) * 128]))
        p.sync(lambda e: e.dma_start(out=sBias, in_=b_in.rearrange("(o f) -> o f", o=1)))
        p.vector(lambda e: e.tensor_copy(sBiasb, sBias))
        p.vector(lambda e: e.memset(sOnes, 1.0))

        x_r = x_in.rearrange("(c tb q) f -> c q tb f", q=128, tb=TB)
        xo_r = x_out.rearrange("(c tb q) f -> c q tb f", q=128, tb=TB)
        for c in range(nch):
            p.sync(lambda e, c=c: e.dma_start(out=sX, in_=x_r[c]))
            emit_ln(p, sX, sU.rearrange("q (n f) -> q n f", f=D), stat_a, stat_b, bA, TB)
            emit_quant_fp6(p, sU, sUd, bA, bB, bC, bD, dscr)
            emit_transpose_blocks(p, sUd, sUdT.rearrange("q a b -> q (a b)"), TB * 8)
            for tb in range(TB):
                for half in range(2):
                    osl = slice(half * 512, (half + 1) * 512)
                    for kc in range(8):
                        p.tensor(lambda e, tb=tb, kc=kc, osl=osl: e.matmul(
                            pT[:, osl], sUdT[:, tb * 8 + kc, :], sWT[:, kc, osl],
                            start=(kc == 0), stop=False))
                    p.tensor(lambda e, osl=osl: e.matmul(
                        pT[:, osl], sOnes, sBiasb[:, osl], start=False, stop=True))
                p.vector(lambda e, tb=tb: e.scalar_tensor_tensor(
                    sX[:, tb, :], pT, 0.5, sX[:, tb, :], ALU.mult, ALU.add))
            p.sync(lambda e, c=c: e.dma_start(out=xo_r[c], in_=sX))

        p.emit(block, sem)
    return nc


def run_ffn0(x, w, b):
    ntok = x.shape[1]
    nc = build_ffn0(ntok)
    in_maps = [{"x_in": np.ascontiguousarray(x[i]), "w_in": w, "b_in": b}
               for i in range(x.shape[0])]
    res = run_bass_kernel_spmd(nc, in_maps, core_ids=list(range(x.shape[0])))
    return np.stack([r["x_out"] for r in res.results])


IFF = 2816


def build_ffn1(ntok):
    nc = bass.Bass("TRN2", target_bir_lowering=False, debug=False)
    x_in = nc.dram_tensor("x_in", [ntok, D], F32, kind="ExternalInput").ap()
    wg_in = nc.dram_tensor("wg_in", [IFF, D], F32, kind="ExternalInput").ap()
    wu_in = nc.dram_tensor("wu_in", [IFF, D], F32, kind="ExternalInput").ap()
    wd_in = nc.dram_tensor("wd_in", [D, IFF], F32, kind="ExternalInput").ap()
    x_out = nc.dram_tensor("x_out", [ntok, D], F32, kind="ExternalOutput").ap()
    hscr = nc.dram_tensor("hscr", [ntok // 128, 128, IFF], BF16).ap()

    nch = ntok // 128
    OSEG = [(s, min(s + 512, IFF)) for s in range(0, IFF, 512)]

    p = Prog()
    with contextlib.ExitStack() as es:
        sb = lambda name, shape, dt=F32: es.enter_context(nc.sbuf_tensor(name, shape, dt))[:]
        sX = sb("sX", [128, D])
        sU = sb("sU", [128, D])
        sUd = sb("sUd", [128, D], BF16)
        sUdT = sb("sUdT", [128, 8, 128], BF16)
        bA = sb("bA", [128, IFF])
        bB = sb("bB", [128, IFF])
        bC = sb("bC", [128, IFF])
        bD = sb("bD", [128, IFF])
        dscr6 = sb("dscr6", [128, D // 16])
        dscrH = sb("dscrH", [128, IFF // 16])
        stat_a = sb("stat_a", [128, 1])
        stat_b = sb("stat_b", [128, 1])
        sWrow = sb("sWrow", [128, IFF])
        sWq = sb("sWq", [128, IFF], BF16)
        sGate = sb("sGate", [128, IFF])
        sY = sb("sY", [128, IFF])
        sHq = sb("sHq", [128, IFF], BF16)
        sHqT = sb("sHqT", [128, 22, 128], BF16)
        pP = es.enter_context(nc.psum_tensor("pP", [128, 1024], F32))[:]
        sem = es.enter_context(nc.semaphore("sem"))
        block = es.enter_context(nc.Block())

        x_r = x_in.rearrange("(c q) f -> c q f", q=128)
        xo_r = x_out.rearrange("(c q) f -> c q f", q=128)

        def quant_weight_and_transpose(w_ap, n_oc, kdim, wT):
            w_r = w_ap.rearrange("(oc q) k -> oc q k", q=128)
            for oc in range(n_oc):
                p.sync(lambda e, oc=oc: e.dma_start(out=sWrow[:, 0:kdim], in_=w_r[oc]))
                emit_quant_fp4(p, sWrow[:, 0:kdim], sWq[:, 0:kdim],
                               bA[:, 0:kdim], bB[:, 0:kdim], bC[:, 0:kdim],
                               bD[:, 0:kdim], dscrH[:, 0:kdim // 16])
                for kc in range(kdim // 128):
                    p.sync(lambda e, oc=oc, kc=kc: e.dma_start_transpose(
                        wT[:, kc, oc * 128:(oc + 1) * 128],
                        sWq[:, kc * 128:(kc + 1) * 128]))

        with contextlib.ExitStack() as es2:
            wgT = es2.enter_context(nc.sbuf_tensor("wgT", [128, 8, IFF], BF16))[:]
            wuT = es2.enter_context(nc.sbuf_tensor("wuT", [128, 8, IFF], BF16))[:]
            quant_weight_and_transpose(wg_in, 22, D, wgT)
            quant_weight_and_transpose(wu_in, 22, D, wuT)

            for c in range(nch):
                p.sync(lambda e, c=c: e.dma_start(out=sX, in_=x_r[c]))
                emit_ln(p, sX.rearrange("q (n f) -> q n f", n=1), sU.rearrange("q (n f) -> q n f", n=1),
                        stat_a, stat_b, bA[:, 0:D], 1)
                emit_quant_fp4(p, sU, sUd, bA[:, 0:D], bB[:, 0:D], bC[:, 0:D],
                               bD[:, 0:D], dscr6)
                emit_transpose_blocks(p, sUd, sUdT.rearrange("q a b -> q (a b)"), 8)
                for dst, wT in ((sGate, wgT), (sY, wuT)):
                    for (o0, o1) in OSEG:
                        for kc in range(8):
                            p.tensor(lambda e, kc=kc, o0=o0, o1=o1, wT=wT: e.matmul(
                                pP[:, 0:o1 - o0], sUdT[:, kc, :], wT[:, kc, o0:o1],
                                start=(kc == 0), stop=(kc == 7)))
                        p.vector(lambda e, dst=dst, o0=o0, o1=o1: e.tensor_copy(
                            dst[:, o0:o1], pP[:, 0:o1 - o0]))
                p.scalar(lambda e: e.activation(sGate, sGate, AF.Silu))
                p.vector(lambda e: e.tensor_mul(sY, sY, sGate))
                emit_quant_fp4(p, sY, sHq, bA, bB, bC, bD, dscrH)
                emit_transpose_blocks(p, sHq, sHqT.rearrange("q a b -> q (a b)"), 22)
                p.sync(lambda e, c=c: e.dma_start(
                    out=hscr.rearrange("c q f -> c q f")[c],
                    in_=sHqT.rearrange("q a b -> q (a b)")))

        wdT = es.enter_context(nc.sbuf_tensor("wdT", [128, 22, D], BF16))[:]
        quant_weight_and_transpose(wd_in, 8, IFF, wdT)
        for c in range(nch):
            p.sync(lambda e, c=c: e.dma_start(
                out=sHqT.rearrange("q a b -> q (a b)"), in_=hscr[c]))
            p.sync(lambda e, c=c: e.dma_start(out=sX, in_=x_r[c]))
            for half in range(2):
                osl = slice(half * 512, (half + 1) * 512)
                for kc in range(22):
                    p.tensor(lambda e, kc=kc, osl=osl: e.matmul(
                        pP[:, osl], sHqT[:, kc, :], wdT[:, kc, osl],
                        start=(kc == 0), stop=(kc == 21)))
            p.vector(lambda e: e.scalar_tensor_tensor(
                sX, pP, 0.5, sX, ALU.mult, ALU.add))
            p.sync(lambda e, c=c: e.dma_start(out=xo_r[c], in_=sX))

        p.emit(block, sem)
    return nc


def run_ffn1(x, wg, wu, wd):
    ntok = x.shape[1]
    nc = build_ffn1(ntok)
    in_maps = [{"x_in": np.ascontiguousarray(x[i]), "wg_in": wg, "wu_in": wu,
                "wd_in": wd} for i in range(x.shape[0])]
    res = run_bass_kernel_spmd(nc, in_maps, core_ids=list(range(x.shape[0])))
    return np.stack([r["x_out"] for r in res.results])


def build_l1qkv(ntok):
    nc = bass.Bass("TRN2", target_bir_lowering=False, debug=False)
    x_in = nc.dram_tensor("x_in", [ntok, D], F32, kind="ExternalInput").ap()
    wqT_in = nc.dram_tensor("wqT_in", [D, D], F32, kind="ExternalInput").ap()
    wkT_in = nc.dram_tensor("wkT_in", [D, D], F32, kind="ExternalInput").ap()
    wvT_in = nc.dram_tensor("wvT_in", [D, D], F32, kind="ExternalInput").ap()
    qT_out = nc.dram_tensor("qT_out", [D, ntok], BF16, kind="ExternalOutput").ap()
    kT_out = nc.dram_tensor("kT_out", [D, ntok], BF16, kind="ExternalOutput").ap()
    v_out = nc.dram_tensor("v_out", [ntok, D], BF16, kind="ExternalOutput").ap()

    ntile = ntok // 128
    p = Prog()
    with contextlib.ExitStack() as es:
        sb = lambda name, shape, dt=F32: es.enter_context(nc.sbuf_tensor(name, shape, dt))[:]
        sX = sb("sX", [128, D])
        sU = sb("sU", [128, D])
        sUb = sb("sUb", [128, D], BF16)
        sUbT = sb("sUbT", [128, 8, 128], BF16)
        bA = sb("bA", [128, D])
        stat_a = sb("stat_a", [128, 1])
        stat_b = sb("stat_b", [128, 1])
        sWqT = sb("sWqT", [128, 8, D], BF16)
        sWkT = sb("sWkT", [128, 8, D], BF16)
        sWvT = sb("sWvT", [128, 8, D], BF16)
        sQTt = sb("sQTt", [128, 8, 128], BF16)
        sKTt = sb("sKTt", [128, 8, 128], BF16)
        sV = sb("sV", [128, D], BF16)
        pF = es.enter_context(nc.psum_tensor("pF", [128, 128], F32))[:]
        pV = es.enter_context(nc.psum_tensor("pV", [128, 1024], F32))[:]
        sem = es.enter_context(nc.semaphore("sem"))
        block = es.enter_context(nc.Block())

        for w_in, sW in ((wqT_in, sWqT), (wkT_in, sWkT), (wvT_in, sWvT)):
            w_r = w_in.rearrange("(kc q) o -> kc q o", q=128)
            for kc in range(8):
                p.gps_dma(lambda e, kc=kc, w_r=w_r, sW=sW: e.dma_start(
                    out=sW[:, kc, :], in_=w_r[kc]))

        x_r = x_in.rearrange("(c q) f -> c q f", q=128)
        qT_r = qT_out.rearrange("(oc p) t -> p oc t", p=128)
        kT_r = kT_out.rearrange("(oc p) t -> p oc t", p=128)
        v_r = v_out.rearrange("(c q) f -> c q f", q=128)
        for c in range(ntile):
            tsl = slice(c * 128, (c + 1) * 128)
            p.sync(lambda e, c=c: e.dma_start(out=sX, in_=x_r[c]))
            emit_ln(p, sX.rearrange("q (n f) -> q n f", n=1),
                    sU.rearrange("q (n f) -> q n f", n=1), stat_a, stat_b, bA, 1)
            p.vector(lambda e: e.tensor_copy(sUb, sU))
            emit_transpose_blocks(p, sUb, sUbT.rearrange("q a b -> q (a b)"), 8)
            for sW, sOut in ((sWqT, sQTt), (sWkT, sKTt)):
                for oc in range(8):
                    for kc in range(8):
                        p.tensor(lambda e, oc=oc, kc=kc, sW=sW: e.matmul(
                            pF, sW[:, kc, oc * 128:(oc + 1) * 128], sUbT[:, kc, :],
                            start=(kc == 0), stop=(kc == 7)))
                    p.vector(lambda e, oc=oc, sOut=sOut: e.tensor_copy(sOut[:, oc, :], pF))
            for half in range(2):
                osl = slice(half * 512, (half + 1) * 512)
                for kc in range(8):
                    p.tensor(lambda e, kc=kc, osl=osl: e.matmul(
                        pV[:, osl], sUbT[:, kc, :], sWvT[:, kc, osl],
                        start=(kc == 0), stop=(kc == 7)))
            p.vector(lambda e: e.tensor_copy(sV, pV))
            p.sync(lambda e, tsl=tsl: e.dma_start(out=qT_r[:, :, tsl], in_=sQTt))
            p.sync(lambda e, tsl=tsl: e.dma_start(out=kT_r[:, :, tsl], in_=sKTt))
            p.sync(lambda e, c=c: e.dma_start(out=v_r[c], in_=sV))
        p.emit(block, sem)
    return nc


def run_l1qkv(x, wq, wk, wv):
    ntok = x.shape[1]
    nc = build_l1qkv(ntok)
    wqT = np.ascontiguousarray(wq.T)
    wkT = np.ascontiguousarray(wk.T)
    wvT = np.ascontiguousarray(wv.T)
    in_maps = [{"x_in": np.ascontiguousarray(x[i]), "wqT_in": wqT,
                "wkT_in": wkT, "wvT_in": wvT} for i in range(x.shape[0])]
    res = run_bass_kernel_spmd(nc, in_maps, core_ids=list(range(x.shape[0])))
    return (np.stack([r["qT_out"] for r in res.results]),
            np.stack([r["kT_out"] for r in res.results]),
            np.stack([r["v_out"] for r in res.results]))



def build_l1flash(nbh, seq):
    nc = bass.Bass("TRN2", target_bir_lowering=False, debug=False)
    qT_in = nc.dram_tensor("qT_in", [nbh, 64, seq], BF16, kind="ExternalInput").ap()
    kT_in = nc.dram_tensor("kT_in", [nbh, 64, seq], BF16, kind="ExternalInput").ap()
    v_in = nc.dram_tensor("v_in", [nbh, seq, 64], BF16, kind="ExternalInput").ap()
    ot_out = nc.dram_tensor("ot_out", [nbh, 65, seq], F32, kind="ExternalOutput").ap()

    nsc = seq // 128
    nts = seq // 512
    GRP = 6
    p = Prog()
    with contextlib.ExitStack() as es:
        sb = lambda name, shape, dt=F32: es.enter_context(nc.sbuf_tensor(name, shape, dt))[:]
        sKT = sb("sKT", [64, seq], BF16)
        sQT = sb("sQT", [64, seq], BF16)
        sVext = sb("sVext", [128, nsc, 65], BF16)
        sP = sb("sP", [128, GRP * 512], BF16)
        sOT = sb("sOT", [65, 512])
        sInv = sb("sInv", [1, 512])
        sOnes64 = sb("sOnes64", [1, 64])
        pS = es.enter_context(nc.psum_tensor("pS", [128, GRP * 512], F32))[:]
        pO = es.enter_context(nc.psum_tensor("pO", [65, 512], F32))[:]
        pB = es.enter_context(nc.psum_tensor("pB", [64, 512], F32))[:]
        sem = es.enter_context(nc.semaphore("sem"))
        block = es.enter_context(nc.Block())

        p.vector(lambda e: e.memset(sOnes64, 1.0))

        p.vector(lambda e: e.memset(sVext[:, :, 64:65], 1.0))
        v_r = v_in.rearrange("u (c q) d -> u q c d", q=128)
        for u in range(nbh):
            p.sync(lambda e, u=u: e.dma_start(out=sKT, in_=kT_in[u]))
            p.sync(lambda e, u=u: e.dma_start(out=sQT, in_=qT_in[u]))
            p.sync(lambda e, u=u: e.dma_start(out=sVext[:, :, 0:64], in_=v_r[u]))
            for ts in range(nts):
                tsl = slice(ts * 512, (ts + 1) * 512)
                sc0 = 0
                while sc0 < nsc:
                    n = min(GRP, nsc - sc0)
                    for i in range(n):
                        sc = sc0 + i
                        p.tensor(lambda e, sc=sc, i=i, tsl=tsl: e.matmul(
                            pS[:, i * 512:(i + 1) * 512],
                            sKT[:, sc * 128:(sc + 1) * 128], sQT[:, tsl],
                            start=True, stop=True))
                    p.scalar(lambda e, n=n: e.activation(
                        sP[:, 0:n * 512], pS[:, 0:n * 512], AF.Exp, scale=1.0 / 32.0))
                    for i in range(n):
                        sc = sc0 + i
                        p.tensor(lambda e, sc=sc, i=i: e.matmul(
                            pO, sVext[:, sc, :], sP[:, i * 512:(i + 1) * 512],
                            start=(sc == 0), stop=(sc == nsc - 1)))
                    sc0 += n
                p.vector(lambda e: e.tensor_copy(sOT, pO))
                p.vector(lambda e: e.reciprocal(sInv, sOT[64:65, :]))
                p.tensor(lambda e: e.matmul(pB, sOnes64, sInv, start=True, stop=True))
                p.vector(lambda e: e.tensor_tensor(sOT[0:64, :], sOT[0:64, :], pB, ALU.mult))
                p.sync(lambda e, u=u, tsl=tsl: e.dma_start(
                    out=ot_out[u][:, tsl], in_=sOT))
        p.emit(block, sem)
    return nc


def run_l1flash(qT, kT, v):
    ncore, nbh, _, seq = qT.shape
    nc = build_l1flash(nbh, seq)
    in_maps = [{"qT_in": np.ascontiguousarray(qT[i]),
                "kT_in": np.ascontiguousarray(kT[i]),
                "v_in": np.ascontiguousarray(v[i])} for i in range(ncore)]
    res = run_bass_kernel_spmd(nc, in_maps, core_ids=list(range(ncore)))
    return np.stack([r["ot_out"] for r in res.results])



def build_l1out(ntok):
    nc = bass.Bass("TRN2", target_bir_lowering=False, debug=False)
    x_in = nc.dram_tensor("x_in", [ntok, D], F32, kind="ExternalInput").ap()
    ot_in = nc.dram_tensor("ot_in", [D, ntok], F32, kind="ExternalInput").ap()
    wuT_in = nc.dram_tensor("wuT_in", [D, D], F32, kind="ExternalInput").ap()
    bu_in = nc.dram_tensor("bu_in", [D], F32, kind="ExternalInput").ap()
    x_out = nc.dram_tensor("x_out", [ntok, D], F32, kind="ExternalOutput").ap()

    ntile = ntok // 128
    p = Prog()
    with contextlib.ExitStack() as es:
        sb = lambda name, shape, dt=F32: es.enter_context(nc.sbuf_tensor(name, shape, dt))[:]
        sX = sb("sX", [128, D])
        sOTb = sb("sOTb", [128, 8, ntok], BF16)
        sWuT = sb("sWuT", [128, 8, D], BF16)
        sBu = sb("sBu", [1, D])
        sBub = sb("sBub", [1, D], BF16)
        sOnes = sb("sOnes", [1, 128], BF16)
        pP = es.enter_context(nc.psum_tensor("pP", [128, 1024], F32))[:]
        sem = es.enter_context(nc.semaphore("sem"))
        block = es.enter_context(nc.Block())

        ot_r = ot_in.rearrange("(kc q) t -> kc q t", q=128)
        for kc in range(8):
            p.gps_dma(lambda e, kc=kc: e.dma_start(out=sOTb[:, kc, :], in_=ot_r[kc]))
        w_r = wuT_in.rearrange("(kc q) o -> kc q o", q=128)
        for kc in range(8):
            p.gps_dma(lambda e, kc=kc: e.dma_start(out=sWuT[:, kc, :], in_=w_r[kc]))
        p.sync(lambda e: e.dma_start(out=sBu, in_=bu_in.rearrange("(o f) -> o f", o=1)))
        p.vector(lambda e: e.tensor_copy(sBub, sBu))
        p.vector(lambda e: e.memset(sOnes, 1.0))

        sBiasTile = sb("sBiasTile", [128, D])
        for half in range(2):
            osl = slice(half * 512, (half + 1) * 512)
            p.tensor(lambda e, osl=osl: e.matmul(
                pP[:, osl], sOnes, sBub[:, osl], start=True, stop=True))
        p.vector(lambda e: e.tensor_copy(sBiasTile, pP))

        x_r = x_in.rearrange("(c q) f -> c q f", q=128)
        xo_r = x_out.rearrange("(c q) f -> c q f", q=128)
        for c in range(ntile):
            p.sync(lambda e, c=c: e.dma_start(out=sX, in_=x_r[c]))
            p.vector(lambda e: e.tensor_add(sX, sX, sBiasTile))
            for half in range(2):
                osl = slice(half * 512, (half + 1) * 512)
                for kc in range(8):
                    p.tensor(lambda e, kc=kc, osl=osl, c=c: e.matmul(
                        pP[:, osl], sOTb[:, kc, c * 128:(c + 1) * 128], sWuT[:, kc, osl],
                        start=(kc == 0), stop=(kc == 7)))
            p.vector(lambda e: e.tensor_tensor(sX, pP, sX, ALU.add))
            p.sync(lambda e, c=c: e.dma_start(out=xo_r[c], in_=sX))
        p.emit(block, sem)
    return nc


def run_l1out(x, ot, wu, bu):
    ntok = x.shape[1]
    nc = build_l1out(ntok)
    wuT = np.ascontiguousarray(wu.T)
    in_maps = [{"x_in": np.ascontiguousarray(x[i]),
                "ot_in": np.ascontiguousarray(ot[i]),
                "wuT_in": wuT, "bu_in": bu} for i in range(x.shape[0])]
    res = run_bass_kernel_spmd(nc, in_maps, core_ids=list(range(x.shape[0])))
    return np.stack([r["x_out"] for r in res.results])


def build_l0mix(ntok):
    nc = bass.Bass("TRN2", target_bir_lowering=False, debug=False)
    xh_in = nc.dram_tensor("xh_in", [1024 + ntok, D], F32, kind="ExternalInput").ap()
    tt_in = nc.dram_tensor("tt_in", [128, 16 * 9 * 128], F32, kind="ExternalInput").ap()
    mphiT_in = nc.dram_tensor("mphiT_in", [16384, D], F32, kind="ExternalInput").ap()
    mb_in = nc.dram_tensor("mb_in", [128, 8], F32, kind="ExternalInput").ap()
    wqT_in = nc.dram_tensor("wqT_in", [D, D], F32, kind="ExternalInput").ap()
    wkT_in = nc.dram_tensor("wkT_in", [D, D], F32, kind="ExternalInput").ap()
    wvT_in = nc.dram_tensor("wvT_in", [D, D], F32, kind="ExternalInput").ap()
    qb_in = nc.dram_tensor("qb_in", [128, 8], F32, kind="ExternalInput").ap()
    kb_in = nc.dram_tensor("kb_in", [128, 8], F32, kind="ExternalInput").ap()
    vb_in = nc.dram_tensor("vb_in", [128, 8], F32, kind="ExternalInput").ap()
    qT_out = nc.dram_tensor("qT_out", [D, ntok], BF16, kind="ExternalOutput").ap()
    kT_out = nc.dram_tensor("kT_out", [D, ntok], BF16, kind="ExternalOutput").ap()
    vT_out = nc.dram_tensor("vT_out", [D, ntok], BF16, kind="ExternalOutput").ap()

    T = 256
    nch = ntok // T
    NRING = 10
    NSLAB = 4

    p = Prog()
    with contextlib.ExitStack() as es:
        sb = lambda name, shape, dt=F32: es.enter_context(nc.sbuf_tensor(name, shape, dt))[:]
        sXb = sb("sXb", [128, D])
        sU = sb("sU", [128, D])
        bA = sb("bA", [128, D])
        stat_a = sb("stat_a", [128, 1])
        stat_b = sb("stat_b", [128, 1])
        ring = sb("ring", [128, NRING, D], BF16)
        sTT = sb("sTT", [128, 144, 128], BF16)
        sConv = sb("sConv", [128, D], BF16)
        sCT = sb("sCT", [128, 128, T], BF16)
        sMph = sb("sMph", [128, NSLAB, D], BF16)
        sXtT = sb("sXtT", [128, 8, T], BF16)
        sMb = sb("sMb", [128, 8])
        sQb = sb("sQb", [128, 8])
        sKb = sb("sKb", [128, 8])
        sVb = sb("sVb", [128, 8])
        sWqT = sb("sWqT", [128, 8, D], BF16)
        sWkT = sb("sWkT", [128, 8, D], BF16)
        sWvT = sb("sWvT", [128, 8, D], BF16)
        sQo = sb("sQo", [128, 8, T], BF16)
        pC = es.enter_context(nc.psum_tensor("pC", [128, D], F32))[:]
        pXT = es.enter_context(nc.psum_tensor("pXT", [128, 8 * T], F32))[:]
        pQ = es.enter_context(nc.psum_tensor("pQ", [128, T], F32))[:]
        sem = es.enter_context(nc.semaphore("sem"))
        block = es.enter_context(nc.Block())

        p.gps_dma(lambda e: e.dma_start(
            out=sTT.rearrange("q a b -> q (a b)"), in_=tt_in))
        for w_in, sW in ((wqT_in, sWqT), (wkT_in, sWkT), (wvT_in, sWvT)):
            w_r = w_in.rearrange("(kc q) o -> kc q o", q=128)
            for kc in range(8):
                p.gps_dma(lambda e, kc=kc, w_r=w_r, sW=sW: e.dma_start(
                    out=sW[:, kc, :], in_=w_r[kc]))
        for b_in_, sB in ((mb_in, sMb), (qb_in, sQb), (kb_in, sKb), (vb_in, sVb)):
            p.sync(lambda e, b_in_=b_in_, sB=sB: e.dma_start(out=sB, in_=b_in_))

        xh_r = xh_in.rearrange("(ib q) f -> ib q f", q=128)

        def ln_to_ring(ib):
            p.sync(lambda e, ib=ib: e.dma_start(out=sXb, in_=xh_r[ib]))
            emit_ln(p, sXb.rearrange("q (n f) -> q n f", n=1),
                    sU.rearrange("q (n f) -> q n f", n=1), stat_a, stat_b, bA, 1)
            p.vector(lambda e, ib=ib: e.tensor_copy(ring[:, ib % NRING, :], sU))

        for ib in range(8):
            ln_to_ring(ib)

        qT_r = qT_out.rearrange("(oc p) t -> p oc t", p=128)
        kT_r = kT_out.rearrange("(oc p) t -> p oc t", p=128)
        vT_r = vT_out.rearrange("(oc p) t -> p oc t", p=128)
        mphi_r = mphiT_in.rearrange("(sl j q) o -> sl q j o", q=128, j=NSLAB)

        for c in range(nch):
            for bi in range(2):
                ln_to_ring(8 + 2 * c + bi)
            for bi in range(2):
                ob = 2 * c + bi
                for kf in range(16):
                    for d in range(9):
                        ib = ob + 8 - d
                        for half in range(2):
                            osl = slice(half * 512, (half + 1) * 512)
                            p.tensor(lambda e, kf=kf, d=d, ib=ib, osl=osl: e.matmul(
                                pC[:, osl], sTT[:, kf * 9 + d, :],
                                ring[:, ib % NRING, osl],
                                start=(d == 0), stop=(d == 8)))
                    p.vector(lambda e: e.tensor_copy(sConv, pC))
                    for cc in range(8):
                        p.sync(lambda e, kf=kf, cc=cc, bi=bi: e.dma_start_transpose(
                            sCT[:, kf * 8 + cc, bi * 128:(bi + 1) * 128],
                            sConv[:, cc * 128:(cc + 1) * 128]))
            for sl in range(128 // NSLAB):
                p.gps_dma(lambda e, sl=sl: e.dma_start(out=sMph, in_=mphi_r[sl]))
                for j in range(NSLAB):
                    kc = sl * NSLAB + j
                    slot = (kc // 8) * 8 + (kc % 8)
                    for oc in range(8):
                        p.tensor(lambda e, j=j, oc=oc, slot=slot, kc=kc: e.matmul(
                            pXT[:, oc * T:(oc + 1) * T],
                            sMph[:, j, oc * 128:(oc + 1) * 128],
                            sCT[:, slot, :],
                            start=(kc == 0 and oc % 2 == 0), stop=(kc == 127)))
            for oc in range(8):
                p.scalar(lambda e, oc=oc: e.activation(
                    sXtT[:, oc, :], pXT[:, oc * T:(oc + 1) * T], AF.Identity,
                    bias=sMb[:, oc:oc + 1]))
            for sW, sBv, out_r in ((sWqT, sQb, qT_r), (sWkT, sKb, kT_r),
                                   (sWvT, sVb, vT_r)):
                for oc in range(8):
                    for kc in range(8):
                        p.tensor(lambda e, oc=oc, kc=kc, sW=sW: e.matmul(
                            pQ, sW[:, kc, oc * 128:(oc + 1) * 128], sXtT[:, kc, :],
                            start=(kc == 0), stop=(kc == 7)))
                    p.scalar(lambda e, oc=oc, sBv=sBv: e.activation(
                        sQo[:, oc, :], pQ, AF.Identity, bias=sBv[:, oc:oc + 1]))
                p.sync(lambda e, c=c, out_r=out_r: e.dma_start(
                    out=out_r[:, :, c * T:(c + 1) * T], in_=sQo))
        p.emit(block, sem)
    return nc


def make_toeplitz(basis):
    TT = np.zeros((16, 9, 128, 128), dtype=np.float32)
    l_idx = np.arange(128)[None, :]
    m_idx = np.arange(128)[:, None]
    for kf in range(16):
        fpad = np.zeros(1280, np.float32)
        fpad[128:1152] = basis[:, kf]
        for d in range(9):
            TT[kf, d] = fpad[128 + 128 * d + l_idx - m_idx]
    return np.ascontiguousarray(TT.transpose(2, 0, 1, 3).reshape(128, 16 * 9 * 128))


def run_l0mix(xh, basis, mphi, mb, wq, qb, wk, kb, wv, vb):
    ntok = xh.shape[1] - 1024
    nc = build_l0mix(ntok)
    tt = make_toeplitz(basis)
    mphiT = np.ascontiguousarray(mphi.T)
    wqT = np.ascontiguousarray(wq.T)
    wkT = np.ascontiguousarray(wk.T)
    wvT = np.ascontiguousarray(wv.T)
    rb = lambda b: np.ascontiguousarray(b.reshape(8, 128).T)
    in_maps = [{"xh_in": np.ascontiguousarray(xh[i]), "tt_in": tt,
                "mphiT_in": mphiT, "mb_in": rb(mb), "wqT_in": wqT, "qb_in": rb(qb),
                "wkT_in": wkT, "kb_in": rb(kb), "wvT_in": wvT, "vb_in": rb(vb)}
               for i in range(xh.shape[0])]
    res = run_bass_kernel_spmd(nc, in_maps, core_ids=list(range(xh.shape[0])))
    return (np.stack([r["qT_out"] for r in res.results]),
            np.stack([r["kT_out"] for r in res.results]),
            np.stack([r["vT_out"] for r in res.results]))


def build_l0scan(nbh, seq):
    nc = bass.Bass("TRN2", target_bir_lowering=False, debug=False)
    qT_in = nc.dram_tensor("qT_in", [nbh, 64, seq], BF16, kind="ExternalInput").ap()
    kT_in = nc.dram_tensor("kT_in", [nbh, 64, seq], BF16, kind="ExternalInput").ap()
    vT_in = nc.dram_tensor("vT_in", [nbh, 64, seq], BF16, kind="ExternalInput").ap()
    wgT_in = nc.dram_tensor("wgT_in", [64, 64], F32, kind="ExternalInput").ap()
    kvs_in = nc.dram_tensor("kvs_in", [1, nbh], F32, kind="ExternalInput").ap()
    wgb_in = nc.dram_tensor("wgb_in", [1, nbh], F32, kind="ExternalInput").ap()
    ident_in = nc.dram_tensor("ident_in", [128, 128], F32, kind="ExternalInput").ap()
    triu_in = nc.dram_tensor("triu_in", [128, 128], F32, kind="ExternalInput").ap()
    y_out = nc.dram_tensor("y_out", [nbh, seq, 64], F32, kind="ExternalOutput").ap()

    SEG = 1024
    nseg = seq // SEG
    EPS = 1e-5

    p = Prog()
    with contextlib.ExitStack() as es:
        sb = lambda name, shape, dt=F32: es.enter_context(nc.sbuf_tensor(name, shape, dt))[:]
        sQT = sb("sQT", [64, seq], BF16)
        sKT = sb("sKT", [64, seq], BF16)
        sVT = sb("sVT", [64, seq], BF16)
        sWgb = sb("sWgb", [64, 64], BF16)
        sKvs = sb("sKvs", [1, nbh])
        sWgbias = sb("sWgbias", [1, nbh])
        sIdF = sb("sIdF", [128, 128])
        sIdB = sb("sIdB", [128, 128], BF16)
        sTriu = sb("sTriu", [128, 128])
        sOnes64b = sb("sOnes64b", [64, 1], BF16)
        sOnesF = sb("sOnesF", [1, 64])
        sOneF = sb("sOneF", [1, 1])
        sE = sb("sE", [64, SEG], BF16)
        L = {nm: sb("L_" + nm, [1, SEG]) for nm in
             ["qk", "ksq", "vsq", "vkg", "qsum", "gcum", "m", "msh", "nrm",
              "w", "ones"]}
        sLast = sb("sLast", [1, 8])
        vc_last = sb("vc_last", [64, 1])
        dkb = sb("dkb", [64, SEG])
        w0b = sb("w0b", [64, SEG])
        vcT = sb("vcT", [64, SEG])
        k_tm = sb("k_tm", [128, 64], BF16)
        Va = sb("Va", [128, 64], BF16)
        VaT = sb("VaT", [64, 128], BF16)
        PmT = sb("PmT", [128, 128], BF16)
        S_sb = sb("S_sb", [64, 64], BF16)
        sCols = sb("sCols", [128, 3])
        sY = sb("sY", [128, 64])
        pA = es.enter_context(nc.psum_tensor("pA", [64, 512], F32))[:]
        pL = es.enter_context(nc.psum_tensor("pL", [1, 512], F32))[:]
        pT = es.enter_context(nc.psum_tensor("pT", [128, 128], BF16))[:]
        pcols = es.enter_context(nc.psum_tensor("pcols", [128, 3], F32))[:]
        pYb = es.enter_context(nc.psum_tensor("pYb", [128, 64], F32))[:]
        pP = es.enter_context(nc.psum_tensor("pP", [128, 128], F32))[:]
        pS64 = es.enter_context(nc.psum_tensor("pS64", [64, 64], F32))[:]
        pVc = es.enter_context(nc.psum_tensor("pVc", [128, 64], F32))[:]
        sem = es.enter_context(nc.semaphore("sem"))
        block = es.enter_context(nc.Block())

        p.sync(lambda e: e.dma_start(out=sIdF, in_=ident_in))
        p.sync(lambda e: e.dma_start(out=sTriu, in_=triu_in))
        p.sync(lambda e: e.dma_start(out=sKvs, in_=kvs_in))
        p.sync(lambda e: e.dma_start(out=sWgbias, in_=wgb_in))
        p.gps_dma(lambda e: e.dma_start(out=sWgb, in_=wgT_in))
        p.vector(lambda e: e.tensor_copy(sIdB, sIdF))
        p.vector(lambda e: e.memset(sOnes64b, 1.0))
        p.vector(lambda e: e.memset(sOnesF, 1.0))
        p.vector(lambda e: e.memset(sOneF, 1.0))
        p.vector(lambda e: e.memset(L["ones"], 1.0))

        def lane_sum(dst, mov, width=SEG):
            for s0 in range(0, width, 512):
                p.tensor(lambda e, s0=s0, mov=mov: e.matmul(
                    pL, sOnes64b, mov[:, s0:s0 + 512], start=True, stop=True))
                p.vector(lambda e, s0=s0, dst=dst: e.tensor_copy(
                    dst[:, s0:s0 + 512], pL))

        for u in range(nbh):
            p.sync(lambda e, u=u: e.dma_start(out=sQT, in_=qT_in[u]))
            p.sync(lambda e, u=u: e.dma_start(out=sKT, in_=kT_in[u]))
            p.sync(lambda e, u=u: e.dma_start(out=sVT, in_=vT_in[u]))
            p.vector(lambda e: e.memset(S_sb, 0.0))
            p.vector(lambda e: e.memset(vc_last, 0.0))
            p.vector(lambda e: e.memset(sLast[:, 0:1], -3.0e38))
            p.vector(lambda e: e.memset(sLast[:, 1:3], 0.0))
            kv_ap = sKvs[0:1, u:u + 1]
            wgb_ap = sWgbias[0:1, u:u + 1]
            for g in range(nseg):
                gs = slice(g * SEG, (g + 1) * SEG)
                qTg, kTg, vTg = sQT[:, gs], sKT[:, gs], sVT[:, gs]
                p.vector(lambda e, qTg=qTg, kTg=kTg: e.tensor_tensor(sE, qTg, kTg, ALU.mult))
                lane_sum(L["qk"], sE)
                p.scalar(lambda e, kTg=kTg: e.activation(sE, kTg, AF.Square))
                lane_sum(L["ksq"], sE)
                p.scalar(lambda e, vTg=vTg: e.activation(sE, vTg, AF.Square))
                lane_sum(L["vsq"], sE)
                lane_sum(L["qsum"], qTg)
                for s0 in range(0, SEG, 512):
                    p.tensor(lambda e, s0=s0, kTg=kTg: e.matmul(
                        pA, sWgb, kTg[:, s0:s0 + 512], start=True, stop=True))
                    p.vector(lambda e, s0=s0, vTg=vTg: e.tensor_tensor(
                        sE[:, s0:s0 + 512], vTg[:, s0:s0 + 512], pA, ALU.mult))
                lane_sum(L["vkg"], sE)
                for nm in ("ksq", "vsq"):
                    p.scalar(lambda e, nm=nm: e.activation(L[nm], L[nm], AF.Sqrt))
                    p.vector(lambda e, nm=nm: e.tensor_scalar(L[nm], L[nm], 1e-12, None, ALU.max))
                    p.vector(lambda e, nm=nm: e.reciprocal(L[nm], L[nm]))
                p.vector(lambda e: e.tensor_tensor(L["qk"], L["qk"], L["ksq"], ALU.mult))
                p.vector(lambda e: e.tensor_tensor(L["ksq"], L["ksq"], L["vsq"], ALU.mult))
                p.vector(lambda e, kv_ap=kv_ap: e.tensor_scalar(L["ksq"], L["ksq"], kv_ap, None, ALU.mult))
                p.vector(lambda e: e.tensor_tensor(L["vkg"], L["vkg"], L["ksq"], ALU.mult))
                p.vector(lambda e, wgb_ap=wgb_ap: e.tensor_scalar(L["vkg"], L["vkg"], wgb_ap, None, ALU.add))
                p.scalar(lambda e: e.activation(L["vkg"], L["vkg"], AF.Lrelu, alpha=0.01))
                p.scalar(lambda e: e.activation(L["vkg"], L["vkg"], AF.Square))
                p.vector(lambda e: e.tensor_scalar(L["vkg"], L["vkg"], EPS, None, ALU.add))
                p.vector(lambda e: e.tensor_tensor_scan(
                    L["gcum"], L["ones"], L["vkg"], sLast[0:1, 2:3], ALU.mult, ALU.add))
                p.vector(lambda e: e.tensor_copy(sLast[:, 2:3], L["gcum"][:, SEG - 1:SEG]))
                p.vector(lambda e: e.tensor_tensor(L["vkg"], L["vkg"], L["ksq"], ALU.mult))
                p.vector(lambda e: e.tensor_tensor_scan(
                    L["m"], L["qk"], L["qk"], sLast[0:1, 0:1], ALU.max, ALU.bypass))
                p.vector(lambda e: e.tensor_copy(L["msh"][:, 1:SEG], L["m"][:, 0:SEG - 1]))
                p.vector(lambda e: e.tensor_copy(L["msh"][:, 0:1], sLast[:, 0:1]))
                p.vector(lambda e: e.tensor_copy(sLast[:, 0:1], L["m"][:, SEG - 1:SEG]))
                p.vector(lambda e: e.tensor_tensor(L["msh"], L["msh"], L["m"], ALU.subtract))
                p.scalar(lambda e: e.activation(L["msh"], L["msh"], AF.Exp))
                p.vector(lambda e: e.tensor_tensor(L["qk"], L["qk"], L["m"], ALU.subtract))
                p.scalar(lambda e: e.activation(L["qk"], L["qk"], AF.Exp))
                p.vector(lambda e: e.tensor_tensor_scan(
                    L["nrm"], L["msh"], L["qk"], sLast[0:1, 1:2], ALU.mult, ALU.add))
                p.vector(lambda e: e.tensor_copy(sLast[:, 1:2], L["nrm"][:, SEG - 1:SEG]))
                p.vector(lambda e: e.tensor_scalar(L["nrm"], L["nrm"], EPS, None, ALU.add))
                p.vector(lambda e: e.reciprocal(L["nrm"], L["nrm"]))
                p.vector(lambda e: e.tensor_tensor(L["w"], L["qk"], L["nrm"], ALU.mult))
                p.vector(lambda e: e.tensor_tensor(L["qsum"], L["qsum"], L["w"], ALU.mult))
                p.vector(lambda e: e.tensor_tensor(L["qsum"], L["qsum"], L["nrm"], ALU.mult))
                p.vector(lambda e: e.tensor_tensor(L["qk"], L["qk"], L["vsq"], ALU.mult))
                p.vector(lambda e: e.tensor_scalar(L["gcum"], L["gcum"], EPS, None, ALU.add))
                p.vector(lambda e: e.reciprocal(L["gcum"], L["gcum"]))
                p.vector(lambda e: e.tensor_scalar(L["w"], L["w"], -1.0, 1.0, ALU.mult, ALU.add))
                p.vector(lambda e: e.tensor_tensor(L["w"], L["w"], L["gcum"], ALU.mult))
                for s0 in range(0, SEG, 512):
                    p.tensor(lambda e, s0=s0: e.matmul(
                        pA, sOnesF, L["msh"][:, s0:s0 + 512], start=True, stop=True))
                    p.vector(lambda e, s0=s0: e.tensor_copy(dkb[:, s0:s0 + 512], pA))
                for s0 in range(0, SEG, 512):
                    p.tensor(lambda e, s0=s0: e.matmul(
                        pA, sOnesF, L["qk"][:, s0:s0 + 512], start=True, stop=True))
                    p.vector(lambda e, s0=s0: e.tensor_copy(w0b[:, s0:s0 + 512], pA))
                p.vector(lambda e, vTg=vTg: e.tensor_tensor(w0b, vTg, w0b, ALU.mult))
                p.vector(lambda e: e.tensor_tensor_scan(
                    vcT, dkb, w0b, vc_last, ALU.mult, ALU.add))
                p.vector(lambda e: e.tensor_copy(vc_last, vcT[:, SEG - 1:SEG]))
                for cc in range(SEG // 128):
                    c = g * (SEG // 128) + cc
                    csl = slice(cc * 128, (cc + 1) * 128)
                    qTc = qTg[:, csl]
                    p.tensor(lambda e, kTg=kTg, csl=csl: e.transpose(
                        pT[:, 0:64], kTg[:, csl], sIdB[0:64, 0:64]))
                    p.vector(lambda e: e.tensor_copy(k_tm, pT[:, 0:64]))
                    p.tensor(lambda e, csl=csl: e.matmul(
                        pcols[:, 0:1], L["vkg"][:, csl], sOneF, start=True, stop=True))
                    p.tensor(lambda e, csl=csl: e.matmul(
                        pcols[:, 1:2], L["w"][:, csl], sOneF, start=True, stop=True))
                    p.tensor(lambda e, csl=csl: e.matmul(
                        pcols[:, 2:3], L["qsum"][:, csl], sOneF, start=True, stop=True))
                    p.vector(lambda e: e.tensor_copy(sCols, pcols))
                    p.tensor(lambda e, vTg=vTg, csl=csl: e.transpose(
                        pT[:, 0:64], vTg[:, csl], sIdB[0:64, 0:64]))
                    p.vector(lambda e: e.tensor_scalar(
                        Va, pT[:, 0:64], sCols[:, 0:1], None, ALU.mult))
                    p.tensor(lambda e: e.transpose(pT[0:64, :], Va, sIdB))
                    p.vector(lambda e: e.tensor_copy(VaT, pT[0:64, :]))
                    p.tensor(lambda e, qTc=qTc: e.matmul(
                        pYb, qTc, S_sb, start=True, stop=False))
                    p.tensor(lambda e, qTc=qTc: e.matmul(
                        pP, VaT, qTc, start=True, stop=True))
                    p.vector(lambda e: e.tensor_tensor(PmT, pP, sTriu, ALU.mult))
                    p.tensor(lambda e: e.matmul(
                        pYb, PmT, k_tm, start=False, stop=True))
                    p.tensor(lambda e, c=c: e.matmul(
                        pS64, Va, k_tm, start=(c == 0), stop=(c == seq // 128 - 1)))
                    p.vector(lambda e: e.tensor_copy(S_sb, pS64))
                    p.tensor(lambda e, csl=csl: e.transpose(
                        pVc[:, 0:64], vcT[:, csl], sIdF[0:64, 0:64]))
                    p.vector(lambda e: e.tensor_scalar(
                        sY, pYb, sCols[:, 1:2], None, ALU.mult))
                    p.vector(lambda e: e.scalar_tensor_tensor(
                        sY, pVc[:, 0:64], sCols[:, 2:3], sY, ALU.mult, ALU.add))
                    p.sync(lambda e, u=u, c=c: e.dma_start(
                        out=y_out[u][c * 128:(c + 1) * 128, :], in_=sY))
        p.emit(block, sem)
    return nc


def run_l0scan(qT, kT, vT, wg_z_w, wg_z_b, kv_norm_scale, bh_assign):
    ncore, nbh, _, seq = qT.shape
    nc = build_l0scan(nbh, seq)
    wgT = np.ascontiguousarray(wg_z_w.reshape(64, 64).T)
    kvs_flat = kv_norm_scale.reshape(16)
    ident = np.eye(128, dtype=np.float32)
    triu = np.triu(np.ones((128, 128), np.float32))
    in_maps = []
    for i in range(ncore):
        kvs = np.ascontiguousarray(kvs_flat[bh_assign[i]].reshape(1, nbh))
        wgb = np.full((1, nbh), float(wg_z_b[0]), np.float32)
        in_maps.append({"qT_in": np.ascontiguousarray(qT[i]),
                        "kT_in": np.ascontiguousarray(kT[i]),
                        "vT_in": np.ascontiguousarray(vT[i]),
                        "wgT_in": wgT, "kvs_in": kvs, "wgb_in": wgb,
                        "ident_in": ident, "triu_in": triu})
    res = run_bass_kernel_spmd(nc, in_maps, core_ids=list(range(ncore)))
    return np.stack([r["y_out"] for r in res.results])


def build_l0out(ntok):
    nc = bass.Bass("TRN2", target_bir_lowering=False, debug=False)
    x_in = nc.dram_tensor("x_in", [ntok, D], F32, kind="ExternalInput").ap()
    y_in = nc.dram_tensor("y_in", [ntok, D], F32, kind="ExternalInput").ap()
    woT_in = nc.dram_tensor("woT_in", [D, D], F32, kind="ExternalInput").ap()
    wob_in = nc.dram_tensor("wob_in", [1, D], F32, kind="ExternalInput").ap()
    x_out = nc.dram_tensor("x_out", [ntok, D], F32, kind="ExternalOutput").ap()

    ntile = ntok // 128
    p = Prog()
    with contextlib.ExitStack() as es:
        sb = lambda name, shape, dt=F32: es.enter_context(nc.sbuf_tensor(name, shape, dt))[:]
        sX = sb("sX", [128, D])
        sY = sb("sY", [128, D])
        sYb = sb("sYb", [128, D], BF16)
        sYbT = sb("sYbT", [128, 8, 128], BF16)
        sWoT = sb("sWoT", [128, 8, D], BF16)
        sBo = sb("sBo", [1, D])
        sBob = sb("sBob", [1, D], BF16)
        sOnes = sb("sOnes", [1, 128], BF16)
        sBiasTile = sb("sBiasTile", [128, D])
        stat = sb("stat", [128, 1])
        pP = es.enter_context(nc.psum_tensor("pP", [128, 1024], F32))[:]
        sem = es.enter_context(nc.semaphore("sem"))
        block = es.enter_context(nc.Block())

        w_r = woT_in.rearrange("(kc q) o -> kc q o", q=128)
        for kc in range(8):
            p.gps_dma(lambda e, kc=kc: e.dma_start(out=sWoT[:, kc, :], in_=w_r[kc]))
        p.sync(lambda e: e.dma_start(out=sBo, in_=wob_in))
        p.vector(lambda e: e.tensor_copy(sBob, sBo))
        p.vector(lambda e: e.memset(sOnes, 1.0))
        for half in range(2):
            osl = slice(half * 512, (half + 1) * 512)
            p.tensor(lambda e, osl=osl: e.matmul(
                pP[:, osl], sOnes, sBob[:, osl], start=True, stop=True))
        p.vector(lambda e: e.tensor_copy(sBiasTile, pP))

        x_r = x_in.rearrange("(c q) f -> c q f", q=128)
        y_r = y_in.rearrange("(c q) f -> c q f", q=128)
        xo_r = x_out.rearrange("(c q) f -> c q f", q=128)
        sSq = sb("sSq", [128, D])
        for c in range(ntile):
            p.sync(lambda e, c=c: e.dma_start(out=sX, in_=x_r[c]))
            p.sync(lambda e, c=c: e.dma_start(out=sY, in_=y_r[c]))
            p.scalar(lambda e: e.activation(sSq, sY, AF.Square, accum_out=stat))
            p.scalar(lambda e: e.activation(stat, stat, AF.Sqrt))
            p.vector(lambda e: e.tensor_scalar(stat, stat, 1e-12, None, ALU.max))
            p.vector(lambda e: e.reciprocal(stat, stat))
            p.vector(lambda e: e.tensor_scalar(sYb, sY, stat, None, ALU.mult))
            emit_transpose_blocks(p, sYb, sYbT.rearrange("q a b -> q (a b)"), 8)
            for half in range(2):
                osl = slice(half * 512, (half + 1) * 512)
                for kc in range(8):
                    p.tensor(lambda e, kc=kc, osl=osl: e.matmul(
                        pP[:, osl], sYbT[:, kc, :], sWoT[:, kc, osl],
                        start=(kc == 0), stop=(kc == 7)))
            p.vector(lambda e: e.tensor_add(sX, sX, sBiasTile))
            p.vector(lambda e: e.tensor_tensor(sX, pP, sX, ALU.add))
            p.sync(lambda e, c=c: e.dma_start(out=xo_r[c], in_=sX))
        p.emit(block, sem)
    return nc


def run_l0out(x, y, wo, wob):
    ntok = x.shape[1]
    nc = build_l0out(ntok)
    woT = np.ascontiguousarray(wo.T)
    in_maps = [{"x_in": np.ascontiguousarray(x[i]), "y_in": np.ascontiguousarray(y[i]),
                "woT_in": woT, "wob_in": wob.reshape(1, D)}
               for i in range(x.shape[0])]
    res = run_bass_kernel_spmd(nc, in_maps, core_ids=list(range(x.shape[0])))
    return np.stack([r["x_out"] for r in res.results])


def kernel(L0_attn_x, L0_attn_spectral_basis, L0_attn_M_phi_w, L0_attn_M_phi_b,
           L0_attn_wq_w, L0_attn_wq_b, L0_attn_wk_w, L0_attn_wk_b,
           L0_attn_wv_w, L0_attn_wv_b, L0_attn_wo_w, L0_attn_wo_b,
           L0_attn_wg_z_w, L0_attn_wg_z_b, L0_attn_kv_norm_scale,
           L0_ffn_weight, L0_ffn_bias,
           L1_attn_Wq, L1_attn_Wk, L1_attn_Wv, L1_attn_Wu, L1_attn_bu,
           L1_ffn_w_gate, L1_ffn_w_up, L1_ffn_w_down):
    x0 = np.asarray(L0_attn_x)
    B, L, Dm = x0.shape
    ncore = NCORE
    if FUSED:
        inp = dict(
            L0_attn_spectral_basis=L0_attn_spectral_basis,
            L0_attn_M_phi_w=L0_attn_M_phi_w, L0_attn_M_phi_b=L0_attn_M_phi_b,
            L0_attn_wq_w=L0_attn_wq_w, L0_attn_wq_b=L0_attn_wq_b,
            L0_attn_wk_w=L0_attn_wk_w, L0_attn_wk_b=L0_attn_wk_b,
            L0_attn_wv_w=L0_attn_wv_w, L0_attn_wv_b=L0_attn_wv_b,
            L0_attn_wo_w=L0_attn_wo_w, L0_attn_wo_b=L0_attn_wo_b,
            L0_attn_wg_z_w=L0_attn_wg_z_w, L0_attn_wg_z_b=L0_attn_wg_z_b,
            L0_attn_kv_norm_scale=L0_attn_kv_norm_scale,
            L0_ffn_weight=L0_ffn_weight, L0_ffn_bias=L0_ffn_bias,
            L1_attn_Wq=L1_attn_Wq, L1_attn_Wk=L1_attn_Wk, L1_attn_Wv=L1_attn_Wv,
            L1_attn_Wu=L1_attn_Wu, L1_attn_bu=L1_attn_bu,
            L1_ffn_w_gate=L1_ffn_w_gate, L1_ffn_w_up=L1_ffn_w_up,
            L1_ffn_w_down=L1_ffn_w_down,
        )
        return np.ascontiguousarray(run_fused(x0, inp, ncore)).astype(np.float32)
    ntok = B * L // ncore
    seq = L
    cores_per_b = ncore // B
    nbh = 16 // cores_per_b

    def shard(x):
        return x.reshape(ncore, ntok, Dm)

    def unshard(xs):
        return xs.reshape(-1, Dm)

    def to_bh(aT):
        out = np.empty((ncore, nbh, 64, seq), aT.dtype)
        for c in range(ncore):
            b = c // cores_per_b
            full = np.concatenate([aT[b * cores_per_b + i] for i in range(cores_per_b)], axis=1)
            for j in range(nbh):
                h = (c % cores_per_b) * nbh + j
                out[c, j] = full[h * 64:(h + 1) * 64]
        return out

    x = np.ascontiguousarray(x0.reshape(-1, Dm))
    x = unshard(run_ffn0(shard(x), L0_ffn_weight, L0_ffn_bias))

    xb = x.reshape(B, L, Dm)
    xh = []
    for c in range(ncore):
        b = c // cores_per_b
        t0 = (c % cores_per_b) * ntok
        pad = np.concatenate([np.zeros((1024, Dm), np.float32), xb[b]], 0)
        xh.append(pad[t0:t0 + ntok + 1024])
    xh = np.stack(xh)
    qT, kT, vT = run_l0mix(xh, L0_attn_spectral_basis, L0_attn_M_phi_w,
                           L0_attn_M_phi_b, L0_attn_wq_w, L0_attn_wq_b,
                           L0_attn_wk_w, L0_attn_wk_b, L0_attn_wv_w, L0_attn_wv_b)
    bh_assign = np.array([[(c % cores_per_b) * nbh + j for j in range(nbh)]
                          for c in range(ncore)])
    y = run_l0scan(to_bh(qT), to_bh(kT), to_bh(vT), L0_attn_wg_z_w,
                   L0_attn_wg_z_b, L0_attn_kv_norm_scale, bh_assign)
    Y = np.empty((B, L, 16, 64), np.float32)
    for c in range(ncore):
        b = c // cores_per_b
        for j in range(nbh):
            h = (c % cores_per_b) * nbh + j
            Y[b, :, h] = y[c, j]
    Y = Y.reshape(-1, Dm)
    x = unshard(run_l0out(shard(x), shard(Y), L0_attn_wo_w, L0_attn_wo_b))
    x = unshard(run_ffn0(shard(x), L0_ffn_weight, L0_ffn_bias))
    x = unshard(run_ffn1(shard(x), L1_ffn_w_gate, L1_ffn_w_up, L1_ffn_w_down))

    qT1, kT1, v1 = run_l1qkv(shard(x), L1_attn_Wq, L1_attn_Wk, L1_attn_Wv)
    vs = np.empty((ncore, nbh, seq, 64), v1.dtype)
    vfull = v1.reshape(B, L, Dm)
    for c in range(ncore):
        b = c // cores_per_b
        for j in range(nbh):
            h = (c % cores_per_b) * nbh + j
            vs[c, j] = vfull[b, :, h * 64:(h + 1) * 64]
    ot = run_l1flash(to_bh(qT1), to_bh(kT1), vs)
    OT = np.empty((B, 16, 64, L), np.float32)
    for c in range(ncore):
        b = c // cores_per_b
        for j in range(nbh):
            h = (c % cores_per_b) * nbh + j
            OT[b, h] = ot[c, j, 0:64]
    ot_sh = np.empty((ncore, Dm, ntok), np.float32)
    for c in range(ncore):
        b = c // cores_per_b
        t0 = (c % cores_per_b) * ntok
        ot_sh[c] = OT[b].reshape(Dm, L)[:, t0:t0 + ntok]
    x = unshard(run_l1out(shard(x), ot_sh, L1_attn_Wu, L1_attn_bu))
    x = unshard(run_ffn1(shard(x), L1_ffn_w_gate, L1_ffn_w_up, L1_ffn_w_down))
    return np.ascontiguousarray(x.reshape(B, L, Dm)).astype(np.float32)


def build_fused(ntok):
    seq = 2 * ntok
    nch128 = ntok // 128
    nc = bass.Bass("TRN2", target_bir_lowering=False, debug=False)
    x_in = nc.dram_tensor("x_in", [ntok, D], F32, kind="ExternalInput").ap()
    nonce_in = nc.dram_tensor("nonce_in", [1, 128], I32, kind="ExternalInput").ap()
    w0_in = nc.dram_tensor("w0_in", [D, D], F32, kind="ExternalInput").ap()
    b0_in = nc.dram_tensor("b0_in", [D], F32, kind="ExternalInput").ap()
    tt_in = nc.dram_tensor("tt_in", [128, 16 * 9 * 128], F32, kind="ExternalInput").ap()
    mphiT_in = nc.dram_tensor("mphiT_in", [16384, D], F32, kind="ExternalInput").ap()
    mb_in = nc.dram_tensor("mb_in", [128, 8], F32, kind="ExternalInput").ap()
    wqT_in = nc.dram_tensor("wqT_in", [D, D], F32, kind="ExternalInput").ap()
    wkT_in = nc.dram_tensor("wkT_in", [D, D], F32, kind="ExternalInput").ap()
    wvT_in = nc.dram_tensor("wvT_in", [D, D], F32, kind="ExternalInput").ap()
    qb_in = nc.dram_tensor("qb_in", [128, 8], F32, kind="ExternalInput").ap()
    kb_in = nc.dram_tensor("kb_in", [128, 8], F32, kind="ExternalInput").ap()
    vb_in = nc.dram_tensor("vb_in", [128, 8], F32, kind="ExternalInput").ap()
    wgT_in = nc.dram_tensor("wgT_in", [64, 64], F32, kind="ExternalInput").ap()
    kvs_in = nc.dram_tensor("kvs_in", [1, 16], F32, kind="ExternalInput").ap()
    wgb_in = nc.dram_tensor("wgb_in", [1, 16], F32, kind="ExternalInput").ap()
    ident_in = nc.dram_tensor("ident_in", [128, 128], F32, kind="ExternalInput").ap()
    triu_in = nc.dram_tensor("triu_in", [128, 128], F32, kind="ExternalInput").ap()
    woT_in = nc.dram_tensor("woT_in", [D, D], F32, kind="ExternalInput").ap()
    wob_in = nc.dram_tensor("wob_in", [1, D], F32, kind="ExternalInput").ap()
    wg1_in = nc.dram_tensor("wg1_in", [IFF, D], F32, kind="ExternalInput").ap()
    wu1_in = nc.dram_tensor("wu1_in", [IFF, D], F32, kind="ExternalInput").ap()
    wd1_in = nc.dram_tensor("wd1_in", [D, IFF], F32, kind="ExternalInput").ap()
    waqT_in = nc.dram_tensor("waqT_in", [D, D], F32, kind="ExternalInput").ap()
    wakT_in = nc.dram_tensor("wakT_in", [D, D], F32, kind="ExternalInput").ap()
    wavT_in = nc.dram_tensor("wavT_in", [D, D], F32, kind="ExternalInput").ap()
    wauT_in = nc.dram_tensor("wauT_in", [D, D], F32, kind="ExternalInput").ap()
    bau_in = nc.dram_tensor("bau_in", [1, D], F32, kind="ExternalInput").ap()
    x_out = nc.dram_tensor("x_out", [ntok, D], F32, kind="ExternalOutput").ap()
    xres = nc.dram_tensor("xres", [ntok, D], F32).ap()
    hscr = nc.dram_tensor("hscr", [ntok // 128, 128, IFF], BF16).ap()
    mphi_bf = nc.dram_tensor("mphi_bf", [16384, D], BF16).ap()
    uqscr = nc.dram_tensor("uqscr", [ntok // 128, 128, D], BF16).ap()
    xtscr = nc.dram_tensor("xtscr", [ntok // 256, 128, 8 * 256], BF16).ap()
    gyscr = nc.dram_tensor("gyscr", [ntok // 128, 128, 2 * IFF], F32).ap()
    halo_sh = nc.dram_tensor("halo_sh", [3 * 1024, D], F32, addr_space="Shared").ap()
    q0_sh = nc.dram_tensor("q0_sh", [D, seq], BF16, addr_space="Shared").ap()
    k0_sh = nc.dram_tensor("k0_sh", [D, seq], BF16, addr_space="Shared").ap()
    v0_sh = nc.dram_tensor("v0_sh", [D, seq], BF16, addr_space="Shared").ap()
    y_sh = nc.dram_tensor("y_sh", [seq, D], F32, addr_space="Shared").ap()
    q1_sh = nc.dram_tensor("q1_sh", [D, seq], BF16, addr_space="Shared").ap()
    k1_sh = nc.dram_tensor("k1_sh", [D, seq], BF16, addr_space="Shared").ap()
    v1_sh = nc.dram_tensor("v1_sh", [seq, D], BF16, addr_space="Shared").ap()
    ot_sh = nc.dram_tensor("ot_sh", [D, seq], F32, addr_space="Shared").ap()
    flg = nc.dram_tensor("flg", [2, 16], I32, addr_space="Shared").ap()

    p = Prog()
    es = contextlib.ExitStack()
    with es:
        sb = lambda name, shape, dt=F32: es.enter_context(nc.sbuf_tensor(name, shape, dt))[:]
        sNon = sb("sNon", [1, 128], I32)
        sFw = sb("sFw", [1, 1], I32)
        sFr = sb("sFr", [1, 1], I32)
        sem = es.enter_context(nc.semaphore("sem"))
        bsem = es.enter_context(nc.semaphore("bsem"))
        psem = es.enter_context(nc.semaphore("psem"))
        tsem = es.enter_context(nc.semaphore("tsem"))
        msem = es.enter_context(nc.semaphore("msem"))
        esem = es.enter_context(nc.semaphore("esem"))
        qsem = es.enter_context(nc.semaphore("qsem"))
        vsem = es.enter_context(nc.semaphore("vsem"))
        block = es.enter_context(nc.Block())

        sIdF2 = es.enter_context(nc.sbuf_tensor("sIdF2", [128, 128], F32))[:]
        sIdB2 = es.enter_context(nc.sbuf_tensor("sIdB2", [128, 128], BF16))[:]
        p.sync(lambda e: e.dma_start(out=sNon, in_=nonce_in))
        p.sync(lambda e: e.dma_start(out=sIdF2, in_=ident_in))
        p.vector(lambda e: e.tensor_copy(sIdB2, sIdF2))

        bar_state = p.ctx

        def gps_pre(g):
            bar_state["b_par"] = g.alloc_register("b_par")
            g.reg_alu(bar_state["b_par"], g.partition_id(), 1, ALU.bitwise_and)
            bar_state["rn"] = g.alloc_register("b_rn")
            bar_state["rt"] = g.alloc_register("b_rt")
            bar_state["rv"] = g.alloc_register("b_rv")
            bar_state["rf"] = g.alloc_register("b_rf")
            bar_state["pc"] = g.alloc_register("b_pc")
            g.reg_mov(bar_state["pc"], 0)

        p.pre["gpsimd"] = gps_pre
        bar_n = [0]
        tcount = [0]
        mcount = [0]
        qcount = [0]
        vcount = [0]

        def barrier():
            bar_n[0] += 1
            k = bar_n[0]

            def fn(g, t0, sem_):
                if t0 > 0:
                    g.wait_ge(sem_, t0)
                rn, rt, rv, rf, pc = (bar_state[n] for n in ("rn", "rt", "rv", "rf", "pc"))
                if k == 1:
                    g.reg_load(rn, sNon[0:1, 0:1])
                g.reg_alu(rt, rn, k, ALU.add)
                g.store(sFw[0:1, 0:1], rt)
                with g.If_eq(bar_state["b_par"], 0):
                    g.dma_start(out=flg[0:1, k:k + 1], in_=sFw[0:1, 0:1]).then_inc(bsem, 16)
                with g.Else():
                    g.dma_start(out=flg[1:2, k:k + 1], in_=sFw[0:1, 0:1]).then_inc(bsem, 16)
                g.wait_ge(bsem, 16 * k)
                g.reg_mov(rv, 0)
                g.reg_alu(rf, rv, rt, ALU.not_equal)
                with g.While(rf):
                    g.reg_add(pc, pc, 16)
                    with g.If_eq(bar_state["b_par"], 0):
                        g.dma_start(out=sFr[0:1, 0:1], in_=flg[1:2, k:k + 1]).then_inc(psem, 16)
                    with g.Else():
                        g.dma_start(out=sFr[0:1, 0:1], in_=flg[0:1, k:k + 1]).then_inc(psem, 16)
                    g.wait_ge(psem, pc)
                    g.reg_load(rv, sFr[0:1, 0:1])
                    g.reg_alu(rf, rv, rt, ALU.not_equal)
                g.nop().then_inc(sem_, 1)

            p.raw("gpsimd", fn, 1)

        es_f0 = contextlib.ExitStack()
        w0T = es_f0.enter_context(nc.sbuf_tensor("r_w0T", [128, 8, D], BF16))[:]
        sB0b = es_f0.enter_context(nc.sbuf_tensor("r_sB0b", [1, D], BF16))[:]
        sOnes1 = es_f0.enter_context(nc.sbuf_tensor("r_sOnes1", [1, 128], BF16))[:]
        f0_init = [False]

        def emit_ffn0(src_x, dst_x, publish_halo, tag):
            CH = 512
            TB = CH // 128
            nchk = ntok // CH
            with contextlib.ExitStack() as s2:
                sb2 = lambda name, shape, dt=F32: s2.enter_context(nc.sbuf_tensor(tag + name, shape, dt))[:]
                if not f0_init[0]:
                    f0_init[0] = True
                    with contextlib.ExitStack() as s3:
                        sb3 = lambda name, shape, dt=F32: s3.enter_context(nc.sbuf_tensor(tag + name, shape, dt))[:]
                        sWrow = sb3("w_sWrow", [128, D])
                        sWd = sb3("w_sWd", [128, D], BF16)
                        wA = sb3("w_A", [128, D])
                        wB = sb3("w_B", [128, D])
                        wC = sb3("w_C", [128, D])
                        wD = sb3("w_D", [128, D])
                        wq_dscr = sb3("w_dscr", [128, D // 32])
                        sBias = sb3("w_sBias", [1, D])
                        w_r = w0_in.rearrange("(oc q) k -> oc q k", q=128)
                        for oc in range(8):
                            p.sync(lambda e, oc=oc: e.dma_start(out=sWrow, in_=w_r[oc]))
                            emit_quant_fp6(p, sWrow, sWd, wA, wB, wC, wD, wq_dscr)
                            for kc in range(8):
                                p.sync(lambda e, oc=oc, kc=kc: e.dma_start_transpose(
                                    w0T[:, kc, oc * 128:(oc + 1) * 128],
                                    sWd[:, kc * 128:(kc + 1) * 128]), fuse=(kc > 0))
                        p.sync(lambda e: e.dma_start(out=sBias, in_=b0_in.rearrange("(o f) -> o f", o=1)))
                        p.vector(lambda e: e.tensor_copy(sB0b, sBias))
                        p.vector(lambda e: e.memset(sOnes1, 1.0))

                sX = sb2("f0_sX", [128, TB, D])
                sU = sb2("f0_sU", [128, TB * D])
                sUd = sb2("f0_sUd", [128, TB * D], BF16)
                sUdT = sb2("f0_sUdT", [128, TB * 8, 128], BF16)
                bA = sb2("f0_bA", [128, TB * D])
                bB = sb2("f0_bB", [128, TB * D])
                bC = sb2("f0_bC", [128, TB * D])
                bD = sb2("f0_bD", [128, TB * D])
                dscr = sb2("f0_dscr", [128, TB * D // 32])
                stat_a = sb2("f0_stat_a", [128, TB])
                stat_b = sb2("f0_stat_b", [128, TB])
                sZ = sb2("f0_sZ", [128, 1024]) if publish_halo else None
                if publish_halo:
                    p.vector(lambda e: e.memset(sZ, 0.0))
                    hv3 = halo_sh.rearrange("(blk q) f -> q blk f", q=128)
                    for zb in range(8):
                        p.sync(lambda e, zb=zb: e.dma_start(
                            out=hv3[:, zb:zb + 1, :], in_=sZ.rearrange("q (a f) -> q a f", a=1)))
                x_r = src_x.rearrange("(c tb q) f -> c q tb f", q=128, tb=TB)
                uq_r = uqscr.rearrange("(c tb) q f -> c q tb f", tb=TB)
                for c in range(nchk):
                    p.sync(lambda e, c=c: e.dma_start(out=sX, in_=x_r[c]))
                    emit_ln(p, sX, sU.rearrange("q (n f) -> q n f", f=D), stat_a, stat_b, bA, TB)
                    emit_quant_fp6(p, sU, sUd, bA, bB, bC, bD, dscr)
                    emit_transpose_blocks(p, sUd, sUdT.rearrange("q a b -> q (a b)"), TB * 8)
                    p.sync(lambda e, c=c: e.dma_start(
                        out=uq_r[c], in_=sUdT.rearrange("q (tb k) b -> q tb (k b)", tb=TB)))

                sX2 = sb2("f0_sX2", [128, 2, TB, D])
                sUdT2 = sb2("f0_sUdT2", [128, 2, TB * 8, 128], BF16)
                pT2 = [s2.enter_context(nc.psum_tensor(tag + "f0_pT%d" % i,
                                                       [128, D], F32))[:]
                       for i in range(4)]
                xo_r = dst_x.rearrange("(c tb q) f -> c q tb f", q=128, tb=TB)
                mtf = {}
                qtf = {}
                fmark = {}

                def f0_dma(c):
                    qdep = qtf.get(c - 2, 0)
                    def f(e, t0, sem_, c=c, qdep=qdep):
                        e.wait_ge(sem_, t0)
                        if qdep:
                            e.wait_ge(qsem, qdep)
                        e.dma_start(out=sUdT2[:, c % 2, :, :].rearrange(
                            "q (tb k) b -> q tb (k b)", tb=TB),
                                    in_=uq_r[c]).then_inc(msem, 16)
                        e.dma_start(out=sX2[:, c % 2, :, :], in_=x_r[c]).then_inc(msem, 16)
                    p.raw("gpsimd", f, 0)
                    mcount[0] += 32
                    mtf[c] = mcount[0]

                f0_dma(0)
                f0_dma(1)
                for c in range(nchk):
                    for tb in range(TB):
                        idx = c * TB + tb
                        def fgrp(e, t0, sem_, c=c, tb=tb, nm=mtf[c],
                                 wm=fmark.get(idx - 4)):
                            if t0 > 0 and tb == 0:
                                e.wait_ge(sem_, t0)
                            if wm is not None:
                                e.wait_ge(sem_, p.resolve(wm))
                            e.wait_ge(msem, nm)
                            for half in range(2):
                                osl = slice(half * 512, (half + 1) * 512)
                                for kc in range(8):
                                    e.matmul(pT2[(c * TB + tb) % 4][:, osl],
                                             sUdT2[:, c % 2, tb * 8 + kc, :],
                                             w0T[:, kc, osl],
                                             start=(kc == 0), stop=False).then_inc(qsem, 1)
                                e.matmul(pT2[(c * TB + tb) % 4][:, osl], sOnes1,
                                         sB0b[:, osl], start=False,
                                         stop=True).then_inc(qsem, 1)
                        p.raw("tensor", fgrp, 0)
                        qcount[0] += 18
                        qd = qcount[0]
                        def fcomb(e, t0, sem_, c=c, tb=tb, qd=qd):
                            if t0 > 0:
                                e.wait_ge(sem_, t0)
                            e.wait_ge(qsem, qd)
                            e.scalar_tensor_tensor(
                                sX2[:, c % 2, tb, :], pT2[(c * TB + tb) % 4], 0.5,
                                sX2[:, c % 2, tb, :], ALU.mult, ALU.add).then_inc(sem_, 1)
                        p.raw("vector", fcomb, 1)
                        fmark[idx] = p.mark()
                    qtf[c] = qcount[0]
                    p.sync(lambda e, c=c: e.dma_start(out=xo_r[c], in_=sX2[:, c % 2, :, :]))
                    if publish_halo and c >= nchk - 2:
                        hv5 = halo_sh.rearrange("(a blk q) f -> a q blk f", a=3, q=128)
                        cc0 = c - (nchk - 2)
                        def pub(par, cc0=cc0, c=c):
                            return lambda e: e.dma_start(
                                out=hv5[par + 1, :, cc0 * TB:(cc0 + 1) * TB, :],
                                in_=sX2[:, c % 2, :, :])
                        p.par_dma(pub)
                    if c + 2 < nchk:
                        f0_dma(c + 2)

        emit_ffn0(x_in, xres, True, "A")
        barrier()

        def emit_l0mix():
            T = 256
            nchk = ntok // T
            NRING = 10
            NSLAB = 2
            with contextlib.ExitStack() as s2:
                sb2 = lambda name, shape, dt=F32: s2.enter_context(nc.sbuf_tensor(name, shape, dt))[:]
                sXb = sb2("m_sXb", [128, D])
                sU = sb2("m_sU", [128, D])
                bA = sb2("m_bA", [128, D])
                stat_a = sb2("m_stat_a", [128, 1])
                stat_b = sb2("m_stat_b", [128, 1])
                ring = sb2("m_ring", [128, NRING, D], BF16)
                sTT = sb2("m_sTT", [128, 144, 128], BF16)
                sConv = sb2("m_sConv", [128, 2, D], BF16)
                sCT = sb2("m_sCT", [128, 128, T], BF16)
                sMph = sb2("m_sMph", [128, 2, NSLAB, D], BF16)
                sXtT = sb2("m_sXtT", [128, 8, T], BF16)
                sMb = sb2("m_sMb", [128, 8])
                pC = s2.enter_context(nc.psum_tensor("m_pC", [128, D], F32))[:]
                pXT = s2.enter_context(nc.psum_tensor("m_pXT", [128, 8 * T], F32))[:]

                p.gps_dma(lambda e: e.dma_start(
                    out=sTT.rearrange("q a b -> q (a b)"), in_=tt_in))
                for qq in range(8):
                    p.gps_dma(lambda e, qq=qq: e.dma_start(
                        out=mphi_bf[qq * 2048:(qq + 1) * 2048, :],
                        in_=mphiT_in[qq * 2048:(qq + 1) * 2048, :]), fuse=(qq > 0))
                p.sync(lambda e: e.dma_start(out=sMb, in_=mb_in))

                xr_r = xres.rearrange("(ib q) f -> ib q f", q=128)

                def ln_to_ring(ib):
                    if ib < 8:
                        hv4 = halo_sh.rearrange("(a r) f -> a r f", a=3)
                        def ld(par, ib=ib):
                            return lambda e: e.dma_start(
                                out=sXb, in_=hv4[par, ib * 128:(ib + 1) * 128, :])
                        p.par_dma(ld)
                    else:
                        p.sync(lambda e, ib=ib: e.dma_start(out=sXb, in_=xr_r[ib - 8]))
                    emit_ln(p, sXb.rearrange("q (n f) -> q n f", n=1),
                            sU.rearrange("q (n f) -> q n f", n=1), stat_a, stat_b, bA, 1)
                    p.vector(lambda e, ib=ib: e.tensor_copy(ring[:, ib % NRING, :], sU))

                for ib in range(8):
                    ln_to_ring(ib)

                mphi_r = mphi_bf.rearrange("(sl j q) o -> sl q j o", q=128, j=NSLAB)
                for c in range(nchk):
                    for bi in range(2):
                        ln_to_ring(8 + 2 * c + bi)
                    for bi in range(2):
                        ob = 2 * c + bi
                        for kf in range(16):
                            kk = bi * 16 + kf
                            for d in range(9):
                                ib = ob + 8 - d
                                for half in range(2):
                                    osl = slice(half * 512, (half + 1) * 512)
                                    first = (d == 0 and half == 0)
                                    p.tensor(lambda e, kf=kf, d=d, ib=ib, osl=osl: e.matmul(
                                        pC[:, osl], sTT[:, kf * 9 + d, :],
                                        ring[:, ib % NRING, osl],
                                        start=(d == 0), stop=(d == 8)),
                                        fuse=not first)
                            need_t = max(0, tcount[0] - 16 * 8)
                            def evac(e, t0, sem_, kk=kk, need_t=need_t):
                                e.wait_ge(sem_, t0)
                                if need_t:
                                    e.wait_ge(tsem, need_t)
                                e.tensor_copy(sConv[:, kk % 2, :], pC).then_inc(sem_, 1)
                            p.raw("vector", evac, 1)
                            def transp(e, t0, sem_, kf=kf, bi=bi, kk=kk):
                                e.wait_ge(sem_, t0)
                                for cc in range(8):
                                    e.dma_start_transpose(
                                        sCT[:, kf * 8 + cc, bi * 128:(bi + 1) * 128],
                                        sConv[:, kk % 2, cc * 128:(cc + 1) * 128]
                                    ).then_inc(tsem, 16)
                            p.raw("sync", transp, 0)
                            tcount[0] += 16 * 8
                    NSL = 128 // NSLAB
                    mtarget0 = mcount[0]

                    def slab_dma(sl):
                        need_t = tcount[0] if sl < 2 else 0
                        def f(e, t0, sem_, sl=sl, need_t=need_t):
                            e.wait_ge(sem_, t0)
                            if need_t:
                                e.wait_ge(tsem, need_t)
                            e.dma_start(out=sMph[:, sl % 2, :, :],
                                        in_=mphi_r[sl]).then_inc(msem, 16)
                        p.raw("gpsimd", f, 0)
                        mcount[0] += 16

                    slab_dma(0)
                    slab_dma(1)
                    for sl in range(NSL):
                        need_m = mtarget0 + 16 * (sl + 1)
                        need_t = tcount[0]
                        for j in range(NSLAB):
                            kc = sl * NSLAB + j
                            for oc in range(8):
                                first = (j == 0 and oc == 0)
                                if first:
                                    def fmm(e, t0, sem_, sl=sl, j=j, oc=oc, kc=kc,
                                            need_m=need_m, need_t=need_t):
                                        e.wait_ge(sem_, t0)
                                        e.wait_ge(msem, need_m)
                                        if sl == 0:
                                            e.wait_ge(tsem, need_t)
                                        e.matmul(
                                            pXT[:, oc * T:(oc + 1) * T],
                                            sMph[:, sl % 2, j, oc * 128:(oc + 1) * 128],
                                            sCT[:, kc, :],
                                            start=(kc == 0 and oc % 2 == 0),
                                            stop=(kc == 127)).then_inc(sem_, 1)
                                    p.raw("tensor", fmm, 1)
                                else:
                                    p.tensor(lambda e, sl=sl, j=j, oc=oc, kc=kc: e.matmul(
                                        pXT[:, oc * T:(oc + 1) * T],
                                        sMph[:, sl % 2, j, oc * 128:(oc + 1) * 128],
                                        sCT[:, kc, :],
                                        start=(kc == 0 and oc % 2 == 0), stop=(kc == 127)),
                                        fuse=True)
                        if sl + 2 < NSL:
                            slab_dma(sl + 2)
                    for oc in range(8):
                        p.scalar(lambda e, oc=oc: e.activation(
                            sXtT[:, oc, :], pXT[:, oc * T:(oc + 1) * T], AF.Identity,
                            bias=sMb[:, oc:oc + 1]))
                    p.sync(lambda e, c=c: e.dma_start(
                        out=xtscr[c], in_=sXtT.rearrange("q a b -> q (a b)")))

        emit_l0mix()

        def emit_l0qkv():
            T = 256
            nchk = ntok // T
            with contextlib.ExitStack() as s2:
                sb2 = lambda name, shape, dt=F32: s2.enter_context(nc.sbuf_tensor(name, shape, dt))[:]
                sWqT = sb2("k_sWqT", [128, 8, D], BF16)
                sWkT = sb2("k_sWkT", [128, 8, D], BF16)
                sWvT = sb2("k_sWvT", [128, 8, D], BF16)
                sQb = sb2("k_sQb", [128, 8])
                sKb = sb2("k_sKb", [128, 8])
                sVb = sb2("k_sVb", [128, 8])
                sXt2 = sb2("k_sXt2", [128, 2, 8, T], BF16)
                sQo2 = sb2("k_sQo2", [128, 2, 8, T], BF16)
                pQa = s2.enter_context(nc.psum_tensor("k_pQa", [128, T], F32))[:]
                pQb = s2.enter_context(nc.psum_tensor("k_pQb", [128, T], F32))[:]
                pQ2 = [pQa, pQb]
                for w_in, sW in ((wqT_in, sWqT), (wkT_in, sWkT), (wvT_in, sWvT)):
                    w_r = w_in.rearrange("(kc q) o -> kc q o", q=128)
                    for kc in range(8):
                        p.gps_dma(lambda e, kc=kc, w_r=w_r, sW=sW: e.dma_start(
                            out=sW[:, kc, :], in_=w_r[kc]))
                for b_in_, sB in ((qb_in, sQb), (kb_in, sKb), (vb_in, sVb)):
                    p.sync(lambda e, b_in_=b_in_, sB=sB: e.dma_start(out=sB, in_=b_in_))
                q_r = q0_sh.rearrange("(oc p) (h t) -> p oc h t", p=128, h=2)
                k_r = k0_sh.rearrange("(oc p) (h t) -> p oc h t", p=128, h=2)
                v_r = v0_sh.rearrange("(oc p) (h t) -> p oc h t", p=128, h=2)
                mtq = {}
                qtq = {}

                def xt_dma(c):
                    qdep = qtq.get(c - 2, 0)
                    def f(e, t0, sem_, c=c, qdep=qdep):
                        e.wait_ge(sem_, t0)
                        if qdep:
                            e.wait_ge(qsem, qdep)
                        e.dma_start(out=sXt2[:, c % 2, :, :].rearrange("q a b -> q (a b)"),
                                    in_=xtscr[c]).then_inc(msem, 16)
                    p.raw("gpsimd", f, 0)
                    mcount[0] += 16
                    mtq[c] = mcount[0]

                xt_dma(0)
                xt_dma(1)
                gq = [0]
                wmark = {}
                for c in range(nchk):
                    for wi, (sW, sBv, out_r) in enumerate(
                            ((sWqT, sQb, q_r), (sWkT, sKb, k_r), (sWvT, sVb, v_r))):
                        gw = c * 3 + wi
                        for oc in range(8):
                            gg = gq[0]
                            vdep = max(0, vcount[0] - 1)
                            def fgrp(e, t0, sem_, c=c, oc=oc, sW=sW, gg=gg,
                                     nm=mtq[c], vdep=vdep):
                                e.wait_ge(msem, nm)
                                if vdep:
                                    e.wait_ge(vsem, vdep)
                                for kc in range(8):
                                    e.matmul(pQ2[gg % 2], sW[:, kc, oc * 128:(oc + 1) * 128],
                                             sXt2[:, c % 2, kc, :],
                                             start=(kc == 0), stop=(kc == 7)).then_inc(qsem, 1)
                            p.raw("tensor", fgrp, 0)
                            qcount[0] += 8
                            qd = qcount[0]
                            def fev(e, t0, sem_, oc=oc, sBv=sBv, gg=gg, qd=qd, gw=gw,
                                    wm=None if gw < 2 or oc > 0 else wmark[gw - 2]):
                                if wm is not None:
                                    e.wait_ge(sem_, p.resolve(wm))
                                e.wait_ge(qsem, qd)
                                e.activation(sQo2[:, gw % 2, oc, :], pQ2[gg % 2],
                                             AF.Identity,
                                             bias=sBv[:, oc:oc + 1]).then_inc(vsem, 1)
                            p.raw("scalar", fev, 0)
                            vcount[0] += 1
                            gq[0] += 1
                        vd = vcount[0]
                        def wrout(e, t0, sem_, c=c, out_r=out_r, vd=vd, gw=gw):
                            if t0 > 0:
                                e.wait_ge(sem_, t0)
                            e.wait_ge(vsem, vd)
                            with e.If_eq(p.ctx["b_par"], 0):
                                e.dma_start(out=out_r[:, :, 0, c * T:(c + 1) * T],
                                            in_=sQo2[:, gw % 2, :, :]).then_inc(sem_, 16)
                            with e.Else():
                                e.dma_start(out=out_r[:, :, 1, c * T:(c + 1) * T],
                                            in_=sQo2[:, gw % 2, :, :]).then_inc(sem_, 16)
                        p.raw("gpsimd", wrout, 16)
                        wmark[gw] = p.mark()
                    qtq[c] = qcount[0]
                    if c + 2 < nchk:
                        xt_dma(c + 2)

        emit_l0qkv()
        barrier()
        def emit_l0scan():
            SEG = 1024
            nseg = seq // SEG
            EPS = 1e-5
            nbh = 8
            with contextlib.ExitStack() as s2:
                sb2 = lambda name, shape, dt=F32: s2.enter_context(nc.sbuf_tensor(name, shape, dt))[:]
                sQT = sb2("s_sQT", [64, seq], BF16)
                sKT = sb2("s_sKT", [64, seq], BF16)
                sVT = sb2("s_sVT", [64, seq], BF16)
                sWgb = sb2("s_sWgb", [64, 64], BF16)
                sKvsU = sb2("s_sKvsU", [1, 1])
                sWgbU = sb2("s_sWgbU", [1, 1])
                sIdF = sb2("s_sIdF", [128, 128])
                sIdB = sb2("s_sIdB", [128, 128], BF16)
                sTriu = sb2("s_sTriu", [128, 128])
                sOnes64b = sb2("s_sOnes64b", [64, 1], BF16)
                sOnesF = sb2("s_sOnesF", [1, 64])
                sOneF = sb2("s_sOneF", [1, 1])
                sE = sb2("s_sE", [64, SEG], BF16)
                L = {nm: sb2("s_L_" + nm, [1, SEG]) for nm in
                     ["qk", "ksq", "vsq", "vkg", "qsum", "gcum", "m", "msh", "nrm",
                      "w", "ones"]}
                sLast = sb2("s_sLast", [1, 8])
                vc_last = sb2("s_vc_last", [64, 1])
                dkb = sb2("s_dkb", [64, SEG])
                w0b = sb2("s_w0b", [64, SEG])
                vcT = sb2("s_vcT", [64, SEG])
                k_tm8 = sb2("s_k_tm8", [128, 8, 64], BF16)
                v_tm8 = sb2("s_v_tm8", [128, 8, 64], BF16)
                Va8 = sb2("s_Va8", [128, 8, 64], BF16)
                VaT8 = sb2("s_VaT8", [64, 8, 128], BF16)
                PmT = sb2("s_PmT", [128, 128], BF16)
                S_sb = sb2("s_S_sb", [64, 64], BF16)
                sCols8 = sb2("s_sCols8", [128, 24])
                sY8 = sb2("s_sY8", [128, 8, 64])
                pA = s2.enter_context(nc.psum_tensor("s_pA", [64, 512], F32))[:]
                pTK = s2.enter_context(nc.psum_tensor("s_pTK", [128, 512], BF16))[:]
                pVaT = s2.enter_context(nc.psum_tensor("s_pVaT", [64, 1024], BF16))[:]
                pcols = s2.enter_context(nc.psum_tensor("s_pcols", [128, 24], F32))[:]
                pYb = s2.enter_context(nc.psum_tensor("s_pYb", [128, 64], F32))[:]
                pP = s2.enter_context(nc.psum_tensor("s_pP", [128, 128], F32))[:]
                pS64 = s2.enter_context(nc.psum_tensor("s_pS64", [64, 64], F32))[:]
                pVc8 = s2.enter_context(nc.psum_tensor("s_pVc8", [128, 512], F32))[:]

                p.sync(lambda e: e.dma_start(out=sIdF, in_=ident_in))
                p.sync(lambda e: e.dma_start(out=sTriu, in_=triu_in))
                p.gps_dma(lambda e: e.dma_start(out=sWgb, in_=wgT_in))
                p.vector(lambda e: e.tensor_copy(sIdB, sIdF))
                p.vector(lambda e: e.memset(sOnes64b, 1.0))
                p.vector(lambda e: e.memset(sOnesF, 1.0))
                p.vector(lambda e: e.memset(sOneF, 1.0))
                p.vector(lambda e: e.memset(L["ones"], 1.0))

                def lane_sum(dst, mov, width=SEG):
                    for s0 in range(0, width, 512):
                        p.tensor(lambda e, s0=s0, mov=mov: e.matmul(
                            pA[0:1, :], sOnes64b, mov[:, s0:s0 + 512], start=True, stop=True))
                        p.vector(lambda e, s0=s0, dst=dst: e.tensor_copy(
                            dst[:, s0:s0 + 512], pA[0:1, :]))

                for u in range(nbh):
                    def mk_ld(dst, src, u=u):
                        sv = src.rearrange("(h r) t -> h r t", h=2)
                        def f(par, sv=sv, dst=dst):
                            return lambda e: e.dma_start(
                                out=dst, in_=sv[par, u * 64:(u + 1) * 64, :])
                        return f
                    p.par_dma(mk_ld(sQT, q0_sh))
                    p.par_dma(mk_ld(sKT, k0_sh))
                    p.par_dma(mk_ld(sVT, v0_sh))
                    p.vector(lambda e: e.memset(S_sb, 0.0))
                    p.vector(lambda e: e.memset(vc_last, 0.0))
                    p.vector(lambda e: e.memset(sLast[:, 0:1], -3.0e38))
                    p.vector(lambda e: e.memset(sLast[:, 1:3], 0.0))

                    def mk_scal_ld(dst, src_d, u=u):
                        sv = src_d.rearrange("o (h u2) -> o h u2", h=2)
                        def f(par, sv=sv, dst=dst):
                            return lambda e: e.dma_start(
                                out=dst, in_=sv[0:1, par, u:u + 1])
                        return f
                    p.par_dma(mk_scal_ld(sKvsU, kvs_in))
                    p.par_dma(mk_scal_ld(sWgbU, wgb_in))

                    for g in range(nseg):
                        gs = slice(g * SEG, (g + 1) * SEG)
                        qTg, kTg, vTg = sQT[:, gs], sKT[:, gs], sVT[:, gs]
                        p.vector(lambda e, qTg=qTg, kTg=kTg: e.tensor_tensor(sE, qTg, kTg, ALU.mult))
                        lane_sum(L["qk"], sE)
                        p.scalar(lambda e, kTg=kTg: e.activation(sE, kTg, AF.Square))
                        lane_sum(L["ksq"], sE)
                        p.scalar(lambda e, vTg=vTg: e.activation(sE, vTg, AF.Square))
                        lane_sum(L["vsq"], sE)
                        lane_sum(L["qsum"], qTg)
                        for s0 in range(0, SEG, 512):
                            p.tensor(lambda e, s0=s0, kTg=kTg: e.matmul(
                                pA, sWgb, kTg[:, s0:s0 + 512], start=True, stop=True))
                            p.vector(lambda e, s0=s0, vTg=vTg: e.tensor_tensor(
                                sE[:, s0:s0 + 512], vTg[:, s0:s0 + 512], pA, ALU.mult))
                        lane_sum(L["vkg"], sE)
                        for nm in ("ksq", "vsq"):
                            p.scalar(lambda e, nm=nm: e.activation(L[nm], L[nm], AF.Sqrt))
                            p.vector(lambda e, nm=nm: e.tensor_scalar(L[nm], L[nm], 1e-12, None, ALU.max))
                            p.vector(lambda e, nm=nm: e.reciprocal(L[nm], L[nm]))
                        p.vector(lambda e: e.tensor_tensor(L["qk"], L["qk"], L["ksq"], ALU.mult))
                        p.vector(lambda e: e.tensor_tensor(L["ksq"], L["ksq"], L["vsq"], ALU.mult))
                        p.vector(lambda e: e.tensor_scalar(L["ksq"], L["ksq"], sKvsU, None, ALU.mult))
                        p.vector(lambda e: e.tensor_tensor(L["vkg"], L["vkg"], L["ksq"], ALU.mult))
                        p.vector(lambda e: e.tensor_scalar(L["vkg"], L["vkg"], sWgbU, None, ALU.add))
                        p.scalar(lambda e: e.activation(L["vkg"], L["vkg"], AF.Lrelu, alpha=0.01))
                        p.scalar(lambda e: e.activation(L["vkg"], L["vkg"], AF.Square))
                        p.vector(lambda e: e.tensor_scalar(L["vkg"], L["vkg"], EPS, None, ALU.add))
                        p.vector(lambda e: e.tensor_tensor_scan(
                            L["gcum"], L["ones"], L["vkg"], sLast[0:1, 2:3], ALU.mult, ALU.add))
                        p.vector(lambda e: e.tensor_copy(sLast[:, 2:3], L["gcum"][:, SEG - 1:SEG]))
                        p.vector(lambda e: e.tensor_tensor(L["vkg"], L["vkg"], L["ksq"], ALU.mult))
                        p.vector(lambda e: e.tensor_tensor_scan(
                            L["m"], L["qk"], L["qk"], sLast[0:1, 0:1], ALU.max, ALU.bypass))
                        p.vector(lambda e: e.tensor_copy(L["msh"][:, 1:SEG], L["m"][:, 0:SEG - 1]))
                        p.vector(lambda e: e.tensor_copy(L["msh"][:, 0:1], sLast[:, 0:1]))
                        p.vector(lambda e: e.tensor_copy(sLast[:, 0:1], L["m"][:, SEG - 1:SEG]))
                        p.vector(lambda e: e.tensor_tensor(L["msh"], L["msh"], L["m"], ALU.subtract))
                        p.scalar(lambda e: e.activation(L["msh"], L["msh"], AF.Exp))
                        p.vector(lambda e: e.tensor_tensor(L["qk"], L["qk"], L["m"], ALU.subtract))
                        p.scalar(lambda e: e.activation(L["qk"], L["qk"], AF.Exp))
                        p.vector(lambda e: e.tensor_tensor_scan(
                            L["nrm"], L["msh"], L["qk"], sLast[0:1, 1:2], ALU.mult, ALU.add))
                        p.vector(lambda e: e.tensor_copy(sLast[:, 1:2], L["nrm"][:, SEG - 1:SEG]))
                        p.vector(lambda e: e.tensor_scalar(L["nrm"], L["nrm"], EPS, None, ALU.add))
                        p.vector(lambda e: e.reciprocal(L["nrm"], L["nrm"]))
                        p.vector(lambda e: e.tensor_tensor(L["w"], L["qk"], L["nrm"], ALU.mult))
                        p.vector(lambda e: e.tensor_tensor(L["qsum"], L["qsum"], L["w"], ALU.mult))
                        p.vector(lambda e: e.tensor_tensor(L["qsum"], L["qsum"], L["nrm"], ALU.mult))
                        p.vector(lambda e: e.tensor_tensor(L["qk"], L["qk"], L["vsq"], ALU.mult))
                        p.vector(lambda e: e.tensor_scalar(L["gcum"], L["gcum"], EPS, None, ALU.add))
                        p.vector(lambda e: e.reciprocal(L["gcum"], L["gcum"]))
                        p.vector(lambda e: e.tensor_scalar(L["w"], L["w"], -1.0, 1.0, ALU.mult, ALU.add))
                        p.vector(lambda e: e.tensor_tensor(L["w"], L["w"], L["gcum"], ALU.mult))
                        for s0 in range(0, SEG, 512):
                            p.tensor(lambda e, s0=s0: e.matmul(
                                pA, sOnesF, L["msh"][:, s0:s0 + 512], start=True, stop=True))
                            p.vector(lambda e, s0=s0: e.tensor_copy(dkb[:, s0:s0 + 512], pA))
                        for s0 in range(0, SEG, 512):
                            p.tensor(lambda e, s0=s0: e.matmul(
                                pA, sOnesF, L["qk"][:, s0:s0 + 512], start=True, stop=True))
                            p.vector(lambda e, s0=s0: e.tensor_copy(w0b[:, s0:s0 + 512], pA))
                        p.vector(lambda e, vTg=vTg: e.tensor_tensor(w0b, vTg, w0b, ALU.mult))
                        p.vector(lambda e: e.tensor_tensor_scan(
                            vcT, dkb, w0b, vc_last, ALU.mult, ALU.add))
                        p.vector(lambda e: e.tensor_copy(vc_last, vcT[:, SEG - 1:SEG]))
                        for cb in range(8):
                            p.tensor(lambda e, g=g, cb=cb: e.transpose(
                                pTK[:, cb * 64:(cb + 1) * 64],
                                sKT[:, g * SEG + cb * 128:g * SEG + (cb + 1) * 128],
                                sIdB[0:64, 0:64]), fuse=(cb > 0))
                        p.vector(lambda e: e.tensor_copy(
                            k_tm8.rearrange("q a b -> q (a b)"), pTK))
                        for cb in range(8):
                            p.tensor(lambda e, g=g, cb=cb: e.transpose(
                                pTK[:, cb * 64:(cb + 1) * 64],
                                sVT[:, g * SEG + cb * 128:g * SEG + (cb + 1) * 128],
                                sIdB[0:64, 0:64]), fuse=(cb > 0))
                        p.vector(lambda e: e.tensor_copy(
                            v_tm8.rearrange("q a b -> q (a b)"), pTK))
                        for cb in range(8):
                            for j3, lane in ((0, "vkg"), (1, "w"), (2, "qsum")):
                                p.tensor(lambda e, cb=cb, j3=j3, lane=lane: e.matmul(
                                    pcols[:, cb * 3 + j3:cb * 3 + j3 + 1],
                                    L[lane][:, cb * 128:(cb + 1) * 128], sOneF,
                                    start=True, stop=True), fuse=(cb + j3 > 0))
                        p.vector(lambda e: e.tensor_copy(sCols8, pcols))
                        c3 = sCols8.rearrange("q (cb three) -> q cb three", three=3)
                        p.vector(lambda e: e.tensor_tensor(
                            Va8, v_tm8,
                            c3[:, :, 0:1].broadcast_to((128, 8, 64)), ALU.mult))
                        for cb in range(8):
                            p.tensor(lambda e, cb=cb: e.transpose(
                                pVaT[:, cb * 128:(cb + 1) * 128], Va8[:, cb, :], sIdB),
                                fuse=(cb > 0))
                        p.vector(lambda e: e.tensor_copy(
                            VaT8.rearrange("q a b -> q (a b)"), pVaT))
                        for cb in range(8):
                            p.tensor(lambda e, cb=cb: e.transpose(
                                pVc8[:, cb * 64:(cb + 1) * 64], vcT[:, cb * 128:(cb + 1) * 128],
                                sIdF[0:64, 0:64]), fuse=(cb > 0))
                        for cb in range(8):
                            c = g * 8 + cb
                            qTc = qTg[:, cb * 128:(cb + 1) * 128]
                            p.tensor(lambda e, qTc=qTc: e.matmul(
                                pYb, qTc, S_sb, start=True, stop=False))
                            p.tensor(lambda e, qTc=qTc, cb=cb: e.matmul(
                                pP, VaT8[:, cb, :], qTc, start=True, stop=True), fuse=True)
                            p.vector(lambda e: e.tensor_tensor(PmT, pP, sTriu, ALU.mult))
                            p.tensor(lambda e, cb=cb: e.matmul(
                                pYb, PmT, k_tm8[:, cb, :], start=False, stop=True))
                            p.tensor(lambda e, c=c, cb=cb: e.matmul(
                                pS64, Va8[:, cb, :], k_tm8[:, cb, :],
                                start=(c == 0), stop=(c == seq // 128 - 1)), fuse=True)
                            p.vector(lambda e: e.tensor_copy(S_sb, pS64))
                            p.vector(lambda e, cb=cb: e.tensor_scalar(
                                sY8[:, cb, :], pYb, c3[:, cb, 1:2], None, ALU.mult))
                            p.vector(lambda e, cb=cb: e.scalar_tensor_tensor(
                                sY8[:, cb, :], pVc8[:, cb * 64:(cb + 1) * 64],
                                c3[:, cb, 2:3], sY8[:, cb, :], ALU.mult, ALU.add))
                        yv = y_sh.rearrange("(cb q) (h cc2) -> q cb h cc2", q=128, h=2)
                        def ywr(par, u=u, g=g, yv=yv):
                            return lambda e: e.dma_start(
                                out=yv[:, g * 8:(g + 1) * 8, par, u * 64:(u + 1) * 64],
                                in_=sY8)
                        p.par_dma(ywr)
        emit_l0scan()
        barrier()

        def emit_l0out():
            with contextlib.ExitStack() as s2:
                sb2 = lambda name, shape, dt=F32: s2.enter_context(nc.sbuf_tensor(name, shape, dt))[:]
                sX = sb2("o_sX", [128, D])
                sY = sb2("o_sY", [128, D])
                sYb = sb2("o_sYb", [128, D], BF16)
                sYbT = sb2("o_sYbT", [128, 8, 128], BF16)
                sWoT = sb2("o_sWoT", [128, 8, D], BF16)
                sBo = sb2("o_sBo", [1, D])
                sBob = sb2("o_sBob", [1, D], BF16)
                sOnes = sb2("o_sOnes", [1, 128], BF16)
                sBiasTile = sb2("o_sBiasTile", [128, D])
                stat = sb2("o_stat", [128, 1])
                sSq = sb2("o_sSq", [128, D])
                pP = s2.enter_context(nc.psum_tensor("o_pP", [128, 1024], F32))[:]

                w_r = woT_in.rearrange("(kc q) o -> kc q o", q=128)
                for kc in range(8):
                    p.gps_dma(lambda e, kc=kc: e.dma_start(out=sWoT[:, kc, :], in_=w_r[kc]))
                p.sync(lambda e: e.dma_start(out=sBo, in_=wob_in))
                p.vector(lambda e: e.tensor_copy(sBob, sBo))
                p.vector(lambda e: e.memset(sOnes, 1.0))
                for half in range(2):
                    osl = slice(half * 512, (half + 1) * 512)
                    p.tensor(lambda e, osl=osl: e.matmul(
                        pP[:, osl], sOnes, sBob[:, osl], start=True, stop=True))
                p.vector(lambda e: e.tensor_copy(sBiasTile, pP))

                x_r = xres.rearrange("(c q) f -> c q f", q=128)
                for c in range(nch128):
                    p.sync(lambda e, c=c: e.dma_start(out=sX, in_=x_r[c]))
                    yv2 = y_sh.rearrange("(h r) f -> h r f", h=2)
                    def yld(par, c=c, yv2=yv2):
                        return lambda e: e.dma_start(
                            out=sY, in_=yv2[par, c * 128:(c + 1) * 128, :])
                    p.par_dma(yld)
                    p.scalar(lambda e: e.activation(sSq, sY, AF.Square, accum_out=stat))
                    p.scalar(lambda e: e.activation(stat, stat, AF.Sqrt))
                    p.vector(lambda e: e.tensor_scalar(stat, stat, 1e-12, None, ALU.max))
                    p.vector(lambda e: e.reciprocal(stat, stat))
                    p.vector(lambda e: e.tensor_scalar(sYb, sY, stat, None, ALU.mult))
                    emit_transpose_blocks(p, sYb, sYbT.rearrange("q a b -> q (a b)"), 8)
                    for half in range(2):
                        osl = slice(half * 512, (half + 1) * 512)
                        for kc in range(8):
                            p.tensor(lambda e, kc=kc, osl=osl: e.matmul(
                                pP[:, osl], sYbT[:, kc, :], sWoT[:, kc, osl],
                                start=(kc == 0), stop=(kc == 7)))
                    p.vector(lambda e: e.tensor_add(sX, sX, sBiasTile))
                    p.vector(lambda e: e.tensor_tensor(sX, pP, sX, ALU.add))
                    p.sync(lambda e, c=c: e.dma_start(out=x_r[c], in_=sX))

        emit_l0out()
        emit_ffn0(xres, xres, False, "B")
        es_f0.close()

        wgT = es.enter_context(nc.sbuf_tensor("r_wgT", [128, 8, IFF], BF16))[:]
        wuT = es.enter_context(nc.sbuf_tensor("r_wuT", [128, 8, IFF], BF16))[:]
        wdT = es.enter_context(nc.sbuf_tensor("r_wdT", [128, 22, D], BF16))[:]

        def emit_ffn1(dst_x, tag):
            OSEG = [(s, min(s + 512, IFF)) for s in range(0, IFF, 512)]
            with contextlib.ExitStack() as s2:
                x_r = dst_x.rearrange("(c q) f -> c q f", q=128)

                if tag == "C":
                    with contextlib.ExitStack() as s3:
                        sb3 = lambda name, shape, dt=F32: s3.enter_context(nc.sbuf_tensor(tag + name, shape, dt))[:]
                        sWrow = sb3("f1_sWrow", [128, IFF])
                        sWq = sb3("f1_sWq", [128, IFF], BF16)
                        qA = sb3("f1_qA", [128, IFF])
                        qB = sb3("f1_qB", [128, IFF])
                        qC = sb3("f1_qC", [128, IFF])
                        qD = sb3("f1_qD", [128, IFF])
                        dscrH = sb3("f1_dscrH", [128, IFF // 16])

                        def quant_w(w_ap, n_oc, kdim, wT):
                            w_r = w_ap.rearrange("(oc q) k -> oc q k", q=128)
                            for oc in range(n_oc):
                                p.sync(lambda e, oc=oc, w_r=w_r: e.dma_start(
                                    out=sWrow[:, 0:kdim], in_=w_r[oc]))
                                emit_quant_fp4(p, sWrow[:, 0:kdim], sWq[:, 0:kdim],
                                               qA[:, 0:kdim], qB[:, 0:kdim], qC[:, 0:kdim],
                                               qD[:, 0:kdim], dscrH[:, 0:kdim // 16])
                                for kc in range(kdim // 128):
                                    p.sync(lambda e, oc=oc, kc=kc, wT=wT: e.dma_start_transpose(
                                        wT[:, kc, oc * 128:(oc + 1) * 128],
                                        sWq[:, kc * 128:(kc + 1) * 128]), fuse=(kc > 0))

                        quant_w(wg1_in, 22, D, wgT)
                        quant_w(wu1_in, 22, D, wuT)
                        quant_w(wd1_in, 8, IFF, wdT)

                with contextlib.ExitStack() as s3:
                    sb3 = lambda name, shape, dt=F32: s3.enter_context(nc.sbuf_tensor(tag + name, shape, dt))[:]
                    sX = sb3("p1_sX", [128, D])
                    sU = sb3("p1_sU", [128, D])
                    sUd = sb3("p1_sUd", [128, D], BF16)
                    sUdT = sb3("p1_sUdT", [128, 8, 128], BF16)
                    qA = sb3("p1_qA", [128, D])
                    qB = sb3("p1_qB", [128, D])
                    qC = sb3("p1_qC", [128, D])
                    qD = sb3("p1_qD", [128, D])
                    dscr6 = sb3("p1_dscr6", [128, D // 16])
                    stat_a = sb3("p1_stat_a", [128, 1])
                    stat_b = sb3("p1_stat_b", [128, 1])
                    for c in range(nch128):
                        p.sync(lambda e, c=c: e.dma_start(out=sX, in_=x_r[c]))
                        emit_ln(p, sX.rearrange("q (n f) -> q n f", n=1),
                                sU.rearrange("q (n f) -> q n f", n=1),
                                stat_a, stat_b, qA[:, 0:D], 1)
                        emit_quant_fp4(p, sU, sUd, qA, qB, qC, qD, dscr6)
                        emit_transpose_blocks(p, sUd, sUdT.rearrange("q a b -> q (a b)"), 8)
                        p.sync(lambda e, c=c: e.dma_start(
                            out=uqscr[c], in_=sUdT.rearrange("q a b -> q (a b)")))

                with contextlib.ExitStack() as s3:
                    sb3 = lambda name, shape, dt=F32: s3.enter_context(nc.sbuf_tensor(tag + name, shape, dt))[:]
                    sUdT2 = sb3("p2_sUdT", [128, 2, 8, 128], BF16)
                    sGY = sb3("p2_sGY", [128, 2 * IFF])
                    pPa = s3.enter_context(nc.psum_tensor(tag + "p2_pPa", [128, 512], F32))[:]
                    pPb = s3.enter_context(nc.psum_tensor(tag + "p2_pPb", [128, 512], F32))[:]
                    pP2 = [pPa, pPb]

                    mtarg = {}
                    qtarg = {}

                    def uq_dma(c, qdep):
                        def f(e, t0, sem_, c=c, qdep=qdep):
                            e.wait_ge(sem_, t0)
                            if qdep:
                                e.wait_ge(qsem, qdep)
                            e.dma_start(out=sUdT2[:, c % 2, :, :].rearrange("q a b -> q (a b)"),
                                        in_=uqscr[c]).then_inc(msem, 16)
                        p.raw("gpsimd", f, 0)
                        mcount[0] += 16
                        mtarg[c] = mcount[0]

                    uq_dma(0, 0)
                    uq_dma(1, 0)
                    gidx_all = [0]
                    vtarg = {}
                    for c in range(nch128):
                        for si, (o0, o1) in enumerate(OSEG):
                            for wi, wT in enumerate((wgT, wuT)):
                                gg = gidx_all[0]
                                vdep = vtarg.get(gg - 2, 0)
                                def fgrp(e, t0, sem_, c=c, o0=o0, o1=o1, wT=wT,
                                         gg=gg, vdep=vdep, nm=mtarg[c]):
                                    e.wait_ge(msem, nm)
                                    if vdep:
                                        e.wait_ge(vsem, vdep)
                                    for kc in range(8):
                                        e.matmul(pP2[gg % 2][:, 0:o1 - o0],
                                                 sUdT2[:, c % 2, kc, :], wT[:, kc, o0:o1],
                                                 start=(kc == 0),
                                                 stop=(kc == 7)).then_inc(qsem, 1)
                                p.raw("tensor", fgrp, 0)
                                qcount[0] += 8
                                qtarg[gg] = qcount[0]
                                doff = wi * IFF + o0
                                first_of_chunk = (si == 0 and wi == 0)
                                def fev(e, t0, sem_, gg=gg, o0=o0, o1=o1, doff=doff,
                                        foc=first_of_chunk):
                                    if foc and t0 > 0:
                                        e.wait_ge(sem_, t0)
                                    e.wait_ge(qsem, qtarg[gg])
                                    e.tensor_copy(sGY[:, doff:doff + (o1 - o0)],
                                                  pP2[gg % 2][:, 0:o1 - o0]).then_inc(vsem, 1)
                                p.raw("vector", fev, 0)
                                vcount[0] += 1
                                vtarg[gg] = vcount[0]
                                gidx_all[0] += 1
                        vd = vcount[0]
                        def fst(e, t0, sem_, c=c, vd=vd):
                            if t0 > 0:
                                e.wait_ge(sem_, t0)
                            e.wait_ge(vsem, vd)
                            e.dma_start(out=gyscr[c], in_=sGY).then_inc(sem_, 16)
                        p.raw("sync", fst, 16)
                        if c + 2 < nch128:
                            uq_dma(c + 2, qtarg.get(gidx_all[0] - 1, 0))

                with contextlib.ExitStack() as s3:
                    sb3 = lambda name, shape, dt=F32: s3.enter_context(nc.sbuf_tensor(tag + name, shape, dt))[:]
                    sH = sb3("p3_sH", [128, 2 * IFF])
                    sHq = sb3("p3_sHq", [128, IFF], BF16)
                    sHqT = sb3("p3_sHqT", [128, 22, 128], BF16)
                    qA = sb3("p3_qA", [128, IFF])
                    qB = sb3("p3_qB", [128, IFF])
                    qC = sb3("p3_qC", [128, IFF])
                    qD = sH[:, IFF:2 * IFF]
                    dscrH = sb3("p3_dscrH", [128, IFF // 16])
                    pTT = s3.enter_context(nc.psum_tensor(tag + "p3_pTT", [128, 1024], BF16))[:]
                    for c in range(nch128):
                        p.sync(lambda e, c=c: e.dma_start(out=sH, in_=gyscr[c]))
                        p.scalar(lambda e: e.activation(sH[:, 0:IFF], sH[:, 0:IFF], AF.Silu))
                        p.vector(lambda e: e.tensor_mul(sH[:, 0:IFF], sH[:, 0:IFF], sH[:, IFF:2 * IFF]))
                        emit_quant_fp4(p, sH[:, 0:IFF], sHq, qA, qB, qC, qD, dscrH)
                        for grp in range(3):
                            nblk = min(8, 22 - grp * 8)
                            qd = qcount[0]
                            def ftr(e, t0, sem_, grp=grp, nblk=nblk, qd=qd):
                                e.wait_ge(sem_, t0)
                                for j in range(nblk):
                                    kc = grp * 8 + j
                                    e.transpose(pTT[:, j * 128:(j + 1) * 128],
                                                sHq[:, kc * 128:(kc + 1) * 128],
                                                sIdB2).then_inc(qsem, 1)
                            p.raw("tensor", ftr, 0)
                            qcount[0] += nblk
                            qd2 = qcount[0]
                            def fcp(e, t0, sem_, grp=grp, nblk=nblk, qd2=qd2):
                                if t0 > 0:
                                    e.wait_ge(sem_, t0)
                                e.wait_ge(qsem, qd2)
                                e.tensor_copy(
                                    sHqT.rearrange("q a b -> q (a b)")[:, grp * 1024:grp * 1024 + nblk * 128],
                                    pTT[:, 0:nblk * 128]).then_inc(sem_, 1)
                            p.raw("vector", fcp, 1)
                        p.sync(lambda e, c=c: e.dma_start(
                            out=hscr[c], in_=sHqT.rearrange("q a b -> q (a b)")))

                with contextlib.ExitStack() as s3:
                    sb3 = lambda name, shape, dt=F32: s3.enter_context(nc.sbuf_tensor(tag + name, shape, dt))[:]
                    sX = sb3("p4_sX", [128, D])
                    sHqT2 = sb3("p4_sHqT", [128, 2, 22, 128], BF16)
                    pP4 = s3.enter_context(nc.psum_tensor(tag + "p4_pP", [128, 1024], F32))[:]

                    mtarg4 = {}
                    qtarg4 = {}

                    def hq_dma(c):
                        qdep = qtarg4.get(c - 2, 0)
                        def f(e, t0, sem_, c=c, qdep=qdep):
                            e.wait_ge(sem_, t0)
                            if qdep:
                                e.wait_ge(qsem, qdep)
                            e.dma_start(out=sHqT2[:, c % 2, :, :].rearrange("q a b -> q (a b)"),
                                        in_=hscr[c]).then_inc(msem, 16)
                        p.raw("gpsimd", f, 0)
                        mcount[0] += 16
                        mtarg4[c] = mcount[0]

                    hq_dma(0)
                    hq_dma(1)
                    for c in range(nch128):
                        p.sync(lambda e, c=c: e.dma_start(out=sX, in_=x_r[c]))
                        def fgrp(e, t0, sem_, c=c, nm=mtarg4[c]):
                            if t0 > 0:
                                e.wait_ge(sem_, t0)
                            e.wait_ge(msem, nm)
                            for half in range(2):
                                osl = slice(half * 512, (half + 1) * 512)
                                for kc in range(22):
                                    e.matmul(pP4[:, osl], sHqT2[:, c % 2, kc, :],
                                             wdT[:, kc, osl], start=(kc == 0),
                                             stop=(kc == 21)).then_inc(qsem, 1)
                        p.raw("tensor", fgrp, 0)
                        qcount[0] += 44
                        qtarg4[c] = qcount[0]
                        qd = qcount[0]
                        def fcomb(e, t0, sem_, c=c, qd=qd):
                            if t0 > 0:
                                e.wait_ge(sem_, t0)
                            e.wait_ge(qsem, qd)
                            e.scalar_tensor_tensor(
                                sX, pP4, 0.5, sX, ALU.mult, ALU.add).then_inc(sem_, 1)
                        p.raw("vector", fcomb, 1)
                        p.sync(lambda e, c=c: e.dma_start(out=x_r[c], in_=sX))
                        if c + 2 < nch128:
                            hq_dma(c + 2)

        emit_ffn1(xres, "C")

        def emit_l1qkv():
            with contextlib.ExitStack() as s2:
                sb2 = lambda name, shape, dt=F32: s2.enter_context(nc.sbuf_tensor(name, shape, dt))[:]
                sX = sb2("a_sX", [128, D])
                sU = sb2("a_sU", [128, D])
                sUb = sb2("a_sUb", [128, D], BF16)
                sUbT = sb2("a_sUbT", [128, 8, 128], BF16)
                bA = sb2("a_bA", [128, D])
                stat_a = sb2("a_stat_a", [128, 1])
                stat_b = sb2("a_stat_b", [128, 1])
                sWqT = sb2("a_sWqT", [128, 8, D], BF16)
                sWkT = sb2("a_sWkT", [128, 8, D], BF16)
                sWvT = sb2("a_sWvT", [128, 8, D], BF16)
                sQTt = sb2("a_sQTt", [128, 8, 128], BF16)
                sKTt = sb2("a_sKTt", [128, 8, 128], BF16)
                sV = sb2("a_sV", [128, D], BF16)
                pF = s2.enter_context(nc.psum_tensor("a_pF", [128, 128], F32))[:]
                pV = s2.enter_context(nc.psum_tensor("a_pV", [128, 1024], F32))[:]

                for w_in, sW in ((waqT_in, sWqT), (wakT_in, sWkT), (wavT_in, sWvT)):
                    w_r = w_in.rearrange("(kc q) o -> kc q o", q=128)
                    for kc in range(8):
                        p.gps_dma(lambda e, kc=kc, w_r=w_r, sW=sW: e.dma_start(
                            out=sW[:, kc, :], in_=w_r[kc]))

                x_r = xres.rearrange("(c q) f -> c q f", q=128)
                q_r = q1_sh.rearrange("(oc p) (h t) -> p oc h t", p=128, h=2)
                k_r = k1_sh.rearrange("(oc p) (h t) -> p oc h t", p=128, h=2)
                v1v = v1_sh.rearrange("(h r) f -> h r f", h=2)
                for c in range(nch128):
                    p.sync(lambda e, c=c: e.dma_start(out=sX, in_=x_r[c]))
                    emit_ln(p, sX.rearrange("q (n f) -> q n f", n=1),
                            sU.rearrange("q (n f) -> q n f", n=1), stat_a, stat_b, bA, 1)
                    p.vector(lambda e: e.tensor_copy(sUb, sU))
                    emit_transpose_blocks(p, sUb, sUbT.rearrange("q a b -> q (a b)"), 8)
                    for sW, sOut in ((sWqT, sQTt), (sWkT, sKTt)):
                        for oc in range(8):
                            for kc in range(8):
                                p.tensor(lambda e, oc=oc, kc=kc, sW=sW: e.matmul(
                                    pF, sW[:, kc, oc * 128:(oc + 1) * 128], sUbT[:, kc, :],
                                    start=(kc == 0), stop=(kc == 7)))
                            p.vector(lambda e, oc=oc, sOut=sOut: e.tensor_copy(sOut[:, oc, :], pF))
                    for half in range(2):
                        osl = slice(half * 512, (half + 1) * 512)
                        for kc in range(8):
                            p.tensor(lambda e, kc=kc, osl=osl: e.matmul(
                                pV[:, osl], sUbT[:, kc, :], sWvT[:, kc, osl],
                                start=(kc == 0), stop=(kc == 7)))
                    p.vector(lambda e: e.tensor_copy(sV, pV))
                    def qwr(par, c=c, r=q_r, src=sQTt):
                        return lambda e: e.dma_start(
                            out=r[:, :, par, c * 128:(c + 1) * 128], in_=src)
                    def kwr(par, c=c, r=k_r, src=sKTt):
                        return lambda e: e.dma_start(
                            out=r[:, :, par, c * 128:(c + 1) * 128], in_=src)
                    def vwr(par, c=c):
                        return lambda e: e.dma_start(
                            out=v1v[par, c * 128:(c + 1) * 128, :], in_=sV)
                    p.par_dma(qwr)
                    p.par_dma(kwr)
                    p.par_dma(vwr)

        emit_l1qkv()
        barrier()

        def emit_l1flash():
            nbh = 8
            nsc = seq // 128
            nts = seq // 512
            GRP = 3
            ngrp = (nsc + GRP - 1) // GRP
            assert ngrp % 2 == 0, "pS parity scheme needs even group count"
            ecount = [0]
            with contextlib.ExitStack() as s2:
                sb2 = lambda name, shape, dt=F32: s2.enter_context(nc.sbuf_tensor(name, shape, dt))[:]
                sKT = sb2("fl_sKT", [128, seq], BF16)
                sQT = sb2("fl_sQT", [128, seq], BF16)
                sVext = sb2("fl_sVext", [128, nsc, 128], BF16)
                sP = sb2("fl_sP", [128, 2, GRP * 512], BF16)
                sOT = sb2("fl_sOT", [65, 512])
                sInv = sb2("fl_sInv", [1, 512])
                sOnes64 = sb2("fl_sOnes64", [1, 64])
                pSa = s2.enter_context(nc.psum_tensor("fl_pSa", [128, GRP * 512], F32))[:]
                pSb = s2.enter_context(nc.psum_tensor("fl_pSb", [128, GRP * 512], F32))[:]
                pO = s2.enter_context(nc.psum_tensor("fl_pO", [128, 512], F32))[:]
                pB = s2.enter_context(nc.psum_tensor("fl_pB", [64, 512], F32))[:]
                pS2 = [pSa, pSb]

                p.vector(lambda e: e.memset(sOnes64, 1.0))
                p.vector(lambda e: e.memset(sKT[64:128, :], 0.0))
                p.vector(lambda e: e.memset(sQT[64:128, :], 0.0))
                p.vector(lambda e: e.memset(
                    sVext.rearrange("q a b -> q (a b)"), 0.0))
                p.vector(lambda e: e.memset(sVext[:, :, 64:65], 1.0))
                v_r = v1_sh.rearrange("(c q) (h d2) -> q c h d2", q=128, h=2)
                for u in range(nbh):
                    def mk_ld(dst, src, u=u):
                        sv = src.rearrange("(h r) t -> h r t", h=2)
                        def f(par, sv=sv, dst=dst):
                            return lambda e: e.dma_start(
                                out=dst[0:64, :], in_=sv[par, u * 64:(u + 1) * 64, :])
                        return f
                    p.par_dma(mk_ld(sKT, k1_sh))
                    p.par_dma(mk_ld(sQT, q1_sh))
                    def vld(par, u=u):
                        return lambda e: e.dma_start(
                            out=sVext[:, :, 0:64],
                            in_=v_r[:, :, par, u * 64:(u + 1) * 64])
                    p.par_dma(vld)
                    for ts in range(nts):
                        tsl = slice(ts * 512, (ts + 1) * 512)
                        e_base = ecount[0]

                        def scores(g):
                            sc0 = g * GRP
                            n = min(GRP, nsc - sc0)
                            for i in range(n):
                                sc = sc0 + i
                                if i == 0:
                                    need_e = max(0, e_base + g - 1)
                                    def fsc(e, t0, sem_, sc=sc, g=g, tsl=tsl,
                                            need_e=need_e, gw=(g == 0)):
                                        if gw:
                                            e.wait_ge(sem_, t0)
                                        if need_e:
                                            e.wait_ge(esem, need_e)
                                        e.matmul(pS2[g % 2][:, 0:512],
                                                 sKT[:, sc * 128:(sc + 1) * 128],
                                                 sQT[:, tsl],
                                                 start=True, stop=True).then_inc(sem_, 1)
                                    p.raw("tensor", fsc, 1)
                                else:
                                    p.tensor(lambda e, sc=sc, i=i, g=g, tsl=tsl: e.matmul(
                                        pS2[g % 2][:, i * 512:(i + 1) * 512],
                                        sKT[:, sc * 128:(sc + 1) * 128], sQT[:, tsl],
                                        start=True, stop=True), fuse=True)
                            return n

                        def expg(g, n):
                            def fex(e, t0, sem_, g=g, n=n):
                                e.wait_ge(sem_, t0)
                                e.activation(sP[:, g % 2, 0:n * 512],
                                             pS2[g % 2][:, 0:n * 512], AF.Exp,
                                             scale=1.0 / 32.0).then_inc(esem, 1)
                            p.raw("scalar", fex, 0)
                            ecount[0] += 1

                        def pvg(g, n):
                            for i in range(n):
                                sc = g * GRP + i
                                if i == 0:
                                    need_e = e_base + g + 1
                                    def fpv(e, t0, sem_, sc=sc, g=g, need_e=need_e,
                                            gw=(g == 0)):
                                        if gw:
                                            e.wait_ge(sem_, t0)
                                        e.wait_ge(esem, need_e)
                                        e.matmul(pO, sVext[:, sc, :],
                                                 sP[:, g % 2, 0:512],
                                                 start=(sc == 0),
                                                 stop=(sc == nsc - 1)).then_inc(sem_, 1)
                                    p.raw("tensor", fpv, 1)
                                else:
                                    p.tensor(lambda e, sc=sc, i=i, g=g: e.matmul(
                                        pO, sVext[:, sc, :],
                                        sP[:, g % 2, i * 512:(i + 1) * 512],
                                        start=(sc == 0), stop=(sc == nsc - 1)),
                                        fuse=True)

                        nlist = [None] * ngrp
                        nlist[0] = scores(0)
                        expg(0, nlist[0])
                        for g in range(1, ngrp):
                            nlist[g] = scores(g)
                            expg(g, nlist[g])
                            pvg(g - 1, nlist[g - 1])
                        pvg(ngrp - 1, nlist[ngrp - 1])

                        p.vector(lambda e: e.tensor_copy(sOT, pO[0:65, :]))
                        p.vector(lambda e: e.reciprocal(sInv, sOT[64:65, :]))
                        p.tensor(lambda e: e.matmul(pB, sOnes64, sInv, start=True, stop=True))
                        p.vector(lambda e: e.tensor_tensor(sOT[0:64, :], sOT[0:64, :], pB, ALU.mult))
                        ov = ot_sh.rearrange("(h r) t -> h r t", h=2)
                        def owr(par, u=u, tsl=tsl, ov=ov):
                            return lambda e: e.dma_start(
                                out=ov[par, u * 64:(u + 1) * 64, tsl], in_=sOT[0:64, :])
                        p.par_dma(owr)

        emit_l1flash()
        barrier()

        def emit_l1out():
            with contextlib.ExitStack() as s2:
                sb2 = lambda name, shape, dt=F32: s2.enter_context(nc.sbuf_tensor(name, shape, dt))[:]
                sX = sb2("u_sX", [128, D])
                QT4 = min(ntok, 1024)
                sOTb = sb2("u_sOTb", [128, 8, QT4], BF16)
                sWuT = sb2("u_sWuT", [128, 8, D], BF16)
                sBu = sb2("u_sBu", [1, D])
                sBub = sb2("u_sBub", [1, D], BF16)
                sOnes = sb2("u_sOnes", [1, 128], BF16)
                sBiasTile = sb2("u_sBiasTile", [128, D])
                pP = s2.enter_context(nc.psum_tensor("u_pP", [128, 1024], F32))[:]

                otv = ot_sh.rearrange("r (h t) -> r h t", h=2)

                def load_quarter(qt):
                    for kc in range(8):
                        def otld(par, kc=kc, qt=qt):
                            return lambda e: e.dma_start(
                                out=sOTb[:, kc, :],
                                in_=otv[kc * 128:(kc + 1) * 128, par,
                                        qt * QT4:(qt + 1) * QT4])
                        p.par_dma(otld)
                w_r = wauT_in.rearrange("(kc q) o -> kc q o", q=128)
                for kc in range(8):
                    p.gps_dma(lambda e, kc=kc: e.dma_start(out=sWuT[:, kc, :], in_=w_r[kc]))
                p.sync(lambda e: e.dma_start(out=sBu, in_=bau_in))
                p.vector(lambda e: e.tensor_copy(sBub, sBu))
                p.vector(lambda e: e.memset(sOnes, 1.0))
                for half in range(2):
                    osl = slice(half * 512, (half + 1) * 512)
                    p.tensor(lambda e, osl=osl: e.matmul(
                        pP[:, osl], sOnes, sBub[:, osl], start=True, stop=True))
                p.vector(lambda e: e.tensor_copy(sBiasTile, pP))

                x_r = xres.rearrange("(c q) f -> c q f", q=128)
                for c in range(nch128):
                    if c % (QT4 // 128) == 0:
                        load_quarter(c // (QT4 // 128))
                    cq = c % (QT4 // 128)
                    p.sync(lambda e, c=c: e.dma_start(out=sX, in_=x_r[c]))
                    p.vector(lambda e: e.tensor_add(sX, sX, sBiasTile))
                    for half in range(2):
                        osl = slice(half * 512, (half + 1) * 512)
                        for kc in range(8):
                            p.tensor(lambda e, kc=kc, osl=osl, cq=cq: e.matmul(
                                pP[:, osl], sOTb[:, kc, cq * 128:(cq + 1) * 128], sWuT[:, kc, osl],
                                start=(kc == 0), stop=(kc == 7)))
                    p.vector(lambda e: e.tensor_tensor(sX, pP, sX, ALU.add))
                    p.sync(lambda e, c=c: e.dma_start(out=x_r[c], in_=sX))

        emit_l1out()
        emit_ffn1(xres, "D")
        with contextlib.ExitStack() as s2:
            sFin = s2.enter_context(nc.sbuf_tensor("sFin", [128, D], F32))[:]
            xi_r = xres.rearrange("(c q) f -> c q f", q=128)
            xo_r = x_out.rearrange("(c q) f -> c q f", q=128)
            for c in range(nch128):
                p.sync(lambda e, c=c: e.dma_start(out=sFin, in_=xi_r[c]))
                p.sync(lambda e, c=c: e.dma_start(out=xo_r[c], in_=sFin))

        p.emit(block, sem)
    return nc


_fused_cache = {}


def run_fused(x0, inp, ncore):
    B, L, Dm = x0.shape
    ntok = B * L // ncore
    if ntok not in _fused_cache:
        _fused_cache[ntok] = build_fused(ntok)
    nc = _fused_cache[ntok]
    tt = make_toeplitz(inp["L0_attn_spectral_basis"])
    rb = lambda b: np.ascontiguousarray(b.reshape(8, 128).T)
    tr = lambda w: np.ascontiguousarray(w.T)
    nonce = np.full((1, 128), np.random.randint(1, 1 << 28), np.int32)
    common = {
        "nonce_in": nonce,
        "w0_in": inp["L0_ffn_weight"], "b0_in": inp["L0_ffn_bias"],
        "tt_in": tt, "mphiT_in": tr(inp["L0_attn_M_phi_w"]),
        "mb_in": rb(inp["L0_attn_M_phi_b"]),
        "wqT_in": tr(inp["L0_attn_wq_w"]), "qb_in": rb(inp["L0_attn_wq_b"]),
        "wkT_in": tr(inp["L0_attn_wk_w"]), "kb_in": rb(inp["L0_attn_wk_b"]),
        "wvT_in": tr(inp["L0_attn_wv_w"]), "vb_in": rb(inp["L0_attn_wv_b"]),
        "wgT_in": np.ascontiguousarray(inp["L0_attn_wg_z_w"].reshape(64, 64).T),
        "kvs_in": np.ascontiguousarray(inp["L0_attn_kv_norm_scale"].reshape(1, 16)),
        "wgb_in": np.full((1, 16), float(np.asarray(inp["L0_attn_wg_z_b"]).reshape(-1)[0]), np.float32),
        "ident_in": np.eye(128, dtype=np.float32),
        "triu_in": np.triu(np.ones((128, 128), np.float32)),
        "woT_in": tr(inp["L0_attn_wo_w"]), "wob_in": inp["L0_attn_wo_b"].reshape(1, D),
        "wg1_in": inp["L1_ffn_w_gate"], "wu1_in": inp["L1_ffn_w_up"],
        "wd1_in": inp["L1_ffn_w_down"],
        "waqT_in": tr(inp["L1_attn_Wq"]), "wakT_in": tr(inp["L1_attn_Wk"]),
        "wavT_in": tr(inp["L1_attn_Wv"]), "wauT_in": tr(inp["L1_attn_Wu"]),
        "bau_in": inp["L1_attn_bu"].reshape(1, D),
    }
    xs = x0.reshape(ncore, ntok, Dm)
    in_maps = [{"x_in": np.ascontiguousarray(xs[c]), **common} for c in range(ncore)]
    res = run_bass_kernel_spmd(nc, in_maps, core_ids=list(range(ncore)))
    return np.stack([r["x_out"] for r in res.results]).reshape(B, L, Dm)
